# version 08
# speedup vs baseline: 1.5303x; 1.0647x over previous
import contextlib

import numpy as np
import ml_dtypes

import concourse.bass as bass
import concourse.mybir as mybir
from concourse.bass_utils import run_bass_kernel_spmd

F32 = mybir.dt.float32
BF16 = mybir.dt.bfloat16
AF = mybir.ActivationFunctionType
ALU = mybir.AluOpType
AX = mybir.AxisListType

NCORES = 8
TSEQ = 16384
T = TSEQ // NCORES
D = 1024
KC = D // 128
TT = 512
NTT = T // TT

ENGINES = ("sync", "vector", "scalar", "tensor", "gpsimd")


class Ctx:
    def __init__(self, nc):
        self.nc = nc
        self.es = contextlib.ExitStack()
        self.sems = {}
        self.cnt = {}

    def sem(self, name):
        if name in self.sems:
            return name
        h = self.es.enter_context(self.nc.semaphore(name))
        self.sems[name] = h
        self.cnt[name] = 0
        return name

    def sbuf(self, name, shape, dt):
        return self.es.enter_context(self.nc.sbuf_tensor(name, shape, dt))

    def psum(self, name, shape, dt=F32):
        return self.es.enter_context(self.nc.psum_tensor(name, shape, dt))

    @contextlib.contextmanager
    def scope(self):
        with contextlib.ExitStack() as es:
            class _S:
                def sbuf(_, name, shape, dt):
                    return es.enter_context(self.nc.sbuf_tensor(name, shape, dt))

                def psum(_, name, shape, dt=F32):
                    return es.enter_context(self.nc.psum_tensor(name, shape, dt))
            yield _S()

    def close(self):
        self.es.close()


class Ph:

    def __init__(self, ctx, name=""):
        self.ctx = ctx
        self.name = name
        self.q = {e: [] for e in ENGINES}

    def wait(self, eng, sem, val=None):
        v = self.ctx.cnt[sem] if val is None else val
        if v <= 0:
            return
        h = self.ctx.sems[sem]
        self.q[eng].append(lambda e: e.wait_ge(h, v))

    def op(self, eng, build, inc=None, n=1):
        if inc is not None:
            self.ctx.cnt[inc] += n
            h = self.ctx.sems[inc]
            self.q[eng].append(lambda e: build(e).then_inc(h, n))
            return self.ctx.cnt[inc]
        self.q[eng].append(lambda e: build(e))
        return None

    def dma(self, eng, sem, build):
        return self.op(eng, build, inc=sem, n=16)

    def run(self):
        with self.ctx.nc.Block(self.name or None) as blk:
            for e in ENGINES:
                if self.q[e]:
                    def body(eng, q=tuple(self.q[e])):
                        for f in q:
                            f(eng)
                    getattr(blk, e)(body)


def emit_transpose_in(ph, *, hbm_tm, ar, sem_ld, sem_tp, sem_ev, ps_t, tmp_tm):
    ctx = ph.ctx
    nb = len(ps_t)
    nt = 0
    for ti in range(T // 128):
        tm = tmp_tm[ti % 2]
        if ti >= 2:
            ph.wait("sync", sem_tp, ctx.cnt[sem_tp] - KC)
        ph.dma("sync", sem_ld[ti % 2],
               lambda e, tm=tm, ti=ti: e.dma_start(
                   out=tm[:], in_=hbm_tm[ti * 128:(ti + 1) * 128, :]))
        ph.wait("tensor", sem_ld[ti % 2])
        for kc in range(KC):
            ps = ps_t[nt % nb]
            if nt >= nb:
                ph.wait("tensor", sem_ev, ctx.cnt[sem_ev] - nb + 1)
            ph.op("tensor",
                  lambda e, tm=tm, kc=kc, ps=ps: e.transpose(
                      ps[:, 0:128], tm[:, kc * 128:(kc + 1) * 128], ctx.ident_f32[:]),
                  inc=sem_tp)
            ph.wait("vector", sem_tp)
            col = kc * T + ti * 128
            ph.op("vector",
                  lambda e, ps=ps, col=col: e.tensor_copy(
                      ar[:, col:col + 128], ps[:, 0:128]),
                  inc=sem_ev)
            nt += 1


def emit_transpose_out(ph, *, ar, hbm_tm, sem_tp, sem_ev, sem_st, ps_t, tmp_tm):
    ctx = ph.ctx
    nb = len(ps_t)
    nt = 0
    for ti in range(T // 128):
        tm = tmp_tm[ti % 2]
        if ti >= 2:
            ph.wait("vector", sem_st[ti % 2])
        for kc in range(KC):
            ps = ps_t[nt % nb]
            if nt >= nb:
                ph.wait("tensor", sem_ev, ctx.cnt[sem_ev] - nb + 1)
            col = kc * T + ti * 128
            ph.op("tensor",
                  lambda e, col=col, ps=ps: e.transpose(
                      ps[:, 0:128], ar[:, col:col + 128], ctx.ident_f32[:]),
                  inc=sem_tp)
            ph.wait("vector", sem_tp)
            ph.op("vector",
                  lambda e, ps=ps, tm=tm, kc=kc: e.tensor_copy(
                      tm[:, kc * 128:(kc + 1) * 128], ps[:, 0:128]),
                  inc=sem_ev)
            nt += 1
        ph.wait("sync", sem_ev)
        ph.dma("sync", sem_st[ti % 2],
               lambda e, tm=tm, ti=ti: e.dma_start(
                   out=hbm_tm[ti * 128:(ti + 1) * 128, :], in_=tm[:]))


def emit_postln(ph, *, ar, sq, ps_stat, ps_b, sem_mm, sem_ev, out_bf=None, eps_ap=None, stop_after=None):
    ctx = ph.ctx
    rows = ctx.rows
    for tt in range(NTT):
        c0 = tt * TT
        r_m = rows[0:1, 0 * TT:1 * TT]
        r_v = rows[0:1, 1 * TT:2 * TT]
        r_rs = rows[0:1, 2 * TT:3 * TT]
        r_mr = rows[0:1, 3 * TT:4 * TT]
        r_t = rows[0:1, 2 * TT:3 * TT]
        mm_sq0 = ctx.cnt[sem_mm]
        for kc in range(KC):
            sqt = sq[:, (kc % 2) * TT:(kc % 2 + 1) * TT]
            if kc >= 2:
                ph.wait("scalar", sem_mm, mm_sq0 + kc - 1)
            ph.op("scalar",
                  lambda e, kc=kc, c0=c0, sqt=sqt: e.activation(
                      sqt, ar[:, kc * T + c0: kc * T + c0 + TT],
                      AF.Square, bias=0.0, scale=1.0),
                  inc=sem_ev)
            ph.op("tensor",
                  lambda e, kc=kc, c0=c0: e.matmul(
                      ps_stat[0][0:1, :], ctx.ones_col_f32[:, 0:1],
                      ar[:, kc * T + c0: kc * T + c0 + TT],
                      start=(kc == 0), stop=(kc == KC - 1)))
            ph.wait("tensor", sem_ev)
            ph.op("tensor",
                  lambda e, kc=kc, sqt=sqt: e.matmul(
                      ps_stat[1][0:1, :], ctx.ones_col_f32[:, 0:1],
                      sqt, start=(kc == 0), stop=(kc == KC - 1)),
                  inc=sem_mm)
        ph.wait("scalar", sem_mm)
        if stop_after == "stats":
            continue
        ph.op("scalar", lambda e, r_m=r_m: e.mul(r_m, ps_stat[0][0:1, :], 1.0 / D))
        ph.op("scalar", lambda e, r_v=r_v: e.mul(r_v, ps_stat[1][0:1, :], 1.0 / D),
              inc=sem_ev)
        ph.wait("vector", sem_ev)
        ph.op("vector", lambda e, r_t=r_t, r_m=r_m: e.tensor_tensor(
            r_t, r_m, r_m, ALU.mult))
        ph.op("vector", lambda e, r_v=r_v, r_t=r_t: e.tensor_sub(r_v, r_v, r_t),
              inc=sem_ev)
        ph.wait("scalar", sem_ev)
        ph.op("scalar", lambda e, r_rs=r_rs, r_v=r_v: e.activation(
            r_rs, r_v, AF.Sqrt, bias=eps_ap, scale=1.0), inc=sem_ev)
        ph.wait("vector", sem_ev)
        ph.op("vector", lambda e, r_rs=r_rs: e.reciprocal(r_rs, r_rs))
        ph.op("vector", lambda e, r_mr=r_mr, r_m=r_m, r_rs=r_rs: e.tensor_tensor(
            r_mr, r_m, r_rs, ALU.mult), inc=sem_ev)
        ph.wait("tensor", sem_ev)
        if stop_after == "rows":
            continue
        ph.op("tensor", lambda e, r_rs=r_rs: e.matmul(
            ps_b[0][:, :], ctx.ones_row_f32[0:1, :], r_rs, start=True, stop=True))
        ph.op("tensor", lambda e, r_mr=r_mr: e.matmul(
            ps_b[1][:, :], ctx.ones_row_f32[0:1, :], r_mr, start=True, stop=True),
            inc=sem_mm)
        ph.wait("scalar", sem_mm)
        if stop_after == "bcast":
            continue
        ph.op("scalar", lambda e: e.copy(sq[:, 0:TT], ps_b[0][:, :]))
        ph.op("scalar", lambda e: e.copy(sq[:, TT:2 * TT], ps_b[1][:, :]), inc=sem_ev)
        ph.wait("vector", sem_ev)
        for kc in range(KC):
            c = kc * T + c0
            ph.op("vector",
                  lambda e, c=c: e.tensor_tensor(
                      ar[:, c:c + TT], ar[:, c:c + TT], sq[:, 0:TT], ALU.mult))
            ph.op("vector",
                  lambda e, c=c: e.tensor_tensor(
                      ar[:, c:c + TT], ar[:, c:c + TT], sq[:, TT:2 * TT], ALU.subtract),
                  inc=sem_ev)
            if out_bf is not None:
                ph.wait("scalar", sem_ev)
                ph.op("scalar",
                      lambda e, c=c: e.copy(out_bf[:, c:c + TT], ar[:, c:c + TT]),
                      inc=sem_ev)
        ph.wait("tensor", sem_ev)
        ph.wait("scalar", sem_ev)


def build_l1ffn(ctx, ar, w, sc):
    HC = 32
    s_ld = ctx.sem("ff_ld")
    s_cv = ctx.sem("ff_cv")
    s_cv1 = ctx.sem("ff_cv1")
    s_mm = ctx.sem("ff_mm")
    s_ev = ctx.sem("ff_ev")

    h_a = sc.sbuf("ff_ha", [128, HC * TT], BF16)
    h_b = sc.sbuf("ff_hb", [128, HC * TT], BF16)
    stg2 = sc.sbuf("ff_stg", [128, 2 * 4096], F32)
    stg = stg2[:, 0:4096]
    wb = sc.sbuf("ff_wb", [128, 2 * 4096], BF16)
    xq = sc.sbuf("ff_xq", [128, KC * TT], BF16)
    silu_t = [sc.sbuf(f"ff_silu{i}", [128, TT], BF16) for i in range(2)]
    out_t = [sc.sbuf(f"ff_out{i}", [128, TT], F32) for i in range(2)]
    psA = [sc.psum(f"ff_psA{i}", [128, TT]) for i in range(2)]
    psB = [sc.psum(f"ff_psB{i}", [128, TT]) for i in range(2)]

    sv = lambda base, ch: ctx.vecs[:, base + ch: base + ch + 1]

    for q in range(NTT):
        t0 = q * TT

        ph = Ph(ctx, f"ffA{q}")
        for kc in range(KC):
            ph.op("vector",
                  lambda e, kc=kc, t0=t0: e.tensor_copy(
                      xq[:, kc * TT:(kc + 1) * TT], ar[:, kc * T + t0: kc * T + t0 + TT]),
                  inc=(s_cv if kc == KC - 1 else None))
        ph.wait("tensor", s_cv)
        cv0 = ctx.cnt[s_cv]
        mm0 = ctx.cnt[s_mm]
        ev0 = ctx.cnt[s_ev]
        for hch in range(HC):
            par = hch % 2
            sb = stg2[:, par * 4096: par * 4096 + 2048]
            if hch >= 2:
                ph.wait("sync", s_cv, cv0 + hch - 1)
            ph.dma("sync", s_ld,
                   lambda e, hch=hch, sb=sb: e.dma_start(
                       out=sb[:, 0:1024].rearrange("p (kc n) -> p kc n", kc=8),
                       in_=w["w1T"][:, hch * 128:(hch + 1) * 128].rearrange(
                           "(kc p) n -> p kc n", p=128)))
            ph.dma("sync", s_ld,
                   lambda e, hch=hch, sb=sb: e.dma_start(
                       out=sb[:, 1024:2048].rearrange("p (kc n) -> p kc n", kc=8),
                       in_=w["w2T"][:, hch * 128:(hch + 1) * 128].rearrange(
                           "(kc p) n -> p kc n", p=128)))
            ph.wait("gpsimd", s_ld)
            if hch >= 2:
                ph.wait("gpsimd", s_mm, mm0 + 2 * (hch - 1))
            ph.op("gpsimd",
                  lambda e, par=par, sb=sb: e.tensor_copy(
                      wb[:, par * 2048: par * 2048 + 2048], sb[:, 0:2048]),
                  inc=s_cv)
            ph.wait("tensor", s_cv)
            if hch >= 2:
                ph.wait("tensor", s_ev, ev0 + 2 * (hch - 1))
            for kc in range(KC):
                ph.op("tensor",
                      lambda e, kc=kc, par=par: e.matmul(
                          psA[par][:], wb[:, par * 2048 + kc * 128: par * 2048 + (kc + 1) * 128],
                          xq[:, kc * TT:(kc + 1) * TT],
                          start=(kc == 0), stop=(kc == KC - 1)),
                      inc=(s_mm if kc == KC - 1 else None))
            for kc in range(KC):
                ph.op("tensor",
                      lambda e, kc=kc, par=par: e.matmul(
                          psB[par][:], wb[:, par * 2048 + 1024 + kc * 128: par * 2048 + 1024 + (kc + 1) * 128],
                          xq[:, kc * TT:(kc + 1) * TT],
                          start=(kc == 0), stop=(kc == KC - 1)),
                      inc=(s_mm if kc == KC - 1 else None))
            ph.wait("scalar", s_mm, mm0 + 2 * hch + 1)
            if hch >= 2:
                ph.wait("scalar", s_ev, ev0 + 2 * (hch - 1))
            ph.op("scalar",
                  lambda e, par=par, hch=hch: e.activation(
                      silu_t[par][:, :], psA[par][:], AF.Silu,
                      bias=0.0, scale=sv(w["s1c"], hch)),
                  inc=s_ev)
            ph.wait("vector", s_ev, ev0 + 2 * hch + 1)
            ph.wait("vector", s_mm, mm0 + 2 * hch + 2)
            ph.op("vector",
                  lambda e, par=par, hch=hch: e.scalar_tensor_tensor(
                      h_a[:, hch * TT:(hch + 1) * TT], psB[par][:],
                      sv(w["s2c"], hch), silu_t[par][:, :], ALU.mult, ALU.mult),
                  inc=s_ev)
        ph.run()

        for nm, wt, sb, src, dst in (("ffB", "h1T", "sh1c", h_a, h_b),
                                     ("ffC", "h2T", "sh2c", h_b, h_a)):
            ph = Ph(ctx, f"{nm}{q}")
            mm0 = ctx.cnt[s_mm]
            ev0 = ctx.cnt[s_ev]
            for nch in range(HC):
                par = nch % 2
                cs = s_cv if par == 0 else s_cv1
                ceng = "gpsimd" if par == 0 else "scalar"
                stgp = stg2[:, par * 4096:(par + 1) * 4096]
                if nch >= 2:
                    ph.wait("sync", cs, ctx.cnt[cs])
                ph.dma("sync", s_ld,
                       lambda e, nch=nch, wt=wt, stgp=stgp: e.dma_start(
                           out=stgp.rearrange("p (kc n) -> p kc n", kc=32),
                           in_=w[wt][:, nch * 128:(nch + 1) * 128].rearrange(
                               "(kc p) n -> p kc n", p=128)))
                ph.wait(ceng, s_ld)
                if nch >= 2:
                    ph.wait(ceng, s_mm, mm0 + nch - 1)
                if par == 0:
                    ph.op("gpsimd",
                          lambda e, par=par, stgp=stgp: e.tensor_copy(
                              wb[:, par * 4096: par * 4096 + 4096], stgp),
                          inc=cs)
                else:
                    ph.op("scalar",
                          lambda e, par=par, stgp=stgp: e.copy(
                              wb[:, par * 4096: par * 4096 + 4096], stgp),
                          inc=cs)
                ph.wait("tensor", cs)
                if nch >= 2:
                    ph.wait("tensor", s_ev, ev0 + nch - 1)
                for kch in range(HC):
                    ph.op("tensor",
                          lambda e, kch=kch, par=par, src=src: e.matmul(
                              psA[par][:], wb[:, par * 4096 + kch * 128: par * 4096 + (kch + 1) * 128],
                              src[:, kch * TT:(kch + 1) * TT],
                              start=(kch == 0), stop=(kch == HC - 1)),
                          inc=(s_mm if kch == HC - 1 else None))
                ph.wait("scalar", s_mm, mm0 + nch + 1)
                ph.op("scalar",
                      lambda e, par=par, nch=nch, dst=dst, sb=sb: e.activation(
                          dst[:, nch * TT:(nch + 1) * TT], psA[par][:], AF.Gelu,
                          bias=0.0, scale=sv(w[sb], nch)),
                      inc=s_ev)
            ph.run()

        ph = Ph(ctx, f"ffD{q}")
        cv0 = ctx.cnt[s_cv]
        mm0 = ctx.cnt[s_mm]
        ev0 = ctx.cnt[s_ev]
        for dch in range(KC):
            par = dch % 2
            sb = stg2[:, par * 4096:(par + 1) * 4096]
            if dch >= 2:
                ph.wait("sync", s_cv, cv0 + dch - 1)
            ph.dma("sync", s_ld,
                   lambda e, dch=dch, sb=sb: e.dma_start(
                       out=sb.rearrange("p (kc n) -> p kc n", kc=32),
                       in_=w["w3T"][:, dch * 128:(dch + 1) * 128].rearrange(
                           "(kc p) n -> p kc n", p=128)))
            ph.wait("gpsimd", s_ld)
            if dch >= 2:
                ph.wait("gpsimd", s_mm, mm0 + dch - 1)
            ph.op("gpsimd",
                  lambda e, par=par, sb=sb: e.tensor_copy(
                      wb[:, par * 4096: par * 4096 + 4096], sb),
                  inc=s_cv)
            ph.wait("tensor", s_cv)
            if dch >= 2:
                ph.wait("tensor", s_ev, ev0 + 2 * (dch - 1))
            for kch in range(HC):
                ph.op("tensor",
                      lambda e, kch=kch, par=par: e.matmul(
                          psA[par][:], wb[:, par * 4096 + kch * 128: par * 4096 + (kch + 1) * 128],
                          h_a[:, kch * TT:(kch + 1) * TT],
                          start=(kch == 0), stop=(kch == HC - 1)),
                      inc=(s_mm if kch == HC - 1 else None))
            ph.wait("scalar", s_mm, mm0 + dch + 1)
            if dch >= 2:
                ph.wait("scalar", s_ev, ev0 + 2 * (dch - 1))
            ph.op("scalar",
                  lambda e, par=par, dch=dch: e.activation(
                      out_t[par][:, :], psA[par][:], AF.Copy,
                      bias=0.0, scale=sv(w["s3c"], dch)),
                  inc=s_ev)
            ph.wait("vector", s_ev, ev0 + 2 * dch + 1)
            c = dch * T + t0
            ph.op("vector",
                  lambda e, par=par, c=c: e.tensor_add(
                      ar[:, c:c + TT], ar[:, c:c + TT], out_t[par][:, :]),
                  inc=s_ev)
        ph.run()

    ph = Ph(ctx, "ffLN")
    emit_postln(ph, ar=ar, sq=stg, ps_stat=[psA[0], psA[1]], ps_b=[psB[0], psB[1]],
                sem_mm=s_mm, sem_ev=s_ev,
                eps_ap=ctx.vecs[0:1, w["keps"]:w["keps"] + 1])
    ph.run()


def emit_load_w1024(ph, sc_hbm, stg, wb, s_ld, s_cv, nin=1024, nout=1024):
    kc = nin // 128
    half = nout // 2
    for h in range(2):
        n0 = h * half
        ph.dma("sync", s_ld,
               lambda e, n0=n0: e.dma_start(
                   out=stg[:, 0:kc * half].rearrange("p (k n) -> p k n", k=kc),
                   in_=sc_hbm[:, n0:n0 + half].rearrange("(k p) n -> p k n", p=128)))
        ph.wait("gpsimd", s_ld)
        for k in range(kc):
            ph.op("gpsimd",
                  lambda e, k=k, n0=n0: e.tensor_copy(
                      wb[:, k * nout + n0: k * nout + n0 + half],
                      stg[:, k * half:(k + 1) * half]),
                  inc=(s_cv if k == kc - 1 else None))


def build_l0attn(ctx, ar, w, cc, sc, upto="H"):
    nc = ctx.nc
    EPS = 1e-6
    s_ld = ctx.sem("a_ld")
    s_cv = ctx.sem("a_cv")
    s_mm = ctx.sem("a_mm")
    s_ev = ctx.sem("a_ev")
    s_tp = ctx.sem("a_tp")
    s_cc = ctx.sem("a_cc")
    s_gld = ctx.sem("a_gld")

    Qb = sc.sbuf("a_Q", [128, KC * T], BF16)
    Kb = sc.sbuf("a_K", [128, KC * T], BF16)
    Vb = sc.sbuf("a_V", [128, KC * T], BF16)
    stats = sc.sbuf("a_stats", [16, 3 * T], BF16)
    ecol = sc.sbuf("a_ecol", [16, 8], F32)
    dnm = sc.sbuf("a_dnm", [16, 2], F32)
    blkA = sc.sbuf("a_blkA", [128, 8 * 16], BF16)
    blkB = sc.sbuf("a_blkB", [128, 8 * 16], BF16)
    r1 = sc.sbuf("a_r1", [128, 16], F32)
    r1e = sc.sbuf("a_r1e", [128, 16], F32)
    r2 = sc.sbuf("a_r2", [128, 16], F32)
    r2e = sc.sbuf("a_r2e", [128, 16], F32)
    kvb = sc.sbuf("a_kvb", [128, 16 * 64], BF16)
    blkkv = sc.sbuf("a_blkkv", [128, 128], BF16)
    ktt = sc.sbuf("a_ktt", [128, 2 * 128], BF16)
    vtt = sc.sbuf("a_vtt", [128, 2 * 128], BF16)
    stg = sc.sbuf("a_stg", [128, 2048], F32)
    s8k = stg
    kvsb = stg[:, 0:1024]
    sia_sb = stg[:, 1024:1024 + TT]
    srow32 = stg[0:16, 1536:1536 + TT]
    wb = sc.sbuf("a_wb", [128, 8 * 512], BF16)

    psA = sc.psum("a_psA", [128, TT])
    psB = sc.psum("a_psB", [128, TT])
    psD = sc.psum("a_psD", [16, TT])
    psC = sc.psum("a_psC", [16, 16])
    pstK = sc.psum("a_pstK", [128, 128], BF16)
    pstV = sc.psum("a_pstV", [128, 128], BF16)
    kve = sc.psum("a_kve", [64, 64])
    kvo = sc.psum("a_kvo", [64, 64])

    def xcvt(kc):
        return s8k[:, kc * 256:(kc + 1) * 256].bitcast(BF16)

    S_SI, S_SC, S_SA = 0, 1, 2

    def srow(blk, tt=None):
        if tt is None:
            return stats[:, blk * T:(blk + 1) * T]
        return stats[:, blk * T + tt * TT: blk * T + (tt + 1) * TT]

    kon = ctx.vecs[:, w["kone"]:w["kone"] + 1]

    for pi, (wt, bcol, func, arena) in enumerate((
            ("WqT", "bq", AF.Sigmoid, Qb),
            ("WkT", "bk", AF.Sigmoid, Kb),
            ("WvT", "bv", AF.Identity, Vb))):
        ph = Ph(ctx, f"a_proj{pi}")
        for h in range(2):
            n0 = h * 512
            if h == 1:
                ph.wait("sync", s_mm)
            for qtr in range(2):
                if qtr == 1:
                    ph.wait("sync", s_cv)
                ph.dma("sync", s_ld,
                       lambda e, n0=n0, qtr=qtr, wt=wt: e.dma_start(
                           out=stg[:, 0:2048].rearrange("p (k n) -> p k n", k=8),
                           in_=w[wt][:, n0 + qtr * 256: n0 + qtr * 256 + 256].rearrange(
                               "(k p) n -> p k n", p=128)))
                ph.wait("gpsimd", s_ld)
                for k in range(KC):
                    ph.op("gpsimd",
                          lambda e, k=k, qtr=qtr: e.tensor_copy(
                              wb[:, k * 512 + qtr * 256: k * 512 + qtr * 256 + 256],
                              stg[:, k * 256:(k + 1) * 256]),
                          inc=(s_cv if k == KC - 1 else None))
            cvh = ctx.cnt[s_cv]
            mm0 = ctx.cnt[s_mm]
            nidx = 0
            for tt in range(NTT):
                ph.wait("vector", s_cv, cvh)
                ph.wait("vector", s_mm, mm0 + 4 * tt)
                for kc in range(KC):
                    ph.op("vector",
                          lambda e, kc=kc, tt=tt: e.tensor_copy(
                              xcvt(kc), ar[:, kc * T + tt * TT: kc * T + (tt + 1) * TT]),
                          inc=(s_cv if kc == KC - 1 else None))
                ph.wait("tensor", s_cv)
                for nch in range(4):
                    ps = psA if nidx % 2 == 0 else psB
                    ph.wait("tensor", s_ev, ctx.cnt[s_ev] - 1)
                    for kc in range(KC):
                        ph.op("tensor",
                              lambda e, kc=kc, nch=nch, ps=ps: e.matmul(
                                  ps[:], wb[:, kc * 512 + nch * 128: kc * 512 + (nch + 1) * 128],
                                  xcvt(kc), start=(kc == 0), stop=(kc == KC - 1)),
                              inc=(s_mm if kc == KC - 1 else None))
                    ph.wait("scalar", s_mm)
                    nabs = h * 4 + nch
                    ph.op("scalar",
                          lambda e, ps=ps, nabs=nabs, tt=tt, arena=arena, func=func, bcol=bcol: e.activation(
                              arena[:, nabs * T + tt * TT: nabs * T + (tt + 1) * TT],
                              ps[:], func, bias=ctx.vecs[:, w[bcol] + nabs: w[bcol] + nabs + 1],
                              scale=1.0),
                          inc=s_ev)
                    nidx += 1
        ph.run()

    if upto == "A":
        return (Qb, Kb, Vb, stats, r1, r2, kvb, dnm, ecol)
    ph = Ph(ctx, "a_sums")
    for kc in range(KC):
        ph.op("vector",
              lambda e, kc=kc: e.reduce_sum(r1[:, kc:kc + 1], Kb[:, kc * T:(kc + 1) * T], axis=AX.X))
        ph.op("vector",
              lambda e, kc=kc: e.reduce_sum(r1[:, 8 + kc:9 + kc], Qb[:, kc * T:(kc + 1) * T], axis=AX.X),
              inc=(s_ev if kc == KC - 1 else None))
    ph.wait("gpsimd", s_ev)
    ph.dma("gpsimd", s_gld, lambda e: e.dma_start(out=cc["r16_in"], in_=r1[:, :]))
    ph.wait("gpsimd", s_gld)
    ph.op("gpsimd", lambda e: e.collective_compute(
        "AllReduce", ALU.add, replica_groups=[list(range(NCORES))],
        ins=[cc["r16_in"].opt()], outs=[cc["r16_out"].opt()]), inc=s_cc)
    ph.wait("gpsimd", s_cc)
    ph.dma("gpsimd", s_gld, lambda e: e.dma_start(out=r1[:, :], in_=cc["r16_out"]))
    ph.wait("vector", s_gld)
    ph.op("vector", lambda e: e.tensor_scalar_add(
        r1e[:, :], r1[:, :], ctx.vecs[:, w["keps6"]:w["keps6"] + 1]))
    ph.op("vector", lambda e: e.memset(blkA[:, :], 0.0))
    ph.op("vector", lambda e: e.memset(blkB[:, :], 0.0), inc=s_ev)
    for kc in range(KC):
        ph.op("vector", lambda e, kc=kc: e.tensor_copy(
            blkA[0:64, kc * 16 + 2 * kc: kc * 16 + 2 * kc + 1], r1e[0:64, kc:kc + 1]))
        ph.op("vector", lambda e, kc=kc: e.tensor_copy(
            blkA[64:128, kc * 16 + 2 * kc + 1: kc * 16 + 2 * kc + 2], r1e[64:128, kc:kc + 1]))
        ph.op("vector", lambda e, kc=kc: e.tensor_copy(
            blkB[0:64, kc * 16 + 2 * kc: kc * 16 + 2 * kc + 1], r1e[0:64, 8 + kc:9 + kc]))
        ph.op("vector", lambda e, kc=kc: e.tensor_copy(
            blkB[64:128, kc * 16 + 2 * kc + 1: kc * 16 + 2 * kc + 2], r1e[64:128, 8 + kc:9 + kc]),
            inc=(s_ev if kc == KC - 1 else None))
    ph.wait("tensor", s_ev)
    for j, blk in enumerate((blkA, blkB)):
        for kc in range(KC):
            ph.op("tensor",
                  lambda e, blk=blk, kc=kc: e.matmul(
                      psC[0:16, 0:1], blk[:, kc * 16:(kc + 1) * 16],
                      ctx.ones_col_bf[:, 0:1], start=(kc == 0), stop=(kc == KC - 1)),
                  inc=(s_mm if kc == KC - 1 else None))
        ph.wait("scalar", s_mm)
        ph.op("scalar", lambda e, j=j: e.mul(ecol[:, j:j + 1], psC[0:16, 0:1], EPS),
              inc=s_ev)
        ph.wait("tensor", s_ev)
    ph.wait("scalar", s_ev)
    for j, (blk, rhs_a, sblk, ec) in enumerate(
            ((blkA, Qb, S_SI, 0), (blkB, Kb, S_SC, 1))):
        for tt in range(NTT):
            for kc in range(KC):
                ph.op("tensor",
                      lambda e, blk=blk, kc=kc, rhs_a=rhs_a, tt=tt: e.matmul(
                          psD[0:16, :], blk[:, kc * 16:(kc + 1) * 16],
                          rhs_a[:, kc * T + tt * TT: kc * T + (tt + 1) * TT],
                          start=(kc == 0), stop=(kc == KC - 1)),
                      inc=(s_mm if kc == KC - 1 else None))
            ph.wait("scalar", s_mm)
            ph.wait("scalar", s_ev)
            ph.op("scalar",
                  lambda e, ec=ec: e.activation(
                      srow32[0:16, :], psD[0:16, :], AF.Identity,
                      bias=ecol[0:16, ec:ec + 1], scale=1.0),
                  inc=s_ev)
            ph.wait("vector", s_ev)
            ph.op("vector",
                  lambda e: e.reciprocal(srow32[0:16, :], srow32[0:16, :]))
            ph.op("vector",
                  lambda e, sblk=sblk, tt=tt: e.tensor_copy(srow(sblk, tt), srow32[0:16, :]),
                  inc=s_ev)
            ph.wait("tensor", s_ev)
    ph.run()

    if upto == "B":
        return (Qb, Kb, Vb, stats, r1, r2, kvb, dnm, ecol)
    ph = Ph(ctx, "a_r2")
    for j, (arena, sblk, col0) in enumerate(((Kb, S_SC, 0), (Qb, S_SI, 8))):
        for kc in range(KC):
            for tt in range(NTT):
                ph.op("tensor",
                      lambda e, kc=kc, tt=tt, sblk=sblk: e.matmul(
                          psB[:], ctx.ind_bf[0:16, kc * 128:(kc + 1) * 128],
                          srow(sblk, tt), start=True, stop=True),
                      inc=s_mm)
                ph.wait("vector", s_mm)
                ph.op("vector",
                      lambda e, kc=kc, tt=tt, arena=arena: e.scalar_tensor_tensor(
                          s8k[:, tt * TT:(tt + 1) * TT],
                          arena[:, kc * T + tt * TT: kc * T + (tt + 1) * TT],
                          kon, psB[:], ALU.bypass, ALU.mult),
                      inc=s_ev)
                ph.wait("tensor", s_ev)
            ph.wait("vector", s_ev)
            ph.op("vector",
                  lambda e, kc=kc, col0=col0: e.reduce_sum(
                      r2[:, col0 + kc: col0 + kc + 1], s8k[:, :], axis=AX.X),
                  inc=s_ev)
    ph.wait("gpsimd", s_ev)
    ph.dma("gpsimd", s_gld, lambda e: e.dma_start(out=cc["r16_in"], in_=r2[:, :]))
    ph.wait("gpsimd", s_gld)
    ph.op("gpsimd", lambda e: e.collective_compute(
        "AllReduce", ALU.add, replica_groups=[list(range(NCORES))],
        ins=[cc["r16_in"].opt()], outs=[cc["r16_out"].opt()]), inc=s_cc)
    ph.wait("gpsimd", s_cc)
    ph.dma("gpsimd", s_gld, lambda e: e.dma_start(out=r2[:, :], in_=cc["r16_out"]))
    ph.run()

    if upto == "C":
        return (Qb, Kb, Vb, stats, r1, r2, kvb, dnm, ecol)
    ph = Ph(ctx, "a_cs")
    ph.wait("vector", s_gld)
    ph.op("vector", lambda e: e.tensor_scalar_add(
        r2e[:, :], r2[:, :], ctx.vecs[:, w["keps6"]:w["keps6"] + 1]))
    ph.op("vector", lambda e: e.memset(blkA[:, :], 0.0))
    ph.op("vector", lambda e: e.memset(blkB[:, :], 0.0), inc=s_ev)
    for kc in range(KC):
        ph.op("vector", lambda e, kc=kc: e.tensor_copy(
            blkA[0:64, kc * 16 + 2 * kc: kc * 16 + 2 * kc + 1], r2e[0:64, kc:kc + 1]))
        ph.op("vector", lambda e, kc=kc: e.tensor_copy(
            blkA[64:128, kc * 16 + 2 * kc + 1: kc * 16 + 2 * kc + 2], r2e[64:128, kc:kc + 1]))
        ph.op("vector", lambda e, kc=kc: e.tensor_copy(
            blkB[0:64, kc * 16 + 2 * kc: kc * 16 + 2 * kc + 1], r2e[0:64, 8 + kc:9 + kc]))
        ph.op("vector", lambda e, kc=kc: e.tensor_copy(
            blkB[64:128, kc * 16 + 2 * kc + 1: kc * 16 + 2 * kc + 2], r2e[64:128, 8 + kc:9 + kc]),
            inc=(s_ev if kc == KC - 1 else None))
    ph.wait("tensor", s_ev)
    for j, blk in enumerate((blkA, blkB)):
        for kc in range(KC):
            ph.op("tensor",
                  lambda e, blk=blk, kc=kc: e.matmul(
                      psC[0:16, 0:1], blk[:, kc * 16:(kc + 1) * 16],
                      ctx.ones_col_bf[:, 0:1], start=(kc == 0), stop=(kc == KC - 1)),
                  inc=(s_mm if kc == KC - 1 else None))
        ph.wait("scalar", s_mm)
        ph.op("scalar", lambda e, j=j: e.mul(ecol[:, 2 + j:3 + j], psC[0:16, 0:1], EPS),
              inc=s_ev)
        ph.wait("tensor", s_ev)
    ph.wait("scalar", s_ev)
    for j, (blk, rhs_a, sblk, ec) in enumerate(
            ((blkA, Qb, S_SA, 2), (blkB, Kb, S_SC, 3))):
        for tt in range(NTT):
            for kc in range(KC):
                ph.op("tensor",
                      lambda e, blk=blk, kc=kc, rhs_a=rhs_a, tt=tt: e.matmul(
                          psD[0:16, :], blk[:, kc * 16:(kc + 1) * 16],
                          rhs_a[:, kc * T + tt * TT: kc * T + (tt + 1) * TT],
                          start=(kc == 0), stop=(kc == KC - 1)),
                      inc=(s_mm if kc == KC - 1 else None))
            ph.wait("scalar", s_mm)
            if j == 0:
                ph.op("scalar",
                      lambda e, tt=tt: e.activation(
                          srow(S_SA, tt), psD[0:16, :], AF.Sigmoid,
                          bias=ecol[0:16, 2:3], scale=1.0),
                      inc=s_ev)
            else:
                ph.op("scalar",
                      lambda e, tt=tt: e.activation(
                          srow(S_SC, tt), psD[0:16, :], AF.Identity,
                          bias=ecol[0:16, 3:4], scale=1.0),
                      inc=s_ev)
                ph.wait("vector", s_ev)
                ph.op("vector",
                      lambda e, tt=tt: e.tensor_scalar(
                          srow(S_SC, tt), srow(S_SC, tt), 1.0, -1.0,
                          ALU.min, ALU.max),
                      inc=s_ev)
                ph.wait("scalar", s_ev)
                ph.op("scalar",
                      lambda e, tt=tt: e.activation(
                          srow(S_SC, tt), srow(S_SC, tt), AF.Exp,
                          bias=0.0, scale=1.0),
                      inc=s_ev)
            ph.wait("tensor", s_ev)
    ph.wait("vector", s_ev)
    ph.op("vector", lambda e: e.reduce_sum(dnm[:, 0:1], srow(S_SC), axis=AX.X),
          inc=s_ev)
    ph.wait("gpsimd", s_ev)
    ph.dma("gpsimd", s_gld, lambda e: e.dma_start(out=cc["d_in"], in_=dnm[0:16, 0:1]))
    ph.wait("gpsimd", s_gld)
    ph.op("gpsimd", lambda e: e.collective_compute(
        "AllReduce", ALU.add, replica_groups=[list(range(NCORES))],
        ins=[cc["d_in"].opt()], outs=[cc["d_out"].opt()]), inc=s_cc)
    ph.wait("gpsimd", s_cc)
    ph.dma("gpsimd", s_gld, lambda e: e.dma_start(out=dnm[0:16, 1:2], in_=cc["d_out"]))
    ph.wait("vector", s_gld)
    ph.op("vector", lambda e: e.reciprocal(dnm[:, 1:2], dnm[:, 1:2]), inc=s_ev)
    ph.wait("vector", s_ev)
    ph.op("vector", lambda e: e.tensor_scalar(
        srow(S_SC), srow(S_SC), dnm[0:16, 1:2], float(TSEQ), ALU.mult, ALU.mult))
    ph.op("vector", lambda e: e.tensor_tensor(
        srow(S_SI), srow(S_SI), srow(S_SA), ALU.mult), inc=s_ev)
    ph.run()

    if upto == "D":
        return (Qb, Kb, Vb, stats, r1, r2, kvb, dnm, ecol)
    ph = Ph(ctx, "a_kv")
    mm0 = ctx.cnt[s_mm]
    for kc in range(KC):
        for tt in range(NTT):
            ph.op("tensor",
                  lambda e, kc=kc, tt=tt: e.matmul(
                      psB[:], ctx.ind_bf[0:16, kc * 128:(kc + 1) * 128],
                      srow(S_SC, tt), start=True, stop=True),
                  inc=s_mm)
            ph.wait("vector", s_mm)
            ph.op("vector",
                  lambda e, kc=kc, tt=tt: e.scalar_tensor_tensor(
                      Vb[:, kc * T + tt * TT: kc * T + (tt + 1) * TT],
                      Vb[:, kc * T + tt * TT: kc * T + (tt + 1) * TT],
                      kon, psB[:], ALU.bypass, ALU.mult),
                  inc=s_ev)
            ph.wait("tensor", s_ev)
    tp0 = ctx.cnt[s_tp]
    ev0 = ctx.cnt[s_ev]
    mm0 = ctx.cnt[s_mm]
    for kc in range(KC):
        for ts in range(T // 128):
            par = ts % 2
            i = kc * (T // 128) + ts
            if i >= 1:
                ph.wait("tensor", s_ev, ev0 + 2 * i)
            if i >= 2:
                ph.wait("vector", s_mm, mm0 + 2 * (i - 1))
            ph.op("tensor",
                  lambda e, kc=kc, ts=ts: e.transpose(
                      pstK[:, 0:128], Kb[:, kc * T + ts * 128: kc * T + (ts + 1) * 128],
                      ctx.ident_bf[:]),
                  inc=s_tp)
            ph.op("tensor",
                  lambda e, kc=kc, ts=ts: e.transpose(
                      pstV[:, 0:128], Vb[:, kc * T + ts * 128: kc * T + (ts + 1) * 128],
                      ctx.ident_bf[:]),
                  inc=s_tp)
            ph.wait("vector", s_tp)
            ph.op("vector",
                  lambda e, par=par: e.tensor_copy(ktt[:, par * 128:(par + 1) * 128], pstK[:, 0:128]),
                  inc=s_ev)
            ph.op("vector",
                  lambda e, par=par: e.tensor_copy(vtt[:, par * 128:(par + 1) * 128], pstV[:, 0:128]),
                  inc=s_ev)
            ph.wait("tensor", s_ev)
            ph.op("tensor",
                  lambda e, par=par, ts=ts: e.matmul(
                      kve[0:64, 0:64], ktt[:, par * 128: par * 128 + 64],
                      vtt[:, par * 128: par * 128 + 64],
                      start=(ts == 0), stop=(ts == T // 128 - 1)),
                  inc=s_mm)
            ph.op("tensor",
                  lambda e, par=par, ts=ts: e.matmul(
                      kvo[0:64, 0:64], ktt[:, par * 128 + 64: par * 128 + 128],
                      vtt[:, par * 128 + 64: par * 128 + 128],
                      start=(ts == 0), stop=(ts == T // 128 - 1)),
                  inc=s_mm)
        ph.wait("scalar", s_mm)
        ph.op("scalar",
              lambda e, kc=kc: e.copy(kvsb[0:64, (2 * kc) * 64:(2 * kc + 1) * 64], kve[0:64, 0:64]))
        ph.op("scalar",
              lambda e, kc=kc: e.copy(kvsb[0:64, (2 * kc + 1) * 64:(2 * kc + 2) * 64], kvo[0:64, 0:64]),
              inc=s_ev)
        ph.wait("tensor", s_ev)
    ph.wait("gpsimd", s_ev)
    ph.dma("gpsimd", s_gld, lambda e: e.dma_start(out=cc["kv_in"], in_=kvsb[:, :]))
    ph.wait("gpsimd", s_gld)
    ph.op("gpsimd", lambda e: e.collective_compute(
        "AllReduce", ALU.add, replica_groups=[list(range(NCORES))],
        ins=[cc["kv_in"].opt()], outs=[cc["kv_out"].opt()]), inc=s_cc)
    ph.wait("gpsimd", s_cc)
    ph.dma("gpsimd", s_gld, lambda e: e.dma_start(out=kvsb[:, :], in_=cc["kv_out"]))
    ph.wait("vector", s_gld)
    ph.op("vector", lambda e: e.tensor_copy(kvb[:, :], kvsb[:, :]), inc=s_cv)
    ph.run()

    if upto == "E":
        return (Qb, Kb, Vb, stats, r1, r2, kvb, dnm, ecol)
    ph = Ph(ctx, "a_out")
    out_a = Kb
    for kc in range(KC):
        ph.wait("vector", s_mm)
        ph.op("vector", lambda e: e.memset(blkkv[:, :], 0.0))
        ph.op("vector", lambda e, kc=kc: e.tensor_copy(
            blkkv[0:64, 0:64], kvb[0:64, (2 * kc) * 64:(2 * kc + 1) * 64]))
        ph.op("vector", lambda e, kc=kc: e.tensor_copy(
            blkkv[64:128, 64:128], kvb[0:64, (2 * kc + 1) * 64:(2 * kc + 2) * 64]),
            inc=s_cv)
        ph.wait("tensor", s_cv)
        for tt in range(NTT):
            ph.op("tensor",
                  lambda e, kc=kc, tt=tt: e.matmul(
                      psA[:], blkkv[:, :], Qb[:, kc * T + tt * TT: kc * T + (tt + 1) * TT],
                      start=True, stop=True),
                  inc=s_mm)
            ph.op("tensor",
                  lambda e, kc=kc, tt=tt: e.matmul(
                      psB[:], ctx.ind_bf[0:16, kc * 128:(kc + 1) * 128],
                      srow(S_SI, tt), start=True, stop=True),
                  inc=s_mm)
            ph.wait("scalar", s_mm)
            ph.op("scalar", lambda e: e.copy(sia_sb[:, :], psB[:]), inc=s_ev)
            ph.wait("vector", s_ev)
            ph.op("vector",
                  lambda e, kc=kc, tt=tt: e.scalar_tensor_tensor(
                      out_a[:, kc * T + tt * TT: kc * T + (tt + 1) * TT],
                      psA[:], kon, sia_sb[:, :], ALU.bypass, ALU.mult),
                  inc=s_ev)
            ph.wait("tensor", s_ev)
    ph.run()

    if upto == "F":
        return (Qb, Kb, Vb, stats, r1, r2, kvb, dnm, ecol)
    ph = Ph(ctx, "a_wo")
    for h in range(2):
        n0 = h * 512
        if h == 1:
            ph.wait("sync", s_mm)
        for qtr in range(2):
            if qtr == 1 or h == 1:
                ph.wait("sync", s_cv)
            ph.dma("sync", s_ld,
                   lambda e, n0=n0, qtr=qtr: e.dma_start(
                       out=stg[:, 0:2048].rearrange("p (k n) -> p k n", k=8),
                       in_=w["WoT"][:, n0 + qtr * 256: n0 + qtr * 256 + 256].rearrange(
                           "(k p) n -> p k n", p=128)))
            ph.wait("gpsimd", s_ld)
            for k in range(KC):
                ph.op("gpsimd",
                      lambda e, k=k, qtr=qtr: e.tensor_copy(
                          wb[:, k * 512 + qtr * 256: k * 512 + qtr * 256 + 256],
                          stg[:, k * 256:(k + 1) * 256]),
                      inc=(s_cv if k == KC - 1 else None))
        ph.wait("tensor", s_cv)
        nidx = 0
        for nch in range(4):
            nabs = h * 4 + nch
            for tt in range(NTT):
                ps = psA if nidx % 2 == 0 else psB
                ph.wait("tensor", s_ev, ctx.cnt[s_ev] - 1)
                for kc in range(KC):
                    ph.op("tensor",
                          lambda e, kc=kc, nch=nch, tt=tt, ps=ps: e.matmul(
                              ps[:], wb[:, kc * 512 + nch * 128: kc * 512 + (nch + 1) * 128],
                              out_a[:, kc * T + tt * TT: kc * T + (tt + 1) * TT],
                              start=(kc == 0), stop=(kc == KC - 1)),
                          inc=(s_mm if kc == KC - 1 else None))
                ph.wait("vector", s_mm)
                c = nabs * T + tt * TT
                ph.op("vector",
                      lambda e, ps=ps, nabs=nabs, c=c: e.scalar_tensor_tensor(
                          ar[:, c:c + TT], ps[:],
                          ctx.vecs[:, w["bo"] + nabs: w["bo"] + nabs + 1],
                          ar[:, c:c + TT], ALU.add, ALU.add),
                      inc=s_ev)
                nidx += 1
    ph.run()

    if upto == "G":
        return (Qb, Kb, Vb, stats, r1, r2, kvb, dnm, ecol)
    ph = Ph(ctx, "a_ln")
    emit_postln(ph, ar=ar, sq=stg, ps_stat=[psA, psB], ps_b=[psA, psB],
                sem_mm=s_mm, sem_ev=s_ev,
                eps_ap=ctx.vecs[0:1, w["keps"]:w["keps"] + 1],
                stop_after={"H1": "stats", "H2": "rows", "H3": "bcast"}.get(upto))
    ph.run()
    return (Qb, Kb, Vb, stats, r1, r2, kvb, dnm, ecol)


def emit_rms_rows(ph, ctx, *, src_ar, rrow, sq, ps_stat, ps_b, s_mm, s_ev, eps_ap):
    for tt in range(NTT):
        c0 = tt * TT
        mm_sq0 = ctx.cnt[s_mm]
        for kc in range(KC):
            sqt = sq[:, (kc % 2) * TT:(kc % 2 + 1) * TT]
            if kc >= 2:
                ph.wait("scalar", s_mm, mm_sq0 + kc - 1)
            ph.op("scalar",
                  lambda e, kc=kc, c0=c0, sqt=sqt: e.activation(
                      sqt, src_ar[:, kc * T + c0: kc * T + c0 + TT],
                      AF.Square, bias=0.0, scale=1.0),
                  inc=s_ev)
            ph.wait("tensor", s_ev)
            ph.op("tensor",
                  lambda e, kc=kc, sqt=sqt: e.matmul(
                      ps_stat[0:1, :], ctx.ones_col_f32[:, 0:1],
                      sqt, start=(kc == 0), stop=(kc == KC - 1)),
                  inc=s_mm)
        ph.wait("scalar", s_mm)
        ph.op("scalar",
              lambda e, c0=c0: e.activation(
                  rrow[0:1, c0:c0 + TT], ps_stat[0:1, :], AF.Sqrt,
                  bias=eps_ap, scale=1.0 / D),
              inc=s_ev)
        ph.wait("vector", s_ev)
        ph.op("vector",
              lambda e, c0=c0: e.reciprocal(rrow[0:1, c0:c0 + TT], rrow[0:1, c0:c0 + TT]),
              inc=s_ev)
        ph.wait("tensor", s_ev)


def build_l1attn(ctx, ar, w, cc, sc, x_spill, dbg_hook=None):
    nc = ctx.nc
    s_ld = ctx.sem("m_ld")
    s_gld = ctx.sem("m_gld")
    s_cv = ctx.sem("m_cv")
    s_mm = ctx.sem("m_mm")
    s_ev = ctx.sem("m_ev")
    s_cc = ctx.sem("m_cc")

    ybf = sc.sbuf("m_ybf", [128, KC * T], BF16)
    mbuf = sc.sbuf("m_mbuf", [128, 8 * T], BF16)
    stg = sc.sbuf("m_stg", [128, 4096], F32)
    wb = sc.sbuf("m_wb", [128, 2 * 4096], BF16)
    rrow = sc.sbuf("m_rrow", [1, T], F32)
    bxs = sc.sbuf("m_bxs", [128, T], F32)
    yss = sc.sbuf("m_yss", [128, T], F32)
    scb = sc.sbuf("m_scb", [128, TT], F32)
    av = sc.sbuf("m_av", [128, 8 * 4], F32)
    lc = sc.sbuf("m_lc", [128, 8], F32)
    lg = sc.sbuf("m_lg", [128, 64], F32)

    psA = sc.psum("m_psA", [128, TT])
    psB = sc.psum("m_psB", [128, TT])
    psS = sc.psum("m_psS", [1, TT])

    atv = av[:, 0:8]
    a2k = av[:, 8:16]
    cac = av[:, 16:24]
    cse = av[:, 24:32]

    vcol = lambda base, i: ctx.vecs[:, base + i: base + i + 1]

    ph = Ph(ctx, "m_rms1")
    ph.op("scalar", lambda e: e.activation(
        atv, ctx.vecs[:, w["a"]:w["a"] + 8], AF.Tanh, bias=0.0, scale=1.0),
        inc=s_ev)
    ph.wait("vector", s_ev)
    ph.op("vector", lambda e: e.tensor_mul(a2k, atv, atv), inc=s_ev)
    for i in range(10):
        src_b, dst_b = (a2k, cac) if i % 2 == 0 else (cac, a2k)
        ph.wait("vector", s_ev)
        ph.op("vector", lambda e, s=src_b, d=dst_b: e.tensor_mul(d, s, s), inc=s_ev)
    ph.wait("vector", s_ev)
    ph.op("vector", lambda e: e.memset(cse, 0.0), inc=s_ev)
    ph.wait("vector", s_ev)
    ph.op("vector", lambda e: e.memset(cac, 0.0), inc=s_ev)
    emit_rms_rows(ph, ctx, src_ar=ar, rrow=rrow, sq=stg, ps_stat=psS, ps_b=None,
                  s_mm=s_mm, s_ev=s_ev, eps_ap=ctx.vecs[0:1, w["keps5"]:w["keps5"] + 1])
    ph.run()

    def emit_bx(ph, kc):
        for tt in range(NTT):
            ph.op("tensor",
                  lambda e, tt=tt: e.matmul(
                      psB[:], ctx.ones_row_f32[0:1, :], rrow[0:1, tt * TT:(tt + 1) * TT],
                      start=True, stop=True),
                  inc=s_mm)
            ph.wait("scalar", s_mm)
            ph.op("scalar", lambda e: e.copy(scb[:, :], psB[:]), inc=s_ev)
            ph.wait("vector", s_ev)
            ph.op("vector",
                  lambda e, kc=kc, tt=tt: e.scalar_tensor_tensor(
                      bxs[:, tt * TT:(tt + 1) * TT],
                      ar[:, kc * T + tt * TT: kc * T + (tt + 1) * TT],
                      vcol(w["n1wb"], kc), scb[:, :], ALU.mult, ALU.mult),
                  inc=s_ev)
            ph.wait("tensor", s_ev)

    for kc in range(KC):
        ph = Ph(ctx, f"m_scan1_{kc}")
        if kc > 0:
            ph.op("scalar",
                  lambda e, kc=kc: e.copy(lc[:, kc - 1:kc], yss[:, T - 1:T]),
                  inc=s_ev)
            ph.wait("vector", s_ev)
        emit_bx(ph, kc)
        ph.op("vector",
              lambda e, kc=kc: e.tensor_tensor_scan(
                  yss[:, :], atv[:, kc:kc + 1].broadcast_to([128, T]), bxs[:, :],
                  0.0, ALU.mult, ALU.add))
        ph.run()
    ph = Ph(ctx, "m_scan1f")
    ph.op("scalar",
          lambda e: e.copy(lc[:, KC - 1:KC], yss[:, T - 1:T]),
          inc=s_ev)
    ph.wait("gpsimd", s_ev)
    ph.dma("gpsimd", s_gld, lambda e: e.dma_start(out=cc["ag_in"], in_=lc[:, :]))
    ph.wait("gpsimd", s_gld)
    ph.op("gpsimd", lambda e: e.collective_compute(
        "AllGather", ALU.bypass, replica_groups=[list(range(NCORES))],
        ins=[cc["ag_in"].opt()], outs=[cc["ag_out"].opt()]), inc=s_cc)
    ph.wait("gpsimd", s_cc)
    ph.dma("gpsimd", s_gld,
           lambda e: e.dma_start(out=lg[:, :].rearrange("p (r k) -> p r k", r=NCORES),
                                 in_=cc["ag_out"].rearrange("(r p) k -> p r k", p=128)))
    ph.run()

    ph = Ph(ctx, "m_carry")
    ph.wait("vector", s_gld)
    for j in range(NCORES):
        ph.op("vector",
              lambda e, j=j: e.scalar_tensor_tensor(
                  cse, cac, ctx.coreoh[:, j:j + 1], cse, ALU.mult, ALU.add),
              inc=s_ev)
        ph.wait("vector", s_ev)
        if j < NCORES - 1:
            ph.op("vector",
                  lambda e, j=j: e.scalar_tensor_tensor(
                      cac, cac, ctx.vecs[:, w["kone"]:w["kone"] + 1], a2k,
                      ALU.bypass, ALU.mult),
                  inc=s_ev)
            ph.wait("vector", s_ev)
            ph.op("vector",
                  lambda e, j=j: e.tensor_add(cac, cac, lg[:, j * 8:(j + 1) * 8]),
                  inc=s_ev)
            ph.wait("vector", s_ev)
    ph.run()

    if dbg_hook is not None:
        dbg_hook({"d_lc": lc, "d_lg": lg, "d_av": av, "d_rrow": rrow,
                  "d_bxs": bxs, "d_yss": yss})
    for kc in range(KC):
        ph = Ph(ctx, f"m_scan2_{kc}")
        if kc > 0:
            ph.op("scalar",
                  lambda e, kc=kc: e.copy(ybf[:, (kc - 1) * T: kc * T], yss[:, :]),
                  inc=s_ev)
            ph.wait("vector", s_ev)
        emit_bx(ph, kc)
        ph.op("vector",
              lambda e, kc=kc: e.scalar_tensor_tensor(
                  bxs[:, 0:1], cse[:, kc:kc + 1], atv[:, kc:kc + 1],
                  bxs[:, 0:1], ALU.mult, ALU.add),
              inc=s_ev)
        ph.wait("vector", s_ev)
        ph.op("vector",
              lambda e, kc=kc: e.tensor_tensor_scan(
                  yss[:, :], atv[:, kc:kc + 1].broadcast_to([128, T]), bxs[:, :],
                  0.0, ALU.mult, ALU.add))
        ph.run()
    ph = Ph(ctx, "m_scan2f")
    ph.op("scalar",
          lambda e: e.copy(ybf[:, (KC - 1) * T: KC * T], yss[:, :]),
          inc=s_ev)
    ph.run()

    s_sp = ctx.sem("m_sp")
    ph = Ph(ctx, "m_proj")
    ph.dma("sync", s_sp, lambda e: e.dma_start(out=x_spill, in_=ar[:, :]))
    for h in range(2):
        n0 = h * 512
        if h == 1:
            ph.wait("sync", s_mm)
        for qtr in range(2):
            if qtr == 1 or h == 1:
                ph.wait("sync", s_cv)
            ph.dma("sync", s_ld,
                   lambda e, n0=n0, qtr=qtr: e.dma_start(
                       out=stg[:, 0:2048].rearrange("p (k n) -> p k n", k=8),
                       in_=w["out_wT"][:, n0 + qtr * 256: n0 + qtr * 256 + 256].rearrange(
                           "(k p) n -> p k n", p=128)))
            ph.wait("gpsimd", s_ld)
            for k in range(KC):
                ph.op("gpsimd",
                      lambda e, k=k, qtr=qtr: e.tensor_copy(
                          wb[:, k * 512 + qtr * 256: k * 512 + qtr * 256 + 256],
                          stg[:, k * 256:(k + 1) * 256]),
                      inc=(s_cv if k == KC - 1 else None))
        ph.wait("tensor", s_cv)
        if h == 0:
            ph.wait("vector", s_sp)
        for nch in range(4):
            nabs = h * 4 + nch
            for tt in range(NTT):
                ps = psA if (nabs * NTT + tt) % 2 == 0 else psB
                ph.wait("tensor", s_ev, ctx.cnt[s_ev] - 1)
                for kc in range(KC):
                    ph.op("tensor",
                          lambda e, kc=kc, nch=nch, tt=tt, ps=ps: e.matmul(
                              ps[:], wb[:, kc * 512 + nch * 128: kc * 512 + (nch + 1) * 128],
                              ybf[:, kc * T + tt * TT: kc * T + (tt + 1) * TT],
                              start=(kc == 0), stop=(kc == KC - 1)),
                          inc=(s_mm if kc == KC - 1 else None))
                ph.wait("vector", s_mm)
                c = nabs * T + tt * TT
                ph.op("vector",
                      lambda e, ps=ps, nabs=nabs, c=c: e.scalar_tensor_tensor(
                          ar[:, c:c + TT], ps[:], vcol(w["out_b"], nabs),
                          ar[:, c:c + TT], ALU.add, ALU.add),
                      inc=s_ev)
    ph.run()

    ph = Ph(ctx, "m_rms2")
    emit_rms_rows(ph, ctx, src_ar=ar, rrow=rrow, sq=stg, ps_stat=psS, ps_b=None,
                  s_mm=s_mm, s_ev=s_ev, eps_ap=ctx.vecs[0:1, w["keps5"]:w["keps5"] + 1])
    for kc in range(KC):
        for tt in range(NTT):
            ph.op("tensor",
                  lambda e, tt=tt: e.matmul(
                      psB[:], ctx.ones_row_f32[0:1, :], rrow[0:1, tt * TT:(tt + 1) * TT],
                      start=True, stop=True),
                  inc=s_mm)
            ph.wait("scalar", s_mm)
            ph.op("scalar", lambda e: e.copy(scb[:, :], psB[:]), inc=s_ev)
            ph.wait("vector", s_ev)
            ph.op("vector",
                  lambda e, kc=kc, tt=tt: e.scalar_tensor_tensor(
                      ybf[:, kc * T + tt * TT: kc * T + (tt + 1) * TT],
                      ar[:, kc * T + tt * TT: kc * T + (tt + 1) * TT],
                      vcol(w["n2w"], kc), scb[:, :], ALU.mult, ALU.mult),
                  inc=s_ev)
            ph.wait("tensor", s_ev)
    ph.run()

    HC = 32
    for grp in range(4):
        ph = Ph(ctx, f"m_mlpA{grp}")
        cv0 = ctx.cnt[s_cv]
        for hh in range(8):
            hch = grp * 8 + hh
            par = hh % 2
            if hh >= 1:
                ph.wait("sync", s_cv, cv0 + hh)
            ph.dma("sync", s_ld,
                   lambda e, hch=hch: e.dma_start(
                       out=stg[:, 0:1024].rearrange("p (kc n) -> p kc n", kc=8),
                       in_=w["w2T"][:, hch * 128:(hch + 1) * 128].rearrange(
                           "(kc p) n -> p kc n", p=128)))
            ph.dma("sync", s_ld,
                   lambda e, hch=hch: e.dma_start(
                       out=stg[:, 1024:2048].rearrange("p (kc n) -> p kc n", kc=8),
                       in_=w["w1T"][:, hch * 128:(hch + 1) * 128].rearrange(
                           "(kc p) n -> p kc n", p=128)))
            ph.wait("gpsimd", s_ld)
            if hh >= 2:
                ph.wait("gpsimd", s_mm, ctx.cnt[s_mm] - NTT)
            ph.op("gpsimd",
                  lambda e, par=par: e.tensor_copy(
                      wb[:, par * 2048: par * 2048 + 2048], stg[:, 0:2048]),
                  inc=s_cv)
            ph.wait("tensor", s_cv)
            for tt in range(NTT):
                ph.wait("tensor", s_ev, ctx.cnt[s_ev] - 1)
                for kc in range(KC):
                    ph.op("tensor",
                          lambda e, kc=kc, tt=tt, par=par: e.matmul(
                              psA[:], wb[:, par * 2048 + kc * 128: par * 2048 + (kc + 1) * 128],
                              ybf[:, kc * T + tt * TT: kc * T + (tt + 1) * TT],
                              start=(kc == 0), stop=(kc == KC - 1)))
                for kc in range(KC):
                    ph.op("tensor",
                          lambda e, kc=kc, tt=tt, par=par: e.matmul(
                              psB[:], wb[:, par * 2048 + 1024 + kc * 128: par * 2048 + 1024 + (kc + 1) * 128],
                              ybf[:, kc * T + tt * TT: kc * T + (tt + 1) * TT],
                              start=(kc == 0), stop=(kc == KC - 1)),
                          inc=(s_mm if kc == KC - 1 else None))
                ph.wait("scalar", s_mm)
                ph.op("scalar",
                      lambda e, hch=hch: e.activation(
                          scb[:, :], psA[:], AF.Silu, bias=vcol(w["w2b"], hch), scale=1.0),
                      inc=s_ev)
                ph.wait("vector", s_ev)
                ph.op("vector",
                      lambda e, hh=hh, tt=tt, hch=hch: e.scalar_tensor_tensor(
                          mbuf[:, hh * T + tt * TT: hh * T + (tt + 1) * TT],
                          psB[:], vcol(w["w1b"], hch), scb[:, :], ALU.add, ALU.mult),
                      inc=s_ev)
        ph.run()

        ph = Ph(ctx, f"m_mlpB{grp}")
        cv0 = ctx.cnt[s_cv]
        for dch in range(KC):
            par = dch % 2
            if dch >= 1:
                ph.wait("sync", s_cv, cv0 + dch)
            ph.dma("sync", s_ld,
                   lambda e, grp=grp, dch=dch: e.dma_start(
                       out=stg[:, 0:1024].rearrange("p (kc n) -> p kc n", kc=8),
                       in_=w["w3T"][grp * 1024:(grp + 1) * 1024, dch * 128:(dch + 1) * 128].rearrange(
                           "(kc p) n -> p kc n", p=128)))
            ph.wait("gpsimd", s_ld)
            if dch >= 2:
                ph.wait("gpsimd", s_mm, ctx.cnt[s_mm] - NTT)
            ph.op("gpsimd",
                  lambda e, par=par: e.tensor_copy(
                      wb[:, par * 1024: par * 1024 + 1024], stg[:, 0:1024]),
                  inc=s_cv)
            ph.wait("tensor", s_cv)
            for tt in range(NTT):
                ps = psA if (dch * NTT + tt) % 2 == 0 else psB
                ph.wait("tensor", s_ev, ctx.cnt[s_ev] - 1)
                for kh in range(8):
                    ph.op("tensor",
                          lambda e, kh=kh, tt=tt, par=par, ps=ps: e.matmul(
                              ps[:], wb[:, par * 1024 + kh * 128: par * 1024 + (kh + 1) * 128],
                              mbuf[:, kh * T + tt * TT: kh * T + (tt + 1) * TT],
                              start=(kh == 0), stop=(kh == 7)),
                          inc=(s_mm if kh == 7 else None))
                ph.wait("vector", s_mm)
                c = dch * T + tt * TT
                bcol = w["w3b"] if grp == 0 else w["kzero"]
                ph.op("vector",
                      lambda e, ps=ps, dch=dch, c=c, bcol=bcol: e.scalar_tensor_tensor(
                          ar[:, c:c + TT], ps[:], vcol(bcol, dch if bcol == w["w3b"] else 0),
                          ar[:, c:c + TT], ALU.add, ALU.add),
                      inc=s_ev)
        ph.run()

    ph = Ph(ctx, "m_fin")
    for kc in range(KC):
        for hf in range(2):
            ph.wait("sync", s_ev)
            ph.dma("sync", s_ld,
                   lambda e, kc=kc, hf=hf: e.dma_start(
                       out=stg[:, 0:1024],
                       in_=x_spill[:, kc * T + hf * 1024: kc * T + (hf + 1) * 1024]))
            ph.wait("vector", s_ld)
            for tt in range(2):
                c = kc * T + hf * 1024 + tt * TT
                ph.op("vector",
                      lambda e, c=c, tt=tt: e.tensor_add(
                          ar[:, c:c + TT], ar[:, c:c + TT], stg[:, tt * TT:(tt + 1) * TT]),
                      inc=s_ev)
    ph.run()

    ph = Ph(ctx, "m_ln")
    emit_postln(ph, ar=ar, sq=stg, ps_stat=[psA, psB], ps_b=[psA, psB],
                sem_mm=s_mm, sem_ev=s_ev,
                eps_ap=ctx.vecs[0:1, w["keps"]:w["keps"] + 1])
    ph.run()


def l1attn_weights_host(inp, vb):
    w = {
        "out_wT": _t(inp["L1_attn_out_w"]),
        "w1T": _t(inp["L1_attn_w1_w"]),
        "w2T": _t(inp["L1_attn_w2_w"]),
        "w3T": _t(inp["L1_attn_w3_w"]),
    }
    n1wb = np.asarray(inp["L1_attn_n1_w"], np.float32) * np.asarray(inp["L1_attn_b"], np.float32)
    idx = {
        "a": vb.add("m_a", _chunkvec(inp["L1_attn_a"])),
        "n1wb": vb.add("m_n1wb", _chunkvec(n1wb)),
        "n2w": vb.add("m_n2w", _chunkvec(inp["L1_attn_n2_w"])),
        "out_b": vb.add("m_outb", _chunkvec(inp["L1_attn_out_b"])),
        "w1b": vb.add("m_w1b", _chunkvec(inp["L1_attn_w1_b"])),
        "w2b": vb.add("m_w2b", _chunkvec(inp["L1_attn_w2_b"])),
        "w3b": vb.add("m_w3b", _chunkvec(inp["L1_attn_w3_b"])),
        "kzero": vb.add("k_zero", np.zeros((128, 1), np.float32)),
        "kone": vb.add("k_one1", np.full((128, 1), 1.0, np.float32)),
        "keps5": vb.add("k_eps5", np.full((128, 1), 1e-5, np.float32)),
        "keps": vb.add("k_eps_ln1", np.full((128, 1), 1e-5, np.float32)),
    }
    return w, idx


def make_l1_cc(nc):
    return {
        "ag_in": nc.dram_tensor("cc_agi", [128, 8], F32).ap(),
        "ag_out": nc.dram_tensor("cc_ago", [NCORES * 128, 8], F32).ap(),
    }


def build_l1attn_program(nblob, vec_idx, debug=False):
    nc = bass.Bass("TRN2", target_bir_lowering=False, detect_race_conditions=False)
    ctx = Ctx(nc)
    x_in = nc.dram_tensor("x_in", [T, D], F32, kind="ExternalInput").ap()
    y_out = nc.dram_tensor("y_out", [T, D], F32, kind="ExternalOutput").ap()
    consts = nc.dram_tensor("consts", [130, 128], F32, kind="ExternalInput").ap()
    coreoh = nc.dram_tensor("coreoh", [128, 8], F32, kind="ExternalInput").ap()
    vecblob = nc.dram_tensor("vecblob", [128, nblob], F32, kind="ExternalInput").ap()
    w = {"out_wT": nc.dram_tensor("out_wT", [1024, 1024], F32, kind="ExternalInput").ap(),
         "w1T": nc.dram_tensor("w1T", [1024, 4096], F32, kind="ExternalInput").ap(),
         "w2T": nc.dram_tensor("w2T", [1024, 4096], F32, kind="ExternalInput").ap(),
         "w3T": nc.dram_tensor("w3T", [4096, 1024], F32, kind="ExternalInput").ap()}
    w.update(vec_idx)
    cc = make_l1_cc(nc)
    x_spill = nc.dram_tensor("x_spill", [128, KC * T], F32).ap()

    common_setup(ctx, nblob)
    ctx.coreoh = ctx.sbuf("c_coreoh", [128, 8], F32)
    ar = ctx.sbuf("ar", [128, KC * T], F32)

    s_ld = [ctx.sem("g_ld0"), ctx.sem("g_ld1")]
    s_cst = ctx.sem("g_cst")
    s_tp = ctx.sem("g_tp")
    s_ev = ctx.sem("g_ev")
    s_st = [ctx.sem("g_st0"), ctx.sem("g_st1")]

    with ctx.scope() as sc:
        tmp_tm = [sc.sbuf(f"tin{i}", [128, D], F32) for i in range(2)]
        ps_t = [sc.psum(f"g_ps{i}", [128, TT]) for i in range(2)]
        ph = Ph(ctx, "in")
        emit_load_consts(ph, ctx, consts, None, vecblob, s_cst)
        ph.dma("sync", s_cst, lambda e: e.dma_start(out=ctx.coreoh[:], in_=coreoh))
        ph.wait("vector", s_cst)
        emit_transpose_in(ph, hbm_tm=x_in, ar=ar, sem_ld=s_ld, sem_tp=s_tp,
                          sem_ev=s_ev, ps_t=ps_t, tmp_tm=tmp_tm)
        ph.run()

    hook = None
    if debug:
        shapes = {"d_lc": [128, 8], "d_lg": [128, 64], "d_av": [128, 32], "d_rrow": [1, T],
                  "d_bxs": [128, T], "d_yss": [128, T]}
        douts = {nm: nc.dram_tensor(nm, shp, F32, kind="ExternalOutput").ap()
                 for nm, shp in shapes.items()}
        s_dbg = ctx.sem("dbg_st")

        def hook(tens):
            ph = Ph(ctx, "dbg")
            for nm, t in tens.items():
                ph.dma("sync", s_dbg, lambda e, t=t, nm=nm: e.dma_start(out=douts[nm], in_=t[:, :] if nm != "d_rrow" else t[0:1, :]))
            ph.wait("gpsimd", s_dbg)
            ph.run()

    with ctx.scope() as sc:
        build_l1attn(ctx, ar, w, cc, sc, x_spill, dbg_hook=hook)

    with ctx.scope() as sc:
        tmp_tm = [sc.sbuf(f"tout{i}", [128, D], F32) for i in range(2)]
        ps_t = [sc.psum(f"g_pso{i}", [128, TT]) for i in range(2)]
        ph = Ph(ctx, "out")
        emit_transpose_out(ph, ar=ar, hbm_tm=y_out, sem_tp=s_tp, sem_ev=s_ev,
                           sem_st=s_st, ps_t=ps_t, tmp_tm=tmp_tm)
        ph.wait("gpsimd", s_st[0])
        ph.wait("gpsimd", s_st[1])
        ph.run()

    ctx.close()
    return nc


def build_l0ffn(ctx, ar, w, sc, x_spill, rw_hbm):
    nc = ctx.nc
    s_ld = ctx.sem("f_ld")
    s_cv = ctx.sem("f_cv")
    s_mm = ctx.sem("f_mm")
    s_ev = ctx.sem("f_ev")
    s_sp = ctx.sem("f_sp")
    s_rw = ctx.sem("f_rw")
    s_hp = ctx.sem("f_hp")

    x_bf = sc.sbuf("f_xbf", [128, KC * T], BF16)
    routed = ar

    psA = sc.psum("f_psA", [128, TT])
    psB = sc.psum("f_psB", [128, TT])
    psR = sc.psum("f_psR", [128, TT])
    psD = sc.psum("f_psD", [128, TT])
    psA2 = sc.psum("f_psA2", [128, TT])
    psB2 = sc.psum("f_psB2", [128, TT])
    psD2 = sc.psum("f_psD2", [128, TT])

    NCH = [(0, 128), (128, 128), (256, 128), (384, 128), (512, 128), (640, 64)]
    kone = ctx.vecs[:, w["kone"]:w["kone"] + 1]
    kzero = ctx.vecs[:, w["kzero"]:w["kzero"] + 1]

    with ctx.scope() as rsc:
        sco_fm = rsc.sbuf("f_sco", [64, T], F32)
        sfc_fm = rsc.sbuf("f_sfc", [64, T], F32)
        sco_tm = rsc.sbuf("f_scot", [128, 64], F32)
        sfc_tm = rsc.sbuf("f_sfct", [128, 64], F32)
        wrk = rsc.sbuf("f_wrk", [128, 64], F32)
        sel = rsc.sbuf("f_sel", [128, 64], F32)
        gsc = rsc.sbuf("f_gsc", [128, 8], F32)
        gmk = rsc.sbuf("f_gmk", [128, 8], F32)
        rwt = rsc.sbuf("f_rwt", [128, 64], F32)
        m1 = rsc.sbuf("f_m1", [128, 4], F32)
        gwtb = rsc.sbuf("f_gwtb", [128, 8 * 64], BF16)
        stg = rsc.sbuf("f_gstg", [128, 512], F32)
        psT = rsc.psum("f_psT", [128, 128])

        ph = Ph(ctx, "f_route")

        def vop(build):
            ph.op("vector", build, inc=s_ev)
            ph.wait("vector", s_ev)

        for kc in range(KC):
            ph.op("vector",
                  lambda e, kc=kc: e.tensor_copy(
                      x_bf[:, kc * T:(kc + 1) * T], ar[:, kc * T:(kc + 1) * T]),
                  inc=(s_cv if kc == KC - 1 else None))
        ph.wait("sync", s_cv)
        ph.dma("sync", s_sp, lambda e: e.dma_start(out=x_spill, in_=ar[:, :]))
        ph.wait("gpsimd", s_sp)
        ph.op("gpsimd", lambda e: e.memset(routed[:, :], 0.0), inc=s_cv)
        ph.dma("sync", s_ld,
               lambda e: e.dma_start(
                   out=stg[:, 0:512].rearrange("p (kc n) -> p kc n", kc=8),
                   in_=w["gate_wT"].rearrange("(kc p) n -> p kc n", p=128)))
        ph.wait("gpsimd", s_ld)
        ph.op("gpsimd", lambda e: e.tensor_copy(gwtb[:, :], stg[:, 0:512]), inc=s_cv)
        ph.wait("tensor", s_cv)
        for tt in range(NTT):
            if tt > 0:
                ph.wait("tensor", s_ev)
            for kc in range(KC):
                ph.op("tensor",
                      lambda e, kc=kc, tt=tt: e.matmul(
                          psA[0:64, 0:TT], gwtb[:, kc * 64:(kc + 1) * 64],
                          x_bf[:, kc * T + tt * TT: kc * T + (tt + 1) * TT],
                          start=(kc == 0), stop=(kc == KC - 1)),
                      inc=(s_mm if kc == KC - 1 else None))
            ph.wait("scalar", s_mm)
            ph.op("scalar",
                  lambda e, tt=tt: e.activation(
                      sco_fm[0:64, tt * TT:(tt + 1) * TT], psA[0:64, 0:TT],
                      AF.Sigmoid, bias=0.0, scale=1.0),
                  inc=s_ev)
        ph.wait("vector", s_ev)
        ph.op("vector", lambda e: e.tensor_scalar_add(
            sfc_fm[0:64, :], sco_fm[0:64, :], ctx.vecs[0:64, w["gbias"]:w["gbias"] + 1]),
            inc=s_ev)
        for ti in range(T // 128):
            ph.wait("tensor", s_ev)
            ph.op("tensor",
                  lambda e, ti=ti: e.transpose(
                      psT[:, 0:64], sco_fm[0:64, ti * 128:(ti + 1) * 128],
                      ctx.ident_f32[0:64, 0:64]),
                  inc=s_mm)
            ph.wait("vector", s_mm)
            vop(lambda e: e.tensor_copy(sco_tm[:, :], psT[:, 0:64]))
            ph.wait("tensor", s_ev)
            ph.op("tensor",
                  lambda e, ti=ti: e.transpose(
                      psT[:, 0:64], sfc_fm[0:64, ti * 128:(ti + 1) * 128],
                      ctx.ident_f32[0:64, 0:64]),
                  inc=s_mm)
            ph.wait("vector", s_mm)
            vop(lambda e: e.tensor_copy(sfc_tm[:, :], psT[:, 0:64]))
            for g in range(8):
                vop(lambda e, g=g: e.reduce_max(m1[:, 0:1], sfc_tm[:, g * 8:(g + 1) * 8], axis=AX.X))
                vop(lambda e, g=g: e.tensor_scalar(
                    wrk[:, 0:8], sfc_tm[:, g * 8:(g + 1) * 8], m1[:, 0:1], None, ALU.is_ge))
                vop(lambda e, g=g: e.scalar_tensor_tensor(
                    wrk[:, 8:16], wrk[:, 0:8], -1e30, sfc_tm[:, g * 8:(g + 1) * 8],
                    ALU.mult, ALU.add))
                vop(lambda e: e.reduce_max(m1[:, 1:2], wrk[:, 8:16], axis=AX.X))
                vop(lambda e, g=g: e.tensor_add(gsc[:, g:g + 1], m1[:, 0:1], m1[:, 1:2]))
            vop(lambda e: e.memset(gmk[:, :], 0.0))
            vop(lambda e: e.tensor_copy(wrk[:, 0:8], gsc[:, :]))
            for _ in range(4):
                vop(lambda e: e.reduce_max(m1[:, 0:1], wrk[:, 0:8], axis=AX.X))
                vop(lambda e: e.tensor_scalar(
                    wrk[:, 8:16], wrk[:, 0:8], m1[:, 0:1], None, ALU.is_ge))
                vop(lambda e: e.tensor_add(gmk[:, :], gmk[:, :], wrk[:, 8:16]))
                vop(lambda e: e.scalar_tensor_tensor(
                    wrk[:, 0:8], wrk[:, 8:16], -1e30, wrk[:, 0:8], ALU.mult, ALU.add))
            vop(lambda e: e.tensor_copy(
                wrk[:, :].rearrange("p (g x) -> p g x", g=8),
                gmk[:, :].rearrange("p (g o) -> p g o", o=1).broadcast_to([128, 8, 8])))
            vop(lambda e: e.tensor_mul(wrk[:, :], wrk[:, :], sfc_tm[:, :]))
            vop(lambda e: e.memset(sel[:, :], 0.0))
            for _ in range(8):
                vop(lambda e: e.reduce_max(m1[:, 0:1], wrk[:, :], axis=AX.X))
                vop(lambda e: e.tensor_scalar(
                    rwt[:, :], wrk[:, :], m1[:, 0:1], None, ALU.is_ge))
                vop(lambda e: e.tensor_add(sel[:, :], sel[:, :], rwt[:, :]))
                vop(lambda e: e.scalar_tensor_tensor(
                    wrk[:, :], rwt[:, :], -1e30, wrk[:, :], ALU.mult, ALU.add))
            vop(lambda e: e.tensor_mul(rwt[:, :], sel[:, :], sco_tm[:, :]))
            vop(lambda e: e.reduce_sum(m1[:, 2:3], rwt[:, :], axis=AX.X))
            vop(lambda e: e.tensor_scalar_add(m1[:, 2:3], m1[:, 2:3], 1e-20))
            vop(lambda e: e.reciprocal(m1[:, 2:3], m1[:, 2:3]))
            vop(lambda e: e.tensor_scalar(
                rwt[:, :], rwt[:, :], m1[:, 2:3], 2.5, ALU.mult, ALU.mult))
            ph.wait("tensor", s_ev)
            ph.op("tensor",
                  lambda e: e.transpose(psT[0:64, 0:128], rwt[:, 0:64], ctx.ident_f32[:]),
                  inc=s_mm)
            ph.wait("scalar", s_mm)
            ph.op("scalar",
                  lambda e, ti=ti: e.copy(
                      sco_fm[0:64, ti * 128:(ti + 1) * 128], psT[0:64, 0:128]),
                  inc=s_ev)
        ph.wait("sync", s_ev)
        ph.dma("sync", s_rw, lambda e: e.dma_start(out=rw_hbm, in_=sco_fm[0:64, :]))
        ph.wait("gpsimd", s_rw)
        ph.run()

    gwb = [sc.sbuf(f"f_gw{i}", [128, 8 * 704], BF16) for i in range(2)]
    uwb = [sc.sbuf(f"f_uw{i}", [128, 8 * 704], BF16) for i in range(2)]
    dwb = [sc.sbuf(f"f_dw{i}", [128, 6 * 1024], BF16) for i in range(2)]
    stg = sc.sbuf("f_stg", [128, 2816], F32)
    sgt = sc.sbuf("f_sg", [128, 2 * TT], BF16)
    ht = sc.sbuf("f_h", [128, 6 * TT], BF16)
    rwrow = sc.sbuf("f_rwrow", [65, T], F32)
    rwb = sc.sbuf("f_rwb", [128, TT], F32)
    NEXP = 66

    def wsrc(e):
        if e < 64:
            return (w["WgT"][e], w["WuT"][e], w["WdT"][e])
        half = e - 64
        return (w["SgT"][:, half * 704:(half + 1) * 704],
                w["SuT"][:, half * 704:(half + 1) * 704],
                w["SdT"][half * 704:(half + 1) * 704, :])

    def wload_jobs(e, par):
        gsrc, usrc, dsrc = wsrc(e)
        jobs = []
        for mat, dst, nc_ in ((gsrc, gwb[par], 704), (usrc, uwb[par], 704)):
            jobs.append((mat, dst, 0, 4, nc_, 128))
            jobs.append((mat, dst, 4, 4, nc_, 128))
        jobs.append((dsrc, dwb[par], 0, 2, 1024, 128))
        jobs.append((dsrc, dwb[par], 2, 2, 1024, 128))
        jobs.append((dsrc, dwb[par], 4, 1, 1024, 128))
        jobs.append((dsrc, dwb[par], 5, 1, 1024, 64))
        return jobs

    def emit_wjob(ph, job):
        mat, dst, k0, kn, nc_, prt = job
        ph.wait("sync", s_cv)
        if prt == 128:
            ph.dma("sync", s_ld,
                   lambda e_, mat=mat, k0=k0, kn=kn, nc_=nc_: e_.dma_start(
                       out=stg[:, 0:kn * nc_].rearrange("p (k n) -> p k n", k=kn),
                       in_=mat[k0 * 128:(k0 + kn) * 128, :].rearrange(
                           "(k p) n -> p k n", p=128)))
        else:
            ph.dma("sync", s_ld,
                   lambda e_, mat=mat, k0=k0, nc_=nc_: e_.dma_start(
                       out=stg[0:64, 0:nc_], in_=mat[k0 * 128: k0 * 128 + 64, :]))
        ph.wait("scalar", s_ld)
        if prt == 128:
            ph.op("scalar",
                  lambda e_, dst=dst, k0=k0, kn=kn, nc_=nc_: e_.copy(
                      dst[:, k0 * nc_:(k0 + kn) * nc_], stg[:, 0:kn * nc_]),
                  inc=s_cv)
        else:
            ph.op("scalar",
                  lambda e_, dst=dst, k0=k0, nc_=nc_: e_.copy(
                      dst[0:64, k0 * nc_:(k0 + 1) * nc_], stg[0:64, 0:nc_]),
                  inc=s_cv)

    def emit_wload(ph, e, par):
        for j in wload_jobs(e, par):
            emit_wjob(ph, j)

    conv_done = {}
    ph = Ph(ctx, "f_experts")
    for e in range(NEXP):
        par = e % 2
        if e == 0:
            emit_wload(ph, 0, 0)
            conv_done[0] = ctx.cnt[s_cv]
        is_shared = e >= 64
        if not is_shared:
            if e >= 2:
                ph.wait("sync", s_mm)
            ph.dma("sync", s_rw,
                   lambda e_, e=e, par=par: e_.dma_start(
                       out=rwrow[par * 64: par * 64 + 1, :], in_=rw_hbm[e:e + 1, :]))
        pjobs = []
        if e + 1 < NEXP:
            ph.wait("scalar", s_mm)
            pjobs = wload_jobs(e + 1, 1 - par)
            for j in pjobs[0:2]:
                emit_wjob(ph, j)
        if not is_shared:
            ph.wait("tensor", s_rw)
        ph.wait("tensor", s_cv, conv_done[e])
        nev = 2
        for tt in range(NTT):
            for j in pjobs[2 + tt * 2: 4 + tt * 2]:
                emit_wjob(ph, j)
            if not is_shared:
                ph.wait("tensor", s_ev)
                ph.op("tensor",
                      lambda e_, tt=tt, par=par: e_.matmul(
                          psR[:], ctx.ones_row65[par * 64: par * 64 + 1, :],
                          rwrow[par * 64: par * 64 + 1, tt * TT:(tt + 1) * TT],
                          start=True, stop=True),
                      inc=s_mm)
                ph.wait("scalar", s_mm)
                ph.wait("scalar", s_hp)
                ph.op("scalar", lambda e_: e_.copy(rwb[:, :], psR[:]), inc=s_ev)
            ph.wait("vector", s_mm)
            for ci, (c0, cw) in enumerate(NCH):
                pg = psA if ci % 2 == 0 else psA2
                pu = psB if ci % 2 == 0 else psB2
                sg = sgt[0:cw, (ci % 2) * TT:(ci % 2) * TT + TT]
                ph.wait("tensor", s_ev, ctx.cnt[s_ev] - nev)
                for kc in range(KC):
                    ph.op("tensor",
                          lambda e_, kc=kc, tt=tt, c0=c0, cw=cw, par=par, pg=pg: e_.matmul(
                              pg[0:cw, 0:TT], gwb[par][:, kc * 704 + c0: kc * 704 + c0 + cw],
                              x_bf[:, kc * T + tt * TT: kc * T + (tt + 1) * TT],
                              start=(kc == 0), stop=(kc == KC - 1)),
                          inc=(s_mm if kc == KC - 1 else None))
                for kc in range(KC):
                    ph.op("tensor",
                          lambda e_, kc=kc, tt=tt, c0=c0, cw=cw, par=par, pu=pu: e_.matmul(
                              pu[0:cw, 0:TT], uwb[par][:, kc * 704 + c0: kc * 704 + c0 + cw],
                              x_bf[:, kc * T + tt * TT: kc * T + (tt + 1) * TT],
                              start=(kc == 0), stop=(kc == KC - 1)),
                          inc=(s_mm if kc == KC - 1 else None))
                ph.wait("scalar", s_mm, ctx.cnt[s_mm] - 1)
                ph.wait("scalar", s_ev, ctx.cnt[s_ev] - 2)
                if not is_shared:
                    ph.wait("scalar", s_hp, ctx.cnt[s_hp] - 1)
                ph.op("scalar",
                      lambda e_, cw=cw, pg=pg, sg=sg: e_.activation(
                          sg, pg[0:cw, 0:TT], AF.Silu, bias=0.0, scale=1.0),
                      inc=s_ev)
                ph.wait("vector", s_ev)
                ph.wait("vector", s_mm)
                if is_shared:
                    ph.op("vector",
                          lambda e_, ci=ci, cw=cw, pu=pu, sg=sg: e_.scalar_tensor_tensor(
                              ht[0:cw, ci * TT:(ci + 1) * TT], pu[0:cw, 0:TT],
                              kone[0:cw, :], sg, ALU.bypass, ALU.mult),
                          inc=s_ev)
                else:
                    ph.op("vector",
                          lambda e_, cw=cw, pu=pu, sg=sg: e_.scalar_tensor_tensor(
                              sg, pu[0:cw, 0:TT],
                              kone[0:cw, :], sg, ALU.bypass, ALU.mult),
                          inc=s_ev)
                    ph.wait("gpsimd", s_ev)
                    ph.op("gpsimd",
                          lambda e_, ci=ci, cw=cw, sg=sg: e_.tensor_mul(
                              ht[0:cw, ci * TT:(ci + 1) * TT], sg, rwb[0:cw, :]),
                          inc=s_hp)
            dn0 = ctx.cnt[s_ev]
            hp0 = ctx.cnt[s_hp]
            for dch in range(KC):
                pd = psD if dch % 2 == 0 else psD2
                if dch < 2:
                    ph.wait("tensor", s_ev, dn0)
                    if not is_shared:
                        ph.wait("tensor", s_hp, hp0)
                else:
                    ph.wait("tensor", s_ev, dn0 + dch - 1)
                for ci in range(6):
                    kw = 128 if ci < 5 else 64
                    ph.op("tensor",
                          lambda e_, ci=ci, kw=kw, dch=dch, par=par, pd=pd: e_.matmul(
                              pd[:], dwb[par][0:kw, ci * 1024 + dch * 128: ci * 1024 + (dch + 1) * 128],
                              ht[0:kw, ci * TT:(ci + 1) * TT],
                              start=(ci == 0), stop=(ci == 5)),
                          inc=(s_mm if ci == 5 else None))
                ph.wait("vector", s_mm)
                c = dch * T + tt * TT
                ph.op("vector",
                      lambda e_, c=c, pd=pd: e_.scalar_tensor_tensor(
                          routed[:, c:c + TT], pd[:], kzero, routed[:, c:c + TT],
                          ALU.add, ALU.add),
                      inc=s_ev)
        if e + 1 < NEXP:
            conv_done[e + 1] = ctx.cnt[s_cv]
    ph.run()

    ph = Ph(ctx, "f_fin")
    for kc in range(KC):
        for hf in range(2):
            ph.wait("sync", s_ev)
            ph.dma("sync", s_ld,
                   lambda e_, kc=kc, hf=hf: e_.dma_start(
                       out=stg[:, 0:1024],
                       in_=x_spill[:, kc * T + hf * 1024: kc * T + (hf + 1) * 1024]))
            ph.wait("vector", s_ld)
            for q in range(2):
                c = kc * T + hf * 1024 + q * TT
                ph.op("vector",
                      lambda e_, c=c, q=q: e_.tensor_add(
                          ar[:, c:c + TT], ar[:, c:c + TT], stg[:, q * TT:(q + 1) * TT]),
                      inc=s_ev)
    ph.run()

    ph = Ph(ctx, "f_ln")
    emit_postln(ph, ar=ar, sq=stg, ps_stat=[psA, psB], ps_b=[psA, psB],
                sem_mm=s_mm, sem_ev=s_ev,
                eps_ap=ctx.vecs[0:1, w["keps"]:w["keps"] + 1])
    ph.run()


def l0ffn_weights_host(inp, vb):
    w = {
        "gate_wT": _t(inp["L0_ffn_gate_w"]),
        "WgT": np.ascontiguousarray(np.transpose(np.asarray(inp["L0_ffn_Wg"], np.float32), (0, 2, 1))),
        "WuT": np.ascontiguousarray(np.transpose(np.asarray(inp["L0_ffn_Wu"], np.float32), (0, 2, 1))),
        "WdT": np.ascontiguousarray(np.transpose(np.asarray(inp["L0_ffn_Wd"], np.float32), (0, 2, 1))),
        "SgT": _t(inp["L0_ffn_Sg"]),
        "SuT": _t(inp["L0_ffn_Su"]),
        "SdT": _t(inp["L0_ffn_Sd"]),
    }
    gb = np.zeros((128, 1), np.float32)
    gb[0:64, 0] = np.asarray(inp["L0_ffn_gate_bias"], np.float32)
    idx = {
        "gbias": vb.add("f_gbias", gb),
        "kone": vb.add("k_one_f", np.full((128, 1), 1.0, np.float32)),
        "kzero": vb.add("k_zero_f", np.zeros((128, 1), np.float32)),
        "keps": vb.add("k_eps_f", np.full((128, 1), 1e-5, np.float32)),
    }
    return w, idx


def build_l0ffn_program(nblob, vec_idx):
    nc = bass.Bass("TRN2", target_bir_lowering=False, detect_race_conditions=False)
    ctx = Ctx(nc)
    x_in = nc.dram_tensor("x_in", [T, D], F32, kind="ExternalInput").ap()
    y_out = nc.dram_tensor("y_out", [T, D], F32, kind="ExternalOutput").ap()
    consts = nc.dram_tensor("consts", [130, 128], F32, kind="ExternalInput").ap()
    vecblob = nc.dram_tensor("vecblob", [128, nblob], F32, kind="ExternalInput").ap()
    w = {
        "gate_wT": nc.dram_tensor("gate_wT", [1024, 64], F32, kind="ExternalInput").ap(),
        "WgT": nc.dram_tensor("WgT", [64, 1024, 704], F32, kind="ExternalInput").ap(),
        "WuT": nc.dram_tensor("WuT", [64, 1024, 704], F32, kind="ExternalInput").ap(),
        "WdT": nc.dram_tensor("WdT", [64, 704, 1024], F32, kind="ExternalInput").ap(),
        "SgT": nc.dram_tensor("SgT", [1024, 1408], F32, kind="ExternalInput").ap(),
        "SuT": nc.dram_tensor("SuT", [1024, 1408], F32, kind="ExternalInput").ap(),
        "SdT": nc.dram_tensor("SdT", [1408, 1024], F32, kind="ExternalInput").ap(),
    }
    w.update(vec_idx)
    x_spill = nc.dram_tensor("x_spill", [128, KC * T], F32).ap()
    rw_hbm = nc.dram_tensor("rw_scr", [64, T], F32).ap()

    common_setup(ctx, nblob)
    ar = ctx.sbuf("ar", [128, KC * T], F32)

    s_ld = [ctx.sem("g_ld0"), ctx.sem("g_ld1")]
    s_cst = ctx.sem("g_cst")
    s_tp = ctx.sem("g_tp")
    s_ev = ctx.sem("g_ev")
    s_st = [ctx.sem("g_st0"), ctx.sem("g_st1")]

    with ctx.scope() as sc:
        tmp_tm = [sc.sbuf(f"tin{i}", [128, D], F32) for i in range(2)]
        ps_t = [sc.psum(f"g_ps{i}", [128, TT]) for i in range(2)]
        ph = Ph(ctx, "in")
        emit_load_consts(ph, ctx, consts, None, vecblob, s_cst)
        emit_transpose_in(ph, hbm_tm=x_in, ar=ar, sem_ld=s_ld, sem_tp=s_tp,
                          sem_ev=s_ev, ps_t=ps_t, tmp_tm=tmp_tm)
        ph.run()

    with ctx.scope() as sc:
        build_l0ffn(ctx, ar, w, sc, x_spill, rw_hbm)

    with ctx.scope() as sc:
        tmp_tm = [sc.sbuf(f"tout{i}", [128, D], F32) for i in range(2)]
        ps_t = [sc.psum(f"g_pso{i}", [128, TT]) for i in range(2)]
        ph = Ph(ctx, "out")
        emit_transpose_out(ph, ar=ar, hbm_tm=y_out, sem_tp=s_tp, sem_ev=s_ev,
                           sem_st=s_st, ps_t=ps_t, tmp_tm=tmp_tm)
        ph.wait("gpsimd", s_st[0])
        ph.wait("gpsimd", s_st[1])
        ph.run()

    ctx.close()
    return nc


def common_setup(ctx, nblob):
    ctx.ones_col_f32 = ctx.sbuf("c_ones_col", [128, 1], F32)
    ctx.ones_col_bf = ctx.sbuf("c_ones_col_bf", [128, 1], BF16)
    ctx.ones_row_f32 = ctx.sbuf("c_ones_row", [1, 128], F32)
    ctx.ones_row65 = ctx.sbuf("c_ones_row65", [65, 128], F32)
    ctx.ones_row_bf = ctx.sbuf("c_ones_row_bf", [1, 128], BF16)
    ctx.ident_f32 = ctx.sbuf("c_idf", [128, 128], F32)
    ctx.ident_bf = ctx.sbuf("c_idb", [128, 128], BF16)
    ctx.ind_bf = ctx.sbuf("c_indb", [16, 8 * 128], BF16)
    ctx.vecs = ctx.sbuf("c_vecs", [128, max(nblob, 1)], F32)
    ctx.rows = ctx.sbuf("c_rows", [1, 4 * TT], F32)


def emit_load_consts(ph, ctx, consts_ap, ind_ap, vec_ap, sem_ld, ind_stage=None):
    ph.dma("sync", sem_ld, lambda e: e.dma_start(
        out=ctx.ones_col_f32[:], in_=consts_ap[0:1, :].rearrange("r p -> p r")))
    ph.dma("sync", sem_ld, lambda e: e.dma_start(out=ctx.ones_row_f32[0:1, :], in_=consts_ap[0:1, :]))
    ph.dma("sync", sem_ld, lambda e: e.dma_start(out=ctx.ones_row65[0:1, :], in_=consts_ap[0:1, :]))
    ph.dma("sync", sem_ld, lambda e: e.dma_start(out=ctx.ones_row65[64:65, :], in_=consts_ap[0:1, :]))
    ph.dma("sync", sem_ld, lambda e: e.dma_start(out=ctx.ident_f32[:], in_=consts_ap[2:130, :]))
    if ind_ap is not None:
        ph.dma("sync", sem_ld, lambda e: e.dma_start(out=ind_stage[0:16, :], in_=ind_ap[:, :]))
    if vec_ap is not None:
        ph.dma("sync", sem_ld, lambda e: e.dma_start(out=ctx.vecs[:], in_=vec_ap[:, :]))
    ph.wait("vector", sem_ld)
    ph.wait("tensor", sem_ld)
    ph.wait("scalar", sem_ld)
    ph.wait("gpsimd", sem_ld)
    ph.op("vector", lambda e: e.tensor_copy(ctx.ident_bf[:], ctx.ident_f32[:]))
    ph.op("vector", lambda e: e.tensor_copy(ctx.ones_row_bf[0:1, :], ctx.ones_row_f32[0:1, :]))
    ph.op("vector", lambda e: e.tensor_copy(ctx.ones_col_bf[:], ctx.ones_col_f32[:]))
    if ind_ap is not None:
        ph.op("vector", lambda e: e.tensor_copy(ctx.ind_bf[0:16, :], ind_stage[0:16, :]))


def make_ind():
    ind = np.zeros((16, 8 * 128), dtype=np.float32)
    for kc in range(8):
        ind[2 * kc, kc * 128: kc * 128 + 64] = 1.0
        ind[2 * kc + 1, kc * 128 + 64: kc * 128 + 128] = 1.0
    return ind


def _t(a):
    return np.ascontiguousarray(np.asarray(a, dtype=np.float32).T)


def _chunkvec(v):
    v = np.asarray(v, dtype=np.float32)
    n = v.shape[0] // 128
    return np.ascontiguousarray(v.reshape(n, 128).T)


def make_consts():
    c = np.zeros((130, 128), dtype=np.float32)
    c[0, :] = 1.0
    c[2:130, :] = np.eye(128, dtype=np.float32)
    return c


class VecBlob:
    def __init__(self):
        self.cols = []
        self.idx = {}

    def add(self, name, arr):
        self.idx[name] = sum(a.shape[1] for a in self.cols)
        self.cols.append(np.asarray(arr, dtype=np.float32))
        return self.idx[name]

    def blob(self):
        b = np.concatenate(self.cols, axis=1) if self.cols else np.zeros((128, 1), np.float32)
        return np.ascontiguousarray(b)


def l1ffn_weights_host(inp, vb):
    w = {
        "w1T": _t(inp["L1_ffn_w1_w"]),
        "w2T": _t(inp["L1_ffn_w2_w"]),
        "h1T": _t(inp["L1_ffn_h1_w"]),
        "h2T": _t(inp["L1_ffn_h2_w"]),
        "w3T": _t(inp["L1_ffn_w3_w"]),
    }
    idx = {
        "keps": vb.add("k_eps_ln", np.full((128, 1), 1e-5, np.float32)),
        "s1c": vb.add("ff_s1", _chunkvec(inp["L1_ffn_w1_s"])),
        "s2c": vb.add("ff_s2", _chunkvec(inp["L1_ffn_w2_s"])),
        "sh1c": vb.add("ff_sh1", _chunkvec(inp["L1_ffn_h1_s"])),
        "sh2c": vb.add("ff_sh2", _chunkvec(inp["L1_ffn_h2_s"])),
        "s3c": vb.add("ff_s3", _chunkvec(inp["L1_ffn_w3_s"])),
    }
    return w, idx


def build_l1ffn_program(nblob, vec_idx):
    nc = bass.Bass("TRN2", target_bir_lowering=False, detect_race_conditions=False)
    ctx = Ctx(nc)
    x_in = nc.dram_tensor("x_in", [T, D], F32, kind="ExternalInput").ap()
    y_out = nc.dram_tensor("y_out", [T, D], F32, kind="ExternalOutput").ap()
    consts = nc.dram_tensor("consts", [130, 128], F32, kind="ExternalInput").ap()
    vecblob = nc.dram_tensor("vecblob", [128, nblob], F32, kind="ExternalInput").ap()
    w = {k: nc.dram_tensor(k, list(shp), F32, kind="ExternalInput").ap()
         for k, shp in (("w1T", (1024, 4096)), ("w2T", (1024, 4096)),
                        ("h1T", (4096, 4096)), ("h2T", (4096, 4096)),
                        ("w3T", (4096, 1024)))}
    w.update(vec_idx)

    common_setup(ctx, nblob)
    ar = ctx.sbuf("ar", [128, KC * T], F32)

    s_ld = [ctx.sem("g_ld0"), ctx.sem("g_ld1")]
    s_cst = ctx.sem("g_cst")
    s_tp = ctx.sem("g_tp")
    s_ev = ctx.sem("g_ev")
    s_st = [ctx.sem("g_st0"), ctx.sem("g_st1")]
    ps_t = [ctx.psum(f"g_ps{i}", [128, TT]) for i in range(2)]

    with ctx.scope() as sc:
        tmp_tm = [sc.sbuf(f"tin{i}", [128, D], F32) for i in range(2)]
        ph = Ph(ctx, "in")
        emit_load_consts(ph, ctx, consts, None, vecblob, s_cst)
        emit_transpose_in(ph, hbm_tm=x_in, ar=ar, sem_ld=s_ld, sem_tp=s_tp,
                          sem_ev=s_ev, ps_t=ps_t, tmp_tm=tmp_tm)
        ph.run()

    with ctx.scope() as sc:
        build_l1ffn(ctx, ar, w, sc)

    with ctx.scope() as sc:
        tmp_tm = [sc.sbuf(f"tout{i}", [128, D], F32) for i in range(2)]
        ph = Ph(ctx, "out")
        emit_transpose_out(ph, ar=ar, hbm_tm=y_out, sem_tp=s_tp, sem_ev=s_ev,
                           sem_st=s_st, ps_t=ps_t, tmp_tm=tmp_tm)
        ph.wait("gpsimd", s_st[0])
        ph.wait("gpsimd", s_st[1])
        ph.run()

    ctx.close()
    return nc


def l0attn_weights_host(inp, vb):
    w = {
        "WqT": _t(inp["L0_attn_Wq"]),
        "WkT": _t(inp["L0_attn_Wk"]),
        "WvT": _t(inp["L0_attn_Wv"]),
        "WoT": _t(inp["L0_attn_Wo"]),
    }
    idx = {
        "bq": vb.add("a_bq", _chunkvec(inp["L0_attn_bq"])),
        "bk": vb.add("a_bk", _chunkvec(inp["L0_attn_bk"])),
        "bv": vb.add("a_bv", _chunkvec(inp["L0_attn_bv"])),
        "bo": vb.add("a_bo", _chunkvec(inp["L0_attn_bo"])),
        "kone": vb.add("k_one", np.full((128, 1), 1.0, np.float32)),
        "keps6": vb.add("k_eps6", np.full((128, 1), 1e-6, np.float32)),
        "keps": vb.add("k_eps_ln0", np.full((128, 1), 1e-5, np.float32)),
    }
    return w, idx


def make_l0_cc(nc):
    return {
        "r16_in": nc.dram_tensor("cc_r16i", [128, 16], F32).ap(),
        "r16_out": nc.dram_tensor("cc_r16o", [128, 16], F32).ap(),
        "d_in": nc.dram_tensor("cc_di", [16, 1], F32).ap(),
        "d_out": nc.dram_tensor("cc_do", [16, 1], F32).ap(),
        "kv_in": nc.dram_tensor("cc_kvi", [128, 1024], F32).ap(),
        "kv_out": nc.dram_tensor("cc_kvo", [128, 1024], F32).ap(),
    }


def build_l0attn_program(nblob, vec_idx, upto="H", debug=False):
    nc = bass.Bass("TRN2", target_bir_lowering=False, detect_race_conditions=False)
    ctx = Ctx(nc)
    x_in = nc.dram_tensor("x_in", [T, D], F32, kind="ExternalInput").ap()
    y_out = nc.dram_tensor("y_out", [T, D], F32, kind="ExternalOutput").ap()
    consts = nc.dram_tensor("consts", [130, 128], F32, kind="ExternalInput").ap()
    indc = nc.dram_tensor("indc", [16, 1024], F32, kind="ExternalInput").ap()
    vecblob = nc.dram_tensor("vecblob", [128, nblob], F32, kind="ExternalInput").ap()
    w = {k: nc.dram_tensor(k, [1024, 1024], F32, kind="ExternalInput").ap()
         for k in ("WqT", "WkT", "WvT", "WoT")}
    w.update(vec_idx)
    cc = make_l0_cc(nc)

    common_setup(ctx, nblob)
    ar = ctx.sbuf("ar", [128, KC * T], F32)

    s_ld = [ctx.sem("g_ld0"), ctx.sem("g_ld1")]
    s_cst = ctx.sem("g_cst")
    s_tp = ctx.sem("g_tp")
    s_ev = ctx.sem("g_ev")
    s_st = [ctx.sem("g_st0"), ctx.sem("g_st1")]

    with ctx.scope() as sc:
        tmp_tm = [sc.sbuf(f"tin{i}", [128, D], F32) for i in range(2)]
        ps_t = [sc.psum(f"g_ps{i}", [128, TT]) for i in range(2)]
        ph = Ph(ctx, "in")
        emit_load_consts(ph, ctx, consts, indc, vecblob, s_cst)
        emit_transpose_in(ph, hbm_tm=x_in, ar=ar, sem_ld=s_ld, sem_tp=s_tp,
                          sem_ev=s_ev, ps_t=ps_t, tmp_tm=tmp_tm)
        ph.run()

    dbg = {}
    if debug:
        for nm, shp, dt in (("d_q", [128, KC * T], BF16), ("d_k", [128, KC * T], BF16),
                            ("d_v", [128, KC * T], BF16), ("d_stats", [16, 3 * T], BF16),
                            ("d_r1", [128, 16], F32), ("d_r2", [128, 16], F32),
                            ("d_kvb", [128, 1024], BF16), ("d_dnm", [16, 2], F32),
                            ("d_ecol", [16, 4], F32)):
            dbg[nm] = nc.dram_tensor(nm, shp, dt, kind="ExternalOutput").ap()

    with ctx.scope() as sc:
        arenas = build_l0attn(ctx, ar, w, cc, sc, upto=upto)
        if debug:
            Qb, Kb, Vb, stats, r1, r2, kvb, dnm, ecol = arenas
            s_dbg = ctx.sem("dbg_st")
            ph = Ph(ctx, "dbg")
            for nm, t in (("d_q", Qb), ("d_k", Kb), ("d_v", Vb), ("d_stats", stats[0:16, :]),
                          ("d_r1", r1), ("d_r2", r2), ("d_kvb", kvb), ("d_dnm", dnm[0:16, :]),
                          ("d_ecol", ecol[0:16, 0:4])):
                ap = t[:, :] if not isinstance(t, bass.AP) else t
                ph.dma("sync", s_dbg, lambda e, ap=ap, nm=nm: e.dma_start(out=dbg[nm], in_=ap))
            ph.wait("gpsimd", s_dbg)
            ph.run()

    with ctx.scope() as sc:
        tmp_tm = [sc.sbuf(f"tout{i}", [128, D], F32) for i in range(2)]
        ps_t = [sc.psum(f"g_pso{i}", [128, TT]) for i in range(2)]
        ph = Ph(ctx, "out")
        emit_transpose_out(ph, ar=ar, hbm_tm=y_out, sem_tp=s_tp, sem_ev=s_ev,
                           sem_st=s_st, ps_t=ps_t, tmp_tm=tmp_tm)
        ph.wait("gpsimd", s_st[0])
        ph.wait("gpsimd", s_st[1])
        ph.run()

    ctx.close()
    return nc


def build_fused_program(nblob, aidx, fidx, midx, ffidx):
    nc = bass.Bass("TRN2", target_bir_lowering=False, detect_race_conditions=False)
    ctx = Ctx(nc)
    x_in = nc.dram_tensor("x_in", [T, D], F32, kind="ExternalInput").ap()
    y_out = nc.dram_tensor("y_out", [T, D], F32, kind="ExternalOutput").ap()
    consts = nc.dram_tensor("consts", [130, 128], F32, kind="ExternalInput").ap()
    indc = nc.dram_tensor("indc", [16, 1024], F32, kind="ExternalInput").ap()
    coreoh = nc.dram_tensor("coreoh", [128, 8], F32, kind="ExternalInput").ap()
    vecblob = nc.dram_tensor("vecblob", [128, nblob], F32, kind="ExternalInput").ap()

    wa = {k: nc.dram_tensor(k, [1024, 1024], F32, kind="ExternalInput").ap()
          for k in ("WqT", "WkT", "WvT", "WoT")}
    wa.update(aidx)
    wf = {
        "gate_wT": nc.dram_tensor("gate_wT", [1024, 64], F32, kind="ExternalInput").ap(),
        "WgT": nc.dram_tensor("WgT", [64, 1024, 704], F32, kind="ExternalInput").ap(),
        "WuT": nc.dram_tensor("WuT", [64, 1024, 704], F32, kind="ExternalInput").ap(),
        "WdT": nc.dram_tensor("WdT", [64, 704, 1024], F32, kind="ExternalInput").ap(),
        "SgT": nc.dram_tensor("SgT", [1024, 1408], F32, kind="ExternalInput").ap(),
        "SuT": nc.dram_tensor("SuT", [1024, 1408], F32, kind="ExternalInput").ap(),
        "SdT": nc.dram_tensor("SdT", [1408, 1024], F32, kind="ExternalInput").ap(),
    }
    wf.update(fidx)
    wm = {"out_wT": nc.dram_tensor("out_wT", [1024, 1024], F32, kind="ExternalInput").ap(),
          "w1T": nc.dram_tensor("m_w1T", [1024, 4096], F32, kind="ExternalInput").ap(),
          "w2T": nc.dram_tensor("m_w2T", [1024, 4096], F32, kind="ExternalInput").ap(),
          "w3T": nc.dram_tensor("m_w3T", [4096, 1024], F32, kind="ExternalInput").ap()}
    wm.update(midx)
    wff = {k: nc.dram_tensor("ff_" + k, list(shp), F32, kind="ExternalInput").ap()
           for k, shp in (("w1T", (1024, 4096)), ("w2T", (1024, 4096)),
                          ("h1T", (4096, 4096)), ("h2T", (4096, 4096)),
                          ("w3T", (4096, 1024)))}
    wff.update(ffidx)

    cc0 = make_l0_cc(nc)
    cc1 = make_l1_cc(nc)
    x_spill = nc.dram_tensor("x_spill", [128, KC * T], F32).ap()
    rw_hbm = nc.dram_tensor("rw_scr", [64, T], F32).ap()

    common_setup(ctx, nblob)
    ctx.coreoh = ctx.sbuf("c_coreoh", [128, 8], F32)
    ar = ctx.sbuf("ar", [128, KC * T], F32)

    s_ld = [ctx.sem("g_ld0"), ctx.sem("g_ld1")]
    s_cst = ctx.sem("g_cst")
    s_tp = ctx.sem("g_tp")
    s_ev = ctx.sem("g_ev")
    s_st = [ctx.sem("g_st0"), ctx.sem("g_st1")]

    with ctx.scope() as sc:
        tmp_tm = [sc.sbuf(f"tin{i}", [128, D], F32) for i in range(2)]
        ps_t = [sc.psum(f"g_ps{i}", [128, TT]) for i in range(2)]
        ph = Ph(ctx, "in")
        emit_load_consts(ph, ctx, consts, indc, vecblob, s_cst, ind_stage=tmp_tm[0])
        ph.dma("sync", s_cst, lambda e: e.dma_start(out=ctx.coreoh[:], in_=coreoh))
        ph.wait("vector", s_cst)
        ph.op("vector", lambda e: e.engine_nop(), inc=s_ev)
        ph.wait("sync", s_ev)
        emit_transpose_in(ph, hbm_tm=x_in, ar=ar, sem_ld=s_ld, sem_tp=s_tp,
                          sem_ev=s_ev, ps_t=ps_t, tmp_tm=tmp_tm)
        ph.run()

    with ctx.scope() as sc:
        build_l0attn(ctx, ar, wa, cc0, sc)
    with ctx.scope() as sc:
        build_l0ffn(ctx, ar, wf, sc, x_spill, rw_hbm)
    with ctx.scope() as sc:
        build_l1attn(ctx, ar, wm, cc1, sc, x_spill)
    with ctx.scope() as sc:
        build_l1ffn(ctx, ar, wff, sc)

    with ctx.scope() as sc:
        tmp_tm = [sc.sbuf(f"tout{i}", [128, D], F32) for i in range(2)]
        ps_t = [sc.psum(f"g_pso{i}", [128, TT]) for i in range(2)]
        ph = Ph(ctx, "out")
        emit_transpose_out(ph, ar=ar, hbm_tm=y_out, sem_tp=s_tp, sem_ev=s_ev,
                           sem_st=s_st, ps_t=ps_t, tmp_tm=tmp_tm)
        ph.wait("gpsimd", s_st[0])
        ph.wait("gpsimd", s_st[1])
        ph.run()

    ctx.close()
    return nc


def kernel(**inp):
    consts = make_consts()
    x = np.ascontiguousarray(np.asarray(inp["L0_attn_x"], np.float32).reshape(TSEQ, D))

    vb = VecBlob()
    wa, aidx = l0attn_weights_host(inp, vb)
    wf, fidx = l0ffn_weights_host(inp, vb)
    wm, midx = l1attn_weights_host(inp, vb)
    wff, ffidx = l1ffn_weights_host(inp, vb)
    blob = vb.blob()

    nc = build_fused_program(blob.shape[1], aidx, fidx, midx, ffidx)

    feeds = []
    for c in range(NCORES):
        oh = np.zeros((128, 8), np.float32)
        oh[:, c] = 1.0
        m = {"x_in": np.ascontiguousarray(x[c * T:(c + 1) * T, :]),
             "consts": consts, "indc": make_ind(), "coreoh": oh, "vecblob": blob}
        m.update(wa)
        m.update(wf)
        m.update({("out_wT" if k == "out_wT" else "m_" + k): v for k, v in wm.items()})
        m.update({"ff_" + k: v for k, v in wff.items()})
        feeds.append(m)
    res = run_bass_kernel_spmd(nc, feeds, core_ids=list(range(NCORES)))
    x = np.concatenate([np.asarray(res.results[c]["y_out"]) for c in range(NCORES)], axis=0)
    return x.reshape(1, TSEQ, D)


def _run_chain_launch(nc, per_core_feeds):
    res = run_bass_kernel_spmd(nc, per_core_feeds, core_ids=list(range(NCORES)))
    return np.concatenate([np.asarray(res.results[c]["y_out"]) for c in range(NCORES)], axis=0)


def kernel_multilaunch(**inp):
    consts = make_consts()

    x = np.ascontiguousarray(np.asarray(inp["L0_attn_x"], np.float32).reshape(TSEQ, D))

    vb = VecBlob()
    w, idx = l0attn_weights_host(inp, vb)
    blob = vb.blob()
    nc = build_l0attn_program(blob.shape[1], idx)
    feeds = []
    for c in range(NCORES):
        m = {"x_in": np.ascontiguousarray(x[c * T:(c + 1) * T, :]),
             "consts": consts, "indc": make_ind(), "vecblob": blob}
        m.update(w)
        feeds.append(m)
    x = _run_chain_launch(nc, feeds)

    vb = VecBlob()
    w, idx = l0ffn_weights_host(inp, vb)
    blob = vb.blob()
    nc = build_l0ffn_program(blob.shape[1], idx)
    feeds = []
    for c in range(NCORES):
        m = {"x_in": np.ascontiguousarray(x[c * T:(c + 1) * T, :]),
             "consts": consts, "vecblob": blob}
        m.update(w)
        feeds.append(m)
    x = _run_chain_launch(nc, feeds)

    vb = VecBlob()
    w, idx = l1attn_weights_host(inp, vb)
    blob = vb.blob()
    nc = build_l1attn_program(blob.shape[1], idx)
    feeds = []
    for c in range(NCORES):
        oh = np.zeros((128, 8), np.float32)
        oh[:, c] = 1.0
        m = {"x_in": np.ascontiguousarray(x[c * T:(c + 1) * T, :]),
             "consts": consts, "coreoh": oh, "vecblob": blob}
        m.update(w)
        feeds.append(m)
    x = _run_chain_launch(nc, feeds)

    vb = VecBlob()
    w, idx = l1ffn_weights_host(inp, vb)
    blob = vb.blob()
    nc = build_l1ffn_program(blob.shape[1], idx)
    feeds = []
    for c in range(NCORES):
        m = {"x_in": np.ascontiguousarray(x[c * T:(c + 1) * T, :]),
             "consts": consts, "vecblob": blob}
        m.update(w)
        feeds.append(m)
    x = _run_chain_launch(nc, feeds)

    return x.reshape(1, TSEQ, D)


# version 09
# speedup vs baseline: 1.6976x; 1.1093x over previous
import contextlib

import numpy as np
import ml_dtypes

import concourse.bass as bass
import concourse.mybir as mybir
from concourse.bass_utils import run_bass_kernel_spmd

F32 = mybir.dt.float32
BF16 = mybir.dt.bfloat16
AF = mybir.ActivationFunctionType
ALU = mybir.AluOpType
AX = mybir.AxisListType

NCORES = 8
TSEQ = 16384
T = TSEQ // NCORES
D = 1024
KC = D // 128
TT = 512
NTT = T // TT

ENGINES = ("sync", "vector", "scalar", "tensor", "gpsimd")


class Ctx:
    def __init__(self, nc):
        self.nc = nc
        self.es = contextlib.ExitStack()
        self.sems = {}
        self.cnt = {}

    def sem(self, name):
        if name in self.sems:
            return name
        h = self.es.enter_context(self.nc.semaphore(name))
        self.sems[name] = h
        self.cnt[name] = 0
        return name

    def sbuf(self, name, shape, dt):
        return self.es.enter_context(self.nc.sbuf_tensor(name, shape, dt))

    def psum(self, name, shape, dt=F32):
        return self.es.enter_context(self.nc.psum_tensor(name, shape, dt))

    @contextlib.contextmanager
    def scope(self):
        with contextlib.ExitStack() as es:
            class _S:
                def sbuf(_, name, shape, dt):
                    return es.enter_context(self.nc.sbuf_tensor(name, shape, dt))

                def psum(_, name, shape, dt=F32):
                    return es.enter_context(self.nc.psum_tensor(name, shape, dt))
            yield _S()

    def close(self):
        self.es.close()


class Ph:

    def __init__(self, ctx, name=""):
        self.ctx = ctx
        self.name = name
        self.q = {e: [] for e in ENGINES}

    def wait(self, eng, sem, val=None):
        v = self.ctx.cnt[sem] if val is None else val
        if v <= 0:
            return
        h = self.ctx.sems[sem]
        self.q[eng].append(lambda e: e.wait_ge(h, v))

    def op(self, eng, build, inc=None, n=1):
        if inc is not None:
            self.ctx.cnt[inc] += n
            h = self.ctx.sems[inc]
            self.q[eng].append(lambda e: build(e).then_inc(h, n))
            return self.ctx.cnt[inc]
        self.q[eng].append(lambda e: build(e))
        return None

    def dma(self, eng, sem, build):
        return self.op(eng, build, inc=sem, n=16)

    def run(self):
        with self.ctx.nc.Block(self.name or None) as blk:
            for e in ENGINES:
                if self.q[e]:
                    def body(eng, q=tuple(self.q[e])):
                        for f in q:
                            f(eng)
                    getattr(blk, e)(body)


def emit_transpose_in(ph, *, hbm_tm, ar, sem_ld, sem_tp, sem_ev, ps_t, tmp_tm):
    ctx = ph.ctx
    nb = len(ps_t)
    nt = 0
    for ti in range(T // 128):
        tm = tmp_tm[ti % 2]
        if ti >= 2:
            ph.wait("sync", sem_tp, ctx.cnt[sem_tp] - KC)
        ph.dma("sync", sem_ld[ti % 2],
               lambda e, tm=tm, ti=ti: e.dma_start(
                   out=tm[:], in_=hbm_tm[ti * 128:(ti + 1) * 128, :]))
        ph.wait("tensor", sem_ld[ti % 2])
        for kc in range(KC):
            ps = ps_t[nt % nb]
            if nt >= nb:
                ph.wait("tensor", sem_ev, ctx.cnt[sem_ev] - nb + 1)
            ph.op("tensor",
                  lambda e, tm=tm, kc=kc, ps=ps: e.transpose(
                      ps[:, 0:128], tm[:, kc * 128:(kc + 1) * 128], ctx.ident_f32[:]),
                  inc=sem_tp)
            ph.wait("vector", sem_tp)
            col = kc * T + ti * 128
            ph.op("vector",
                  lambda e, ps=ps, col=col: e.tensor_copy(
                      ar[:, col:col + 128], ps[:, 0:128]),
                  inc=sem_ev)
            nt += 1


def emit_transpose_out(ph, *, ar, hbm_tm, sem_tp, sem_ev, sem_st, ps_t, tmp_tm):
    ctx = ph.ctx
    nb = len(ps_t)
    nt = 0
    for ti in range(T // 128):
        tm = tmp_tm[ti % 2]
        if ti >= 2:
            ph.wait("vector", sem_st[ti % 2])
        for kc in range(KC):
            ps = ps_t[nt % nb]
            if nt >= nb:
                ph.wait("tensor", sem_ev, ctx.cnt[sem_ev] - nb + 1)
            col = kc * T + ti * 128
            ph.op("tensor",
                  lambda e, col=col, ps=ps: e.transpose(
                      ps[:, 0:128], ar[:, col:col + 128], ctx.ident_f32[:]),
                  inc=sem_tp)
            ph.wait("vector", sem_tp)
            ph.op("vector",
                  lambda e, ps=ps, tm=tm, kc=kc: e.tensor_copy(
                      tm[:, kc * 128:(kc + 1) * 128], ps[:, 0:128]),
                  inc=sem_ev)
            nt += 1
        ph.wait("sync", sem_ev)
        ph.dma("sync", sem_st[ti % 2],
               lambda e, tm=tm, ti=ti: e.dma_start(
                   out=hbm_tm[ti * 128:(ti + 1) * 128, :], in_=tm[:]))


def emit_postln(ph, *, ar, sq, ps_stat, ps_b, sem_mm, sem_ev, out_bf=None, eps_ap=None, stop_after=None):
    ctx = ph.ctx
    rows = ctx.rows
    for tt in range(NTT):
        c0 = tt * TT
        r_m = rows[0:1, 0 * TT:1 * TT]
        r_v = rows[0:1, 1 * TT:2 * TT]
        r_rs = rows[0:1, 2 * TT:3 * TT]
        r_mr = rows[0:1, 3 * TT:4 * TT]
        r_t = rows[0:1, 2 * TT:3 * TT]
        mm_sq0 = ctx.cnt[sem_mm]
        for kc in range(KC):
            sqt = sq[:, (kc % 2) * TT:(kc % 2 + 1) * TT]
            if kc >= 2:
                ph.wait("scalar", sem_mm, mm_sq0 + kc - 1)
            ph.op("scalar",
                  lambda e, kc=kc, c0=c0, sqt=sqt: e.activation(
                      sqt, ar[:, kc * T + c0: kc * T + c0 + TT],
                      AF.Square, bias=0.0, scale=1.0),
                  inc=sem_ev)
            ph.op("tensor",
                  lambda e, kc=kc, c0=c0: e.matmul(
                      ps_stat[0][0:1, :], ctx.ones_col_f32[:, 0:1],
                      ar[:, kc * T + c0: kc * T + c0 + TT],
                      start=(kc == 0), stop=(kc == KC - 1)))
            ph.wait("tensor", sem_ev)
            ph.op("tensor",
                  lambda e, kc=kc, sqt=sqt: e.matmul(
                      ps_stat[1][0:1, :], ctx.ones_col_f32[:, 0:1],
                      sqt, start=(kc == 0), stop=(kc == KC - 1)),
                  inc=sem_mm)
        ph.wait("scalar", sem_mm)
        if stop_after == "stats":
            continue
        ph.op("scalar", lambda e, r_m=r_m: e.mul(r_m, ps_stat[0][0:1, :], 1.0 / D))
        ph.op("scalar", lambda e, r_v=r_v: e.mul(r_v, ps_stat[1][0:1, :], 1.0 / D),
              inc=sem_ev)
        ph.wait("vector", sem_ev)
        ph.op("vector", lambda e, r_t=r_t, r_m=r_m: e.tensor_tensor(
            r_t, r_m, r_m, ALU.mult))
        ph.op("vector", lambda e, r_v=r_v, r_t=r_t: e.tensor_sub(r_v, r_v, r_t),
              inc=sem_ev)
        ph.wait("scalar", sem_ev)
        ph.op("scalar", lambda e, r_rs=r_rs, r_v=r_v: e.activation(
            r_rs, r_v, AF.Sqrt, bias=eps_ap, scale=1.0), inc=sem_ev)
        ph.wait("vector", sem_ev)
        ph.op("vector", lambda e, r_rs=r_rs: e.reciprocal(r_rs, r_rs))
        ph.op("vector", lambda e, r_mr=r_mr, r_m=r_m, r_rs=r_rs: e.tensor_tensor(
            r_mr, r_m, r_rs, ALU.mult), inc=sem_ev)
        ph.wait("tensor", sem_ev)
        if stop_after == "rows":
            continue
        ph.op("tensor", lambda e, r_rs=r_rs: e.matmul(
            ps_b[0][:, :], ctx.ones_row_f32[0:1, :], r_rs, start=True, stop=True))
        ph.op("tensor", lambda e, r_mr=r_mr: e.matmul(
            ps_b[1][:, :], ctx.ones_row_f32[0:1, :], r_mr, start=True, stop=True),
            inc=sem_mm)
        ph.wait("scalar", sem_mm)
        if stop_after == "bcast":
            continue
        ph.op("scalar", lambda e: e.copy(sq[:, 0:TT], ps_b[0][:, :]))
        ph.op("scalar", lambda e: e.copy(sq[:, TT:2 * TT], ps_b[1][:, :]), inc=sem_ev)
        ph.wait("vector", sem_ev)
        for kc in range(KC):
            c = kc * T + c0
            ph.op("vector",
                  lambda e, c=c: e.tensor_tensor(
                      ar[:, c:c + TT], ar[:, c:c + TT], sq[:, 0:TT], ALU.mult))
            ph.op("vector",
                  lambda e, c=c: e.tensor_tensor(
                      ar[:, c:c + TT], ar[:, c:c + TT], sq[:, TT:2 * TT], ALU.subtract),
                  inc=sem_ev)
            if out_bf is not None:
                ph.wait("scalar", sem_ev)
                ph.op("scalar",
                      lambda e, c=c: e.copy(out_bf[:, c:c + TT], ar[:, c:c + TT]),
                      inc=sem_ev)
        ph.wait("tensor", sem_ev)
        ph.wait("scalar", sem_ev)


def build_l1ffn(ctx, ar, w, sc):
    HC = 32
    s_ld = ctx.sem("ff_ld")
    s_cv = ctx.sem("ff_cv")
    s_cv1 = ctx.sem("ff_cv1")
    s_mm = ctx.sem("ff_mm")
    s_ev = ctx.sem("ff_ev")

    h_a = sc.sbuf("ff_ha", [128, HC * TT], BF16)
    h_b = sc.sbuf("ff_hb", [128, HC * TT], BF16)
    stg2 = sc.sbuf("ff_stg", [128, 2 * 4096], F32)
    stg = stg2[:, 0:4096]
    wb = sc.sbuf("ff_wb", [128, 2 * 4096], BF16)
    xq = sc.sbuf("ff_xq", [128, KC * TT], BF16)
    silu_t = [sc.sbuf(f"ff_silu{i}", [128, TT], BF16) for i in range(2)]
    out_t = [sc.sbuf(f"ff_out{i}", [128, TT], F32) for i in range(2)]
    psA = [sc.psum(f"ff_psA{i}", [128, TT]) for i in range(2)]
    psB = [sc.psum(f"ff_psB{i}", [128, TT]) for i in range(2)]

    sv = lambda base, ch: ctx.vecs[:, base + ch: base + ch + 1]

    for q in range(NTT):
        t0 = q * TT

        ph = Ph(ctx, f"ffA{q}")
        for kc in range(KC):
            ph.op("vector",
                  lambda e, kc=kc, t0=t0: e.tensor_copy(
                      xq[:, kc * TT:(kc + 1) * TT], ar[:, kc * T + t0: kc * T + t0 + TT]),
                  inc=(s_cv if kc == KC - 1 else None))
        ph.wait("tensor", s_cv)
        cv0 = ctx.cnt[s_cv]
        mm0 = ctx.cnt[s_mm]
        ev0 = ctx.cnt[s_ev]
        for hch in range(HC):
            par = hch % 2
            sb = stg2[:, par * 4096: par * 4096 + 2048]
            if hch >= 2:
                ph.wait("sync", s_cv, cv0 + hch - 1)
            ph.dma("sync", s_ld,
                   lambda e, hch=hch, sb=sb: e.dma_start(
                       out=sb[:, 0:1024].rearrange("p (kc n) -> p kc n", kc=8),
                       in_=w["w1T"][:, hch * 128:(hch + 1) * 128].rearrange(
                           "(kc p) n -> p kc n", p=128)))
            ph.dma("sync", s_ld,
                   lambda e, hch=hch, sb=sb: e.dma_start(
                       out=sb[:, 1024:2048].rearrange("p (kc n) -> p kc n", kc=8),
                       in_=w["w2T"][:, hch * 128:(hch + 1) * 128].rearrange(
                           "(kc p) n -> p kc n", p=128)))
            ph.wait("scalar", s_ld)
            if hch >= 2:
                ph.wait("scalar", s_mm, mm0 + 2 * (hch - 1))
            ph.op("scalar",
                  lambda e, par=par, sb=sb: e.copy(
                      wb[:, par * 2048: par * 2048 + 2048], sb[:, 0:2048]),
                  inc=s_cv)
            ph.wait("tensor", s_cv)
            if hch >= 2:
                ph.wait("tensor", s_ev, ev0 + 2 * (hch - 1))
            for kc in range(KC):
                ph.op("tensor",
                      lambda e, kc=kc, par=par: e.matmul(
                          psA[par][:], wb[:, par * 2048 + kc * 128: par * 2048 + (kc + 1) * 128],
                          xq[:, kc * TT:(kc + 1) * TT],
                          start=(kc == 0), stop=(kc == KC - 1)),
                      inc=(s_mm if kc == KC - 1 else None))
            for kc in range(KC):
                ph.op("tensor",
                      lambda e, kc=kc, par=par: e.matmul(
                          psB[par][:], wb[:, par * 2048 + 1024 + kc * 128: par * 2048 + 1024 + (kc + 1) * 128],
                          xq[:, kc * TT:(kc + 1) * TT],
                          start=(kc == 0), stop=(kc == KC - 1)),
                      inc=(s_mm if kc == KC - 1 else None))
            ph.wait("scalar", s_mm, mm0 + 2 * hch + 1)
            if hch >= 2:
                ph.wait("scalar", s_ev, ev0 + 2 * (hch - 1))
            ph.op("scalar",
                  lambda e, par=par, hch=hch: e.activation(
                      silu_t[par][:, :], psA[par][:], AF.Silu,
                      bias=0.0, scale=sv(w["s1c"], hch)),
                  inc=s_ev)
            ph.wait("vector", s_ev, ev0 + 2 * hch + 1)
            ph.wait("vector", s_mm, mm0 + 2 * hch + 2)
            ph.op("vector",
                  lambda e, par=par, hch=hch: e.scalar_tensor_tensor(
                      h_a[:, hch * TT:(hch + 1) * TT], psB[par][:],
                      sv(w["s2c"], hch), silu_t[par][:, :], ALU.mult, ALU.mult),
                  inc=s_ev)
        ph.run()

        for nm, wt, sb, src, dst in (("ffB", "h1T", "sh1c", h_a, h_b),
                                     ("ffC", "h2T", "sh2c", h_b, h_a)):
            ph = Ph(ctx, f"{nm}{q}")
            mm0 = ctx.cnt[s_mm]
            ev0 = ctx.cnt[s_ev]
            for nch in range(HC):
                par = nch % 2
                cs = s_cv if par == 0 else s_cv1
                stgp = stg2[:, par * 4096:(par + 1) * 4096]
                if nch >= 2:
                    ph.wait("sync", cs, ctx.cnt[cs])
                ph.dma("sync", s_ld,
                       lambda e, nch=nch, wt=wt, stgp=stgp: e.dma_start(
                           out=stgp.rearrange("p (kc n) -> p kc n", kc=32),
                           in_=w[wt][:, nch * 128:(nch + 1) * 128].rearrange(
                               "(kc p) n -> p kc n", p=128)))
                ph.wait("scalar", s_ld)
                if nch >= 2:
                    ph.wait("scalar", s_mm, mm0 + nch - 1)
                ph.op("scalar",
                      lambda e, par=par, stgp=stgp: e.copy(
                          wb[:, par * 4096: par * 4096 + 4096], stgp),
                      inc=cs)
                ph.wait("tensor", cs)
                if nch >= 2:
                    ph.wait("tensor", s_ev, ev0 + nch - 1)
                for kch in range(HC):
                    ph.op("tensor",
                          lambda e, kch=kch, par=par, src=src: e.matmul(
                              psA[par][:], wb[:, par * 4096 + kch * 128: par * 4096 + (kch + 1) * 128],
                              src[:, kch * TT:(kch + 1) * TT],
                              start=(kch == 0), stop=(kch == HC - 1)),
                          inc=(s_mm if kch == HC - 1 else None))
                ph.wait("scalar", s_mm, mm0 + nch + 1)
                ph.op("scalar",
                      lambda e, par=par, nch=nch, dst=dst, sb=sb: e.activation(
                          dst[:, nch * TT:(nch + 1) * TT], psA[par][:], AF.Gelu,
                          bias=0.0, scale=sv(w[sb], nch)),
                      inc=s_ev)
            ph.run()

        ph = Ph(ctx, f"ffD{q}")
        cv0 = ctx.cnt[s_cv]
        mm0 = ctx.cnt[s_mm]
        ev0 = ctx.cnt[s_ev]
        for dch in range(KC):
            par = dch % 2
            sb = stg2[:, par * 4096:(par + 1) * 4096]
            if dch >= 2:
                ph.wait("sync", s_cv, cv0 + dch - 1)
            ph.dma("sync", s_ld,
                   lambda e, dch=dch, sb=sb: e.dma_start(
                       out=sb.rearrange("p (kc n) -> p kc n", kc=32),
                       in_=w["w3T"][:, dch * 128:(dch + 1) * 128].rearrange(
                           "(kc p) n -> p kc n", p=128)))
            ph.wait("scalar", s_ld)
            if dch >= 2:
                ph.wait("scalar", s_mm, mm0 + dch - 1)
            ph.op("scalar",
                  lambda e, par=par, sb=sb: e.copy(
                      wb[:, par * 4096: par * 4096 + 4096], sb),
                  inc=s_cv)
            ph.wait("tensor", s_cv)
            if dch >= 2:
                ph.wait("tensor", s_ev, ev0 + 2 * (dch - 1))
            for kch in range(HC):
                ph.op("tensor",
                      lambda e, kch=kch, par=par: e.matmul(
                          psA[par][:], wb[:, par * 4096 + kch * 128: par * 4096 + (kch + 1) * 128],
                          h_a[:, kch * TT:(kch + 1) * TT],
                          start=(kch == 0), stop=(kch == HC - 1)),
                      inc=(s_mm if kch == HC - 1 else None))
            ph.wait("scalar", s_mm, mm0 + dch + 1)
            if dch >= 2:
                ph.wait("scalar", s_ev, ev0 + 2 * (dch - 1))
            ph.op("scalar",
                  lambda e, par=par, dch=dch: e.activation(
                      out_t[par][:, :], psA[par][:], AF.Copy,
                      bias=0.0, scale=sv(w["s3c"], dch)),
                  inc=s_ev)
            ph.wait("vector", s_ev, ev0 + 2 * dch + 1)
            c = dch * T + t0
            ph.op("vector",
                  lambda e, par=par, c=c: e.tensor_add(
                      ar[:, c:c + TT], ar[:, c:c + TT], out_t[par][:, :]),
                  inc=s_ev)
        ph.run()

    ph = Ph(ctx, "ffLN")
    emit_postln(ph, ar=ar, sq=stg, ps_stat=[psA[0], psA[1]], ps_b=[psB[0], psB[1]],
                sem_mm=s_mm, sem_ev=s_ev,
                eps_ap=ctx.vecs[0:1, w["keps"]:w["keps"] + 1])
    ph.run()


def emit_load_w1024(ph, sc_hbm, stg, wb, s_ld, s_cv, nin=1024, nout=1024):
    kc = nin // 128
    half = nout // 2
    for h in range(2):
        n0 = h * half
        ph.dma("sync", s_ld,
               lambda e, n0=n0: e.dma_start(
                   out=stg[:, 0:kc * half].rearrange("p (k n) -> p k n", k=kc),
                   in_=sc_hbm[:, n0:n0 + half].rearrange("(k p) n -> p k n", p=128)))
        ph.wait("gpsimd", s_ld)
        for k in range(kc):
            ph.op("gpsimd",
                  lambda e, k=k, n0=n0: e.tensor_copy(
                      wb[:, k * nout + n0: k * nout + n0 + half],
                      stg[:, k * half:(k + 1) * half]),
                  inc=(s_cv if k == kc - 1 else None))


def build_l0attn(ctx, ar, w, cc, sc, upto="H"):
    nc = ctx.nc
    EPS = 1e-6
    s_ld = ctx.sem("a_ld")
    s_cv = ctx.sem("a_cv")
    s_mm = ctx.sem("a_mm")
    s_ev = ctx.sem("a_ev")
    s_tp = ctx.sem("a_tp")
    s_cc = ctx.sem("a_cc")
    s_gld = ctx.sem("a_gld")

    Qb = sc.sbuf("a_Q", [128, KC * T], BF16)
    Kb = sc.sbuf("a_K", [128, KC * T], BF16)
    Vb = sc.sbuf("a_V", [128, KC * T], BF16)
    stats = sc.sbuf("a_stats", [16, 3 * T], BF16)
    ecol = sc.sbuf("a_ecol", [16, 8], F32)
    dnm = sc.sbuf("a_dnm", [16, 2], F32)
    blkA = sc.sbuf("a_blkA", [128, 8 * 16], BF16)
    blkB = sc.sbuf("a_blkB", [128, 8 * 16], BF16)
    r1 = sc.sbuf("a_r1", [128, 16], F32)
    r1e = sc.sbuf("a_r1e", [128, 16], F32)
    r2 = sc.sbuf("a_r2", [128, 16], F32)
    r2e = sc.sbuf("a_r2e", [128, 16], F32)
    kvb = sc.sbuf("a_kvb", [128, 16 * 64], BF16)
    blkkv = sc.sbuf("a_blkkv", [128, 128], BF16)
    ktt = sc.sbuf("a_ktt", [128, 2 * 128], BF16)
    vtt = sc.sbuf("a_vtt", [128, 2 * 128], BF16)
    stg = sc.sbuf("a_stg", [128, 2048], F32)
    s8k = stg
    kvsb = stg[:, 0:1024]
    sia_sb = stg[:, 1024:1024 + TT]
    srow32 = stg[0:16, 1536:1536 + TT]
    wb = sc.sbuf("a_wb", [128, 8 * 512], BF16)

    psA = sc.psum("a_psA", [128, TT])
    psB = sc.psum("a_psB", [128, TT])
    psD = sc.psum("a_psD", [16, TT])
    psC = sc.psum("a_psC", [16, 16])
    pstK = sc.psum("a_pstK", [128, 128], BF16)
    pstV = sc.psum("a_pstV", [128, 128], BF16)
    kve = sc.psum("a_kve", [64, 64])
    kvo = sc.psum("a_kvo", [64, 64])

    def xcvt(kc):
        return s8k[:, kc * 256:(kc + 1) * 256].bitcast(BF16)

    S_SI, S_SC, S_SA = 0, 1, 2

    def srow(blk, tt=None):
        if tt is None:
            return stats[:, blk * T:(blk + 1) * T]
        return stats[:, blk * T + tt * TT: blk * T + (tt + 1) * TT]

    kon = ctx.vecs[:, w["kone"]:w["kone"] + 1]

    for pi, (wt, bcol, func, arena) in enumerate((
            ("WqT", "bq", AF.Sigmoid, Qb),
            ("WkT", "bk", AF.Sigmoid, Kb),
            ("WvT", "bv", AF.Identity, Vb))):
        ph = Ph(ctx, f"a_proj{pi}")
        for h in range(2):
            n0 = h * 512
            if h == 1:
                ph.wait("sync", s_mm)
            for qtr in range(2):
                if qtr == 1:
                    ph.wait("sync", s_cv)
                ph.dma("sync", s_ld,
                       lambda e, n0=n0, qtr=qtr, wt=wt: e.dma_start(
                           out=stg[:, 0:2048].rearrange("p (k n) -> p k n", k=8),
                           in_=w[wt][:, n0 + qtr * 256: n0 + qtr * 256 + 256].rearrange(
                               "(k p) n -> p k n", p=128)))
                ph.wait("gpsimd", s_ld)
                for k in range(KC):
                    ph.op("gpsimd",
                          lambda e, k=k, qtr=qtr: e.tensor_copy(
                              wb[:, k * 512 + qtr * 256: k * 512 + qtr * 256 + 256],
                              stg[:, k * 256:(k + 1) * 256]),
                          inc=(s_cv if k == KC - 1 else None))
            cvh = ctx.cnt[s_cv]
            mm0 = ctx.cnt[s_mm]
            nidx = 0
            for tt in range(NTT):
                ph.wait("vector", s_cv, cvh)
                ph.wait("vector", s_mm, mm0 + 4 * tt)
                for kc in range(KC):
                    ph.op("vector",
                          lambda e, kc=kc, tt=tt: e.tensor_copy(
                              xcvt(kc), ar[:, kc * T + tt * TT: kc * T + (tt + 1) * TT]),
                          inc=(s_cv if kc == KC - 1 else None))
                ph.wait("tensor", s_cv)
                for nch in range(4):
                    ps = psA if nidx % 2 == 0 else psB
                    ph.wait("tensor", s_ev, ctx.cnt[s_ev] - 1)
                    for kc in range(KC):
                        ph.op("tensor",
                              lambda e, kc=kc, nch=nch, ps=ps: e.matmul(
                                  ps[:], wb[:, kc * 512 + nch * 128: kc * 512 + (nch + 1) * 128],
                                  xcvt(kc), start=(kc == 0), stop=(kc == KC - 1)),
                              inc=(s_mm if kc == KC - 1 else None))
                    ph.wait("scalar", s_mm)
                    nabs = h * 4 + nch
                    ph.op("scalar",
                          lambda e, ps=ps, nabs=nabs, tt=tt, arena=arena, func=func, bcol=bcol: e.activation(
                              arena[:, nabs * T + tt * TT: nabs * T + (tt + 1) * TT],
                              ps[:], func, bias=ctx.vecs[:, w[bcol] + nabs: w[bcol] + nabs + 1],
                              scale=1.0),
                          inc=s_ev)
                    nidx += 1
        ph.run()

    if upto == "A":
        return (Qb, Kb, Vb, stats, r1, r2, kvb, dnm, ecol)
    ph = Ph(ctx, "a_sums")
    for kc in range(KC):
        ph.op("vector",
              lambda e, kc=kc: e.reduce_sum(r1[:, kc:kc + 1], Kb[:, kc * T:(kc + 1) * T], axis=AX.X))
        ph.op("vector",
              lambda e, kc=kc: e.reduce_sum(r1[:, 8 + kc:9 + kc], Qb[:, kc * T:(kc + 1) * T], axis=AX.X),
              inc=(s_ev if kc == KC - 1 else None))
    ph.wait("gpsimd", s_ev)
    ph.dma("gpsimd", s_gld, lambda e: e.dma_start(out=cc["r16_in"], in_=r1[:, :]))
    ph.wait("gpsimd", s_gld)
    ph.op("gpsimd", lambda e: e.collective_compute(
        "AllReduce", ALU.add, replica_groups=[list(range(NCORES))],
        ins=[cc["r16_in"].opt()], outs=[cc["r16_out"].opt()]), inc=s_cc)
    ph.wait("gpsimd", s_cc)
    ph.dma("gpsimd", s_gld, lambda e: e.dma_start(out=r1[:, :], in_=cc["r16_out"]))
    ph.wait("vector", s_gld)
    ph.op("vector", lambda e: e.tensor_scalar_add(
        r1e[:, :], r1[:, :], ctx.vecs[:, w["keps6"]:w["keps6"] + 1]))
    ph.op("vector", lambda e: e.memset(blkA[:, :], 0.0))
    ph.op("vector", lambda e: e.memset(blkB[:, :], 0.0), inc=s_ev)
    for kc in range(KC):
        ph.op("vector", lambda e, kc=kc: e.tensor_copy(
            blkA[0:64, kc * 16 + 2 * kc: kc * 16 + 2 * kc + 1], r1e[0:64, kc:kc + 1]))
        ph.op("vector", lambda e, kc=kc: e.tensor_copy(
            blkA[64:128, kc * 16 + 2 * kc + 1: kc * 16 + 2 * kc + 2], r1e[64:128, kc:kc + 1]))
        ph.op("vector", lambda e, kc=kc: e.tensor_copy(
            blkB[0:64, kc * 16 + 2 * kc: kc * 16 + 2 * kc + 1], r1e[0:64, 8 + kc:9 + kc]))
        ph.op("vector", lambda e, kc=kc: e.tensor_copy(
            blkB[64:128, kc * 16 + 2 * kc + 1: kc * 16 + 2 * kc + 2], r1e[64:128, 8 + kc:9 + kc]),
            inc=(s_ev if kc == KC - 1 else None))
    ph.wait("tensor", s_ev)
    for j, blk in enumerate((blkA, blkB)):
        for kc in range(KC):
            ph.op("tensor",
                  lambda e, blk=blk, kc=kc: e.matmul(
                      psC[0:16, 0:1], blk[:, kc * 16:(kc + 1) * 16],
                      ctx.ones_col_bf[:, 0:1], start=(kc == 0), stop=(kc == KC - 1)),
                  inc=(s_mm if kc == KC - 1 else None))
        ph.wait("scalar", s_mm)
        ph.op("scalar", lambda e, j=j: e.mul(ecol[:, j:j + 1], psC[0:16, 0:1], EPS),
              inc=s_ev)
        ph.wait("tensor", s_ev)
    ph.wait("scalar", s_ev)
    for j, (blk, rhs_a, sblk, ec) in enumerate(
            ((blkA, Qb, S_SI, 0), (blkB, Kb, S_SC, 1))):
        for tt in range(NTT):
            for kc in range(KC):
                ph.op("tensor",
                      lambda e, blk=blk, kc=kc, rhs_a=rhs_a, tt=tt: e.matmul(
                          psD[0:16, :], blk[:, kc * 16:(kc + 1) * 16],
                          rhs_a[:, kc * T + tt * TT: kc * T + (tt + 1) * TT],
                          start=(kc == 0), stop=(kc == KC - 1)),
                      inc=(s_mm if kc == KC - 1 else None))
            ph.wait("scalar", s_mm)
            ph.wait("scalar", s_ev)
            ph.op("scalar",
                  lambda e, ec=ec: e.activation(
                      srow32[0:16, :], psD[0:16, :], AF.Identity,
                      bias=ecol[0:16, ec:ec + 1], scale=1.0),
                  inc=s_ev)
            ph.wait("vector", s_ev)
            ph.op("vector",
                  lambda e: e.reciprocal(srow32[0:16, :], srow32[0:16, :]))
            ph.op("vector",
                  lambda e, sblk=sblk, tt=tt: e.tensor_copy(srow(sblk, tt), srow32[0:16, :]),
                  inc=s_ev)
            ph.wait("tensor", s_ev)
    ph.run()

    if upto == "B":
        return (Qb, Kb, Vb, stats, r1, r2, kvb, dnm, ecol)
    ph = Ph(ctx, "a_r2")
    for j, (arena, sblk, col0) in enumerate(((Kb, S_SC, 0), (Qb, S_SI, 8))):
        for kc in range(KC):
            for tt in range(NTT):
                ph.op("tensor",
                      lambda e, kc=kc, tt=tt, sblk=sblk: e.matmul(
                          psB[:], ctx.ind_bf[0:16, kc * 128:(kc + 1) * 128],
                          srow(sblk, tt), start=True, stop=True),
                      inc=s_mm)
                ph.wait("vector", s_mm)
                ph.op("vector",
                      lambda e, kc=kc, tt=tt, arena=arena: e.scalar_tensor_tensor(
                          s8k[:, tt * TT:(tt + 1) * TT],
                          arena[:, kc * T + tt * TT: kc * T + (tt + 1) * TT],
                          kon, psB[:], ALU.bypass, ALU.mult),
                      inc=s_ev)
                ph.wait("tensor", s_ev)
            ph.wait("vector", s_ev)
            ph.op("vector",
                  lambda e, kc=kc, col0=col0: e.reduce_sum(
                      r2[:, col0 + kc: col0 + kc + 1], s8k[:, :], axis=AX.X),
                  inc=s_ev)
    ph.wait("gpsimd", s_ev)
    ph.dma("gpsimd", s_gld, lambda e: e.dma_start(out=cc["r16_in"], in_=r2[:, :]))
    ph.wait("gpsimd", s_gld)
    ph.op("gpsimd", lambda e: e.collective_compute(
        "AllReduce", ALU.add, replica_groups=[list(range(NCORES))],
        ins=[cc["r16_in"].opt()], outs=[cc["r16_out"].opt()]), inc=s_cc)
    ph.wait("gpsimd", s_cc)
    ph.dma("gpsimd", s_gld, lambda e: e.dma_start(out=r2[:, :], in_=cc["r16_out"]))
    ph.run()

    if upto == "C":
        return (Qb, Kb, Vb, stats, r1, r2, kvb, dnm, ecol)
    ph = Ph(ctx, "a_cs")
    ph.wait("vector", s_gld)
    ph.op("vector", lambda e: e.tensor_scalar_add(
        r2e[:, :], r2[:, :], ctx.vecs[:, w["keps6"]:w["keps6"] + 1]))
    ph.op("vector", lambda e: e.memset(blkA[:, :], 0.0))
    ph.op("vector", lambda e: e.memset(blkB[:, :], 0.0), inc=s_ev)
    for kc in range(KC):
        ph.op("vector", lambda e, kc=kc: e.tensor_copy(
            blkA[0:64, kc * 16 + 2 * kc: kc * 16 + 2 * kc + 1], r2e[0:64, kc:kc + 1]))
        ph.op("vector", lambda e, kc=kc: e.tensor_copy(
            blkA[64:128, kc * 16 + 2 * kc + 1: kc * 16 + 2 * kc + 2], r2e[64:128, kc:kc + 1]))
        ph.op("vector", lambda e, kc=kc: e.tensor_copy(
            blkB[0:64, kc * 16 + 2 * kc: kc * 16 + 2 * kc + 1], r2e[0:64, 8 + kc:9 + kc]))
        ph.op("vector", lambda e, kc=kc: e.tensor_copy(
            blkB[64:128, kc * 16 + 2 * kc + 1: kc * 16 + 2 * kc + 2], r2e[64:128, 8 + kc:9 + kc]),
            inc=(s_ev if kc == KC - 1 else None))
    ph.wait("tensor", s_ev)
    for j, blk in enumerate((blkA, blkB)):
        for kc in range(KC):
            ph.op("tensor",
                  lambda e, blk=blk, kc=kc: e.matmul(
                      psC[0:16, 0:1], blk[:, kc * 16:(kc + 1) * 16],
                      ctx.ones_col_bf[:, 0:1], start=(kc == 0), stop=(kc == KC - 1)),
                  inc=(s_mm if kc == KC - 1 else None))
        ph.wait("scalar", s_mm)
        ph.op("scalar", lambda e, j=j: e.mul(ecol[:, 2 + j:3 + j], psC[0:16, 0:1], EPS),
              inc=s_ev)
        ph.wait("tensor", s_ev)
    ph.wait("scalar", s_ev)
    for j, (blk, rhs_a, sblk, ec) in enumerate(
            ((blkA, Qb, S_SA, 2), (blkB, Kb, S_SC, 3))):
        for tt in range(NTT):
            for kc in range(KC):
                ph.op("tensor",
                      lambda e, blk=blk, kc=kc, rhs_a=rhs_a, tt=tt: e.matmul(
                          psD[0:16, :], blk[:, kc * 16:(kc + 1) * 16],
                          rhs_a[:, kc * T + tt * TT: kc * T + (tt + 1) * TT],
                          start=(kc == 0), stop=(kc == KC - 1)),
                      inc=(s_mm if kc == KC - 1 else None))
            ph.wait("scalar", s_mm)
            if j == 0:
                ph.op("scalar",
                      lambda e, tt=tt: e.activation(
                          srow(S_SA, tt), psD[0:16, :], AF.Sigmoid,
                          bias=ecol[0:16, 2:3], scale=1.0),
                      inc=s_ev)
            else:
                ph.op("scalar",
                      lambda e, tt=tt: e.activation(
                          srow(S_SC, tt), psD[0:16, :], AF.Identity,
                          bias=ecol[0:16, 3:4], scale=1.0),
                      inc=s_ev)
                ph.wait("vector", s_ev)
                ph.op("vector",
                      lambda e, tt=tt: e.tensor_scalar(
                          srow(S_SC, tt), srow(S_SC, tt), 1.0, -1.0,
                          ALU.min, ALU.max),
                      inc=s_ev)
                ph.wait("scalar", s_ev)
                ph.op("scalar",
                      lambda e, tt=tt: e.activation(
                          srow(S_SC, tt), srow(S_SC, tt), AF.Exp,
                          bias=0.0, scale=1.0),
                      inc=s_ev)
            ph.wait("tensor", s_ev)
    ph.wait("vector", s_ev)
    ph.op("vector", lambda e: e.reduce_sum(dnm[:, 0:1], srow(S_SC), axis=AX.X),
          inc=s_ev)
    ph.wait("gpsimd", s_ev)
    ph.dma("gpsimd", s_gld, lambda e: e.dma_start(out=cc["d_in"], in_=dnm[0:16, 0:1]))
    ph.wait("gpsimd", s_gld)
    ph.op("gpsimd", lambda e: e.collective_compute(
        "AllReduce", ALU.add, replica_groups=[list(range(NCORES))],
        ins=[cc["d_in"].opt()], outs=[cc["d_out"].opt()]), inc=s_cc)
    ph.wait("gpsimd", s_cc)
    ph.dma("gpsimd", s_gld, lambda e: e.dma_start(out=dnm[0:16, 1:2], in_=cc["d_out"]))
    ph.wait("vector", s_gld)
    ph.op("vector", lambda e: e.reciprocal(dnm[:, 1:2], dnm[:, 1:2]), inc=s_ev)
    ph.wait("vector", s_ev)
    ph.op("vector", lambda e: e.tensor_scalar(
        srow(S_SC), srow(S_SC), dnm[0:16, 1:2], float(TSEQ), ALU.mult, ALU.mult))
    ph.op("vector", lambda e: e.tensor_tensor(
        srow(S_SI), srow(S_SI), srow(S_SA), ALU.mult), inc=s_ev)
    ph.run()

    if upto == "D":
        return (Qb, Kb, Vb, stats, r1, r2, kvb, dnm, ecol)
    ph = Ph(ctx, "a_kv")
    mm0 = ctx.cnt[s_mm]
    for kc in range(KC):
        for tt in range(NTT):
            ph.op("tensor",
                  lambda e, kc=kc, tt=tt: e.matmul(
                      psB[:], ctx.ind_bf[0:16, kc * 128:(kc + 1) * 128],
                      srow(S_SC, tt), start=True, stop=True),
                  inc=s_mm)
            ph.wait("vector", s_mm)
            ph.op("vector",
                  lambda e, kc=kc, tt=tt: e.scalar_tensor_tensor(
                      Vb[:, kc * T + tt * TT: kc * T + (tt + 1) * TT],
                      Vb[:, kc * T + tt * TT: kc * T + (tt + 1) * TT],
                      kon, psB[:], ALU.bypass, ALU.mult),
                  inc=s_ev)
            ph.wait("tensor", s_ev)
    tp0 = ctx.cnt[s_tp]
    ev0 = ctx.cnt[s_ev]
    mm0 = ctx.cnt[s_mm]
    for kc in range(KC):
        for ts in range(T // 128):
            par = ts % 2
            i = kc * (T // 128) + ts
            if i >= 1:
                ph.wait("tensor", s_ev, ev0 + 2 * i)
            if i >= 2:
                ph.wait("vector", s_mm, mm0 + 2 * (i - 1))
            ph.op("tensor",
                  lambda e, kc=kc, ts=ts: e.transpose(
                      pstK[:, 0:128], Kb[:, kc * T + ts * 128: kc * T + (ts + 1) * 128],
                      ctx.ident_bf[:]),
                  inc=s_tp)
            ph.op("tensor",
                  lambda e, kc=kc, ts=ts: e.transpose(
                      pstV[:, 0:128], Vb[:, kc * T + ts * 128: kc * T + (ts + 1) * 128],
                      ctx.ident_bf[:]),
                  inc=s_tp)
            ph.wait("vector", s_tp)
            ph.op("vector",
                  lambda e, par=par: e.tensor_copy(ktt[:, par * 128:(par + 1) * 128], pstK[:, 0:128]),
                  inc=s_ev)
            ph.op("vector",
                  lambda e, par=par: e.tensor_copy(vtt[:, par * 128:(par + 1) * 128], pstV[:, 0:128]),
                  inc=s_ev)
            ph.wait("tensor", s_ev)
            ph.op("tensor",
                  lambda e, par=par, ts=ts: e.matmul(
                      kve[0:64, 0:64], ktt[:, par * 128: par * 128 + 64],
                      vtt[:, par * 128: par * 128 + 64],
                      start=(ts == 0), stop=(ts == T // 128 - 1)),
                  inc=s_mm)
            ph.op("tensor",
                  lambda e, par=par, ts=ts: e.matmul(
                      kvo[0:64, 0:64], ktt[:, par * 128 + 64: par * 128 + 128],
                      vtt[:, par * 128 + 64: par * 128 + 128],
                      start=(ts == 0), stop=(ts == T // 128 - 1)),
                  inc=s_mm)
        ph.wait("scalar", s_mm)
        ph.op("scalar",
              lambda e, kc=kc: e.copy(kvsb[0:64, (2 * kc) * 64:(2 * kc + 1) * 64], kve[0:64, 0:64]))
        ph.op("scalar",
              lambda e, kc=kc: e.copy(kvsb[0:64, (2 * kc + 1) * 64:(2 * kc + 2) * 64], kvo[0:64, 0:64]),
              inc=s_ev)
        ph.wait("tensor", s_ev)
    ph.wait("gpsimd", s_ev)
    ph.dma("gpsimd", s_gld, lambda e: e.dma_start(out=cc["kv_in"], in_=kvsb[:, :]))
    ph.wait("gpsimd", s_gld)
    ph.op("gpsimd", lambda e: e.collective_compute(
        "AllReduce", ALU.add, replica_groups=[list(range(NCORES))],
        ins=[cc["kv_in"].opt()], outs=[cc["kv_out"].opt()]), inc=s_cc)
    ph.wait("gpsimd", s_cc)
    ph.dma("gpsimd", s_gld, lambda e: e.dma_start(out=kvsb[:, :], in_=cc["kv_out"]))
    ph.wait("vector", s_gld)
    ph.op("vector", lambda e: e.tensor_copy(kvb[:, :], kvsb[:, :]), inc=s_cv)
    ph.run()

    if upto == "E":
        return (Qb, Kb, Vb, stats, r1, r2, kvb, dnm, ecol)
    ph = Ph(ctx, "a_out")
    out_a = Kb
    for kc in range(KC):
        ph.wait("vector", s_mm)
        ph.op("vector", lambda e: e.memset(blkkv[:, :], 0.0))
        ph.op("vector", lambda e, kc=kc: e.tensor_copy(
            blkkv[0:64, 0:64], kvb[0:64, (2 * kc) * 64:(2 * kc + 1) * 64]))
        ph.op("vector", lambda e, kc=kc: e.tensor_copy(
            blkkv[64:128, 64:128], kvb[0:64, (2 * kc + 1) * 64:(2 * kc + 2) * 64]),
            inc=s_cv)
        ph.wait("tensor", s_cv)
        for tt in range(NTT):
            ph.op("tensor",
                  lambda e, kc=kc, tt=tt: e.matmul(
                      psA[:], blkkv[:, :], Qb[:, kc * T + tt * TT: kc * T + (tt + 1) * TT],
                      start=True, stop=True),
                  inc=s_mm)
            ph.op("tensor",
                  lambda e, kc=kc, tt=tt: e.matmul(
                      psB[:], ctx.ind_bf[0:16, kc * 128:(kc + 1) * 128],
                      srow(S_SI, tt), start=True, stop=True),
                  inc=s_mm)
            ph.wait("scalar", s_mm)
            ph.op("scalar", lambda e: e.copy(sia_sb[:, :], psB[:]), inc=s_ev)
            ph.wait("vector", s_ev)
            ph.op("vector",
                  lambda e, kc=kc, tt=tt: e.scalar_tensor_tensor(
                      out_a[:, kc * T + tt * TT: kc * T + (tt + 1) * TT],
                      psA[:], kon, sia_sb[:, :], ALU.bypass, ALU.mult),
                  inc=s_ev)
            ph.wait("tensor", s_ev)
    ph.run()

    if upto == "F":
        return (Qb, Kb, Vb, stats, r1, r2, kvb, dnm, ecol)
    ph = Ph(ctx, "a_wo")
    for h in range(2):
        n0 = h * 512
        if h == 1:
            ph.wait("sync", s_mm)
        for qtr in range(2):
            if qtr == 1 or h == 1:
                ph.wait("sync", s_cv)
            ph.dma("sync", s_ld,
                   lambda e, n0=n0, qtr=qtr: e.dma_start(
                       out=stg[:, 0:2048].rearrange("p (k n) -> p k n", k=8),
                       in_=w["WoT"][:, n0 + qtr * 256: n0 + qtr * 256 + 256].rearrange(
                           "(k p) n -> p k n", p=128)))
            ph.wait("gpsimd", s_ld)
            for k in range(KC):
                ph.op("gpsimd",
                      lambda e, k=k, qtr=qtr: e.tensor_copy(
                          wb[:, k * 512 + qtr * 256: k * 512 + qtr * 256 + 256],
                          stg[:, k * 256:(k + 1) * 256]),
                      inc=(s_cv if k == KC - 1 else None))
        ph.wait("tensor", s_cv)
        nidx = 0
        for nch in range(4):
            nabs = h * 4 + nch
            for tt in range(NTT):
                ps = psA if nidx % 2 == 0 else psB
                ph.wait("tensor", s_ev, ctx.cnt[s_ev] - 1)
                for kc in range(KC):
                    ph.op("tensor",
                          lambda e, kc=kc, nch=nch, tt=tt, ps=ps: e.matmul(
                              ps[:], wb[:, kc * 512 + nch * 128: kc * 512 + (nch + 1) * 128],
                              out_a[:, kc * T + tt * TT: kc * T + (tt + 1) * TT],
                              start=(kc == 0), stop=(kc == KC - 1)),
                          inc=(s_mm if kc == KC - 1 else None))
                ph.wait("vector", s_mm)
                c = nabs * T + tt * TT
                ph.op("vector",
                      lambda e, ps=ps, nabs=nabs, c=c: e.scalar_tensor_tensor(
                          ar[:, c:c + TT], ps[:],
                          ctx.vecs[:, w["bo"] + nabs: w["bo"] + nabs + 1],
                          ar[:, c:c + TT], ALU.add, ALU.add),
                      inc=s_ev)
                nidx += 1
    ph.run()

    if upto == "G":
        return (Qb, Kb, Vb, stats, r1, r2, kvb, dnm, ecol)
    ph = Ph(ctx, "a_ln")
    emit_postln(ph, ar=ar, sq=stg, ps_stat=[psA, psB], ps_b=[psA, psB],
                sem_mm=s_mm, sem_ev=s_ev,
                eps_ap=ctx.vecs[0:1, w["keps"]:w["keps"] + 1],
                stop_after={"H1": "stats", "H2": "rows", "H3": "bcast"}.get(upto))
    ph.run()
    return (Qb, Kb, Vb, stats, r1, r2, kvb, dnm, ecol)


def emit_rms_rows(ph, ctx, *, src_ar, rrow, sq, ps_stat, ps_b, s_mm, s_ev, eps_ap):
    for tt in range(NTT):
        c0 = tt * TT
        mm_sq0 = ctx.cnt[s_mm]
        for kc in range(KC):
            sqt = sq[:, (kc % 2) * TT:(kc % 2 + 1) * TT]
            if kc >= 2:
                ph.wait("scalar", s_mm, mm_sq0 + kc - 1)
            ph.op("scalar",
                  lambda e, kc=kc, c0=c0, sqt=sqt: e.activation(
                      sqt, src_ar[:, kc * T + c0: kc * T + c0 + TT],
                      AF.Square, bias=0.0, scale=1.0),
                  inc=s_ev)
            ph.wait("tensor", s_ev)
            ph.op("tensor",
                  lambda e, kc=kc, sqt=sqt: e.matmul(
                      ps_stat[0:1, :], ctx.ones_col_f32[:, 0:1],
                      sqt, start=(kc == 0), stop=(kc == KC - 1)),
                  inc=s_mm)
        ph.wait("scalar", s_mm)
        ph.op("scalar",
              lambda e, c0=c0: e.activation(
                  rrow[0:1, c0:c0 + TT], ps_stat[0:1, :], AF.Sqrt,
                  bias=eps_ap, scale=1.0 / D),
              inc=s_ev)
        ph.wait("vector", s_ev)
        ph.op("vector",
              lambda e, c0=c0: e.reciprocal(rrow[0:1, c0:c0 + TT], rrow[0:1, c0:c0 + TT]),
              inc=s_ev)
        ph.wait("tensor", s_ev)


def build_l1attn(ctx, ar, w, cc, sc, x_spill, dbg_hook=None):
    nc = ctx.nc
    s_ld = ctx.sem("m_ld")
    s_gld = ctx.sem("m_gld")
    s_cv = ctx.sem("m_cv")
    s_mm = ctx.sem("m_mm")
    s_ev = ctx.sem("m_ev")
    s_cc = ctx.sem("m_cc")

    ybf = sc.sbuf("m_ybf", [128, KC * T], BF16)
    mbuf = sc.sbuf("m_mbuf", [128, 8 * T], BF16)
    stg = sc.sbuf("m_stg", [128, 4096], F32)
    wb = sc.sbuf("m_wb", [128, 2 * 4096], BF16)
    rrow = sc.sbuf("m_rrow", [1, T], F32)
    bxs = sc.sbuf("m_bxs", [128, T], F32)
    yss = sc.sbuf("m_yss", [128, T], F32)
    scb = sc.sbuf("m_scb", [128, TT], F32)
    av = sc.sbuf("m_av", [128, 8 * 4], F32)
    lc = sc.sbuf("m_lc", [128, 8], F32)
    lg = sc.sbuf("m_lg", [128, 64], F32)

    psA = sc.psum("m_psA", [128, TT])
    psB = sc.psum("m_psB", [128, TT])
    psS = sc.psum("m_psS", [1, TT])

    atv = av[:, 0:8]
    a2k = av[:, 8:16]
    cac = av[:, 16:24]
    cse = av[:, 24:32]

    vcol = lambda base, i: ctx.vecs[:, base + i: base + i + 1]

    ph = Ph(ctx, "m_rms1")
    ph.op("scalar", lambda e: e.activation(
        atv, ctx.vecs[:, w["a"]:w["a"] + 8], AF.Tanh, bias=0.0, scale=1.0),
        inc=s_ev)
    ph.wait("vector", s_ev)
    ph.op("vector", lambda e: e.tensor_mul(a2k, atv, atv), inc=s_ev)
    for i in range(10):
        src_b, dst_b = (a2k, cac) if i % 2 == 0 else (cac, a2k)
        ph.wait("vector", s_ev)
        ph.op("vector", lambda e, s=src_b, d=dst_b: e.tensor_mul(d, s, s), inc=s_ev)
    ph.wait("vector", s_ev)
    ph.op("vector", lambda e: e.memset(cse, 0.0), inc=s_ev)
    ph.wait("vector", s_ev)
    ph.op("vector", lambda e: e.memset(cac, 0.0), inc=s_ev)
    emit_rms_rows(ph, ctx, src_ar=ar, rrow=rrow, sq=stg, ps_stat=psS, ps_b=None,
                  s_mm=s_mm, s_ev=s_ev, eps_ap=ctx.vecs[0:1, w["keps5"]:w["keps5"] + 1])
    ph.run()

    def emit_bx(ph, kc):
        for tt in range(NTT):
            ph.op("tensor",
                  lambda e, tt=tt: e.matmul(
                      psB[:], ctx.ones_row_f32[0:1, :], rrow[0:1, tt * TT:(tt + 1) * TT],
                      start=True, stop=True),
                  inc=s_mm)
            ph.wait("scalar", s_mm)
            ph.op("scalar", lambda e: e.copy(scb[:, :], psB[:]), inc=s_ev)
            ph.wait("vector", s_ev)
            ph.op("vector",
                  lambda e, kc=kc, tt=tt: e.scalar_tensor_tensor(
                      bxs[:, tt * TT:(tt + 1) * TT],
                      ar[:, kc * T + tt * TT: kc * T + (tt + 1) * TT],
                      vcol(w["n1wb"], kc), scb[:, :], ALU.mult, ALU.mult),
                  inc=s_ev)
            ph.wait("tensor", s_ev)

    for kc in range(KC):
        ph = Ph(ctx, f"m_scan1_{kc}")
        if kc > 0:
            ph.op("scalar",
                  lambda e, kc=kc: e.copy(lc[:, kc - 1:kc], yss[:, T - 1:T]),
                  inc=s_ev)
            ph.wait("vector", s_ev)
        emit_bx(ph, kc)
        ph.op("vector",
              lambda e, kc=kc: e.tensor_tensor_scan(
                  yss[:, :], atv[:, kc:kc + 1].broadcast_to([128, T]), bxs[:, :],
                  0.0, ALU.mult, ALU.add))
        ph.run()
    ph = Ph(ctx, "m_scan1f")
    ph.op("scalar",
          lambda e: e.copy(lc[:, KC - 1:KC], yss[:, T - 1:T]),
          inc=s_ev)
    ph.wait("gpsimd", s_ev)
    ph.dma("gpsimd", s_gld, lambda e: e.dma_start(out=cc["ag_in"], in_=lc[:, :]))
    ph.wait("gpsimd", s_gld)
    ph.op("gpsimd", lambda e: e.collective_compute(
        "AllGather", ALU.bypass, replica_groups=[list(range(NCORES))],
        ins=[cc["ag_in"].opt()], outs=[cc["ag_out"].opt()]), inc=s_cc)
    ph.wait("gpsimd", s_cc)
    ph.dma("gpsimd", s_gld,
           lambda e: e.dma_start(out=lg[:, :].rearrange("p (r k) -> p r k", r=NCORES),
                                 in_=cc["ag_out"].rearrange("(r p) k -> p r k", p=128)))
    ph.run()

    ph = Ph(ctx, "m_carry")
    ph.wait("vector", s_gld)
    for j in range(NCORES):
        ph.op("vector",
              lambda e, j=j: e.scalar_tensor_tensor(
                  cse, cac, ctx.coreoh[:, j:j + 1], cse, ALU.mult, ALU.add),
              inc=s_ev)
        ph.wait("vector", s_ev)
        if j < NCORES - 1:
            ph.op("vector",
                  lambda e, j=j: e.scalar_tensor_tensor(
                      cac, cac, ctx.vecs[:, w["kone"]:w["kone"] + 1], a2k,
                      ALU.bypass, ALU.mult),
                  inc=s_ev)
            ph.wait("vector", s_ev)
            ph.op("vector",
                  lambda e, j=j: e.tensor_add(cac, cac, lg[:, j * 8:(j + 1) * 8]),
                  inc=s_ev)
            ph.wait("vector", s_ev)
    ph.run()

    if dbg_hook is not None:
        dbg_hook({"d_lc": lc, "d_lg": lg, "d_av": av, "d_rrow": rrow,
                  "d_bxs": bxs, "d_yss": yss})
    for kc in range(KC):
        ph = Ph(ctx, f"m_scan2_{kc}")
        if kc > 0:
            ph.op("scalar",
                  lambda e, kc=kc: e.copy(ybf[:, (kc - 1) * T: kc * T], yss[:, :]),
                  inc=s_ev)
            ph.wait("vector", s_ev)
        emit_bx(ph, kc)
        ph.op("vector",
              lambda e, kc=kc: e.scalar_tensor_tensor(
                  bxs[:, 0:1], cse[:, kc:kc + 1], atv[:, kc:kc + 1],
                  bxs[:, 0:1], ALU.mult, ALU.add),
              inc=s_ev)
        ph.wait("vector", s_ev)
        ph.op("vector",
              lambda e, kc=kc: e.tensor_tensor_scan(
                  yss[:, :], atv[:, kc:kc + 1].broadcast_to([128, T]), bxs[:, :],
                  0.0, ALU.mult, ALU.add))
        ph.run()
    ph = Ph(ctx, "m_scan2f")
    ph.op("scalar",
          lambda e: e.copy(ybf[:, (KC - 1) * T: KC * T], yss[:, :]),
          inc=s_ev)
    ph.run()

    s_sp = ctx.sem("m_sp")
    ph = Ph(ctx, "m_proj")
    ph.dma("sync", s_sp, lambda e: e.dma_start(out=x_spill, in_=ar[:, :]))
    for h in range(2):
        n0 = h * 512
        if h == 1:
            ph.wait("sync", s_mm)
        for qtr in range(2):
            if qtr == 1 or h == 1:
                ph.wait("sync", s_cv)
            ph.dma("sync", s_ld,
                   lambda e, n0=n0, qtr=qtr: e.dma_start(
                       out=stg[:, 0:2048].rearrange("p (k n) -> p k n", k=8),
                       in_=w["out_wT"][:, n0 + qtr * 256: n0 + qtr * 256 + 256].rearrange(
                           "(k p) n -> p k n", p=128)))
            ph.wait("scalar", s_ld)
            for k in range(KC):
                ph.op("scalar",
                      lambda e, k=k, qtr=qtr: e.copy(
                          wb[:, k * 512 + qtr * 256: k * 512 + qtr * 256 + 256],
                          stg[:, k * 256:(k + 1) * 256]),
                      inc=(s_cv if k == KC - 1 else None))
        ph.wait("tensor", s_cv)
        if h == 0:
            ph.wait("vector", s_sp)
        for nch in range(4):
            nabs = h * 4 + nch
            for tt in range(NTT):
                ps = psA if (nabs * NTT + tt) % 2 == 0 else psB
                ph.wait("tensor", s_ev, ctx.cnt[s_ev] - 1)
                for kc in range(KC):
                    ph.op("tensor",
                          lambda e, kc=kc, nch=nch, tt=tt, ps=ps: e.matmul(
                              ps[:], wb[:, kc * 512 + nch * 128: kc * 512 + (nch + 1) * 128],
                              ybf[:, kc * T + tt * TT: kc * T + (tt + 1) * TT],
                              start=(kc == 0), stop=(kc == KC - 1)),
                          inc=(s_mm if kc == KC - 1 else None))
                ph.wait("vector", s_mm)
                c = nabs * T + tt * TT
                ph.op("vector",
                      lambda e, ps=ps, nabs=nabs, c=c: e.scalar_tensor_tensor(
                          ar[:, c:c + TT], ps[:], vcol(w["out_b"], nabs),
                          ar[:, c:c + TT], ALU.add, ALU.add),
                      inc=s_ev)
    ph.run()

    ph = Ph(ctx, "m_rms2")
    emit_rms_rows(ph, ctx, src_ar=ar, rrow=rrow, sq=stg, ps_stat=psS, ps_b=None,
                  s_mm=s_mm, s_ev=s_ev, eps_ap=ctx.vecs[0:1, w["keps5"]:w["keps5"] + 1])
    for kc in range(KC):
        for tt in range(NTT):
            ph.op("tensor",
                  lambda e, tt=tt: e.matmul(
                      psB[:], ctx.ones_row_f32[0:1, :], rrow[0:1, tt * TT:(tt + 1) * TT],
                      start=True, stop=True),
                  inc=s_mm)
            ph.wait("scalar", s_mm)
            ph.op("scalar", lambda e: e.copy(scb[:, :], psB[:]), inc=s_ev)
            ph.wait("vector", s_ev)
            ph.op("vector",
                  lambda e, kc=kc, tt=tt: e.scalar_tensor_tensor(
                      ybf[:, kc * T + tt * TT: kc * T + (tt + 1) * TT],
                      ar[:, kc * T + tt * TT: kc * T + (tt + 1) * TT],
                      vcol(w["n2w"], kc), scb[:, :], ALU.mult, ALU.mult),
                  inc=s_ev)
            ph.wait("tensor", s_ev)
    ph.run()

    HC = 32
    for grp in range(4):
        ph = Ph(ctx, f"m_mlpA{grp}")
        cv0 = ctx.cnt[s_cv]
        for hh in range(8):
            hch = grp * 8 + hh
            par = hh % 2
            if hh >= 1:
                ph.wait("sync", s_cv, cv0 + hh)
            ph.dma("sync", s_ld,
                   lambda e, hch=hch: e.dma_start(
                       out=stg[:, 0:1024].rearrange("p (kc n) -> p kc n", kc=8),
                       in_=w["w2T"][:, hch * 128:(hch + 1) * 128].rearrange(
                           "(kc p) n -> p kc n", p=128)))
            ph.dma("sync", s_ld,
                   lambda e, hch=hch: e.dma_start(
                       out=stg[:, 1024:2048].rearrange("p (kc n) -> p kc n", kc=8),
                       in_=w["w1T"][:, hch * 128:(hch + 1) * 128].rearrange(
                           "(kc p) n -> p kc n", p=128)))
            ph.wait("scalar", s_ld)
            if hh >= 2:
                ph.wait("scalar", s_mm, ctx.cnt[s_mm] - NTT)
            ph.op("scalar",
                  lambda e, par=par: e.copy(
                      wb[:, par * 2048: par * 2048 + 2048], stg[:, 0:2048]),
                  inc=s_cv)
            ph.wait("tensor", s_cv)
            for tt in range(NTT):
                ph.wait("tensor", s_ev, ctx.cnt[s_ev] - 1)
                for kc in range(KC):
                    ph.op("tensor",
                          lambda e, kc=kc, tt=tt, par=par: e.matmul(
                              psA[:], wb[:, par * 2048 + kc * 128: par * 2048 + (kc + 1) * 128],
                              ybf[:, kc * T + tt * TT: kc * T + (tt + 1) * TT],
                              start=(kc == 0), stop=(kc == KC - 1)))
                for kc in range(KC):
                    ph.op("tensor",
                          lambda e, kc=kc, tt=tt, par=par: e.matmul(
                              psB[:], wb[:, par * 2048 + 1024 + kc * 128: par * 2048 + 1024 + (kc + 1) * 128],
                              ybf[:, kc * T + tt * TT: kc * T + (tt + 1) * TT],
                              start=(kc == 0), stop=(kc == KC - 1)),
                          inc=(s_mm if kc == KC - 1 else None))
                ph.wait("scalar", s_mm)
                ph.op("scalar",
                      lambda e, hch=hch: e.activation(
                          scb[:, :], psA[:], AF.Silu, bias=vcol(w["w2b"], hch), scale=1.0),
                      inc=s_ev)
                ph.wait("vector", s_ev)
                ph.op("vector",
                      lambda e, hh=hh, tt=tt, hch=hch: e.scalar_tensor_tensor(
                          mbuf[:, hh * T + tt * TT: hh * T + (tt + 1) * TT],
                          psB[:], vcol(w["w1b"], hch), scb[:, :], ALU.add, ALU.mult),
                      inc=s_ev)
        ph.run()

        ph = Ph(ctx, f"m_mlpB{grp}")
        cv0 = ctx.cnt[s_cv]
        for dch in range(KC):
            par = dch % 2
            if dch >= 1:
                ph.wait("sync", s_cv, cv0 + dch)
            ph.dma("sync", s_ld,
                   lambda e, grp=grp, dch=dch: e.dma_start(
                       out=stg[:, 0:1024].rearrange("p (kc n) -> p kc n", kc=8),
                       in_=w["w3T"][grp * 1024:(grp + 1) * 1024, dch * 128:(dch + 1) * 128].rearrange(
                           "(kc p) n -> p kc n", p=128)))
            ph.wait("scalar", s_ld)
            if dch >= 2:
                ph.wait("scalar", s_mm, ctx.cnt[s_mm] - NTT)
            ph.op("scalar",
                  lambda e, par=par: e.copy(
                      wb[:, par * 1024: par * 1024 + 1024], stg[:, 0:1024]),
                  inc=s_cv)
            ph.wait("tensor", s_cv)
            for tt in range(NTT):
                ps = psA if (dch * NTT + tt) % 2 == 0 else psB
                ph.wait("tensor", s_ev, ctx.cnt[s_ev] - 1)
                for kh in range(8):
                    ph.op("tensor",
                          lambda e, kh=kh, tt=tt, par=par, ps=ps: e.matmul(
                              ps[:], wb[:, par * 1024 + kh * 128: par * 1024 + (kh + 1) * 128],
                              mbuf[:, kh * T + tt * TT: kh * T + (tt + 1) * TT],
                              start=(kh == 0), stop=(kh == 7)),
                          inc=(s_mm if kh == 7 else None))
                ph.wait("vector", s_mm)
                c = dch * T + tt * TT
                bcol = w["w3b"] if grp == 0 else w["kzero"]
                ph.op("vector",
                      lambda e, ps=ps, dch=dch, c=c, bcol=bcol: e.scalar_tensor_tensor(
                          ar[:, c:c + TT], ps[:], vcol(bcol, dch if bcol == w["w3b"] else 0),
                          ar[:, c:c + TT], ALU.add, ALU.add),
                      inc=s_ev)
        ph.run()

    ph = Ph(ctx, "m_fin")
    for kc in range(KC):
        for hf in range(2):
            ph.wait("sync", s_ev)
            ph.dma("sync", s_ld,
                   lambda e, kc=kc, hf=hf: e.dma_start(
                       out=stg[:, 0:1024],
                       in_=x_spill[:, kc * T + hf * 1024: kc * T + (hf + 1) * 1024]))
            ph.wait("vector", s_ld)
            for tt in range(2):
                c = kc * T + hf * 1024 + tt * TT
                ph.op("vector",
                      lambda e, c=c, tt=tt: e.tensor_add(
                          ar[:, c:c + TT], ar[:, c:c + TT], stg[:, tt * TT:(tt + 1) * TT]),
                      inc=s_ev)
    ph.run()

    ph = Ph(ctx, "m_ln")
    emit_postln(ph, ar=ar, sq=stg, ps_stat=[psA, psB], ps_b=[psA, psB],
                sem_mm=s_mm, sem_ev=s_ev,
                eps_ap=ctx.vecs[0:1, w["keps"]:w["keps"] + 1])
    ph.run()


def l1attn_weights_host(inp, vb):
    w = {
        "out_wT": _t(inp["L1_attn_out_w"]),
        "w1T": _t(inp["L1_attn_w1_w"]),
        "w2T": _t(inp["L1_attn_w2_w"]),
        "w3T": _t(inp["L1_attn_w3_w"]),
    }
    n1wb = np.asarray(inp["L1_attn_n1_w"], np.float32) * np.asarray(inp["L1_attn_b"], np.float32)
    idx = {
        "a": vb.add("m_a", _chunkvec(inp["L1_attn_a"])),
        "n1wb": vb.add("m_n1wb", _chunkvec(n1wb)),
        "n2w": vb.add("m_n2w", _chunkvec(inp["L1_attn_n2_w"])),
        "out_b": vb.add("m_outb", _chunkvec(inp["L1_attn_out_b"])),
        "w1b": vb.add("m_w1b", _chunkvec(inp["L1_attn_w1_b"])),
        "w2b": vb.add("m_w2b", _chunkvec(inp["L1_attn_w2_b"])),
        "w3b": vb.add("m_w3b", _chunkvec(inp["L1_attn_w3_b"])),
        "kzero": vb.add("k_zero", np.zeros((128, 1), np.float32)),
        "kone": vb.add("k_one1", np.full((128, 1), 1.0, np.float32)),
        "keps5": vb.add("k_eps5", np.full((128, 1), 1e-5, np.float32)),
        "keps": vb.add("k_eps_ln1", np.full((128, 1), 1e-5, np.float32)),
    }
    return w, idx


def make_l1_cc(nc):
    return {
        "ag_in": nc.dram_tensor("cc_agi", [128, 8], F32).ap(),
        "ag_out": nc.dram_tensor("cc_ago", [NCORES * 128, 8], F32).ap(),
    }


def build_l1attn_program(nblob, vec_idx, debug=False):
    nc = bass.Bass("TRN2", target_bir_lowering=False, detect_race_conditions=False)
    ctx = Ctx(nc)
    x_in = nc.dram_tensor("x_in", [T, D], F32, kind="ExternalInput").ap()
    y_out = nc.dram_tensor("y_out", [T, D], F32, kind="ExternalOutput").ap()
    consts = nc.dram_tensor("consts", [130, 128], F32, kind="ExternalInput").ap()
    coreoh = nc.dram_tensor("coreoh", [128, 8], F32, kind="ExternalInput").ap()
    vecblob = nc.dram_tensor("vecblob", [128, nblob], F32, kind="ExternalInput").ap()
    w = {"out_wT": nc.dram_tensor("out_wT", [1024, 1024], F32, kind="ExternalInput").ap(),
         "w1T": nc.dram_tensor("w1T", [1024, 4096], F32, kind="ExternalInput").ap(),
         "w2T": nc.dram_tensor("w2T", [1024, 4096], F32, kind="ExternalInput").ap(),
         "w3T": nc.dram_tensor("w3T", [4096, 1024], F32, kind="ExternalInput").ap()}
    w.update(vec_idx)
    cc = make_l1_cc(nc)
    x_spill = nc.dram_tensor("x_spill", [128, KC * T], F32).ap()

    common_setup(ctx, nblob)
    ctx.coreoh = ctx.sbuf("c_coreoh", [128, 8], F32)
    ar = ctx.sbuf("ar", [128, KC * T], F32)

    s_ld = [ctx.sem("g_ld0"), ctx.sem("g_ld1")]
    s_cst = ctx.sem("g_cst")
    s_tp = ctx.sem("g_tp")
    s_ev = ctx.sem("g_ev")
    s_st = [ctx.sem("g_st0"), ctx.sem("g_st1")]

    with ctx.scope() as sc:
        tmp_tm = [sc.sbuf(f"tin{i}", [128, D], F32) for i in range(2)]
        ps_t = [sc.psum(f"g_ps{i}", [128, TT]) for i in range(2)]
        ph = Ph(ctx, "in")
        emit_load_consts(ph, ctx, consts, None, vecblob, s_cst)
        ph.dma("sync", s_cst, lambda e: e.dma_start(out=ctx.coreoh[:], in_=coreoh))
        ph.wait("vector", s_cst)
        emit_transpose_in(ph, hbm_tm=x_in, ar=ar, sem_ld=s_ld, sem_tp=s_tp,
                          sem_ev=s_ev, ps_t=ps_t, tmp_tm=tmp_tm)
        ph.run()

    hook = None
    if debug:
        shapes = {"d_lc": [128, 8], "d_lg": [128, 64], "d_av": [128, 32], "d_rrow": [1, T],
                  "d_bxs": [128, T], "d_yss": [128, T]}
        douts = {nm: nc.dram_tensor(nm, shp, F32, kind="ExternalOutput").ap()
                 for nm, shp in shapes.items()}
        s_dbg = ctx.sem("dbg_st")

        def hook(tens):
            ph = Ph(ctx, "dbg")
            for nm, t in tens.items():
                ph.dma("sync", s_dbg, lambda e, t=t, nm=nm: e.dma_start(out=douts[nm], in_=t[:, :] if nm != "d_rrow" else t[0:1, :]))
            ph.wait("gpsimd", s_dbg)
            ph.run()

    with ctx.scope() as sc:
        build_l1attn(ctx, ar, w, cc, sc, x_spill, dbg_hook=hook)

    with ctx.scope() as sc:
        tmp_tm = [sc.sbuf(f"tout{i}", [128, D], F32) for i in range(2)]
        ps_t = [sc.psum(f"g_pso{i}", [128, TT]) for i in range(2)]
        ph = Ph(ctx, "out")
        emit_transpose_out(ph, ar=ar, hbm_tm=y_out, sem_tp=s_tp, sem_ev=s_ev,
                           sem_st=s_st, ps_t=ps_t, tmp_tm=tmp_tm)
        ph.wait("gpsimd", s_st[0])
        ph.wait("gpsimd", s_st[1])
        ph.run()

    ctx.close()
    return nc


def build_l0ffn(ctx, ar, w, sc, x_spill, rw_hbm):
    nc = ctx.nc
    s_ld = ctx.sem("f_ld")
    s_cv = ctx.sem("f_cv")
    s_mm = ctx.sem("f_mm")
    s_ev = ctx.sem("f_ev")
    s_sp = ctx.sem("f_sp")
    s_rw = ctx.sem("f_rw")
    s_hp = ctx.sem("f_hp")

    x_bf = sc.sbuf("f_xbf", [128, KC * T], BF16)
    routed = ar

    psA = sc.psum("f_psA", [128, TT])
    psB = sc.psum("f_psB", [128, TT])
    psR = sc.psum("f_psR", [128, TT])
    psD = sc.psum("f_psD", [128, TT])
    psA2 = sc.psum("f_psA2", [128, TT])
    psB2 = sc.psum("f_psB2", [128, TT])
    psD2 = sc.psum("f_psD2", [128, TT])

    NCH = [(0, 128), (128, 128), (256, 128), (384, 128), (512, 128), (640, 64)]
    kone = ctx.vecs[:, w["kone"]:w["kone"] + 1]
    kzero = ctx.vecs[:, w["kzero"]:w["kzero"] + 1]

    with ctx.scope() as rsc:
        sco_fm = rsc.sbuf("f_sco", [64, T], F32)
        sfc_fm = rsc.sbuf("f_sfc", [64, T], F32)
        sco_tm = rsc.sbuf("f_scot", [128, 64], F32)
        sfc_tm = rsc.sbuf("f_sfct", [128, 64], F32)
        wrk = rsc.sbuf("f_wrk", [128, 64], F32)
        sel = rsc.sbuf("f_sel", [128, 64], F32)
        gsc = rsc.sbuf("f_gsc", [128, 8], F32)
        gmk = rsc.sbuf("f_gmk", [128, 8], F32)
        rwt = rsc.sbuf("f_rwt", [128, 64], F32)
        m1 = rsc.sbuf("f_m1", [128, 4], F32)
        gwtb = rsc.sbuf("f_gwtb", [128, 8 * 64], BF16)
        stg = rsc.sbuf("f_gstg", [128, 512], F32)
        psT = rsc.psum("f_psT", [128, 128])

        ph = Ph(ctx, "f_route")

        def vop(build):
            ph.op("vector", build, inc=s_ev)
            ph.wait("vector", s_ev)

        for kc in range(KC):
            ph.op("vector",
                  lambda e, kc=kc: e.tensor_copy(
                      x_bf[:, kc * T:(kc + 1) * T], ar[:, kc * T:(kc + 1) * T]),
                  inc=(s_cv if kc == KC - 1 else None))
        ph.wait("sync", s_cv)
        ph.dma("sync", s_sp, lambda e: e.dma_start(out=x_spill, in_=ar[:, :]))
        ph.wait("gpsimd", s_sp)
        ph.op("gpsimd", lambda e: e.memset(routed[:, :], 0.0), inc=s_cv)
        ph.dma("sync", s_ld,
               lambda e: e.dma_start(
                   out=stg[:, 0:512].rearrange("p (kc n) -> p kc n", kc=8),
                   in_=w["gate_wT"].rearrange("(kc p) n -> p kc n", p=128)))
        ph.wait("gpsimd", s_ld)
        ph.op("gpsimd", lambda e: e.tensor_copy(gwtb[:, :], stg[:, 0:512]), inc=s_cv)
        ph.wait("tensor", s_cv)
        for tt in range(NTT):
            if tt > 0:
                ph.wait("tensor", s_ev)
            for kc in range(KC):
                ph.op("tensor",
                      lambda e, kc=kc, tt=tt: e.matmul(
                          psA[0:64, 0:TT], gwtb[:, kc * 64:(kc + 1) * 64],
                          x_bf[:, kc * T + tt * TT: kc * T + (tt + 1) * TT],
                          start=(kc == 0), stop=(kc == KC - 1)),
                      inc=(s_mm if kc == KC - 1 else None))
            ph.wait("scalar", s_mm)
            ph.op("scalar",
                  lambda e, tt=tt: e.activation(
                      sco_fm[0:64, tt * TT:(tt + 1) * TT], psA[0:64, 0:TT],
                      AF.Sigmoid, bias=0.0, scale=1.0),
                  inc=s_ev)
        ph.wait("vector", s_ev)
        ph.op("vector", lambda e: e.tensor_scalar_add(
            sfc_fm[0:64, :], sco_fm[0:64, :], ctx.vecs[0:64, w["gbias"]:w["gbias"] + 1]),
            inc=s_ev)
        for ti in range(T // 128):
            ph.wait("tensor", s_ev)
            ph.op("tensor",
                  lambda e, ti=ti: e.transpose(
                      psT[:, 0:64], sco_fm[0:64, ti * 128:(ti + 1) * 128],
                      ctx.ident_f32[0:64, 0:64]),
                  inc=s_mm)
            ph.wait("vector", s_mm)
            vop(lambda e: e.tensor_copy(sco_tm[:, :], psT[:, 0:64]))
            ph.wait("tensor", s_ev)
            ph.op("tensor",
                  lambda e, ti=ti: e.transpose(
                      psT[:, 0:64], sfc_fm[0:64, ti * 128:(ti + 1) * 128],
                      ctx.ident_f32[0:64, 0:64]),
                  inc=s_mm)
            ph.wait("vector", s_mm)
            vop(lambda e: e.tensor_copy(sfc_tm[:, :], psT[:, 0:64]))
            for g in range(8):
                vop(lambda e, g=g: e.reduce_max(m1[:, 0:1], sfc_tm[:, g * 8:(g + 1) * 8], axis=AX.X))
                vop(lambda e, g=g: e.tensor_scalar(
                    wrk[:, 0:8], sfc_tm[:, g * 8:(g + 1) * 8], m1[:, 0:1], None, ALU.is_ge))
                vop(lambda e, g=g: e.scalar_tensor_tensor(
                    wrk[:, 8:16], wrk[:, 0:8], -1e30, sfc_tm[:, g * 8:(g + 1) * 8],
                    ALU.mult, ALU.add))
                vop(lambda e: e.reduce_max(m1[:, 1:2], wrk[:, 8:16], axis=AX.X))
                vop(lambda e, g=g: e.tensor_add(gsc[:, g:g + 1], m1[:, 0:1], m1[:, 1:2]))
            vop(lambda e: e.memset(gmk[:, :], 0.0))
            vop(lambda e: e.tensor_copy(wrk[:, 0:8], gsc[:, :]))
            for _ in range(4):
                vop(lambda e: e.reduce_max(m1[:, 0:1], wrk[:, 0:8], axis=AX.X))
                vop(lambda e: e.tensor_scalar(
                    wrk[:, 8:16], wrk[:, 0:8], m1[:, 0:1], None, ALU.is_ge))
                vop(lambda e: e.tensor_add(gmk[:, :], gmk[:, :], wrk[:, 8:16]))
                vop(lambda e: e.scalar_tensor_tensor(
                    wrk[:, 0:8], wrk[:, 8:16], -1e30, wrk[:, 0:8], ALU.mult, ALU.add))
            vop(lambda e: e.tensor_copy(
                wrk[:, :].rearrange("p (g x) -> p g x", g=8),
                gmk[:, :].rearrange("p (g o) -> p g o", o=1).broadcast_to([128, 8, 8])))
            vop(lambda e: e.tensor_mul(wrk[:, :], wrk[:, :], sfc_tm[:, :]))
            vop(lambda e: e.memset(sel[:, :], 0.0))
            for _ in range(8):
                vop(lambda e: e.reduce_max(m1[:, 0:1], wrk[:, :], axis=AX.X))
                vop(lambda e: e.tensor_scalar(
                    rwt[:, :], wrk[:, :], m1[:, 0:1], None, ALU.is_ge))
                vop(lambda e: e.tensor_add(sel[:, :], sel[:, :], rwt[:, :]))
                vop(lambda e: e.scalar_tensor_tensor(
                    wrk[:, :], rwt[:, :], -1e30, wrk[:, :], ALU.mult, ALU.add))
            vop(lambda e: e.tensor_mul(rwt[:, :], sel[:, :], sco_tm[:, :]))
            vop(lambda e: e.reduce_sum(m1[:, 2:3], rwt[:, :], axis=AX.X))
            vop(lambda e: e.tensor_scalar_add(m1[:, 2:3], m1[:, 2:3], 1e-20))
            vop(lambda e: e.reciprocal(m1[:, 2:3], m1[:, 2:3]))
            vop(lambda e: e.tensor_scalar(
                rwt[:, :], rwt[:, :], m1[:, 2:3], 2.5, ALU.mult, ALU.mult))
            ph.wait("tensor", s_ev)
            ph.op("tensor",
                  lambda e: e.transpose(psT[0:64, 0:128], rwt[:, 0:64], ctx.ident_f32[:]),
                  inc=s_mm)
            ph.wait("scalar", s_mm)
            ph.op("scalar",
                  lambda e, ti=ti: e.copy(
                      sco_fm[0:64, ti * 128:(ti + 1) * 128], psT[0:64, 0:128]),
                  inc=s_ev)
        ph.wait("sync", s_ev)
        ph.dma("sync", s_rw, lambda e: e.dma_start(out=rw_hbm, in_=sco_fm[0:64, :]))
        ph.wait("gpsimd", s_rw)
        ph.run()

    gwb = [sc.sbuf(f"f_gw{i}", [128, 8 * 704], BF16) for i in range(2)]
    uwb = [sc.sbuf(f"f_uw{i}", [128, 8 * 704], BF16) for i in range(2)]
    dwb = [sc.sbuf(f"f_dw{i}", [128, 6 * 1024], BF16) for i in range(2)]
    stg = sc.sbuf("f_stg", [128, 2816], F32)
    sgt = sc.sbuf("f_sg", [128, 2 * TT], BF16)
    ht = sc.sbuf("f_h", [128, 6 * TT], BF16)
    rwrow = sc.sbuf("f_rwrow", [65, T], F32)
    rwb = sc.sbuf("f_rwb", [128, TT], F32)
    NEXP = 66

    def wsrc(e):
        if e < 64:
            return (w["WgT"][e], w["WuT"][e], w["WdT"][e])
        half = e - 64
        return (w["SgT"][:, half * 704:(half + 1) * 704],
                w["SuT"][:, half * 704:(half + 1) * 704],
                w["SdT"][half * 704:(half + 1) * 704, :])

    def wload_jobs(e, par):
        gsrc, usrc, dsrc = wsrc(e)
        jobs = []
        for mat, dst, nc_ in ((gsrc, gwb[par], 704), (usrc, uwb[par], 704)):
            jobs.append((mat, dst, 0, 4, nc_, 128))
            jobs.append((mat, dst, 4, 4, nc_, 128))
        jobs.append((dsrc, dwb[par], 0, 2, 1024, 128))
        jobs.append((dsrc, dwb[par], 2, 2, 1024, 128))
        jobs.append((dsrc, dwb[par], 4, 1, 1024, 128))
        jobs.append((dsrc, dwb[par], 5, 1, 1024, 64))
        return jobs

    def emit_wjob(ph, job):
        mat, dst, k0, kn, nc_, prt = job
        ph.wait("sync", s_cv)
        if prt == 128:
            ph.dma("sync", s_ld,
                   lambda e_, mat=mat, k0=k0, kn=kn, nc_=nc_: e_.dma_start(
                       out=stg[:, 0:kn * nc_].rearrange("p (k n) -> p k n", k=kn),
                       in_=mat[k0 * 128:(k0 + kn) * 128, :].rearrange(
                           "(k p) n -> p k n", p=128)))
        else:
            ph.dma("sync", s_ld,
                   lambda e_, mat=mat, k0=k0, nc_=nc_: e_.dma_start(
                       out=stg[0:64, 0:nc_], in_=mat[k0 * 128: k0 * 128 + 64, :]))
        ph.wait("scalar", s_ld)
        if prt == 128:
            ph.op("scalar",
                  lambda e_, dst=dst, k0=k0, kn=kn, nc_=nc_: e_.copy(
                      dst[:, k0 * nc_:(k0 + kn) * nc_], stg[:, 0:kn * nc_]),
                  inc=s_cv)
        else:
            ph.op("scalar",
                  lambda e_, dst=dst, k0=k0, nc_=nc_: e_.copy(
                      dst[0:64, k0 * nc_:(k0 + 1) * nc_], stg[0:64, 0:nc_]),
                  inc=s_cv)

    def emit_wload(ph, e, par):
        for j in wload_jobs(e, par):
            emit_wjob(ph, j)

    conv_done = {}
    ph = Ph(ctx, "f_experts")
    for e in range(NEXP):
        par = e % 2
        if e == 0:
            emit_wload(ph, 0, 0)
            conv_done[0] = ctx.cnt[s_cv]
        is_shared = e >= 64
        if not is_shared:
            if e >= 2:
                ph.wait("sync", s_mm)
            ph.dma("sync", s_rw,
                   lambda e_, e=e, par=par: e_.dma_start(
                       out=rwrow[par * 64: par * 64 + 1, :], in_=rw_hbm[e:e + 1, :]))
        pjobs = []
        if e + 1 < NEXP:
            ph.wait("scalar", s_mm)
            pjobs = wload_jobs(e + 1, 1 - par)
            for j in pjobs[0:2]:
                emit_wjob(ph, j)
        if not is_shared:
            ph.wait("tensor", s_rw)
        ph.wait("tensor", s_cv, conv_done[e])
        nev = 2
        for tt in range(NTT):
            for j in pjobs[2 + tt * 2: 4 + tt * 2]:
                emit_wjob(ph, j)
            if not is_shared:
                ph.wait("tensor", s_ev)
                ph.op("tensor",
                      lambda e_, tt=tt, par=par: e_.matmul(
                          psR[:], ctx.ones_row65[par * 64: par * 64 + 1, :],
                          rwrow[par * 64: par * 64 + 1, tt * TT:(tt + 1) * TT],
                          start=True, stop=True),
                      inc=s_mm)
                ph.wait("scalar", s_mm)
                ph.wait("scalar", s_hp)
                ph.op("scalar", lambda e_: e_.copy(rwb[:, :], psR[:]), inc=s_ev)
            ph.wait("vector", s_mm)
            for ci, (c0, cw) in enumerate(NCH):
                pg = psA if ci % 2 == 0 else psA2
                pu = psB if ci % 2 == 0 else psB2
                sg = sgt[0:cw, (ci % 2) * TT:(ci % 2) * TT + TT]
                ph.wait("tensor", s_ev, ctx.cnt[s_ev] - nev)
                for kc in range(KC):
                    ph.op("tensor",
                          lambda e_, kc=kc, tt=tt, c0=c0, cw=cw, par=par, pg=pg: e_.matmul(
                              pg[0:cw, 0:TT], gwb[par][:, kc * 704 + c0: kc * 704 + c0 + cw],
                              x_bf[:, kc * T + tt * TT: kc * T + (tt + 1) * TT],
                              start=(kc == 0), stop=(kc == KC - 1)),
                          inc=(s_mm if kc == KC - 1 else None))
                for kc in range(KC):
                    ph.op("tensor",
                          lambda e_, kc=kc, tt=tt, c0=c0, cw=cw, par=par, pu=pu: e_.matmul(
                              pu[0:cw, 0:TT], uwb[par][:, kc * 704 + c0: kc * 704 + c0 + cw],
                              x_bf[:, kc * T + tt * TT: kc * T + (tt + 1) * TT],
                              start=(kc == 0), stop=(kc == KC - 1)),
                          inc=(s_mm if kc == KC - 1 else None))
                ph.wait("scalar", s_mm, ctx.cnt[s_mm] - 1)
                ph.wait("scalar", s_ev, ctx.cnt[s_ev] - 2)
                if not is_shared:
                    ph.wait("scalar", s_hp, ctx.cnt[s_hp] - 1)
                ph.op("scalar",
                      lambda e_, cw=cw, pg=pg, sg=sg: e_.activation(
                          sg, pg[0:cw, 0:TT], AF.Silu, bias=0.0, scale=1.0),
                      inc=s_ev)
                ph.wait("vector", s_ev)
                ph.wait("vector", s_mm)
                if is_shared:
                    ph.op("vector",
                          lambda e_, ci=ci, cw=cw, pu=pu, sg=sg: e_.scalar_tensor_tensor(
                              ht[0:cw, ci * TT:(ci + 1) * TT], pu[0:cw, 0:TT],
                              kone[0:cw, :], sg, ALU.bypass, ALU.mult),
                          inc=s_ev)
                else:
                    ph.op("vector",
                          lambda e_, cw=cw, pu=pu, sg=sg: e_.scalar_tensor_tensor(
                              sg, pu[0:cw, 0:TT],
                              kone[0:cw, :], sg, ALU.bypass, ALU.mult),
                          inc=s_ev)
                    ph.wait("gpsimd", s_ev)
                    ph.op("gpsimd",
                          lambda e_, ci=ci, cw=cw, sg=sg: e_.tensor_mul(
                              ht[0:cw, ci * TT:(ci + 1) * TT], sg, rwb[0:cw, :]),
                          inc=s_hp)
            dn0 = ctx.cnt[s_ev]
            hp0 = ctx.cnt[s_hp]
            for dch in range(KC):
                pd = psD if dch % 2 == 0 else psD2
                if dch < 2:
                    ph.wait("tensor", s_ev, dn0)
                    if not is_shared:
                        ph.wait("tensor", s_hp, hp0)
                else:
                    ph.wait("tensor", s_ev, dn0 + dch - 1)
                for ci in range(6):
                    kw = 128 if ci < 5 else 64
                    ph.op("tensor",
                          lambda e_, ci=ci, kw=kw, dch=dch, par=par, pd=pd: e_.matmul(
                              pd[:], dwb[par][0:kw, ci * 1024 + dch * 128: ci * 1024 + (dch + 1) * 128],
                              ht[0:kw, ci * TT:(ci + 1) * TT],
                              start=(ci == 0), stop=(ci == 5)),
                          inc=(s_mm if ci == 5 else None))
                ph.wait("vector", s_mm)
                c = dch * T + tt * TT
                ph.op("vector",
                      lambda e_, c=c, pd=pd: e_.scalar_tensor_tensor(
                          routed[:, c:c + TT], pd[:], kzero, routed[:, c:c + TT],
                          ALU.add, ALU.add),
                      inc=s_ev)
        if e + 1 < NEXP:
            conv_done[e + 1] = ctx.cnt[s_cv]
    ph.run()

    ph = Ph(ctx, "f_fin")
    for kc in range(KC):
        for hf in range(2):
            ph.wait("sync", s_ev)
            ph.dma("sync", s_ld,
                   lambda e_, kc=kc, hf=hf: e_.dma_start(
                       out=stg[:, 0:1024],
                       in_=x_spill[:, kc * T + hf * 1024: kc * T + (hf + 1) * 1024]))
            ph.wait("vector", s_ld)
            for q in range(2):
                c = kc * T + hf * 1024 + q * TT
                ph.op("vector",
                      lambda e_, c=c, q=q: e_.tensor_add(
                          ar[:, c:c + TT], ar[:, c:c + TT], stg[:, q * TT:(q + 1) * TT]),
                      inc=s_ev)
    ph.run()

    ph = Ph(ctx, "f_ln")
    emit_postln(ph, ar=ar, sq=stg, ps_stat=[psA, psB], ps_b=[psA, psB],
                sem_mm=s_mm, sem_ev=s_ev,
                eps_ap=ctx.vecs[0:1, w["keps"]:w["keps"] + 1])
    ph.run()


def l0ffn_weights_host(inp, vb):
    w = {
        "gate_wT": _t(inp["L0_ffn_gate_w"]),
        "WgT": np.ascontiguousarray(np.transpose(np.asarray(inp["L0_ffn_Wg"], np.float32), (0, 2, 1))),
        "WuT": np.ascontiguousarray(np.transpose(np.asarray(inp["L0_ffn_Wu"], np.float32), (0, 2, 1))),
        "WdT": np.ascontiguousarray(np.transpose(np.asarray(inp["L0_ffn_Wd"], np.float32), (0, 2, 1))),
        "SgT": _t(inp["L0_ffn_Sg"]),
        "SuT": _t(inp["L0_ffn_Su"]),
        "SdT": _t(inp["L0_ffn_Sd"]),
    }
    gb = np.zeros((128, 1), np.float32)
    gb[0:64, 0] = np.asarray(inp["L0_ffn_gate_bias"], np.float32)
    idx = {
        "gbias": vb.add("f_gbias", gb),
        "kone": vb.add("k_one_f", np.full((128, 1), 1.0, np.float32)),
        "kzero": vb.add("k_zero_f", np.zeros((128, 1), np.float32)),
        "keps": vb.add("k_eps_f", np.full((128, 1), 1e-5, np.float32)),
    }
    return w, idx


def build_l0ffn_program(nblob, vec_idx):
    nc = bass.Bass("TRN2", target_bir_lowering=False, detect_race_conditions=False)
    ctx = Ctx(nc)
    x_in = nc.dram_tensor("x_in", [T, D], F32, kind="ExternalInput").ap()
    y_out = nc.dram_tensor("y_out", [T, D], F32, kind="ExternalOutput").ap()
    consts = nc.dram_tensor("consts", [130, 128], F32, kind="ExternalInput").ap()
    vecblob = nc.dram_tensor("vecblob", [128, nblob], F32, kind="ExternalInput").ap()
    w = {
        "gate_wT": nc.dram_tensor("gate_wT", [1024, 64], F32, kind="ExternalInput").ap(),
        "WgT": nc.dram_tensor("WgT", [64, 1024, 704], F32, kind="ExternalInput").ap(),
        "WuT": nc.dram_tensor("WuT", [64, 1024, 704], F32, kind="ExternalInput").ap(),
        "WdT": nc.dram_tensor("WdT", [64, 704, 1024], F32, kind="ExternalInput").ap(),
        "SgT": nc.dram_tensor("SgT", [1024, 1408], F32, kind="ExternalInput").ap(),
        "SuT": nc.dram_tensor("SuT", [1024, 1408], F32, kind="ExternalInput").ap(),
        "SdT": nc.dram_tensor("SdT", [1408, 1024], F32, kind="ExternalInput").ap(),
    }
    w.update(vec_idx)
    x_spill = nc.dram_tensor("x_spill", [128, KC * T], F32).ap()
    rw_hbm = nc.dram_tensor("rw_scr", [64, T], F32).ap()

    common_setup(ctx, nblob)
    ar = ctx.sbuf("ar", [128, KC * T], F32)

    s_ld = [ctx.sem("g_ld0"), ctx.sem("g_ld1")]
    s_cst = ctx.sem("g_cst")
    s_tp = ctx.sem("g_tp")
    s_ev = ctx.sem("g_ev")
    s_st = [ctx.sem("g_st0"), ctx.sem("g_st1")]

    with ctx.scope() as sc:
        tmp_tm = [sc.sbuf(f"tin{i}", [128, D], F32) for i in range(2)]
        ps_t = [sc.psum(f"g_ps{i}", [128, TT]) for i in range(2)]
        ph = Ph(ctx, "in")
        emit_load_consts(ph, ctx, consts, None, vecblob, s_cst)
        emit_transpose_in(ph, hbm_tm=x_in, ar=ar, sem_ld=s_ld, sem_tp=s_tp,
                          sem_ev=s_ev, ps_t=ps_t, tmp_tm=tmp_tm)
        ph.run()

    with ctx.scope() as sc:
        build_l0ffn(ctx, ar, w, sc, x_spill, rw_hbm)

    with ctx.scope() as sc:
        tmp_tm = [sc.sbuf(f"tout{i}", [128, D], F32) for i in range(2)]
        ps_t = [sc.psum(f"g_pso{i}", [128, TT]) for i in range(2)]
        ph = Ph(ctx, "out")
        emit_transpose_out(ph, ar=ar, hbm_tm=y_out, sem_tp=s_tp, sem_ev=s_ev,
                           sem_st=s_st, ps_t=ps_t, tmp_tm=tmp_tm)
        ph.wait("gpsimd", s_st[0])
        ph.wait("gpsimd", s_st[1])
        ph.run()

    ctx.close()
    return nc


def common_setup(ctx, nblob):
    ctx.ones_col_f32 = ctx.sbuf("c_ones_col", [128, 1], F32)
    ctx.ones_col_bf = ctx.sbuf("c_ones_col_bf", [128, 1], BF16)
    ctx.ones_row_f32 = ctx.sbuf("c_ones_row", [1, 128], F32)
    ctx.ones_row65 = ctx.sbuf("c_ones_row65", [65, 128], F32)
    ctx.ones_row_bf = ctx.sbuf("c_ones_row_bf", [1, 128], BF16)
    ctx.ident_f32 = ctx.sbuf("c_idf", [128, 128], F32)
    ctx.ident_bf = ctx.sbuf("c_idb", [128, 128], BF16)
    ctx.ind_bf = ctx.sbuf("c_indb", [16, 8 * 128], BF16)
    ctx.vecs = ctx.sbuf("c_vecs", [128, max(nblob, 1)], F32)
    ctx.rows = ctx.sbuf("c_rows", [1, 4 * TT], F32)


def emit_load_consts(ph, ctx, consts_ap, ind_ap, vec_ap, sem_ld, ind_stage=None):
    ph.dma("sync", sem_ld, lambda e: e.dma_start(
        out=ctx.ones_col_f32[:], in_=consts_ap[0:1, :].rearrange("r p -> p r")))
    ph.dma("sync", sem_ld, lambda e: e.dma_start(out=ctx.ones_row_f32[0:1, :], in_=consts_ap[0:1, :]))
    ph.dma("sync", sem_ld, lambda e: e.dma_start(out=ctx.ones_row65[0:1, :], in_=consts_ap[0:1, :]))
    ph.dma("sync", sem_ld, lambda e: e.dma_start(out=ctx.ones_row65[64:65, :], in_=consts_ap[0:1, :]))
    ph.dma("sync", sem_ld, lambda e: e.dma_start(out=ctx.ident_f32[:], in_=consts_ap[2:130, :]))
    if ind_ap is not None:
        ph.dma("sync", sem_ld, lambda e: e.dma_start(out=ind_stage[0:16, :], in_=ind_ap[:, :]))
    if vec_ap is not None:
        ph.dma("sync", sem_ld, lambda e: e.dma_start(out=ctx.vecs[:], in_=vec_ap[:, :]))
    ph.wait("vector", sem_ld)
    ph.wait("tensor", sem_ld)
    ph.wait("scalar", sem_ld)
    ph.wait("gpsimd", sem_ld)
    ph.op("vector", lambda e: e.tensor_copy(ctx.ident_bf[:], ctx.ident_f32[:]))
    ph.op("vector", lambda e: e.tensor_copy(ctx.ones_row_bf[0:1, :], ctx.ones_row_f32[0:1, :]))
    ph.op("vector", lambda e: e.tensor_copy(ctx.ones_col_bf[:], ctx.ones_col_f32[:]))
    if ind_ap is not None:
        ph.op("vector", lambda e: e.tensor_copy(ctx.ind_bf[0:16, :], ind_stage[0:16, :]))


def make_ind():
    ind = np.zeros((16, 8 * 128), dtype=np.float32)
    for kc in range(8):
        ind[2 * kc, kc * 128: kc * 128 + 64] = 1.0
        ind[2 * kc + 1, kc * 128 + 64: kc * 128 + 128] = 1.0
    return ind


def _t(a):
    return np.ascontiguousarray(np.asarray(a, dtype=np.float32).T)


def _chunkvec(v):
    v = np.asarray(v, dtype=np.float32)
    n = v.shape[0] // 128
    return np.ascontiguousarray(v.reshape(n, 128).T)


def make_consts():
    c = np.zeros((130, 128), dtype=np.float32)
    c[0, :] = 1.0
    c[2:130, :] = np.eye(128, dtype=np.float32)
    return c


class VecBlob:
    def __init__(self):
        self.cols = []
        self.idx = {}

    def add(self, name, arr):
        self.idx[name] = sum(a.shape[1] for a in self.cols)
        self.cols.append(np.asarray(arr, dtype=np.float32))
        return self.idx[name]

    def blob(self):
        b = np.concatenate(self.cols, axis=1) if self.cols else np.zeros((128, 1), np.float32)
        return np.ascontiguousarray(b)


def l1ffn_weights_host(inp, vb):
    w = {
        "w1T": _t(inp["L1_ffn_w1_w"]),
        "w2T": _t(inp["L1_ffn_w2_w"]),
        "h1T": _t(inp["L1_ffn_h1_w"]),
        "h2T": _t(inp["L1_ffn_h2_w"]),
        "w3T": _t(inp["L1_ffn_w3_w"]),
    }
    idx = {
        "keps": vb.add("k_eps_ln", np.full((128, 1), 1e-5, np.float32)),
        "s1c": vb.add("ff_s1", _chunkvec(inp["L1_ffn_w1_s"])),
        "s2c": vb.add("ff_s2", _chunkvec(inp["L1_ffn_w2_s"])),
        "sh1c": vb.add("ff_sh1", _chunkvec(inp["L1_ffn_h1_s"])),
        "sh2c": vb.add("ff_sh2", _chunkvec(inp["L1_ffn_h2_s"])),
        "s3c": vb.add("ff_s3", _chunkvec(inp["L1_ffn_w3_s"])),
    }
    return w, idx


def build_l1ffn_program(nblob, vec_idx):
    nc = bass.Bass("TRN2", target_bir_lowering=False, detect_race_conditions=False)
    ctx = Ctx(nc)
    x_in = nc.dram_tensor("x_in", [T, D], F32, kind="ExternalInput").ap()
    y_out = nc.dram_tensor("y_out", [T, D], F32, kind="ExternalOutput").ap()
    consts = nc.dram_tensor("consts", [130, 128], F32, kind="ExternalInput").ap()
    vecblob = nc.dram_tensor("vecblob", [128, nblob], F32, kind="ExternalInput").ap()
    w = {k: nc.dram_tensor(k, list(shp), F32, kind="ExternalInput").ap()
         for k, shp in (("w1T", (1024, 4096)), ("w2T", (1024, 4096)),
                        ("h1T", (4096, 4096)), ("h2T", (4096, 4096)),
                        ("w3T", (4096, 1024)))}
    w.update(vec_idx)

    common_setup(ctx, nblob)
    ar = ctx.sbuf("ar", [128, KC * T], F32)

    s_ld = [ctx.sem("g_ld0"), ctx.sem("g_ld1")]
    s_cst = ctx.sem("g_cst")
    s_tp = ctx.sem("g_tp")
    s_ev = ctx.sem("g_ev")
    s_st = [ctx.sem("g_st0"), ctx.sem("g_st1")]
    ps_t = [ctx.psum(f"g_ps{i}", [128, TT]) for i in range(2)]

    with ctx.scope() as sc:
        tmp_tm = [sc.sbuf(f"tin{i}", [128, D], F32) for i in range(2)]
        ph = Ph(ctx, "in")
        emit_load_consts(ph, ctx, consts, None, vecblob, s_cst)
        emit_transpose_in(ph, hbm_tm=x_in, ar=ar, sem_ld=s_ld, sem_tp=s_tp,
                          sem_ev=s_ev, ps_t=ps_t, tmp_tm=tmp_tm)
        ph.run()

    with ctx.scope() as sc:
        build_l1ffn(ctx, ar, w, sc)

    with ctx.scope() as sc:
        tmp_tm = [sc.sbuf(f"tout{i}", [128, D], F32) for i in range(2)]
        ph = Ph(ctx, "out")
        emit_transpose_out(ph, ar=ar, hbm_tm=y_out, sem_tp=s_tp, sem_ev=s_ev,
                           sem_st=s_st, ps_t=ps_t, tmp_tm=tmp_tm)
        ph.wait("gpsimd", s_st[0])
        ph.wait("gpsimd", s_st[1])
        ph.run()

    ctx.close()
    return nc


def l0attn_weights_host(inp, vb):
    w = {
        "WqT": _t(inp["L0_attn_Wq"]),
        "WkT": _t(inp["L0_attn_Wk"]),
        "WvT": _t(inp["L0_attn_Wv"]),
        "WoT": _t(inp["L0_attn_Wo"]),
    }
    idx = {
        "bq": vb.add("a_bq", _chunkvec(inp["L0_attn_bq"])),
        "bk": vb.add("a_bk", _chunkvec(inp["L0_attn_bk"])),
        "bv": vb.add("a_bv", _chunkvec(inp["L0_attn_bv"])),
        "bo": vb.add("a_bo", _chunkvec(inp["L0_attn_bo"])),
        "kone": vb.add("k_one", np.full((128, 1), 1.0, np.float32)),
        "keps6": vb.add("k_eps6", np.full((128, 1), 1e-6, np.float32)),
        "keps": vb.add("k_eps_ln0", np.full((128, 1), 1e-5, np.float32)),
    }
    return w, idx


def make_l0_cc(nc):
    return {
        "r16_in": nc.dram_tensor("cc_r16i", [128, 16], F32).ap(),
        "r16_out": nc.dram_tensor("cc_r16o", [128, 16], F32).ap(),
        "d_in": nc.dram_tensor("cc_di", [16, 1], F32).ap(),
        "d_out": nc.dram_tensor("cc_do", [16, 1], F32).ap(),
        "kv_in": nc.dram_tensor("cc_kvi", [128, 1024], F32).ap(),
        "kv_out": nc.dram_tensor("cc_kvo", [128, 1024], F32).ap(),
    }


def build_l0attn_program(nblob, vec_idx, upto="H", debug=False):
    nc = bass.Bass("TRN2", target_bir_lowering=False, detect_race_conditions=False)
    ctx = Ctx(nc)
    x_in = nc.dram_tensor("x_in", [T, D], F32, kind="ExternalInput").ap()
    y_out = nc.dram_tensor("y_out", [T, D], F32, kind="ExternalOutput").ap()
    consts = nc.dram_tensor("consts", [130, 128], F32, kind="ExternalInput").ap()
    indc = nc.dram_tensor("indc", [16, 1024], F32, kind="ExternalInput").ap()
    vecblob = nc.dram_tensor("vecblob", [128, nblob], F32, kind="ExternalInput").ap()
    w = {k: nc.dram_tensor(k, [1024, 1024], F32, kind="ExternalInput").ap()
         for k in ("WqT", "WkT", "WvT", "WoT")}
    w.update(vec_idx)
    cc = make_l0_cc(nc)

    common_setup(ctx, nblob)
    ar = ctx.sbuf("ar", [128, KC * T], F32)

    s_ld = [ctx.sem("g_ld0"), ctx.sem("g_ld1")]
    s_cst = ctx.sem("g_cst")
    s_tp = ctx.sem("g_tp")
    s_ev = ctx.sem("g_ev")
    s_st = [ctx.sem("g_st0"), ctx.sem("g_st1")]

    with ctx.scope() as sc:
        tmp_tm = [sc.sbuf(f"tin{i}", [128, D], F32) for i in range(2)]
        ps_t = [sc.psum(f"g_ps{i}", [128, TT]) for i in range(2)]
        ph = Ph(ctx, "in")
        emit_load_consts(ph, ctx, consts, indc, vecblob, s_cst)
        emit_transpose_in(ph, hbm_tm=x_in, ar=ar, sem_ld=s_ld, sem_tp=s_tp,
                          sem_ev=s_ev, ps_t=ps_t, tmp_tm=tmp_tm)
        ph.run()

    dbg = {}
    if debug:
        for nm, shp, dt in (("d_q", [128, KC * T], BF16), ("d_k", [128, KC * T], BF16),
                            ("d_v", [128, KC * T], BF16), ("d_stats", [16, 3 * T], BF16),
                            ("d_r1", [128, 16], F32), ("d_r2", [128, 16], F32),
                            ("d_kvb", [128, 1024], BF16), ("d_dnm", [16, 2], F32),
                            ("d_ecol", [16, 4], F32)):
            dbg[nm] = nc.dram_tensor(nm, shp, dt, kind="ExternalOutput").ap()

    with ctx.scope() as sc:
        arenas = build_l0attn(ctx, ar, w, cc, sc, upto=upto)
        if debug:
            Qb, Kb, Vb, stats, r1, r2, kvb, dnm, ecol = arenas
            s_dbg = ctx.sem("dbg_st")
            ph = Ph(ctx, "dbg")
            for nm, t in (("d_q", Qb), ("d_k", Kb), ("d_v", Vb), ("d_stats", stats[0:16, :]),
                          ("d_r1", r1), ("d_r2", r2), ("d_kvb", kvb), ("d_dnm", dnm[0:16, :]),
                          ("d_ecol", ecol[0:16, 0:4])):
                ap = t[:, :] if not isinstance(t, bass.AP) else t
                ph.dma("sync", s_dbg, lambda e, ap=ap, nm=nm: e.dma_start(out=dbg[nm], in_=ap))
            ph.wait("gpsimd", s_dbg)
            ph.run()

    with ctx.scope() as sc:
        tmp_tm = [sc.sbuf(f"tout{i}", [128, D], F32) for i in range(2)]
        ps_t = [sc.psum(f"g_pso{i}", [128, TT]) for i in range(2)]
        ph = Ph(ctx, "out")
        emit_transpose_out(ph, ar=ar, hbm_tm=y_out, sem_tp=s_tp, sem_ev=s_ev,
                           sem_st=s_st, ps_t=ps_t, tmp_tm=tmp_tm)
        ph.wait("gpsimd", s_st[0])
        ph.wait("gpsimd", s_st[1])
        ph.run()

    ctx.close()
    return nc


def build_fused_program(nblob, aidx, fidx, midx, ffidx):
    nc = bass.Bass("TRN2", target_bir_lowering=False, detect_race_conditions=False)
    ctx = Ctx(nc)
    x_in = nc.dram_tensor("x_in", [T, D], F32, kind="ExternalInput").ap()
    y_out = nc.dram_tensor("y_out", [T, D], F32, kind="ExternalOutput").ap()
    consts = nc.dram_tensor("consts", [130, 128], F32, kind="ExternalInput").ap()
    indc = nc.dram_tensor("indc", [16, 1024], F32, kind="ExternalInput").ap()
    coreoh = nc.dram_tensor("coreoh", [128, 8], F32, kind="ExternalInput").ap()
    vecblob = nc.dram_tensor("vecblob", [128, nblob], F32, kind="ExternalInput").ap()

    wa = {k: nc.dram_tensor(k, [1024, 1024], F32, kind="ExternalInput").ap()
          for k in ("WqT", "WkT", "WvT", "WoT")}
    wa.update(aidx)
    wf = {
        "gate_wT": nc.dram_tensor("gate_wT", [1024, 64], F32, kind="ExternalInput").ap(),
        "WgT": nc.dram_tensor("WgT", [64, 1024, 704], F32, kind="ExternalInput").ap(),
        "WuT": nc.dram_tensor("WuT", [64, 1024, 704], F32, kind="ExternalInput").ap(),
        "WdT": nc.dram_tensor("WdT", [64, 704, 1024], F32, kind="ExternalInput").ap(),
        "SgT": nc.dram_tensor("SgT", [1024, 1408], F32, kind="ExternalInput").ap(),
        "SuT": nc.dram_tensor("SuT", [1024, 1408], F32, kind="ExternalInput").ap(),
        "SdT": nc.dram_tensor("SdT", [1408, 1024], F32, kind="ExternalInput").ap(),
    }
    wf.update(fidx)
    wm = {"out_wT": nc.dram_tensor("out_wT", [1024, 1024], F32, kind="ExternalInput").ap(),
          "w1T": nc.dram_tensor("m_w1T", [1024, 4096], F32, kind="ExternalInput").ap(),
          "w2T": nc.dram_tensor("m_w2T", [1024, 4096], F32, kind="ExternalInput").ap(),
          "w3T": nc.dram_tensor("m_w3T", [4096, 1024], F32, kind="ExternalInput").ap()}
    wm.update(midx)
    wff = {k: nc.dram_tensor("ff_" + k, list(shp), F32, kind="ExternalInput").ap()
           for k, shp in (("w1T", (1024, 4096)), ("w2T", (1024, 4096)),
                          ("h1T", (4096, 4096)), ("h2T", (4096, 4096)),
                          ("w3T", (4096, 1024)))}
    wff.update(ffidx)

    cc0 = make_l0_cc(nc)
    cc1 = make_l1_cc(nc)
    x_spill = nc.dram_tensor("x_spill", [128, KC * T], F32).ap()
    rw_hbm = nc.dram_tensor("rw_scr", [64, T], F32).ap()

    common_setup(ctx, nblob)
    ctx.coreoh = ctx.sbuf("c_coreoh", [128, 8], F32)
    ar = ctx.sbuf("ar", [128, KC * T], F32)

    s_ld = [ctx.sem("g_ld0"), ctx.sem("g_ld1")]
    s_cst = ctx.sem("g_cst")
    s_tp = ctx.sem("g_tp")
    s_ev = ctx.sem("g_ev")
    s_st = [ctx.sem("g_st0"), ctx.sem("g_st1")]

    with ctx.scope() as sc:
        tmp_tm = [sc.sbuf(f"tin{i}", [128, D], F32) for i in range(2)]
        ps_t = [sc.psum(f"g_ps{i}", [128, TT]) for i in range(2)]
        ph = Ph(ctx, "in")
        emit_load_consts(ph, ctx, consts, indc, vecblob, s_cst, ind_stage=tmp_tm[0])
        ph.dma("sync", s_cst, lambda e: e.dma_start(out=ctx.coreoh[:], in_=coreoh))
        ph.wait("vector", s_cst)
        ph.op("vector", lambda e: e.engine_nop(), inc=s_ev)
        ph.wait("sync", s_ev)
        emit_transpose_in(ph, hbm_tm=x_in, ar=ar, sem_ld=s_ld, sem_tp=s_tp,
                          sem_ev=s_ev, ps_t=ps_t, tmp_tm=tmp_tm)
        ph.run()

    with ctx.scope() as sc:
        build_l0attn(ctx, ar, wa, cc0, sc)
    with ctx.scope() as sc:
        build_l0ffn(ctx, ar, wf, sc, x_spill, rw_hbm)
    with ctx.scope() as sc:
        build_l1attn(ctx, ar, wm, cc1, sc, x_spill)
    with ctx.scope() as sc:
        build_l1ffn(ctx, ar, wff, sc)

    with ctx.scope() as sc:
        tmp_tm = [sc.sbuf(f"tout{i}", [128, D], F32) for i in range(2)]
        ps_t = [sc.psum(f"g_pso{i}", [128, TT]) for i in range(2)]
        ph = Ph(ctx, "out")
        emit_transpose_out(ph, ar=ar, hbm_tm=y_out, sem_tp=s_tp, sem_ev=s_ev,
                           sem_st=s_st, ps_t=ps_t, tmp_tm=tmp_tm)
        ph.wait("gpsimd", s_st[0])
        ph.wait("gpsimd", s_st[1])
        ph.run()

    ctx.close()
    return nc


def kernel(**inp):
    consts = make_consts()
    x = np.ascontiguousarray(np.asarray(inp["L0_attn_x"], np.float32).reshape(TSEQ, D))

    vb = VecBlob()
    wa, aidx = l0attn_weights_host(inp, vb)
    wf, fidx = l0ffn_weights_host(inp, vb)
    wm, midx = l1attn_weights_host(inp, vb)
    wff, ffidx = l1ffn_weights_host(inp, vb)
    blob = vb.blob()

    nc = build_fused_program(blob.shape[1], aidx, fidx, midx, ffidx)

    feeds = []
    for c in range(NCORES):
        oh = np.zeros((128, 8), np.float32)
        oh[:, c] = 1.0
        m = {"x_in": np.ascontiguousarray(x[c * T:(c + 1) * T, :]),
             "consts": consts, "indc": make_ind(), "coreoh": oh, "vecblob": blob}
        m.update(wa)
        m.update(wf)
        m.update({("out_wT" if k == "out_wT" else "m_" + k): v for k, v in wm.items()})
        m.update({"ff_" + k: v for k, v in wff.items()})
        feeds.append(m)
    res = run_bass_kernel_spmd(nc, feeds, core_ids=list(range(NCORES)))
    x = np.concatenate([np.asarray(res.results[c]["y_out"]) for c in range(NCORES)], axis=0)
    return x.reshape(1, TSEQ, D)


def _run_chain_launch(nc, per_core_feeds):
    res = run_bass_kernel_spmd(nc, per_core_feeds, core_ids=list(range(NCORES)))
    return np.concatenate([np.asarray(res.results[c]["y_out"]) for c in range(NCORES)], axis=0)


def kernel_multilaunch(**inp):
    consts = make_consts()

    x = np.ascontiguousarray(np.asarray(inp["L0_attn_x"], np.float32).reshape(TSEQ, D))

    vb = VecBlob()
    w, idx = l0attn_weights_host(inp, vb)
    blob = vb.blob()
    nc = build_l0attn_program(blob.shape[1], idx)
    feeds = []
    for c in range(NCORES):
        m = {"x_in": np.ascontiguousarray(x[c * T:(c + 1) * T, :]),
             "consts": consts, "indc": make_ind(), "vecblob": blob}
        m.update(w)
        feeds.append(m)
    x = _run_chain_launch(nc, feeds)

    vb = VecBlob()
    w, idx = l0ffn_weights_host(inp, vb)
    blob = vb.blob()
    nc = build_l0ffn_program(blob.shape[1], idx)
    feeds = []
    for c in range(NCORES):
        m = {"x_in": np.ascontiguousarray(x[c * T:(c + 1) * T, :]),
             "consts": consts, "vecblob": blob}
        m.update(w)
        feeds.append(m)
    x = _run_chain_launch(nc, feeds)

    vb = VecBlob()
    w, idx = l1attn_weights_host(inp, vb)
    blob = vb.blob()
    nc = build_l1attn_program(blob.shape[1], idx)
    feeds = []
    for c in range(NCORES):
        oh = np.zeros((128, 8), np.float32)
        oh[:, c] = 1.0
        m = {"x_in": np.ascontiguousarray(x[c * T:(c + 1) * T, :]),
             "consts": consts, "coreoh": oh, "vecblob": blob}
        m.update(w)
        feeds.append(m)
    x = _run_chain_launch(nc, feeds)

    vb = VecBlob()
    w, idx = l1ffn_weights_host(inp, vb)
    blob = vb.blob()
    nc = build_l1ffn_program(blob.shape[1], idx)
    feeds = []
    for c in range(NCORES):
        m = {"x_in": np.ascontiguousarray(x[c * T:(c + 1) * T, :]),
             "consts": consts, "vecblob": blob}
        m.update(w)
        feeds.append(m)
    x = _run_chain_launch(nc, feeds)

    return x.reshape(1, TSEQ, D)


# version 14
# speedup vs baseline: 1.7169x; 1.0114x over previous
import contextlib

import numpy as np
import ml_dtypes

import concourse.bass as bass
import concourse.mybir as mybir
from concourse.bass_utils import run_bass_kernel_spmd

F32 = mybir.dt.float32
BF16 = mybir.dt.bfloat16
AF = mybir.ActivationFunctionType
ALU = mybir.AluOpType
AX = mybir.AxisListType

NCORES = 8
TSEQ = 16384
T = TSEQ // NCORES
D = 1024
KC = D // 128
TT = 512
NTT = T // TT

ENGINES = ("sync", "vector", "scalar", "tensor", "gpsimd")


class Ctx:
    def __init__(self, nc):
        self.nc = nc
        self.es = contextlib.ExitStack()
        self.sems = {}
        self.cnt = {}

    def sem(self, name):
        if name in self.sems:
            return name
        h = self.es.enter_context(self.nc.semaphore(name))
        self.sems[name] = h
        self.cnt[name] = 0
        return name

    def sbuf(self, name, shape, dt):
        return self.es.enter_context(self.nc.sbuf_tensor(name, shape, dt))

    def psum(self, name, shape, dt=F32):
        return self.es.enter_context(self.nc.psum_tensor(name, shape, dt))

    @contextlib.contextmanager
    def scope(self):
        with contextlib.ExitStack() as es:
            class _S:
                def sbuf(_, name, shape, dt):
                    return es.enter_context(self.nc.sbuf_tensor(name, shape, dt))

                def psum(_, name, shape, dt=F32):
                    return es.enter_context(self.nc.psum_tensor(name, shape, dt))
            yield _S()

    def close(self):
        self.es.close()


class Ph:

    def __init__(self, ctx, name=""):
        self.ctx = ctx
        self.name = name
        self.q = {e: [] for e in ENGINES}

    def wait(self, eng, sem, val=None):
        v = self.ctx.cnt[sem] if val is None else val
        if v <= 0:
            return
        h = self.ctx.sems[sem]
        self.q[eng].append(lambda e: e.wait_ge(h, v))

    def op(self, eng, build, inc=None, n=1):
        if inc is not None:
            self.ctx.cnt[inc] += n
            h = self.ctx.sems[inc]
            self.q[eng].append(lambda e: build(e).then_inc(h, n))
            return self.ctx.cnt[inc]
        self.q[eng].append(lambda e: build(e))
        return None

    def dma(self, eng, sem, build):
        return self.op(eng, build, inc=sem, n=16)

    def run(self):
        with self.ctx.nc.Block(self.name or None) as blk:
            for e in ENGINES:
                if self.q[e]:
                    def body(eng, q=tuple(self.q[e])):
                        for f in q:
                            f(eng)
                    getattr(blk, e)(body)


def emit_transpose_in(ph, *, hbm_tm, ar, sem_ld, sem_tp, sem_ev, ps_t, tmp_tm):
    ctx = ph.ctx
    nb = len(ps_t)
    nt = 0
    for ti in range(T // 128):
        tm = tmp_tm[ti % 2]
        if ti >= 2:
            ph.wait("sync", sem_tp, ctx.cnt[sem_tp] - KC)
        ph.dma("sync", sem_ld[ti % 2],
               lambda e, tm=tm, ti=ti: e.dma_start(
                   out=tm[:], in_=hbm_tm[ti * 128:(ti + 1) * 128, :]))
        ph.wait("tensor", sem_ld[ti % 2])
        for kc in range(KC):
            ps = ps_t[nt % nb]
            if nt >= nb:
                ph.wait("tensor", sem_ev, ctx.cnt[sem_ev] - nb + 1)
            ph.op("tensor",
                  lambda e, tm=tm, kc=kc, ps=ps: e.transpose(
                      ps[:, 0:128], tm[:, kc * 128:(kc + 1) * 128], ctx.ident_f32[:]),
                  inc=sem_tp)
            ph.wait("vector", sem_tp)
            col = kc * T + ti * 128
            ph.op("vector",
                  lambda e, ps=ps, col=col: e.tensor_copy(
                      ar[:, col:col + 128], ps[:, 0:128]),
                  inc=sem_ev)
            nt += 1


def emit_transpose_out(ph, *, ar, hbm_tm, sem_tp, sem_ev, sem_st, ps_t, tmp_tm):
    ctx = ph.ctx
    nb = len(ps_t)
    nt = 0
    for ti in range(T // 128):
        tm = tmp_tm[ti % 2]
        if ti >= 2:
            ph.wait("vector", sem_st[ti % 2])
        for kc in range(KC):
            ps = ps_t[nt % nb]
            if nt >= nb:
                ph.wait("tensor", sem_ev, ctx.cnt[sem_ev] - nb + 1)
            col = kc * T + ti * 128
            ph.op("tensor",
                  lambda e, col=col, ps=ps: e.transpose(
                      ps[:, 0:128], ar[:, col:col + 128], ctx.ident_f32[:]),
                  inc=sem_tp)
            ph.wait("vector", sem_tp)
            ph.op("vector",
                  lambda e, ps=ps, tm=tm, kc=kc: e.tensor_copy(
                      tm[:, kc * 128:(kc + 1) * 128], ps[:, 0:128]),
                  inc=sem_ev)
            nt += 1
        ph.wait("sync", sem_ev)
        ph.dma("sync", sem_st[ti % 2],
               lambda e, tm=tm, ti=ti: e.dma_start(
                   out=hbm_tm[ti * 128:(ti + 1) * 128, :], in_=tm[:]))


def emit_postln(ph, *, ar, sq, ps_stat, ps_b, sem_mm, sem_ev, out_bf=None, eps_ap=None, stop_after=None):
    ctx = ph.ctx
    rows = ctx.rows
    for tt in range(NTT):
        c0 = tt * TT
        r_m = rows[0:1, 0 * TT:1 * TT]
        r_v = rows[0:1, 1 * TT:2 * TT]
        r_rs = rows[0:1, 2 * TT:3 * TT]
        r_mr = rows[0:1, 3 * TT:4 * TT]
        r_t = rows[0:1, 2 * TT:3 * TT]
        mm_sq0 = ctx.cnt[sem_mm]
        for kc in range(KC):
            sqt = sq[:, (kc % 2) * TT:(kc % 2 + 1) * TT]
            if kc >= 2:
                ph.wait("scalar", sem_mm, mm_sq0 + kc - 1)
            ph.op("scalar",
                  lambda e, kc=kc, c0=c0, sqt=sqt: e.activation(
                      sqt, ar[:, kc * T + c0: kc * T + c0 + TT],
                      AF.Square, bias=0.0, scale=1.0),
                  inc=sem_ev)
            ph.op("tensor",
                  lambda e, kc=kc, c0=c0: e.matmul(
                      ps_stat[0][0:1, :], ctx.ones_col_f32[:, 0:1],
                      ar[:, kc * T + c0: kc * T + c0 + TT],
                      start=(kc == 0), stop=(kc == KC - 1)))
            ph.wait("tensor", sem_ev)
            ph.op("tensor",
                  lambda e, kc=kc, sqt=sqt: e.matmul(
                      ps_stat[1][0:1, :], ctx.ones_col_f32[:, 0:1],
                      sqt, start=(kc == 0), stop=(kc == KC - 1)),
                  inc=sem_mm)
        ph.wait("scalar", sem_mm)
        if stop_after == "stats":
            continue
        ph.op("scalar", lambda e, r_m=r_m: e.mul(r_m, ps_stat[0][0:1, :], 1.0 / D))
        ph.op("scalar", lambda e, r_v=r_v: e.mul(r_v, ps_stat[1][0:1, :], 1.0 / D),
              inc=sem_ev)
        ph.wait("vector", sem_ev)
        ph.op("vector", lambda e, r_t=r_t, r_m=r_m: e.tensor_tensor(
            r_t, r_m, r_m, ALU.mult))
        ph.op("vector", lambda e, r_v=r_v, r_t=r_t: e.tensor_sub(r_v, r_v, r_t),
              inc=sem_ev)
        ph.wait("scalar", sem_ev)
        ph.op("scalar", lambda e, r_rs=r_rs, r_v=r_v: e.activation(
            r_rs, r_v, AF.Sqrt, bias=eps_ap, scale=1.0), inc=sem_ev)
        ph.wait("vector", sem_ev)
        ph.op("vector", lambda e, r_rs=r_rs: e.reciprocal(r_rs, r_rs))
        ph.op("vector", lambda e, r_mr=r_mr, r_m=r_m, r_rs=r_rs: e.tensor_tensor(
            r_mr, r_m, r_rs, ALU.mult), inc=sem_ev)
        ph.wait("tensor", sem_ev)
        if stop_after == "rows":
            continue
        ph.op("tensor", lambda e, r_rs=r_rs: e.matmul(
            ps_b[0][:, :], ctx.ones_row_f32[0:1, :], r_rs, start=True, stop=True))
        ph.op("tensor", lambda e, r_mr=r_mr: e.matmul(
            ps_b[1][:, :], ctx.ones_row_f32[0:1, :], r_mr, start=True, stop=True),
            inc=sem_mm)
        ph.wait("scalar", sem_mm)
        if stop_after == "bcast":
            continue
        ph.op("scalar", lambda e: e.copy(sq[:, 0:TT], ps_b[0][:, :]))
        ph.op("scalar", lambda e: e.copy(sq[:, TT:2 * TT], ps_b[1][:, :]), inc=sem_ev)
        ph.wait("vector", sem_ev)
        for kc in range(KC):
            c = kc * T + c0
            ph.op("vector",
                  lambda e, c=c: e.tensor_tensor(
                      ar[:, c:c + TT], ar[:, c:c + TT], sq[:, 0:TT], ALU.mult))
            ph.op("vector",
                  lambda e, c=c: e.tensor_tensor(
                      ar[:, c:c + TT], ar[:, c:c + TT], sq[:, TT:2 * TT], ALU.subtract),
                  inc=sem_ev)
            if out_bf is not None:
                ph.wait("scalar", sem_ev)
                ph.op("scalar",
                      lambda e, c=c: e.copy(out_bf[:, c:c + TT], ar[:, c:c + TT]),
                      inc=sem_ev)
        ph.wait("tensor", sem_ev)
        ph.wait("scalar", sem_ev)


def build_l1ffn(ctx, ar, w, sc):
    HC = 32
    s_ld = ctx.sem("ff_ld")
    s_cv = ctx.sem("ff_cv")
    s_cv1 = ctx.sem("ff_cv1")
    s_mm = ctx.sem("ff_mm")
    s_ev = ctx.sem("ff_ev")

    h_a = sc.sbuf("ff_ha", [128, HC * TT], BF16)
    h_b = sc.sbuf("ff_hb", [128, HC * TT], BF16)
    stg2 = sc.sbuf("ff_stg", [128, 2 * 4096], F32)
    stg = stg2[:, 0:4096]
    wb = sc.sbuf("ff_wb", [128, 2 * 4096], BF16)
    xq = sc.sbuf("ff_xq", [128, KC * TT], BF16)
    silu_t = [sc.sbuf(f"ff_silu{i}", [128, TT], BF16) for i in range(2)]
    out_t = [sc.sbuf(f"ff_out{i}", [128, TT], F32) for i in range(2)]
    psA = [sc.psum(f"ff_psA{i}", [128, TT]) for i in range(2)]
    psB = [sc.psum(f"ff_psB{i}", [128, TT]) for i in range(2)]

    sv = lambda base, ch: ctx.vecs[:, base + ch: base + ch + 1]

    for q in range(NTT):
        t0 = q * TT

        ph = Ph(ctx, f"ffA{q}")
        for kc in range(KC):
            ph.op("vector",
                  lambda e, kc=kc, t0=t0: e.tensor_copy(
                      xq[:, kc * TT:(kc + 1) * TT], ar[:, kc * T + t0: kc * T + t0 + TT]),
                  inc=(s_cv if kc == KC - 1 else None))
        ph.wait("tensor", s_cv)
        cv0 = ctx.cnt[s_cv]
        mm0 = ctx.cnt[s_mm]
        ev0 = ctx.cnt[s_ev]
        for hch in range(HC):
            par = hch % 2
            sb = stg2[:, par * 4096: par * 4096 + 2048]
            if hch >= 2:
                ph.wait("sync", s_cv, cv0 + hch - 1)
            ph.dma("sync", s_ld,
                   lambda e, hch=hch, sb=sb: e.dma_start(
                       out=sb[:, 0:1024].rearrange("p (kc n) -> p kc n", kc=8),
                       in_=w["w1T"][:, hch * 128:(hch + 1) * 128].rearrange(
                           "(kc p) n -> p kc n", p=128)))
            ph.dma("sync", s_ld,
                   lambda e, hch=hch, sb=sb: e.dma_start(
                       out=sb[:, 1024:2048].rearrange("p (kc n) -> p kc n", kc=8),
                       in_=w["w2T"][:, hch * 128:(hch + 1) * 128].rearrange(
                           "(kc p) n -> p kc n", p=128)))
            ph.wait("scalar", s_ld)
            if hch >= 2:
                ph.wait("scalar", s_mm, mm0 + 2 * (hch - 1))
            ph.op("scalar",
                  lambda e, par=par, sb=sb: e.copy(
                      wb[:, par * 2048: par * 2048 + 2048], sb[:, 0:2048]),
                  inc=s_cv)
            ph.wait("tensor", s_cv)
            if hch >= 2:
                ph.wait("tensor", s_ev, ev0 + 2 * (hch - 1))
            for kc in range(KC):
                ph.op("tensor",
                      lambda e, kc=kc, par=par: e.matmul(
                          psA[par][:], wb[:, par * 2048 + kc * 128: par * 2048 + (kc + 1) * 128],
                          xq[:, kc * TT:(kc + 1) * TT],
                          start=(kc == 0), stop=(kc == KC - 1)),
                      inc=(s_mm if kc == KC - 1 else None))
            for kc in range(KC):
                ph.op("tensor",
                      lambda e, kc=kc, par=par: e.matmul(
                          psB[par][:], wb[:, par * 2048 + 1024 + kc * 128: par * 2048 + 1024 + (kc + 1) * 128],
                          xq[:, kc * TT:(kc + 1) * TT],
                          start=(kc == 0), stop=(kc == KC - 1)),
                      inc=(s_mm if kc == KC - 1 else None))
            ph.wait("scalar", s_mm, mm0 + 2 * hch + 1)
            if hch >= 2:
                ph.wait("scalar", s_ev, ev0 + 2 * (hch - 1))
            ph.op("scalar",
                  lambda e, par=par, hch=hch: e.activation(
                      silu_t[par][:, :], psA[par][:], AF.Silu,
                      bias=0.0, scale=sv(w["s1c"], hch)),
                  inc=s_ev)
            ph.wait("vector", s_ev, ev0 + 2 * hch + 1)
            ph.wait("vector", s_mm, mm0 + 2 * hch + 2)
            ph.op("vector",
                  lambda e, par=par, hch=hch: e.scalar_tensor_tensor(
                      h_a[:, hch * TT:(hch + 1) * TT], psB[par][:],
                      sv(w["s2c"], hch), silu_t[par][:, :], ALU.mult, ALU.mult),
                  inc=s_ev)
        ph.run()

        for nm, wt, sb, src, dst in (("ffB", "h1T", "sh1c", h_a, h_b),
                                     ("ffC", "h2T", "sh2c", h_b, h_a)):
            ph = Ph(ctx, f"{nm}{q}")
            mm0 = ctx.cnt[s_mm]
            ev0 = ctx.cnt[s_ev]
            for nch in range(HC):
                par = nch % 2
                cs = s_cv if par == 0 else s_cv1
                stgp = stg2[:, par * 4096:(par + 1) * 4096]
                if nch >= 2:
                    ph.wait("sync", cs, ctx.cnt[cs])
                ph.dma("sync", s_ld,
                       lambda e, nch=nch, wt=wt, stgp=stgp: e.dma_start(
                           out=stgp.rearrange("p (kc n) -> p kc n", kc=32),
                           in_=w[wt][:, nch * 128:(nch + 1) * 128].rearrange(
                               "(kc p) n -> p kc n", p=128)))
                ph.wait("scalar", s_ld)
                if nch >= 2:
                    ph.wait("scalar", s_mm, mm0 + nch - 1)
                ph.op("scalar",
                      lambda e, par=par, stgp=stgp: e.copy(
                          wb[:, par * 4096: par * 4096 + 4096], stgp),
                      inc=cs)
                ph.wait("tensor", cs)
                if nch >= 2:
                    ph.wait("tensor", s_ev, ev0 + nch - 1)
                for kch in range(HC):
                    ph.op("tensor",
                          lambda e, kch=kch, par=par, src=src: e.matmul(
                              psA[par][:], wb[:, par * 4096 + kch * 128: par * 4096 + (kch + 1) * 128],
                              src[:, kch * TT:(kch + 1) * TT],
                              start=(kch == 0), stop=(kch == HC - 1)),
                          inc=(s_mm if kch == HC - 1 else None))
                ph.wait("scalar", s_mm, mm0 + nch + 1)
                ph.op("scalar",
                      lambda e, par=par, nch=nch, dst=dst, sb=sb: e.activation(
                          dst[:, nch * TT:(nch + 1) * TT], psA[par][:], AF.Gelu,
                          bias=0.0, scale=sv(w[sb], nch)),
                      inc=s_ev)
            ph.run()

        ph = Ph(ctx, f"ffD{q}")
        cv0 = ctx.cnt[s_cv]
        mm0 = ctx.cnt[s_mm]
        ev0 = ctx.cnt[s_ev]
        for dch in range(KC):
            par = dch % 2
            sb = stg2[:, par * 4096:(par + 1) * 4096]
            if dch >= 2:
                ph.wait("sync", s_cv, cv0 + dch - 1)
            ph.dma("sync", s_ld,
                   lambda e, dch=dch, sb=sb: e.dma_start(
                       out=sb.rearrange("p (kc n) -> p kc n", kc=32),
                       in_=w["w3T"][:, dch * 128:(dch + 1) * 128].rearrange(
                           "(kc p) n -> p kc n", p=128)))
            ph.wait("scalar", s_ld)
            if dch >= 2:
                ph.wait("scalar", s_mm, mm0 + dch - 1)
            ph.op("scalar",
                  lambda e, par=par, sb=sb: e.copy(
                      wb[:, par * 4096: par * 4096 + 4096], sb),
                  inc=s_cv)
            ph.wait("tensor", s_cv)
            if dch >= 2:
                ph.wait("tensor", s_ev, ev0 + 2 * (dch - 1))
            for kch in range(HC):
                ph.op("tensor",
                      lambda e, kch=kch, par=par: e.matmul(
                          psA[par][:], wb[:, par * 4096 + kch * 128: par * 4096 + (kch + 1) * 128],
                          h_a[:, kch * TT:(kch + 1) * TT],
                          start=(kch == 0), stop=(kch == HC - 1)),
                      inc=(s_mm if kch == HC - 1 else None))
            ph.wait("scalar", s_mm, mm0 + dch + 1)
            if dch >= 2:
                ph.wait("scalar", s_ev, ev0 + 2 * (dch - 1))
            ph.op("scalar",
                  lambda e, par=par, dch=dch: e.activation(
                      out_t[par][:, :], psA[par][:], AF.Copy,
                      bias=0.0, scale=sv(w["s3c"], dch)),
                  inc=s_ev)
            ph.wait("vector", s_ev, ev0 + 2 * dch + 1)
            c = dch * T + t0
            ph.op("vector",
                  lambda e, par=par, c=c: e.tensor_add(
                      ar[:, c:c + TT], ar[:, c:c + TT], out_t[par][:, :]),
                  inc=s_ev)
        ph.run()

    ph = Ph(ctx, "ffLN")
    emit_postln(ph, ar=ar, sq=stg, ps_stat=[psA[0], psA[1]], ps_b=[psB[0], psB[1]],
                sem_mm=s_mm, sem_ev=s_ev,
                eps_ap=ctx.vecs[0:1, w["keps"]:w["keps"] + 1])
    ph.run()


def emit_load_w1024(ph, sc_hbm, stg, wb, s_ld, s_cv, nin=1024, nout=1024):
    kc = nin // 128
    half = nout // 2
    for h in range(2):
        n0 = h * half
        ph.dma("sync", s_ld,
               lambda e, n0=n0: e.dma_start(
                   out=stg[:, 0:kc * half].rearrange("p (k n) -> p k n", k=kc),
                   in_=sc_hbm[:, n0:n0 + half].rearrange("(k p) n -> p k n", p=128)))
        ph.wait("gpsimd", s_ld)
        for k in range(kc):
            ph.op("gpsimd",
                  lambda e, k=k, n0=n0: e.tensor_copy(
                      wb[:, k * nout + n0: k * nout + n0 + half],
                      stg[:, k * half:(k + 1) * half]),
                  inc=(s_cv if k == kc - 1 else None))


def build_l0attn(ctx, ar, w, cc, sc, upto="H"):
    nc = ctx.nc
    EPS = 1e-6
    s_ld = ctx.sem("a_ld")
    s_cv = ctx.sem("a_cv")
    s_mm = ctx.sem("a_mm")
    s_ev = ctx.sem("a_ev")
    s_tp = ctx.sem("a_tp")
    s_cc = ctx.sem("a_cc")
    s_gld = ctx.sem("a_gld")

    Qb = sc.sbuf("a_Q", [128, KC * T], BF16)
    Kb = sc.sbuf("a_K", [128, KC * T], BF16)
    Vb = sc.sbuf("a_V", [128, KC * T], BF16)
    stats = sc.sbuf("a_stats", [16, 3 * T], BF16)
    ecol = sc.sbuf("a_ecol", [16, 8], F32)
    dnm = sc.sbuf("a_dnm", [16, 2], F32)
    blkA = sc.sbuf("a_blkA", [128, 8 * 16], BF16)
    blkB = sc.sbuf("a_blkB", [128, 8 * 16], BF16)
    r1 = sc.sbuf("a_r1", [128, 16], F32)
    r1e = sc.sbuf("a_r1e", [128, 16], F32)
    r2 = sc.sbuf("a_r2", [128, 16], F32)
    r2e = sc.sbuf("a_r2e", [128, 16], F32)
    kvb = sc.sbuf("a_kvb", [128, 16 * 64], BF16)
    blkkv = sc.sbuf("a_blkkv", [128, 128], BF16)
    ktt = sc.sbuf("a_ktt", [128, 2 * 128], BF16)
    vtt = sc.sbuf("a_vtt", [128, 2 * 128], BF16)
    stg = sc.sbuf("a_stg", [128, 2048], F32)
    s8k = stg
    kvsb = stg[:, 0:1024]
    sia_sb = stg[:, 1024:1024 + TT]
    srow32 = stg[0:16, 1536:1536 + TT]
    wb = sc.sbuf("a_wb", [128, 8 * 512], BF16)

    psA = sc.psum("a_psA", [128, TT])
    psB = sc.psum("a_psB", [128, TT])
    psD = sc.psum("a_psD", [16, TT])
    psC = sc.psum("a_psC", [16, 16])
    pstK = sc.psum("a_pstK", [128, 128], BF16)
    pstV = sc.psum("a_pstV", [128, 128], BF16)
    kve = sc.psum("a_kve", [64, 64])
    kvo = sc.psum("a_kvo", [64, 64])

    def xcvt(kc):
        return s8k[:, kc * 256:(kc + 1) * 256].bitcast(BF16)

    S_SI, S_SC, S_SA = 0, 1, 2

    def srow(blk, tt=None):
        if tt is None:
            return stats[:, blk * T:(blk + 1) * T]
        return stats[:, blk * T + tt * TT: blk * T + (tt + 1) * TT]

    kon = ctx.vecs[:, w["kone"]:w["kone"] + 1]

    for pi, (wt, bcol, func, arena) in enumerate((
            ("WqT", "bq", AF.Sigmoid, Qb),
            ("WkT", "bk", AF.Sigmoid, Kb),
            ("WvT", "bv", AF.Identity, Vb))):
        ph = Ph(ctx, f"a_proj{pi}")
        for h in range(2):
            n0 = h * 512
            if h == 1:
                ph.wait("sync", s_mm)
            for qtr in range(2):
                if qtr == 1:
                    ph.wait("sync", s_cv)
                ph.dma("sync", s_ld,
                       lambda e, n0=n0, qtr=qtr, wt=wt: e.dma_start(
                           out=stg[:, 0:2048].rearrange("p (k n) -> p k n", k=8),
                           in_=w[wt][:, n0 + qtr * 256: n0 + qtr * 256 + 256].rearrange(
                               "(k p) n -> p k n", p=128)))
                ph.wait("gpsimd", s_ld)
                for k in range(KC):
                    ph.op("gpsimd",
                          lambda e, k=k, qtr=qtr: e.tensor_copy(
                              wb[:, k * 512 + qtr * 256: k * 512 + qtr * 256 + 256],
                              stg[:, k * 256:(k + 1) * 256]),
                          inc=(s_cv if k == KC - 1 else None))
            cvh = ctx.cnt[s_cv]
            mm0 = ctx.cnt[s_mm]
            nidx = 0
            for tt in range(NTT):
                ph.wait("vector", s_cv, cvh)
                ph.wait("vector", s_mm, mm0 + 4 * tt)
                for kc in range(KC):
                    ph.op("vector",
                          lambda e, kc=kc, tt=tt: e.tensor_copy(
                              xcvt(kc), ar[:, kc * T + tt * TT: kc * T + (tt + 1) * TT]),
                          inc=(s_cv if kc == KC - 1 else None))
                ph.wait("tensor", s_cv)
                for nch in range(4):
                    ps = psA if nidx % 2 == 0 else psB
                    ph.wait("tensor", s_ev, ctx.cnt[s_ev] - 1)
                    for kc in range(KC):
                        ph.op("tensor",
                              lambda e, kc=kc, nch=nch, ps=ps: e.matmul(
                                  ps[:], wb[:, kc * 512 + nch * 128: kc * 512 + (nch + 1) * 128],
                                  xcvt(kc), start=(kc == 0), stop=(kc == KC - 1)),
                              inc=(s_mm if kc == KC - 1 else None))
                    ph.wait("scalar", s_mm)
                    nabs = h * 4 + nch
                    ph.op("scalar",
                          lambda e, ps=ps, nabs=nabs, tt=tt, arena=arena, func=func, bcol=bcol: e.activation(
                              arena[:, nabs * T + tt * TT: nabs * T + (tt + 1) * TT],
                              ps[:], func, bias=ctx.vecs[:, w[bcol] + nabs: w[bcol] + nabs + 1],
                              scale=1.0),
                          inc=s_ev)
                    nidx += 1
        ph.run()

    if upto == "A":
        return (Qb, Kb, Vb, stats, r1, r2, kvb, dnm, ecol)
    ph = Ph(ctx, "a_sums")
    for kc in range(KC):
        ph.op("vector",
              lambda e, kc=kc: e.reduce_sum(r1[:, kc:kc + 1], Kb[:, kc * T:(kc + 1) * T], axis=AX.X))
        ph.op("vector",
              lambda e, kc=kc: e.reduce_sum(r1[:, 8 + kc:9 + kc], Qb[:, kc * T:(kc + 1) * T], axis=AX.X),
              inc=(s_ev if kc == KC - 1 else None))
    ph.wait("gpsimd", s_ev)
    ph.dma("gpsimd", s_gld, lambda e: e.dma_start(out=cc["r16_in"], in_=r1[:, :]))
    ph.wait("gpsimd", s_gld)
    ph.op("gpsimd", lambda e: e.collective_compute(
        "AllReduce", ALU.add, replica_groups=[list(range(NCORES))],
        ins=[cc["r16_in"].opt()], outs=[cc["r16_out"].opt()]), inc=s_cc)
    ph.wait("gpsimd", s_cc)
    ph.dma("gpsimd", s_gld, lambda e: e.dma_start(out=r1[:, :], in_=cc["r16_out"]))
    ph.wait("vector", s_gld)
    ph.op("vector", lambda e: e.tensor_scalar_add(
        r1e[:, :], r1[:, :], ctx.vecs[:, w["keps6"]:w["keps6"] + 1]))
    ph.op("vector", lambda e: e.memset(blkA[:, :], 0.0))
    ph.op("vector", lambda e: e.memset(blkB[:, :], 0.0), inc=s_ev)
    for kc in range(KC):
        ph.op("vector", lambda e, kc=kc: e.tensor_copy(
            blkA[0:64, kc * 16 + 2 * kc: kc * 16 + 2 * kc + 1], r1e[0:64, kc:kc + 1]))
        ph.op("vector", lambda e, kc=kc: e.tensor_copy(
            blkA[64:128, kc * 16 + 2 * kc + 1: kc * 16 + 2 * kc + 2], r1e[64:128, kc:kc + 1]))
        ph.op("vector", lambda e, kc=kc: e.tensor_copy(
            blkB[0:64, kc * 16 + 2 * kc: kc * 16 + 2 * kc + 1], r1e[0:64, 8 + kc:9 + kc]))
        ph.op("vector", lambda e, kc=kc: e.tensor_copy(
            blkB[64:128, kc * 16 + 2 * kc + 1: kc * 16 + 2 * kc + 2], r1e[64:128, 8 + kc:9 + kc]),
            inc=(s_ev if kc == KC - 1 else None))
    ph.wait("tensor", s_ev)
    for j, blk in enumerate((blkA, blkB)):
        for kc in range(KC):
            ph.op("tensor",
                  lambda e, blk=blk, kc=kc: e.matmul(
                      psC[0:16, 0:1], blk[:, kc * 16:(kc + 1) * 16],
                      ctx.ones_col_bf[:, 0:1], start=(kc == 0), stop=(kc == KC - 1)),
                  inc=(s_mm if kc == KC - 1 else None))
        ph.wait("scalar", s_mm)
        ph.op("scalar", lambda e, j=j: e.mul(ecol[:, j:j + 1], psC[0:16, 0:1], EPS),
              inc=s_ev)
        ph.wait("tensor", s_ev)
    ph.wait("scalar", s_ev)
    for j, (blk, rhs_a, sblk, ec) in enumerate(
            ((blkA, Qb, S_SI, 0), (blkB, Kb, S_SC, 1))):
        for tt in range(NTT):
            for kc in range(KC):
                ph.op("tensor",
                      lambda e, blk=blk, kc=kc, rhs_a=rhs_a, tt=tt: e.matmul(
                          psD[0:16, :], blk[:, kc * 16:(kc + 1) * 16],
                          rhs_a[:, kc * T + tt * TT: kc * T + (tt + 1) * TT],
                          start=(kc == 0), stop=(kc == KC - 1)),
                      inc=(s_mm if kc == KC - 1 else None))
            ph.wait("scalar", s_mm)
            ph.wait("scalar", s_ev)
            ph.op("scalar",
                  lambda e, ec=ec: e.activation(
                      srow32[0:16, :], psD[0:16, :], AF.Identity,
                      bias=ecol[0:16, ec:ec + 1], scale=1.0),
                  inc=s_ev)
            ph.wait("vector", s_ev)
            ph.op("vector",
                  lambda e: e.reciprocal(srow32[0:16, :], srow32[0:16, :]))
            ph.op("vector",
                  lambda e, sblk=sblk, tt=tt: e.tensor_copy(srow(sblk, tt), srow32[0:16, :]),
                  inc=s_ev)
            ph.wait("tensor", s_ev)
    ph.run()

    if upto == "B":
        return (Qb, Kb, Vb, stats, r1, r2, kvb, dnm, ecol)
    ph = Ph(ctx, "a_r2")
    for j, (arena, sblk, col0) in enumerate(((Kb, S_SC, 0), (Qb, S_SI, 8))):
        for kc in range(KC):
            for tt in range(NTT):
                ph.op("tensor",
                      lambda e, kc=kc, tt=tt, sblk=sblk: e.matmul(
                          psB[:], ctx.ind_bf[0:16, kc * 128:(kc + 1) * 128],
                          srow(sblk, tt), start=True, stop=True),
                      inc=s_mm)
                ph.wait("vector", s_mm)
                ph.op("vector",
                      lambda e, kc=kc, tt=tt, arena=arena: e.scalar_tensor_tensor(
                          s8k[:, tt * TT:(tt + 1) * TT],
                          arena[:, kc * T + tt * TT: kc * T + (tt + 1) * TT],
                          kon, psB[:], ALU.bypass, ALU.mult),
                      inc=s_ev)
                ph.wait("tensor", s_ev)
            ph.wait("vector", s_ev)
            ph.op("vector",
                  lambda e, kc=kc, col0=col0: e.reduce_sum(
                      r2[:, col0 + kc: col0 + kc + 1], s8k[:, :], axis=AX.X),
                  inc=s_ev)
    ph.wait("gpsimd", s_ev)
    ph.dma("gpsimd", s_gld, lambda e: e.dma_start(out=cc["r16_in"], in_=r2[:, :]))
    ph.wait("gpsimd", s_gld)
    ph.op("gpsimd", lambda e: e.collective_compute(
        "AllReduce", ALU.add, replica_groups=[list(range(NCORES))],
        ins=[cc["r16_in"].opt()], outs=[cc["r16_out"].opt()]), inc=s_cc)
    ph.wait("gpsimd", s_cc)
    ph.dma("gpsimd", s_gld, lambda e: e.dma_start(out=r2[:, :], in_=cc["r16_out"]))
    ph.run()

    if upto == "C":
        return (Qb, Kb, Vb, stats, r1, r2, kvb, dnm, ecol)
    ph = Ph(ctx, "a_cs")
    ph.wait("vector", s_gld)
    ph.op("vector", lambda e: e.tensor_scalar_add(
        r2e[:, :], r2[:, :], ctx.vecs[:, w["keps6"]:w["keps6"] + 1]))
    ph.op("vector", lambda e: e.memset(blkA[:, :], 0.0))
    ph.op("vector", lambda e: e.memset(blkB[:, :], 0.0), inc=s_ev)
    for kc in range(KC):
        ph.op("vector", lambda e, kc=kc: e.tensor_copy(
            blkA[0:64, kc * 16 + 2 * kc: kc * 16 + 2 * kc + 1], r2e[0:64, kc:kc + 1]))
        ph.op("vector", lambda e, kc=kc: e.tensor_copy(
            blkA[64:128, kc * 16 + 2 * kc + 1: kc * 16 + 2 * kc + 2], r2e[64:128, kc:kc + 1]))
        ph.op("vector", lambda e, kc=kc: e.tensor_copy(
            blkB[0:64, kc * 16 + 2 * kc: kc * 16 + 2 * kc + 1], r2e[0:64, 8 + kc:9 + kc]))
        ph.op("vector", lambda e, kc=kc: e.tensor_copy(
            blkB[64:128, kc * 16 + 2 * kc + 1: kc * 16 + 2 * kc + 2], r2e[64:128, 8 + kc:9 + kc]),
            inc=(s_ev if kc == KC - 1 else None))
    ph.wait("tensor", s_ev)
    for j, blk in enumerate((blkA, blkB)):
        for kc in range(KC):
            ph.op("tensor",
                  lambda e, blk=blk, kc=kc: e.matmul(
                      psC[0:16, 0:1], blk[:, kc * 16:(kc + 1) * 16],
                      ctx.ones_col_bf[:, 0:1], start=(kc == 0), stop=(kc == KC - 1)),
                  inc=(s_mm if kc == KC - 1 else None))
        ph.wait("scalar", s_mm)
        ph.op("scalar", lambda e, j=j: e.mul(ecol[:, 2 + j:3 + j], psC[0:16, 0:1], EPS),
              inc=s_ev)
        ph.wait("tensor", s_ev)
    ph.wait("scalar", s_ev)
    for j, (blk, rhs_a, sblk, ec) in enumerate(
            ((blkA, Qb, S_SA, 2), (blkB, Kb, S_SC, 3))):
        for tt in range(NTT):
            for kc in range(KC):
                ph.op("tensor",
                      lambda e, blk=blk, kc=kc, rhs_a=rhs_a, tt=tt: e.matmul(
                          psD[0:16, :], blk[:, kc * 16:(kc + 1) * 16],
                          rhs_a[:, kc * T + tt * TT: kc * T + (tt + 1) * TT],
                          start=(kc == 0), stop=(kc == KC - 1)),
                      inc=(s_mm if kc == KC - 1 else None))
            ph.wait("scalar", s_mm)
            if j == 0:
                ph.op("scalar",
                      lambda e, tt=tt: e.activation(
                          srow(S_SA, tt), psD[0:16, :], AF.Sigmoid,
                          bias=ecol[0:16, 2:3], scale=1.0),
                      inc=s_ev)
            else:
                ph.op("scalar",
                      lambda e, tt=tt: e.activation(
                          srow(S_SC, tt), psD[0:16, :], AF.Identity,
                          bias=ecol[0:16, 3:4], scale=1.0),
                      inc=s_ev)
                ph.wait("vector", s_ev)
                ph.op("vector",
                      lambda e, tt=tt: e.tensor_scalar(
                          srow(S_SC, tt), srow(S_SC, tt), 1.0, -1.0,
                          ALU.min, ALU.max),
                      inc=s_ev)
                ph.wait("scalar", s_ev)
                ph.op("scalar",
                      lambda e, tt=tt: e.activation(
                          srow(S_SC, tt), srow(S_SC, tt), AF.Exp,
                          bias=0.0, scale=1.0),
                      inc=s_ev)
            ph.wait("tensor", s_ev)
    ph.wait("vector", s_ev)
    ph.op("vector", lambda e: e.reduce_sum(dnm[:, 0:1], srow(S_SC), axis=AX.X),
          inc=s_ev)
    ph.wait("gpsimd", s_ev)
    ph.dma("gpsimd", s_gld, lambda e: e.dma_start(out=cc["d_in"], in_=dnm[0:16, 0:1]))
    ph.wait("gpsimd", s_gld)
    ph.op("gpsimd", lambda e: e.collective_compute(
        "AllReduce", ALU.add, replica_groups=[list(range(NCORES))],
        ins=[cc["d_in"].opt()], outs=[cc["d_out"].opt()]), inc=s_cc)
    ph.wait("gpsimd", s_cc)
    ph.dma("gpsimd", s_gld, lambda e: e.dma_start(out=dnm[0:16, 1:2], in_=cc["d_out"]))
    ph.wait("vector", s_gld)
    ph.op("vector", lambda e: e.reciprocal(dnm[:, 1:2], dnm[:, 1:2]), inc=s_ev)
    ph.wait("vector", s_ev)
    ph.op("vector", lambda e: e.tensor_scalar(
        srow(S_SC), srow(S_SC), dnm[0:16, 1:2], float(TSEQ), ALU.mult, ALU.mult))
    ph.op("vector", lambda e: e.tensor_tensor(
        srow(S_SI), srow(S_SI), srow(S_SA), ALU.mult), inc=s_ev)
    ph.run()

    if upto == "D":
        return (Qb, Kb, Vb, stats, r1, r2, kvb, dnm, ecol)
    ph = Ph(ctx, "a_kv")
    mm0 = ctx.cnt[s_mm]
    for kc in range(KC):
        for tt in range(NTT):
            ph.op("tensor",
                  lambda e, kc=kc, tt=tt: e.matmul(
                      psB[:], ctx.ind_bf[0:16, kc * 128:(kc + 1) * 128],
                      srow(S_SC, tt), start=True, stop=True),
                  inc=s_mm)
            ph.wait("vector", s_mm)
            ph.op("vector",
                  lambda e, kc=kc, tt=tt: e.scalar_tensor_tensor(
                      Vb[:, kc * T + tt * TT: kc * T + (tt + 1) * TT],
                      Vb[:, kc * T + tt * TT: kc * T + (tt + 1) * TT],
                      kon, psB[:], ALU.bypass, ALU.mult),
                  inc=s_ev)
            ph.wait("tensor", s_ev)
    tp0 = ctx.cnt[s_tp]
    ev0 = ctx.cnt[s_ev]
    mm0 = ctx.cnt[s_mm]
    for kc in range(KC):
        for ts in range(T // 128):
            par = ts % 2
            i = kc * (T // 128) + ts
            if i >= 1:
                ph.wait("tensor", s_ev, ev0 + 2 * i)
            if i >= 2:
                ph.wait("vector", s_mm, mm0 + 2 * (i - 1))
            ph.op("tensor",
                  lambda e, kc=kc, ts=ts: e.transpose(
                      pstK[:, 0:128], Kb[:, kc * T + ts * 128: kc * T + (ts + 1) * 128],
                      ctx.ident_bf[:]),
                  inc=s_tp)
            ph.op("tensor",
                  lambda e, kc=kc, ts=ts: e.transpose(
                      pstV[:, 0:128], Vb[:, kc * T + ts * 128: kc * T + (ts + 1) * 128],
                      ctx.ident_bf[:]),
                  inc=s_tp)
            ph.wait("vector", s_tp)
            ph.op("vector",
                  lambda e, par=par: e.tensor_copy(ktt[:, par * 128:(par + 1) * 128], pstK[:, 0:128]),
                  inc=s_ev)
            ph.op("vector",
                  lambda e, par=par: e.tensor_copy(vtt[:, par * 128:(par + 1) * 128], pstV[:, 0:128]),
                  inc=s_ev)
            ph.wait("tensor", s_ev)
            ph.op("tensor",
                  lambda e, par=par, ts=ts: e.matmul(
                      kve[0:64, 0:64], ktt[:, par * 128: par * 128 + 64],
                      vtt[:, par * 128: par * 128 + 64],
                      start=(ts == 0), stop=(ts == T // 128 - 1)),
                  inc=s_mm)
            ph.op("tensor",
                  lambda e, par=par, ts=ts: e.matmul(
                      kvo[0:64, 0:64], ktt[:, par * 128 + 64: par * 128 + 128],
                      vtt[:, par * 128 + 64: par * 128 + 128],
                      start=(ts == 0), stop=(ts == T // 128 - 1)),
                  inc=s_mm)
        ph.wait("scalar", s_mm)
        ph.op("scalar",
              lambda e, kc=kc: e.copy(kvsb[0:64, (2 * kc) * 64:(2 * kc + 1) * 64], kve[0:64, 0:64]))
        ph.op("scalar",
              lambda e, kc=kc: e.copy(kvsb[0:64, (2 * kc + 1) * 64:(2 * kc + 2) * 64], kvo[0:64, 0:64]),
              inc=s_ev)
        ph.wait("tensor", s_ev)
    ph.wait("gpsimd", s_ev)
    ph.dma("gpsimd", s_gld, lambda e: e.dma_start(out=cc["kv_in"], in_=kvsb[:, :]))
    ph.wait("gpsimd", s_gld)
    ph.op("gpsimd", lambda e: e.collective_compute(
        "AllReduce", ALU.add, replica_groups=[list(range(NCORES))],
        ins=[cc["kv_in"].opt()], outs=[cc["kv_out"].opt()]), inc=s_cc)
    ph.wait("gpsimd", s_cc)
    ph.dma("gpsimd", s_gld, lambda e: e.dma_start(out=kvsb[:, :], in_=cc["kv_out"]))
    ph.wait("vector", s_gld)
    ph.op("vector", lambda e: e.tensor_copy(kvb[:, :], kvsb[:, :]), inc=s_cv)
    ph.run()

    if upto == "E":
        return (Qb, Kb, Vb, stats, r1, r2, kvb, dnm, ecol)
    ph = Ph(ctx, "a_out")
    out_a = Kb
    for kc in range(KC):
        ph.wait("vector", s_mm)
        ph.op("vector", lambda e: e.memset(blkkv[:, :], 0.0))
        ph.op("vector", lambda e, kc=kc: e.tensor_copy(
            blkkv[0:64, 0:64], kvb[0:64, (2 * kc) * 64:(2 * kc + 1) * 64]))
        ph.op("vector", lambda e, kc=kc: e.tensor_copy(
            blkkv[64:128, 64:128], kvb[0:64, (2 * kc + 1) * 64:(2 * kc + 2) * 64]),
            inc=s_cv)
        ph.wait("tensor", s_cv)
        for tt in range(NTT):
            ph.op("tensor",
                  lambda e, kc=kc, tt=tt: e.matmul(
                      psA[:], blkkv[:, :], Qb[:, kc * T + tt * TT: kc * T + (tt + 1) * TT],
                      start=True, stop=True),
                  inc=s_mm)
            ph.op("tensor",
                  lambda e, kc=kc, tt=tt: e.matmul(
                      psB[:], ctx.ind_bf[0:16, kc * 128:(kc + 1) * 128],
                      srow(S_SI, tt), start=True, stop=True),
                  inc=s_mm)
            ph.wait("scalar", s_mm)
            ph.op("scalar", lambda e: e.copy(sia_sb[:, :], psB[:]), inc=s_ev)
            ph.wait("vector", s_ev)
            ph.op("vector",
                  lambda e, kc=kc, tt=tt: e.scalar_tensor_tensor(
                      out_a[:, kc * T + tt * TT: kc * T + (tt + 1) * TT],
                      psA[:], kon, sia_sb[:, :], ALU.bypass, ALU.mult),
                  inc=s_ev)
            ph.wait("tensor", s_ev)
    ph.run()

    if upto == "F":
        return (Qb, Kb, Vb, stats, r1, r2, kvb, dnm, ecol)
    ph = Ph(ctx, "a_wo")
    for h in range(2):
        n0 = h * 512
        if h == 1:
            ph.wait("sync", s_mm)
        for qtr in range(2):
            if qtr == 1 or h == 1:
                ph.wait("sync", s_cv)
            ph.dma("sync", s_ld,
                   lambda e, n0=n0, qtr=qtr: e.dma_start(
                       out=stg[:, 0:2048].rearrange("p (k n) -> p k n", k=8),
                       in_=w["WoT"][:, n0 + qtr * 256: n0 + qtr * 256 + 256].rearrange(
                           "(k p) n -> p k n", p=128)))
            ph.wait("gpsimd", s_ld)
            for k in range(KC):
                ph.op("gpsimd",
                      lambda e, k=k, qtr=qtr: e.tensor_copy(
                          wb[:, k * 512 + qtr * 256: k * 512 + qtr * 256 + 256],
                          stg[:, k * 256:(k + 1) * 256]),
                      inc=(s_cv if k == KC - 1 else None))
        ph.wait("tensor", s_cv)
        nidx = 0
        for nch in range(4):
            nabs = h * 4 + nch
            for tt in range(NTT):
                ps = psA if nidx % 2 == 0 else psB
                ph.wait("tensor", s_ev, ctx.cnt[s_ev] - 1)
                for kc in range(KC):
                    ph.op("tensor",
                          lambda e, kc=kc, nch=nch, tt=tt, ps=ps: e.matmul(
                              ps[:], wb[:, kc * 512 + nch * 128: kc * 512 + (nch + 1) * 128],
                              out_a[:, kc * T + tt * TT: kc * T + (tt + 1) * TT],
                              start=(kc == 0), stop=(kc == KC - 1)),
                          inc=(s_mm if kc == KC - 1 else None))
                ph.wait("vector", s_mm)
                c = nabs * T + tt * TT
                ph.op("vector",
                      lambda e, ps=ps, nabs=nabs, c=c: e.scalar_tensor_tensor(
                          ar[:, c:c + TT], ps[:],
                          ctx.vecs[:, w["bo"] + nabs: w["bo"] + nabs + 1],
                          ar[:, c:c + TT], ALU.add, ALU.add),
                      inc=s_ev)
                nidx += 1
    ph.run()

    if upto == "G":
        return (Qb, Kb, Vb, stats, r1, r2, kvb, dnm, ecol)
    ph = Ph(ctx, "a_ln")
    emit_postln(ph, ar=ar, sq=stg, ps_stat=[psA, psB], ps_b=[psA, psB],
                sem_mm=s_mm, sem_ev=s_ev,
                eps_ap=ctx.vecs[0:1, w["keps"]:w["keps"] + 1],
                stop_after={"H1": "stats", "H2": "rows", "H3": "bcast"}.get(upto))
    ph.run()
    return (Qb, Kb, Vb, stats, r1, r2, kvb, dnm, ecol)


def emit_rms_rows(ph, ctx, *, src_ar, rrow, sq, ps_stat, ps_b, s_mm, s_ev, eps_ap):
    for tt in range(NTT):
        c0 = tt * TT
        mm_sq0 = ctx.cnt[s_mm]
        for kc in range(KC):
            sqt = sq[:, (kc % 2) * TT:(kc % 2 + 1) * TT]
            if kc >= 2:
                ph.wait("scalar", s_mm, mm_sq0 + kc - 1)
            ph.op("scalar",
                  lambda e, kc=kc, c0=c0, sqt=sqt: e.activation(
                      sqt, src_ar[:, kc * T + c0: kc * T + c0 + TT],
                      AF.Square, bias=0.0, scale=1.0),
                  inc=s_ev)
            ph.wait("tensor", s_ev)
            ph.op("tensor",
                  lambda e, kc=kc, sqt=sqt: e.matmul(
                      ps_stat[0:1, :], ctx.ones_col_f32[:, 0:1],
                      sqt, start=(kc == 0), stop=(kc == KC - 1)),
                  inc=s_mm)
        ph.wait("scalar", s_mm)
        ph.op("scalar",
              lambda e, c0=c0: e.activation(
                  rrow[0:1, c0:c0 + TT], ps_stat[0:1, :], AF.Sqrt,
                  bias=eps_ap, scale=1.0 / D),
              inc=s_ev)
        ph.wait("vector", s_ev)
        ph.op("vector",
              lambda e, c0=c0: e.reciprocal(rrow[0:1, c0:c0 + TT], rrow[0:1, c0:c0 + TT]),
              inc=s_ev)
        ph.wait("tensor", s_ev)


def build_l1attn(ctx, ar, w, cc, sc, x_spill, dbg_hook=None):
    nc = ctx.nc
    s_ld = ctx.sem("m_ld")
    s_gld = ctx.sem("m_gld")
    s_cv = ctx.sem("m_cv")
    s_mm = ctx.sem("m_mm")
    s_ev = ctx.sem("m_ev")
    s_cc = ctx.sem("m_cc")

    ybf = sc.sbuf("m_ybf", [128, KC * T], BF16)
    mbuf = sc.sbuf("m_mbuf", [128, 8 * T], BF16)
    stg = sc.sbuf("m_stg", [128, 4096], F32)
    wb = sc.sbuf("m_wb", [128, 2 * 4096], BF16)
    rrow = sc.sbuf("m_rrow", [1, T], F32)
    bxs = sc.sbuf("m_bxs", [128, T], F32)
    yss = sc.sbuf("m_yss", [128, T], F32)
    scb = sc.sbuf("m_scb", [128, TT], F32)
    av = sc.sbuf("m_av", [128, 8 * 4], F32)
    lc = sc.sbuf("m_lc", [128, 8], F32)
    lg = sc.sbuf("m_lg", [128, 64], F32)

    psA = sc.psum("m_psA", [128, TT])
    psB = sc.psum("m_psB", [128, TT])
    psS = sc.psum("m_psS", [1, TT])

    atv = av[:, 0:8]
    a2k = av[:, 8:16]
    cac = av[:, 16:24]
    cse = av[:, 24:32]

    vcol = lambda base, i: ctx.vecs[:, base + i: base + i + 1]

    ph = Ph(ctx, "m_rms1")
    ph.op("scalar", lambda e: e.activation(
        atv, ctx.vecs[:, w["a"]:w["a"] + 8], AF.Tanh, bias=0.0, scale=1.0),
        inc=s_ev)
    ph.wait("vector", s_ev)
    ph.op("vector", lambda e: e.tensor_mul(a2k, atv, atv), inc=s_ev)
    for i in range(10):
        src_b, dst_b = (a2k, cac) if i % 2 == 0 else (cac, a2k)
        ph.wait("vector", s_ev)
        ph.op("vector", lambda e, s=src_b, d=dst_b: e.tensor_mul(d, s, s), inc=s_ev)
    ph.wait("vector", s_ev)
    ph.op("vector", lambda e: e.memset(cse, 0.0), inc=s_ev)
    ph.wait("vector", s_ev)
    ph.op("vector", lambda e: e.memset(cac, 0.0), inc=s_ev)
    emit_rms_rows(ph, ctx, src_ar=ar, rrow=rrow, sq=stg, ps_stat=psS, ps_b=None,
                  s_mm=s_mm, s_ev=s_ev, eps_ap=ctx.vecs[0:1, w["keps5"]:w["keps5"] + 1])
    ph.run()

    def emit_bx(ph, kc):
        for tt in range(NTT):
            ph.op("tensor",
                  lambda e, tt=tt: e.matmul(
                      psB[:], ctx.ones_row_f32[0:1, :], rrow[0:1, tt * TT:(tt + 1) * TT],
                      start=True, stop=True),
                  inc=s_mm)
            ph.wait("scalar", s_mm)
            ph.op("scalar", lambda e: e.copy(scb[:, :], psB[:]), inc=s_ev)
            ph.wait("vector", s_ev)
            ph.op("vector",
                  lambda e, kc=kc, tt=tt: e.scalar_tensor_tensor(
                      bxs[:, tt * TT:(tt + 1) * TT],
                      ar[:, kc * T + tt * TT: kc * T + (tt + 1) * TT],
                      vcol(w["n1wb"], kc), scb[:, :], ALU.mult, ALU.mult),
                  inc=s_ev)
            ph.wait("tensor", s_ev)

    for kc in range(KC):
        ph = Ph(ctx, f"m_scan1_{kc}")
        if kc > 0:
            ph.op("scalar",
                  lambda e, kc=kc: e.copy(lc[:, kc - 1:kc], yss[:, T - 1:T]),
                  inc=s_ev)
            ph.wait("vector", s_ev)
        emit_bx(ph, kc)
        ph.op("vector",
              lambda e, kc=kc: e.tensor_tensor_scan(
                  yss[:, :], atv[:, kc:kc + 1].broadcast_to([128, T]), bxs[:, :],
                  0.0, ALU.mult, ALU.add))
        ph.run()
    ph = Ph(ctx, "m_scan1f")
    ph.op("scalar",
          lambda e: e.copy(lc[:, KC - 1:KC], yss[:, T - 1:T]),
          inc=s_ev)
    ph.wait("gpsimd", s_ev)
    ph.dma("gpsimd", s_gld, lambda e: e.dma_start(out=cc["ag_in"], in_=lc[:, :]))
    ph.wait("gpsimd", s_gld)
    ph.op("gpsimd", lambda e: e.collective_compute(
        "AllGather", ALU.bypass, replica_groups=[list(range(NCORES))],
        ins=[cc["ag_in"].opt()], outs=[cc["ag_out"].opt()]), inc=s_cc)
    ph.wait("gpsimd", s_cc)
    ph.dma("gpsimd", s_gld,
           lambda e: e.dma_start(out=lg[:, :].rearrange("p (r k) -> p r k", r=NCORES),
                                 in_=cc["ag_out"].rearrange("(r p) k -> p r k", p=128)))
    ph.run()

    ph = Ph(ctx, "m_carry")
    ph.wait("vector", s_gld)
    for j in range(NCORES):
        ph.op("vector",
              lambda e, j=j: e.scalar_tensor_tensor(
                  cse, cac, ctx.coreoh[:, j:j + 1], cse, ALU.mult, ALU.add),
              inc=s_ev)
        ph.wait("vector", s_ev)
        if j < NCORES - 1:
            ph.op("vector",
                  lambda e, j=j: e.scalar_tensor_tensor(
                      cac, cac, ctx.vecs[:, w["kone"]:w["kone"] + 1], a2k,
                      ALU.bypass, ALU.mult),
                  inc=s_ev)
            ph.wait("vector", s_ev)
            ph.op("vector",
                  lambda e, j=j: e.tensor_add(cac, cac, lg[:, j * 8:(j + 1) * 8]),
                  inc=s_ev)
            ph.wait("vector", s_ev)
    ph.run()

    if dbg_hook is not None:
        dbg_hook({"d_lc": lc, "d_lg": lg, "d_av": av, "d_rrow": rrow,
                  "d_bxs": bxs, "d_yss": yss})
    for kc in range(KC):
        ph = Ph(ctx, f"m_scan2_{kc}")
        if kc > 0:
            ph.op("scalar",
                  lambda e, kc=kc: e.copy(ybf[:, (kc - 1) * T: kc * T], yss[:, :]),
                  inc=s_ev)
            ph.wait("vector", s_ev)
        emit_bx(ph, kc)
        ph.op("vector",
              lambda e, kc=kc: e.scalar_tensor_tensor(
                  bxs[:, 0:1], cse[:, kc:kc + 1], atv[:, kc:kc + 1],
                  bxs[:, 0:1], ALU.mult, ALU.add),
              inc=s_ev)
        ph.wait("vector", s_ev)
        ph.op("vector",
              lambda e, kc=kc: e.tensor_tensor_scan(
                  yss[:, :], atv[:, kc:kc + 1].broadcast_to([128, T]), bxs[:, :],
                  0.0, ALU.mult, ALU.add))
        ph.run()
    ph = Ph(ctx, "m_scan2f")
    ph.op("scalar",
          lambda e: e.copy(ybf[:, (KC - 1) * T: KC * T], yss[:, :]),
          inc=s_ev)
    ph.run()

    s_sp = ctx.sem("m_sp")
    ph = Ph(ctx, "m_proj")
    ph.dma("sync", s_sp, lambda e: e.dma_start(out=x_spill, in_=ar[:, :]))
    for h in range(2):
        n0 = h * 512
        if h == 1:
            ph.wait("sync", s_mm)
        for qtr in range(2):
            if qtr == 1 or h == 1:
                ph.wait("sync", s_cv)
            ph.dma("sync", s_ld,
                   lambda e, n0=n0, qtr=qtr: e.dma_start(
                       out=stg[:, 0:2048].rearrange("p (k n) -> p k n", k=8),
                       in_=w["out_wT"][:, n0 + qtr * 256: n0 + qtr * 256 + 256].rearrange(
                           "(k p) n -> p k n", p=128)))
            ph.wait("scalar", s_ld)
            for k in range(KC):
                ph.op("scalar",
                      lambda e, k=k, qtr=qtr: e.copy(
                          wb[:, k * 512 + qtr * 256: k * 512 + qtr * 256 + 256],
                          stg[:, k * 256:(k + 1) * 256]),
                      inc=(s_cv if k == KC - 1 else None))
        ph.wait("tensor", s_cv)
        if h == 0:
            ph.wait("vector", s_sp)
        for nch in range(4):
            nabs = h * 4 + nch
            for tt in range(NTT):
                ps = psA if (nabs * NTT + tt) % 2 == 0 else psB
                ph.wait("tensor", s_ev, ctx.cnt[s_ev] - 1)
                for kc in range(KC):
                    ph.op("tensor",
                          lambda e, kc=kc, nch=nch, tt=tt, ps=ps: e.matmul(
                              ps[:], wb[:, kc * 512 + nch * 128: kc * 512 + (nch + 1) * 128],
                              ybf[:, kc * T + tt * TT: kc * T + (tt + 1) * TT],
                              start=(kc == 0), stop=(kc == KC - 1)),
                          inc=(s_mm if kc == KC - 1 else None))
                ph.wait("vector", s_mm)
                c = nabs * T + tt * TT
                ph.op("vector",
                      lambda e, ps=ps, nabs=nabs, c=c: e.scalar_tensor_tensor(
                          ar[:, c:c + TT], ps[:], vcol(w["out_b"], nabs),
                          ar[:, c:c + TT], ALU.add, ALU.add),
                      inc=s_ev)
    ph.run()

    ph = Ph(ctx, "m_rms2")
    emit_rms_rows(ph, ctx, src_ar=ar, rrow=rrow, sq=stg, ps_stat=psS, ps_b=None,
                  s_mm=s_mm, s_ev=s_ev, eps_ap=ctx.vecs[0:1, w["keps5"]:w["keps5"] + 1])
    for kc in range(KC):
        for tt in range(NTT):
            ph.op("tensor",
                  lambda e, tt=tt: e.matmul(
                      psB[:], ctx.ones_row_f32[0:1, :], rrow[0:1, tt * TT:(tt + 1) * TT],
                      start=True, stop=True),
                  inc=s_mm)
            ph.wait("scalar", s_mm)
            ph.op("scalar", lambda e: e.copy(scb[:, :], psB[:]), inc=s_ev)
            ph.wait("vector", s_ev)
            ph.op("vector",
                  lambda e, kc=kc, tt=tt: e.scalar_tensor_tensor(
                      ybf[:, kc * T + tt * TT: kc * T + (tt + 1) * TT],
                      ar[:, kc * T + tt * TT: kc * T + (tt + 1) * TT],
                      vcol(w["n2w"], kc), scb[:, :], ALU.mult, ALU.mult),
                  inc=s_ev)
            ph.wait("tensor", s_ev)
    ph.run()

    HC = 32
    for grp in range(4):
        ph = Ph(ctx, f"m_mlpA{grp}")
        cv0 = ctx.cnt[s_cv]
        for hh in range(8):
            hch = grp * 8 + hh
            par = hh % 2
            if hh >= 1:
                ph.wait("sync", s_cv, cv0 + hh)
            ph.dma("sync", s_ld,
                   lambda e, hch=hch: e.dma_start(
                       out=stg[:, 0:1024].rearrange("p (kc n) -> p kc n", kc=8),
                       in_=w["w2T"][:, hch * 128:(hch + 1) * 128].rearrange(
                           "(kc p) n -> p kc n", p=128)))
            ph.dma("sync", s_ld,
                   lambda e, hch=hch: e.dma_start(
                       out=stg[:, 1024:2048].rearrange("p (kc n) -> p kc n", kc=8),
                       in_=w["w1T"][:, hch * 128:(hch + 1) * 128].rearrange(
                           "(kc p) n -> p kc n", p=128)))
            ph.wait("scalar", s_ld)
            if hh >= 2:
                ph.wait("scalar", s_mm, ctx.cnt[s_mm] - NTT)
            ph.op("scalar",
                  lambda e, par=par: e.copy(
                      wb[:, par * 2048: par * 2048 + 2048], stg[:, 0:2048]),
                  inc=s_cv)
            ph.wait("tensor", s_cv)
            for tt in range(NTT):
                ph.wait("tensor", s_ev, ctx.cnt[s_ev] - 1)
                for kc in range(KC):
                    ph.op("tensor",
                          lambda e, kc=kc, tt=tt, par=par: e.matmul(
                              psA[:], wb[:, par * 2048 + kc * 128: par * 2048 + (kc + 1) * 128],
                              ybf[:, kc * T + tt * TT: kc * T + (tt + 1) * TT],
                              start=(kc == 0), stop=(kc == KC - 1)))
                for kc in range(KC):
                    ph.op("tensor",
                          lambda e, kc=kc, tt=tt, par=par: e.matmul(
                              psB[:], wb[:, par * 2048 + 1024 + kc * 128: par * 2048 + 1024 + (kc + 1) * 128],
                              ybf[:, kc * T + tt * TT: kc * T + (tt + 1) * TT],
                              start=(kc == 0), stop=(kc == KC - 1)),
                          inc=(s_mm if kc == KC - 1 else None))
                ph.wait("scalar", s_mm)
                ph.op("scalar",
                      lambda e, hch=hch: e.activation(
                          scb[:, :], psA[:], AF.Silu, bias=vcol(w["w2b"], hch), scale=1.0),
                      inc=s_ev)
                ph.wait("vector", s_ev)
                ph.op("vector",
                      lambda e, hh=hh, tt=tt, hch=hch: e.scalar_tensor_tensor(
                          mbuf[:, hh * T + tt * TT: hh * T + (tt + 1) * TT],
                          psB[:], vcol(w["w1b"], hch), scb[:, :], ALU.add, ALU.mult),
                      inc=s_ev)
        ph.run()

        ph = Ph(ctx, f"m_mlpB{grp}")
        cv0 = ctx.cnt[s_cv]
        for dch in range(KC):
            par = dch % 2
            if dch >= 1:
                ph.wait("sync", s_cv, cv0 + dch)
            ph.dma("sync", s_ld,
                   lambda e, grp=grp, dch=dch: e.dma_start(
                       out=stg[:, 0:1024].rearrange("p (kc n) -> p kc n", kc=8),
                       in_=w["w3T"][grp * 1024:(grp + 1) * 1024, dch * 128:(dch + 1) * 128].rearrange(
                           "(kc p) n -> p kc n", p=128)))
            ph.wait("scalar", s_ld)
            if dch >= 2:
                ph.wait("scalar", s_mm, ctx.cnt[s_mm] - NTT)
            ph.op("scalar",
                  lambda e, par=par: e.copy(
                      wb[:, par * 1024: par * 1024 + 1024], stg[:, 0:1024]),
                  inc=s_cv)
            ph.wait("tensor", s_cv)
            for tt in range(NTT):
                ps = psA if (dch * NTT + tt) % 2 == 0 else psB
                ph.wait("tensor", s_ev, ctx.cnt[s_ev] - 1)
                for kh in range(8):
                    ph.op("tensor",
                          lambda e, kh=kh, tt=tt, par=par, ps=ps: e.matmul(
                              ps[:], wb[:, par * 1024 + kh * 128: par * 1024 + (kh + 1) * 128],
                              mbuf[:, kh * T + tt * TT: kh * T + (tt + 1) * TT],
                              start=(kh == 0), stop=(kh == 7)),
                          inc=(s_mm if kh == 7 else None))
                ph.wait("vector", s_mm)
                c = dch * T + tt * TT
                bcol = w["w3b"] if grp == 0 else w["kzero"]
                ph.op("vector",
                      lambda e, ps=ps, dch=dch, c=c, bcol=bcol: e.scalar_tensor_tensor(
                          ar[:, c:c + TT], ps[:], vcol(bcol, dch if bcol == w["w3b"] else 0),
                          ar[:, c:c + TT], ALU.add, ALU.add),
                      inc=s_ev)
        ph.run()

    ph = Ph(ctx, "m_fin")
    for kc in range(KC):
        for hf in range(2):
            ph.wait("sync", s_ev)
            ph.dma("sync", s_ld,
                   lambda e, kc=kc, hf=hf: e.dma_start(
                       out=stg[:, 0:1024],
                       in_=x_spill[:, kc * T + hf * 1024: kc * T + (hf + 1) * 1024]))
            ph.wait("vector", s_ld)
            for tt in range(2):
                c = kc * T + hf * 1024 + tt * TT
                ph.op("vector",
                      lambda e, c=c, tt=tt: e.tensor_add(
                          ar[:, c:c + TT], ar[:, c:c + TT], stg[:, tt * TT:(tt + 1) * TT]),
                      inc=s_ev)
    ph.run()

    ph = Ph(ctx, "m_ln")
    emit_postln(ph, ar=ar, sq=stg, ps_stat=[psA, psB], ps_b=[psA, psB],
                sem_mm=s_mm, sem_ev=s_ev,
                eps_ap=ctx.vecs[0:1, w["keps"]:w["keps"] + 1])
    ph.run()


def l1attn_weights_host(inp, vb):
    w = {
        "out_wT": _t(inp["L1_attn_out_w"]),
        "w1T": _t(inp["L1_attn_w1_w"]),
        "w2T": _t(inp["L1_attn_w2_w"]),
        "w3T": _t(inp["L1_attn_w3_w"]),
    }
    n1wb = np.asarray(inp["L1_attn_n1_w"], np.float32) * np.asarray(inp["L1_attn_b"], np.float32)
    idx = {
        "a": vb.add("m_a", _chunkvec(inp["L1_attn_a"])),
        "n1wb": vb.add("m_n1wb", _chunkvec(n1wb)),
        "n2w": vb.add("m_n2w", _chunkvec(inp["L1_attn_n2_w"])),
        "out_b": vb.add("m_outb", _chunkvec(inp["L1_attn_out_b"])),
        "w1b": vb.add("m_w1b", _chunkvec(inp["L1_attn_w1_b"])),
        "w2b": vb.add("m_w2b", _chunkvec(inp["L1_attn_w2_b"])),
        "w3b": vb.add("m_w3b", _chunkvec(inp["L1_attn_w3_b"])),
        "kzero": vb.add("k_zero", np.zeros((128, 1), np.float32)),
        "kone": vb.add("k_one1", np.full((128, 1), 1.0, np.float32)),
        "keps5": vb.add("k_eps5", np.full((128, 1), 1e-5, np.float32)),
        "keps": vb.add("k_eps_ln1", np.full((128, 1), 1e-5, np.float32)),
    }
    return w, idx


def make_l1_cc(nc):
    return {
        "ag_in": nc.dram_tensor("cc_agi", [128, 8], F32).ap(),
        "ag_out": nc.dram_tensor("cc_ago", [NCORES * 128, 8], F32).ap(),
    }


def build_l1attn_program(nblob, vec_idx, debug=False):
    nc = bass.Bass("TRN2", target_bir_lowering=False, detect_race_conditions=False)
    ctx = Ctx(nc)
    x_in = nc.dram_tensor("x_in", [T, D], F32, kind="ExternalInput").ap()
    y_out = nc.dram_tensor("y_out", [T, D], F32, kind="ExternalOutput").ap()
    consts = nc.dram_tensor("consts", [130, 128], F32, kind="ExternalInput").ap()
    coreoh = nc.dram_tensor("coreoh", [128, 8], F32, kind="ExternalInput").ap()
    vecblob = nc.dram_tensor("vecblob", [128, nblob], F32, kind="ExternalInput").ap()
    w = {"out_wT": nc.dram_tensor("out_wT", [1024, 1024], F32, kind="ExternalInput").ap(),
         "w1T": nc.dram_tensor("w1T", [1024, 4096], F32, kind="ExternalInput").ap(),
         "w2T": nc.dram_tensor("w2T", [1024, 4096], F32, kind="ExternalInput").ap(),
         "w3T": nc.dram_tensor("w3T", [4096, 1024], F32, kind="ExternalInput").ap()}
    w.update(vec_idx)
    cc = make_l1_cc(nc)
    x_spill = nc.dram_tensor("x_spill", [128, KC * T], F32).ap()

    common_setup(ctx, nblob)
    ctx.coreoh = ctx.sbuf("c_coreoh", [128, 8], F32)
    ar = ctx.sbuf("ar", [128, KC * T], F32)

    s_ld = [ctx.sem("g_ld0"), ctx.sem("g_ld1")]
    s_cst = ctx.sem("g_cst")
    s_tp = ctx.sem("g_tp")
    s_ev = ctx.sem("g_ev")
    s_st = [ctx.sem("g_st0"), ctx.sem("g_st1")]

    with ctx.scope() as sc:
        tmp_tm = [sc.sbuf(f"tin{i}", [128, D], F32) for i in range(2)]
        ps_t = [sc.psum(f"g_ps{i}", [128, TT]) for i in range(2)]
        ph = Ph(ctx, "in")
        emit_load_consts(ph, ctx, consts, None, vecblob, s_cst)
        ph.dma("sync", s_cst, lambda e: e.dma_start(out=ctx.coreoh[:], in_=coreoh))
        ph.wait("vector", s_cst)
        emit_transpose_in(ph, hbm_tm=x_in, ar=ar, sem_ld=s_ld, sem_tp=s_tp,
                          sem_ev=s_ev, ps_t=ps_t, tmp_tm=tmp_tm)
        ph.run()

    hook = None
    if debug:
        shapes = {"d_lc": [128, 8], "d_lg": [128, 64], "d_av": [128, 32], "d_rrow": [1, T],
                  "d_bxs": [128, T], "d_yss": [128, T]}
        douts = {nm: nc.dram_tensor(nm, shp, F32, kind="ExternalOutput").ap()
                 for nm, shp in shapes.items()}
        s_dbg = ctx.sem("dbg_st")

        def hook(tens):
            ph = Ph(ctx, "dbg")
            for nm, t in tens.items():
                ph.dma("sync", s_dbg, lambda e, t=t, nm=nm: e.dma_start(out=douts[nm], in_=t[:, :] if nm != "d_rrow" else t[0:1, :]))
            ph.wait("gpsimd", s_dbg)
            ph.run()

    with ctx.scope() as sc:
        build_l1attn(ctx, ar, w, cc, sc, x_spill, dbg_hook=hook)

    with ctx.scope() as sc:
        tmp_tm = [sc.sbuf(f"tout{i}", [128, D], F32) for i in range(2)]
        ps_t = [sc.psum(f"g_pso{i}", [128, TT]) for i in range(2)]
        ph = Ph(ctx, "out")
        emit_transpose_out(ph, ar=ar, hbm_tm=y_out, sem_tp=s_tp, sem_ev=s_ev,
                           sem_st=s_st, ps_t=ps_t, tmp_tm=tmp_tm)
        ph.wait("gpsimd", s_st[0])
        ph.wait("gpsimd", s_st[1])
        ph.run()

    ctx.close()
    return nc


def build_l0ffn(ctx, ar, w, sc, x_spill, rw_hbm):
    nc = ctx.nc
    s_ld = ctx.sem("f_ld")
    s_cv = ctx.sem("f_cv")
    s_mm = ctx.sem("f_mm")
    s_ev = ctx.sem("f_ev")
    s_sp = ctx.sem("f_sp")
    s_rw = ctx.sem("f_rw")
    s_rw2 = ctx.sem("f_rw2")
    s_hp = ctx.sem("f_hp")

    x_bf = sc.sbuf("f_xbf", [128, KC * T], BF16)
    routed = ar

    psA = sc.psum("f_psA", [128, TT])
    psB = sc.psum("f_psB", [128, TT])
    psR = sc.psum("f_psR", [128, TT])
    psD = sc.psum("f_psD", [128, TT])
    psA2 = sc.psum("f_psA2", [128, TT])
    psB2 = sc.psum("f_psB2", [128, TT])
    psD2 = sc.psum("f_psD2", [128, TT])

    NCH = [(0, 128), (128, 128), (256, 128), (384, 128), (512, 128), (640, 64)]
    kone = ctx.vecs[:, w["kone"]:w["kone"] + 1]
    kzero = ctx.vecs[:, w["kzero"]:w["kzero"] + 1]

    with ctx.scope() as rsc:
        sco_fm = rsc.sbuf("f_sco", [64, T], F32)
        sfc_fm = rsc.sbuf("f_sfc", [64, T], F32)
        sco_tm = rsc.sbuf("f_scot", [128, 64], F32)
        sfc_tm = rsc.sbuf("f_sfct", [128, 64], F32)
        wrk = rsc.sbuf("f_wrk", [128, 64], F32)
        sel = rsc.sbuf("f_sel", [128, 64], F32)
        gsc = rsc.sbuf("f_gsc", [128, 8], F32)
        gmk = rsc.sbuf("f_gmk", [128, 8], F32)
        rwt = rsc.sbuf("f_rwt", [128, 64], F32)
        m1 = rsc.sbuf("f_m1", [128, 4], F32)
        gwtb = rsc.sbuf("f_gwtb", [128, 8 * 64], BF16)
        stg = rsc.sbuf("f_gstg", [128, 512], F32)
        psT = rsc.psum("f_psT", [128, 128])

        ph = Ph(ctx, "f_route")

        def vop(build):
            ph.op("vector", build, inc=s_ev)
            ph.wait("vector", s_ev)

        for kc in range(KC):
            ph.op("vector",
                  lambda e, kc=kc: e.tensor_copy(
                      x_bf[:, kc * T:(kc + 1) * T], ar[:, kc * T:(kc + 1) * T]),
                  inc=(s_cv if kc == KC - 1 else None))
        ph.wait("sync", s_cv)
        ph.dma("sync", s_sp, lambda e: e.dma_start(out=x_spill, in_=ar[:, :]))
        ph.wait("gpsimd", s_sp)
        ph.op("gpsimd", lambda e: e.memset(routed[:, :], 0.0), inc=s_cv)
        ph.dma("sync", s_ld,
               lambda e: e.dma_start(
                   out=stg[:, 0:512].rearrange("p (kc n) -> p kc n", kc=8),
                   in_=w["gate_wT"].rearrange("(kc p) n -> p kc n", p=128)))
        ph.wait("gpsimd", s_ld)
        ph.op("gpsimd", lambda e: e.tensor_copy(gwtb[:, :], stg[:, 0:512]), inc=s_cv)
        ph.wait("tensor", s_cv)
        for tt in range(NTT):
            if tt > 0:
                ph.wait("tensor", s_ev)
            for kc in range(KC):
                ph.op("tensor",
                      lambda e, kc=kc, tt=tt: e.matmul(
                          psA[0:64, 0:TT], gwtb[:, kc * 64:(kc + 1) * 64],
                          x_bf[:, kc * T + tt * TT: kc * T + (tt + 1) * TT],
                          start=(kc == 0), stop=(kc == KC - 1)),
                      inc=(s_mm if kc == KC - 1 else None))
            ph.wait("scalar", s_mm)
            ph.op("scalar",
                  lambda e, tt=tt: e.activation(
                      sco_fm[0:64, tt * TT:(tt + 1) * TT], psA[0:64, 0:TT],
                      AF.Sigmoid, bias=0.0, scale=1.0),
                  inc=s_ev)
        ph.wait("vector", s_ev)
        ph.op("vector", lambda e: e.tensor_scalar_add(
            sfc_fm[0:64, :], sco_fm[0:64, :], ctx.vecs[0:64, w["gbias"]:w["gbias"] + 1]),
            inc=s_ev)
        for ti in range(T // 128):
            ph.wait("tensor", s_ev)
            ph.op("tensor",
                  lambda e, ti=ti: e.transpose(
                      psT[:, 0:64], sco_fm[0:64, ti * 128:(ti + 1) * 128],
                      ctx.ident_f32[0:64, 0:64]),
                  inc=s_mm)
            ph.wait("vector", s_mm)
            vop(lambda e: e.tensor_copy(sco_tm[:, :], psT[:, 0:64]))
            ph.wait("tensor", s_ev)
            ph.op("tensor",
                  lambda e, ti=ti: e.transpose(
                      psT[:, 0:64], sfc_fm[0:64, ti * 128:(ti + 1) * 128],
                      ctx.ident_f32[0:64, 0:64]),
                  inc=s_mm)
            ph.wait("vector", s_mm)
            vop(lambda e: e.tensor_copy(sfc_tm[:, :], psT[:, 0:64]))
            for g in range(8):
                vop(lambda e, g=g: e.reduce_max(m1[:, 0:1], sfc_tm[:, g * 8:(g + 1) * 8], axis=AX.X))
                vop(lambda e, g=g: e.tensor_scalar(
                    wrk[:, 0:8], sfc_tm[:, g * 8:(g + 1) * 8], m1[:, 0:1], None, ALU.is_ge))
                vop(lambda e, g=g: e.scalar_tensor_tensor(
                    wrk[:, 8:16], wrk[:, 0:8], -1e30, sfc_tm[:, g * 8:(g + 1) * 8],
                    ALU.mult, ALU.add))
                vop(lambda e: e.reduce_max(m1[:, 1:2], wrk[:, 8:16], axis=AX.X))
                vop(lambda e, g=g: e.tensor_add(gsc[:, g:g + 1], m1[:, 0:1], m1[:, 1:2]))
            vop(lambda e: e.memset(gmk[:, :], 0.0))
            vop(lambda e: e.tensor_copy(wrk[:, 0:8], gsc[:, :]))
            for _ in range(4):
                vop(lambda e: e.reduce_max(m1[:, 0:1], wrk[:, 0:8], axis=AX.X))
                vop(lambda e: e.tensor_scalar(
                    wrk[:, 8:16], wrk[:, 0:8], m1[:, 0:1], None, ALU.is_ge))
                vop(lambda e: e.tensor_add(gmk[:, :], gmk[:, :], wrk[:, 8:16]))
                vop(lambda e: e.scalar_tensor_tensor(
                    wrk[:, 0:8], wrk[:, 8:16], -1e30, wrk[:, 0:8], ALU.mult, ALU.add))
            vop(lambda e: e.tensor_copy(
                wrk[:, :].rearrange("p (g x) -> p g x", g=8),
                gmk[:, :].rearrange("p (g o) -> p g o", o=1).broadcast_to([128, 8, 8])))
            vop(lambda e: e.tensor_mul(wrk[:, :], wrk[:, :], sfc_tm[:, :]))
            vop(lambda e: e.memset(sel[:, :], 0.0))
            for _ in range(8):
                vop(lambda e: e.reduce_max(m1[:, 0:1], wrk[:, :], axis=AX.X))
                vop(lambda e: e.tensor_scalar(
                    rwt[:, :], wrk[:, :], m1[:, 0:1], None, ALU.is_ge))
                vop(lambda e: e.tensor_add(sel[:, :], sel[:, :], rwt[:, :]))
                vop(lambda e: e.scalar_tensor_tensor(
                    wrk[:, :], rwt[:, :], -1e30, wrk[:, :], ALU.mult, ALU.add))
            vop(lambda e: e.tensor_mul(rwt[:, :], sel[:, :], sco_tm[:, :]))
            vop(lambda e: e.reduce_sum(m1[:, 2:3], rwt[:, :], axis=AX.X))
            vop(lambda e: e.tensor_scalar_add(m1[:, 2:3], m1[:, 2:3], 1e-20))
            vop(lambda e: e.reciprocal(m1[:, 2:3], m1[:, 2:3]))
            vop(lambda e: e.tensor_scalar(
                rwt[:, :], rwt[:, :], m1[:, 2:3], 2.5, ALU.mult, ALU.mult))
            ph.wait("tensor", s_ev)
            ph.op("tensor",
                  lambda e: e.transpose(psT[0:64, 0:128], rwt[:, 0:64], ctx.ident_f32[:]),
                  inc=s_mm)
            ph.wait("scalar", s_mm)
            ph.op("scalar",
                  lambda e, ti=ti: e.copy(
                      sco_fm[0:64, ti * 128:(ti + 1) * 128], psT[0:64, 0:128]),
                  inc=s_ev)
        ph.wait("vector", s_ev)
        ph.op("vector", lambda e: e.tensor_copy(
            sfc_fm[0:64, :].bitcast(BF16)[0:64, 0:T], sco_fm[0:64, :]), inc=s_ev)
        ph.wait("sync", s_ev)
        ph.dma("sync", s_rw, lambda e: e.dma_start(
            out=rw_hbm, in_=sfc_fm[0:64, :].bitcast(BF16)[0:64, 0:T]))
        ph.wait("gpsimd", s_rw)
        ph.run()

    gwb = [sc.sbuf(f"f_gw{i}", [128, 8 * 704], BF16) for i in range(2)]
    uwb = [sc.sbuf(f"f_uw{i}", [128, 8 * 704], BF16) for i in range(2)]
    dwb = [sc.sbuf(f"f_dw{i}", [128, 6 * 1024], BF16) for i in range(2)]
    stg = sc.sbuf("f_stg", [128, 2048], F32)
    sgt = sc.sbuf("f_sg", [128, 2 * TT], BF16)
    ht = sc.sbuf("f_h", [128, 12 * TT], BF16)
    rwrow = sc.sbuf("f_rwrow", [65, T], BF16)
    rwb = sc.sbuf("f_rwb", [128, TT], BF16)
    NEXP = 66

    def wsrc(e):
        if e < 64:
            return (w["WgT"][e], w["WuT"][e], w["WdT"][e])
        half = e - 64
        return (w["SgT"][:, half * 704:(half + 1) * 704],
                w["SuT"][:, half * 704:(half + 1) * 704],
                w["SdT"][half * 704:(half + 1) * 704, :])

    def wload_jobs(e, par):
        gsrc, usrc, dsrc = wsrc(e)
        jobs = []
        for mat, dst, nc_ in ((gsrc, gwb[par], 704), (usrc, uwb[par], 704)):
            for k0 in (0, 2, 4, 6):
                jobs.append((mat, dst, k0, 2, nc_, 128))
        jobs.append((dsrc, dwb[par], 0, 2, 1024, 128))
        jobs.append((dsrc, dwb[par], 2, 2, 1024, 128))
        jobs.append((dsrc, dwb[par], 4, 1, 1024, 128))
        jobs.append((dsrc, dwb[par], 5, 1, 1024, 64))
        return jobs

    def emit_wjob(ph, job):
        mat, dst, k0, kn, nc_, prt = job
        ph.wait("sync", s_cv)
        if prt == 128:
            ph.dma("sync", s_ld,
                   lambda e_, mat=mat, k0=k0, kn=kn, nc_=nc_: e_.dma_start(
                       out=stg[:, 0:kn * nc_].rearrange("p (k n) -> p k n", k=kn),
                       in_=mat[k0 * 128:(k0 + kn) * 128, :].rearrange(
                           "(k p) n -> p k n", p=128)))
        else:
            ph.dma("sync", s_ld,
                   lambda e_, mat=mat, k0=k0, nc_=nc_: e_.dma_start(
                       out=stg[0:64, 0:nc_], in_=mat[k0 * 128: k0 * 128 + 64, :]))
        ph.wait("scalar", s_ld)
        if prt == 128:
            ph.op("scalar",
                  lambda e_, dst=dst, k0=k0, kn=kn, nc_=nc_: e_.copy(
                      dst[:, k0 * nc_:(k0 + kn) * nc_], stg[:, 0:kn * nc_]),
                  inc=s_cv)
        else:
            ph.op("scalar",
                  lambda e_, dst=dst, k0=k0, nc_=nc_: e_.copy(
                      dst[0:64, k0 * nc_:(k0 + 1) * nc_], stg[0:64, 0:nc_]),
                  inc=s_cv)

    def emit_wload(ph, e, par):
        for j in wload_jobs(e, par):
            emit_wjob(ph, j)

    conv_done = {}
    dn_prev = {}
    ph = Ph(ctx, "f_experts")
    for e in range(NEXP):
        par = e % 2
        if e == 0:
            emit_wload(ph, 0, 0)
            conv_done[0] = ctx.cnt[s_cv]
        is_shared = e >= 64
        if not is_shared:
            if e >= 2:
                ph.wait("sync", s_mm)
            ph.dma("sync", s_rw if par == 0 else s_rw2,
                   lambda e_, e=e, par=par: e_.dma_start(
                       out=rwrow[par * 64: par * 64 + 1, :], in_=rw_hbm[e:e + 1, :]))
        pjobs = []
        if e + 1 < NEXP:
            ph.wait("scalar", s_mm)
            pjobs = wload_jobs(e + 1, 1 - par)
            for j in pjobs[0:2]:
                emit_wjob(ph, j)
        if not is_shared:
            ph.wait("tensor", s_rw if par == 0 else s_rw2)
        ph.wait("tensor", s_cv, conv_done[e])
        nev = 2
        dn_info = {}

        def emit_gu(tt):
            hbase = (tt % 2) * 6 * TT
            for j in pjobs[2 + tt * 3: 5 + tt * 3]:
                emit_wjob(ph, j)
            if not is_shared:
                ph.wait("tensor", s_ev)
                ph.op("tensor",
                      lambda e_, tt=tt, par=par: e_.matmul(
                          psR[:], ctx.ones_row65[par * 64: par * 64 + 1, :],
                          rwrow[par * 64: par * 64 + 1, tt * TT:(tt + 1) * TT],
                          start=True, stop=True),
                      inc=s_mm)
                ph.wait("scalar", s_mm)
                ph.wait("scalar", s_hp)
                ph.op("scalar", lambda e_: e_.copy(rwb[:, :], psR[:]), inc=s_ev)
            ph.wait("vector", s_mm)
            if not is_shared:
                ph.wait("gpsimd", s_mm)
            for ci, (c0, cw) in enumerate(NCH):
                pg = psA if ci % 2 == 0 else psA2
                pu = psB if ci % 2 == 0 else psB2
                sg = sgt[0:cw, (ci % 2) * TT:(ci % 2) * TT + TT]
                ph.wait("tensor", s_ev, ctx.cnt[s_ev] - nev)
                for kc in range(KC):
                    ph.op("tensor",
                          lambda e_, kc=kc, tt=tt, c0=c0, cw=cw, par=par, pg=pg: e_.matmul(
                              pg[0:cw, 0:TT], gwb[par][:, kc * 704 + c0: kc * 704 + c0 + cw],
                              x_bf[:, kc * T + tt * TT: kc * T + (tt + 1) * TT],
                              start=(kc == 0), stop=(kc == KC - 1)),
                          inc=(s_mm if kc == KC - 1 else None))
                for kc in range(KC):
                    ph.op("tensor",
                          lambda e_, kc=kc, tt=tt, c0=c0, cw=cw, par=par, pu=pu: e_.matmul(
                              pu[0:cw, 0:TT], uwb[par][:, kc * 704 + c0: kc * 704 + c0 + cw],
                              x_bf[:, kc * T + tt * TT: kc * T + (tt + 1) * TT],
                              start=(kc == 0), stop=(kc == KC - 1)),
                          inc=(s_mm if kc == KC - 1 else None))
                ph.wait("scalar", s_mm, ctx.cnt[s_mm] - 1)
                ph.wait("scalar", s_ev, ctx.cnt[s_ev] - 2)
                if not is_shared:
                    ph.wait("scalar", s_hp, ctx.cnt[s_hp] - 1)
                ph.op("scalar",
                      lambda e_, cw=cw, pg=pg, sg=sg: e_.activation(
                          sg, pg[0:cw, 0:TT], AF.Silu, bias=0.0, scale=1.0),
                      inc=s_ev)
                ph.wait("vector", s_ev)
                ph.wait("vector", s_mm)
                if is_shared:
                    ph.op("vector",
                          lambda e_, ci=ci, cw=cw, pu=pu, sg=sg, hbase=hbase: e_.scalar_tensor_tensor(
                              ht[0:cw, hbase + ci * TT: hbase + (ci + 1) * TT], pu[0:cw, 0:TT],
                              kone[0:cw, :], sg, ALU.bypass, ALU.mult),
                          inc=s_ev)
                else:
                    ph.op("vector",
                          lambda e_, cw=cw, pu=pu, sg=sg: e_.scalar_tensor_tensor(
                              sg, pu[0:cw, 0:TT],
                              kone[0:cw, :], sg, ALU.bypass, ALU.mult),
                          inc=s_ev)
                    ph.wait("gpsimd", s_ev)
                    ph.op("gpsimd",
                          lambda e_, ci=ci, cw=cw, sg=sg, hbase=hbase: e_.tensor_mul(
                              ht[0:cw, hbase + ci * TT: hbase + (ci + 1) * TT], sg, rwb[0:cw, :]),
                          inc=s_hp)
            dn_info[tt] = (ctx.cnt[s_ev], ctx.cnt[s_hp])

        def emit_down(tt, dn_end):
            hbase = (tt % 2) * 6 * TT
            dn0, hp0 = dn_info[tt]
            for dch in range(KC):
                pd = psD if dch % 2 == 0 else psD2
                if dch < 2:
                    ph.wait("tensor", s_ev, dn0)
                    if dn_end is not None:
                        ph.wait("tensor", s_ev, dn_end)
                    if not is_shared:
                        ph.wait("tensor", s_hp, hp0)
                else:
                    ph.wait("tensor", s_ev, ctx.cnt[s_ev] - 2)
                for ci in range(6):
                    kw = 128 if ci < 5 else 64
                    ph.op("tensor",
                          lambda e_, ci=ci, kw=kw, dch=dch, par=par, pd=pd, hbase=hbase: e_.matmul(
                              pd[:], dwb[par][0:kw, ci * 1024 + dch * 128: ci * 1024 + (dch + 1) * 128],
                              ht[0:kw, hbase + ci * TT: hbase + (ci + 1) * TT],
                              start=(ci == 0), stop=(ci == 5)),
                          inc=(s_mm if ci == 5 else None))
                mmX = ctx.cnt[s_mm]
                ph.wait("vector", s_mm, mmX)
                c = dch * T + tt * TT
                ph.op("vector",
                      lambda e_, c=c, pd=pd: e_.scalar_tensor_tensor(
                          routed[:, c:c + TT], pd[:], kzero, routed[:, c:c + TT],
                          ALU.add, ALU.add),
                      inc=s_ev)
            return ctx.cnt[s_ev]

        dn_end = dn_prev.get("v")
        for tt in range(NTT):
            emit_gu(tt)
            if tt >= 1:
                dn_end = emit_down(tt - 1, dn_end)
        dn_end = emit_down(NTT - 1, dn_end)
        dn_prev["v"] = dn_end
        if e + 1 < NEXP:
            conv_done[e + 1] = ctx.cnt[s_cv]
    ph.run()

    ph = Ph(ctx, "f_fin")
    for kc in range(KC):
        for hf in range(2):
            ph.wait("sync", s_ev)
            ph.dma("sync", s_ld,
                   lambda e_, kc=kc, hf=hf: e_.dma_start(
                       out=stg[:, 0:1024],
                       in_=x_spill[:, kc * T + hf * 1024: kc * T + (hf + 1) * 1024]))
            ph.wait("vector", s_ld)
            for q in range(2):
                c = kc * T + hf * 1024 + q * TT
                ph.op("vector",
                      lambda e_, c=c, q=q: e_.tensor_add(
                          ar[:, c:c + TT], ar[:, c:c + TT], stg[:, q * TT:(q + 1) * TT]),
                      inc=s_ev)
    ph.run()

    ph = Ph(ctx, "f_ln")
    emit_postln(ph, ar=ar, sq=stg, ps_stat=[psA, psB], ps_b=[psA, psB],
                sem_mm=s_mm, sem_ev=s_ev,
                eps_ap=ctx.vecs[0:1, w["keps"]:w["keps"] + 1])
    ph.run()


def l0ffn_weights_host(inp, vb):
    w = {
        "gate_wT": _t(inp["L0_ffn_gate_w"]),
        "WgT": np.ascontiguousarray(np.transpose(np.asarray(inp["L0_ffn_Wg"], np.float32), (0, 2, 1))),
        "WuT": np.ascontiguousarray(np.transpose(np.asarray(inp["L0_ffn_Wu"], np.float32), (0, 2, 1))),
        "WdT": np.ascontiguousarray(np.transpose(np.asarray(inp["L0_ffn_Wd"], np.float32), (0, 2, 1))),
        "SgT": _t(inp["L0_ffn_Sg"]),
        "SuT": _t(inp["L0_ffn_Su"]),
        "SdT": _t(inp["L0_ffn_Sd"]),
    }
    gb = np.zeros((128, 1), np.float32)
    gb[0:64, 0] = np.asarray(inp["L0_ffn_gate_bias"], np.float32)
    idx = {
        "gbias": vb.add("f_gbias", gb),
        "kone": vb.add("k_one_f", np.full((128, 1), 1.0, np.float32)),
        "kzero": vb.add("k_zero_f", np.zeros((128, 1), np.float32)),
        "keps": vb.add("k_eps_f", np.full((128, 1), 1e-5, np.float32)),
    }
    return w, idx


def build_l0ffn_program(nblob, vec_idx):
    nc = bass.Bass("TRN2", target_bir_lowering=False, detect_race_conditions=False)
    ctx = Ctx(nc)
    x_in = nc.dram_tensor("x_in", [T, D], F32, kind="ExternalInput").ap()
    y_out = nc.dram_tensor("y_out", [T, D], F32, kind="ExternalOutput").ap()
    consts = nc.dram_tensor("consts", [130, 128], F32, kind="ExternalInput").ap()
    vecblob = nc.dram_tensor("vecblob", [128, nblob], F32, kind="ExternalInput").ap()
    w = {
        "gate_wT": nc.dram_tensor("gate_wT", [1024, 64], F32, kind="ExternalInput").ap(),
        "WgT": nc.dram_tensor("WgT", [64, 1024, 704], F32, kind="ExternalInput").ap(),
        "WuT": nc.dram_tensor("WuT", [64, 1024, 704], F32, kind="ExternalInput").ap(),
        "WdT": nc.dram_tensor("WdT", [64, 704, 1024], F32, kind="ExternalInput").ap(),
        "SgT": nc.dram_tensor("SgT", [1024, 1408], F32, kind="ExternalInput").ap(),
        "SuT": nc.dram_tensor("SuT", [1024, 1408], F32, kind="ExternalInput").ap(),
        "SdT": nc.dram_tensor("SdT", [1408, 1024], F32, kind="ExternalInput").ap(),
    }
    w.update(vec_idx)
    x_spill = nc.dram_tensor("x_spill", [128, KC * T], F32).ap()
    rw_hbm = nc.dram_tensor("rw_scr", [64, T], BF16).ap()

    common_setup(ctx, nblob)
    ar = ctx.sbuf("ar", [128, KC * T], F32)

    s_ld = [ctx.sem("g_ld0"), ctx.sem("g_ld1")]
    s_cst = ctx.sem("g_cst")
    s_tp = ctx.sem("g_tp")
    s_ev = ctx.sem("g_ev")
    s_st = [ctx.sem("g_st0"), ctx.sem("g_st1")]

    with ctx.scope() as sc:
        tmp_tm = [sc.sbuf(f"tin{i}", [128, D], F32) for i in range(2)]
        ps_t = [sc.psum(f"g_ps{i}", [128, TT]) for i in range(2)]
        ph = Ph(ctx, "in")
        emit_load_consts(ph, ctx, consts, None, vecblob, s_cst)
        emit_transpose_in(ph, hbm_tm=x_in, ar=ar, sem_ld=s_ld, sem_tp=s_tp,
                          sem_ev=s_ev, ps_t=ps_t, tmp_tm=tmp_tm)
        ph.run()

    with ctx.scope() as sc:
        build_l0ffn(ctx, ar, w, sc, x_spill, rw_hbm)

    with ctx.scope() as sc:
        tmp_tm = [sc.sbuf(f"tout{i}", [128, D], F32) for i in range(2)]
        ps_t = [sc.psum(f"g_pso{i}", [128, TT]) for i in range(2)]
        ph = Ph(ctx, "out")
        emit_transpose_out(ph, ar=ar, hbm_tm=y_out, sem_tp=s_tp, sem_ev=s_ev,
                           sem_st=s_st, ps_t=ps_t, tmp_tm=tmp_tm)
        ph.wait("gpsimd", s_st[0])
        ph.wait("gpsimd", s_st[1])
        ph.run()

    ctx.close()
    return nc


def common_setup(ctx, nblob):
    ctx.ones_col_f32 = ctx.sbuf("c_ones_col", [128, 1], F32)
    ctx.ones_col_bf = ctx.sbuf("c_ones_col_bf", [128, 1], BF16)
    ctx.ones_row_f32 = ctx.sbuf("c_ones_row", [1, 128], F32)
    ctx.ones_row65 = ctx.sbuf("c_ones_row65", [65, 128], BF16)
    ctx.ones_row_bf = ctx.sbuf("c_ones_row_bf", [1, 128], BF16)
    ctx.ident_f32 = ctx.sbuf("c_idf", [128, 128], F32)
    ctx.ident_bf = ctx.sbuf("c_idb", [128, 128], BF16)
    ctx.ind_bf = ctx.sbuf("c_indb", [16, 8 * 128], BF16)
    ctx.vecs = ctx.sbuf("c_vecs", [128, max(nblob, 1)], F32)
    ctx.rows = ctx.sbuf("c_rows", [1, 4 * TT], F32)


def emit_load_consts(ph, ctx, consts_ap, ind_ap, vec_ap, sem_ld, ind_stage=None):
    ph.dma("sync", sem_ld, lambda e: e.dma_start(
        out=ctx.ones_col_f32[:], in_=consts_ap[0:1, :].rearrange("r p -> p r")))
    ph.dma("sync", sem_ld, lambda e: e.dma_start(out=ctx.ones_row_f32[0:1, :], in_=consts_ap[0:1, :]))

    ph.dma("sync", sem_ld, lambda e: e.dma_start(out=ctx.ident_f32[:], in_=consts_ap[2:130, :]))
    if ind_ap is not None:
        ph.dma("sync", sem_ld, lambda e: e.dma_start(out=ind_stage[0:16, :], in_=ind_ap[:, :]))
    if vec_ap is not None:
        ph.dma("sync", sem_ld, lambda e: e.dma_start(out=ctx.vecs[:], in_=vec_ap[:, :]))
    ph.wait("vector", sem_ld)
    ph.wait("tensor", sem_ld)
    ph.wait("scalar", sem_ld)
    ph.wait("gpsimd", sem_ld)
    ph.op("vector", lambda e: e.tensor_copy(ctx.ident_bf[:], ctx.ident_f32[:]))
    ph.op("vector", lambda e: e.tensor_copy(ctx.ones_row_bf[0:1, :], ctx.ones_row_f32[0:1, :]))
    ph.op("vector", lambda e: e.tensor_copy(ctx.ones_row65[0:1, :], ctx.ones_row_f32[0:1, :]))
    ph.op("vector", lambda e: e.tensor_copy(ctx.ones_row65[64:65, :], ctx.ones_row_f32[0:1, :]))
    ph.op("vector", lambda e: e.tensor_copy(ctx.ones_col_bf[:], ctx.ones_col_f32[:]))
    if ind_ap is not None:
        ph.op("vector", lambda e: e.tensor_copy(ctx.ind_bf[0:16, :], ind_stage[0:16, :]))


def make_ind():
    ind = np.zeros((16, 8 * 128), dtype=np.float32)
    for kc in range(8):
        ind[2 * kc, kc * 128: kc * 128 + 64] = 1.0
        ind[2 * kc + 1, kc * 128 + 64: kc * 128 + 128] = 1.0
    return ind


def _t(a):
    return np.ascontiguousarray(np.asarray(a, dtype=np.float32).T)


def _chunkvec(v):
    v = np.asarray(v, dtype=np.float32)
    n = v.shape[0] // 128
    return np.ascontiguousarray(v.reshape(n, 128).T)


def make_consts():
    c = np.zeros((130, 128), dtype=np.float32)
    c[0, :] = 1.0
    c[2:130, :] = np.eye(128, dtype=np.float32)
    return c


class VecBlob:
    def __init__(self):
        self.cols = []
        self.idx = {}

    def add(self, name, arr):
        self.idx[name] = sum(a.shape[1] for a in self.cols)
        self.cols.append(np.asarray(arr, dtype=np.float32))
        return self.idx[name]

    def blob(self):
        b = np.concatenate(self.cols, axis=1) if self.cols else np.zeros((128, 1), np.float32)
        return np.ascontiguousarray(b)


def l1ffn_weights_host(inp, vb):
    w = {
        "w1T": _t(inp["L1_ffn_w1_w"]),
        "w2T": _t(inp["L1_ffn_w2_w"]),
        "h1T": _t(inp["L1_ffn_h1_w"]),
        "h2T": _t(inp["L1_ffn_h2_w"]),
        "w3T": _t(inp["L1_ffn_w3_w"]),
    }
    idx = {
        "keps": vb.add("k_eps_ln", np.full((128, 1), 1e-5, np.float32)),
        "s1c": vb.add("ff_s1", _chunkvec(inp["L1_ffn_w1_s"])),
        "s2c": vb.add("ff_s2", _chunkvec(inp["L1_ffn_w2_s"])),
        "sh1c": vb.add("ff_sh1", _chunkvec(inp["L1_ffn_h1_s"])),
        "sh2c": vb.add("ff_sh2", _chunkvec(inp["L1_ffn_h2_s"])),
        "s3c": vb.add("ff_s3", _chunkvec(inp["L1_ffn_w3_s"])),
    }
    return w, idx


def build_l1ffn_program(nblob, vec_idx):
    nc = bass.Bass("TRN2", target_bir_lowering=False, detect_race_conditions=False)
    ctx = Ctx(nc)
    x_in = nc.dram_tensor("x_in", [T, D], F32, kind="ExternalInput").ap()
    y_out = nc.dram_tensor("y_out", [T, D], F32, kind="ExternalOutput").ap()
    consts = nc.dram_tensor("consts", [130, 128], F32, kind="ExternalInput").ap()
    vecblob = nc.dram_tensor("vecblob", [128, nblob], F32, kind="ExternalInput").ap()
    w = {k: nc.dram_tensor(k, list(shp), F32, kind="ExternalInput").ap()
         for k, shp in (("w1T", (1024, 4096)), ("w2T", (1024, 4096)),
                        ("h1T", (4096, 4096)), ("h2T", (4096, 4096)),
                        ("w3T", (4096, 1024)))}
    w.update(vec_idx)

    common_setup(ctx, nblob)
    ar = ctx.sbuf("ar", [128, KC * T], F32)

    s_ld = [ctx.sem("g_ld0"), ctx.sem("g_ld1")]
    s_cst = ctx.sem("g_cst")
    s_tp = ctx.sem("g_tp")
    s_ev = ctx.sem("g_ev")
    s_st = [ctx.sem("g_st0"), ctx.sem("g_st1")]
    ps_t = [ctx.psum(f"g_ps{i}", [128, TT]) for i in range(2)]

    with ctx.scope() as sc:
        tmp_tm = [sc.sbuf(f"tin{i}", [128, D], F32) for i in range(2)]
        ph = Ph(ctx, "in")
        emit_load_consts(ph, ctx, consts, None, vecblob, s_cst)
        emit_transpose_in(ph, hbm_tm=x_in, ar=ar, sem_ld=s_ld, sem_tp=s_tp,
                          sem_ev=s_ev, ps_t=ps_t, tmp_tm=tmp_tm)
        ph.run()

    with ctx.scope() as sc:
        build_l1ffn(ctx, ar, w, sc)

    with ctx.scope() as sc:
        tmp_tm = [sc.sbuf(f"tout{i}", [128, D], F32) for i in range(2)]
        ph = Ph(ctx, "out")
        emit_transpose_out(ph, ar=ar, hbm_tm=y_out, sem_tp=s_tp, sem_ev=s_ev,
                           sem_st=s_st, ps_t=ps_t, tmp_tm=tmp_tm)
        ph.wait("gpsimd", s_st[0])
        ph.wait("gpsimd", s_st[1])
        ph.run()

    ctx.close()
    return nc


def l0attn_weights_host(inp, vb):
    w = {
        "WqT": _t(inp["L0_attn_Wq"]),
        "WkT": _t(inp["L0_attn_Wk"]),
        "WvT": _t(inp["L0_attn_Wv"]),
        "WoT": _t(inp["L0_attn_Wo"]),
    }
    idx = {
        "bq": vb.add("a_bq", _chunkvec(inp["L0_attn_bq"])),
        "bk": vb.add("a_bk", _chunkvec(inp["L0_attn_bk"])),
        "bv": vb.add("a_bv", _chunkvec(inp["L0_attn_bv"])),
        "bo": vb.add("a_bo", _chunkvec(inp["L0_attn_bo"])),
        "kone": vb.add("k_one", np.full((128, 1), 1.0, np.float32)),
        "keps6": vb.add("k_eps6", np.full((128, 1), 1e-6, np.float32)),
        "keps": vb.add("k_eps_ln0", np.full((128, 1), 1e-5, np.float32)),
    }
    return w, idx


def make_l0_cc(nc):
    return {
        "r16_in": nc.dram_tensor("cc_r16i", [128, 16], F32).ap(),
        "r16_out": nc.dram_tensor("cc_r16o", [128, 16], F32).ap(),
        "d_in": nc.dram_tensor("cc_di", [16, 1], F32).ap(),
        "d_out": nc.dram_tensor("cc_do", [16, 1], F32).ap(),
        "kv_in": nc.dram_tensor("cc_kvi", [128, 1024], F32).ap(),
        "kv_out": nc.dram_tensor("cc_kvo", [128, 1024], F32).ap(),
    }


def build_l0attn_program(nblob, vec_idx, upto="H", debug=False):
    nc = bass.Bass("TRN2", target_bir_lowering=False, detect_race_conditions=False)
    ctx = Ctx(nc)
    x_in = nc.dram_tensor("x_in", [T, D], F32, kind="ExternalInput").ap()
    y_out = nc.dram_tensor("y_out", [T, D], F32, kind="ExternalOutput").ap()
    consts = nc.dram_tensor("consts", [130, 128], F32, kind="ExternalInput").ap()
    indc = nc.dram_tensor("indc", [16, 1024], F32, kind="ExternalInput").ap()
    vecblob = nc.dram_tensor("vecblob", [128, nblob], F32, kind="ExternalInput").ap()
    w = {k: nc.dram_tensor(k, [1024, 1024], F32, kind="ExternalInput").ap()
         for k in ("WqT", "WkT", "WvT", "WoT")}
    w.update(vec_idx)
    cc = make_l0_cc(nc)

    common_setup(ctx, nblob)
    ar = ctx.sbuf("ar", [128, KC * T], F32)

    s_ld = [ctx.sem("g_ld0"), ctx.sem("g_ld1")]
    s_cst = ctx.sem("g_cst")
    s_tp = ctx.sem("g_tp")
    s_ev = ctx.sem("g_ev")
    s_st = [ctx.sem("g_st0"), ctx.sem("g_st1")]

    with ctx.scope() as sc:
        tmp_tm = [sc.sbuf(f"tin{i}", [128, D], F32) for i in range(2)]
        ps_t = [sc.psum(f"g_ps{i}", [128, TT]) for i in range(2)]
        ph = Ph(ctx, "in")
        emit_load_consts(ph, ctx, consts, indc, vecblob, s_cst)
        emit_transpose_in(ph, hbm_tm=x_in, ar=ar, sem_ld=s_ld, sem_tp=s_tp,
                          sem_ev=s_ev, ps_t=ps_t, tmp_tm=tmp_tm)
        ph.run()

    dbg = {}
    if debug:
        for nm, shp, dt in (("d_q", [128, KC * T], BF16), ("d_k", [128, KC * T], BF16),
                            ("d_v", [128, KC * T], BF16), ("d_stats", [16, 3 * T], BF16),
                            ("d_r1", [128, 16], F32), ("d_r2", [128, 16], F32),
                            ("d_kvb", [128, 1024], BF16), ("d_dnm", [16, 2], F32),
                            ("d_ecol", [16, 4], F32)):
            dbg[nm] = nc.dram_tensor(nm, shp, dt, kind="ExternalOutput").ap()

    with ctx.scope() as sc:
        arenas = build_l0attn(ctx, ar, w, cc, sc, upto=upto)
        if debug:
            Qb, Kb, Vb, stats, r1, r2, kvb, dnm, ecol = arenas
            s_dbg = ctx.sem("dbg_st")
            ph = Ph(ctx, "dbg")
            for nm, t in (("d_q", Qb), ("d_k", Kb), ("d_v", Vb), ("d_stats", stats[0:16, :]),
                          ("d_r1", r1), ("d_r2", r2), ("d_kvb", kvb), ("d_dnm", dnm[0:16, :]),
                          ("d_ecol", ecol[0:16, 0:4])):
                ap = t[:, :] if not isinstance(t, bass.AP) else t
                ph.dma("sync", s_dbg, lambda e, ap=ap, nm=nm: e.dma_start(out=dbg[nm], in_=ap))
            ph.wait("gpsimd", s_dbg)
            ph.run()

    with ctx.scope() as sc:
        tmp_tm = [sc.sbuf(f"tout{i}", [128, D], F32) for i in range(2)]
        ps_t = [sc.psum(f"g_pso{i}", [128, TT]) for i in range(2)]
        ph = Ph(ctx, "out")
        emit_transpose_out(ph, ar=ar, hbm_tm=y_out, sem_tp=s_tp, sem_ev=s_ev,
                           sem_st=s_st, ps_t=ps_t, tmp_tm=tmp_tm)
        ph.wait("gpsimd", s_st[0])
        ph.wait("gpsimd", s_st[1])
        ph.run()

    ctx.close()
    return nc


def build_fused_program(nblob, aidx, fidx, midx, ffidx):
    nc = bass.Bass("TRN2", target_bir_lowering=False, detect_race_conditions=False)
    ctx = Ctx(nc)
    x_in = nc.dram_tensor("x_in", [T, D], F32, kind="ExternalInput").ap()
    y_out = nc.dram_tensor("y_out", [T, D], F32, kind="ExternalOutput").ap()
    consts = nc.dram_tensor("consts", [130, 128], F32, kind="ExternalInput").ap()
    indc = nc.dram_tensor("indc", [16, 1024], F32, kind="ExternalInput").ap()
    coreoh = nc.dram_tensor("coreoh", [128, 8], F32, kind="ExternalInput").ap()
    vecblob = nc.dram_tensor("vecblob", [128, nblob], F32, kind="ExternalInput").ap()

    wa = {k: nc.dram_tensor(k, [1024, 1024], F32, kind="ExternalInput").ap()
          for k in ("WqT", "WkT", "WvT", "WoT")}
    wa.update(aidx)
    wf = {
        "gate_wT": nc.dram_tensor("gate_wT", [1024, 64], F32, kind="ExternalInput").ap(),
        "WgT": nc.dram_tensor("WgT", [64, 1024, 704], F32, kind="ExternalInput").ap(),
        "WuT": nc.dram_tensor("WuT", [64, 1024, 704], F32, kind="ExternalInput").ap(),
        "WdT": nc.dram_tensor("WdT", [64, 704, 1024], F32, kind="ExternalInput").ap(),
        "SgT": nc.dram_tensor("SgT", [1024, 1408], F32, kind="ExternalInput").ap(),
        "SuT": nc.dram_tensor("SuT", [1024, 1408], F32, kind="ExternalInput").ap(),
        "SdT": nc.dram_tensor("SdT", [1408, 1024], F32, kind="ExternalInput").ap(),
    }
    wf.update(fidx)
    wm = {"out_wT": nc.dram_tensor("out_wT", [1024, 1024], F32, kind="ExternalInput").ap(),
          "w1T": nc.dram_tensor("m_w1T", [1024, 4096], F32, kind="ExternalInput").ap(),
          "w2T": nc.dram_tensor("m_w2T", [1024, 4096], F32, kind="ExternalInput").ap(),
          "w3T": nc.dram_tensor("m_w3T", [4096, 1024], F32, kind="ExternalInput").ap()}
    wm.update(midx)
    wff = {k: nc.dram_tensor("ff_" + k, list(shp), F32, kind="ExternalInput").ap()
           for k, shp in (("w1T", (1024, 4096)), ("w2T", (1024, 4096)),
                          ("h1T", (4096, 4096)), ("h2T", (4096, 4096)),
                          ("w3T", (4096, 1024)))}
    wff.update(ffidx)

    cc0 = make_l0_cc(nc)
    cc1 = make_l1_cc(nc)
    x_spill = nc.dram_tensor("x_spill", [128, KC * T], F32).ap()
    rw_hbm = nc.dram_tensor("rw_scr", [64, T], BF16).ap()

    common_setup(ctx, nblob)
    ctx.coreoh = ctx.sbuf("c_coreoh", [128, 8], F32)
    ar = ctx.sbuf("ar", [128, KC * T], F32)

    s_ld = [ctx.sem("g_ld0"), ctx.sem("g_ld1")]
    s_cst = ctx.sem("g_cst")
    s_tp = ctx.sem("g_tp")
    s_ev = ctx.sem("g_ev")
    s_st = [ctx.sem("g_st0"), ctx.sem("g_st1")]

    with ctx.scope() as sc:
        tmp_tm = [sc.sbuf(f"tin{i}", [128, D], F32) for i in range(2)]
        ps_t = [sc.psum(f"g_ps{i}", [128, TT]) for i in range(2)]
        ph = Ph(ctx, "in")
        emit_load_consts(ph, ctx, consts, indc, vecblob, s_cst, ind_stage=tmp_tm[0])
        ph.dma("sync", s_cst, lambda e: e.dma_start(out=ctx.coreoh[:], in_=coreoh))
        ph.wait("vector", s_cst)
        ph.op("vector", lambda e: e.engine_nop(), inc=s_ev)
        ph.wait("sync", s_ev)
        emit_transpose_in(ph, hbm_tm=x_in, ar=ar, sem_ld=s_ld, sem_tp=s_tp,
                          sem_ev=s_ev, ps_t=ps_t, tmp_tm=tmp_tm)
        ph.run()

    with ctx.scope() as sc:
        build_l0attn(ctx, ar, wa, cc0, sc)
    with ctx.scope() as sc:
        build_l0ffn(ctx, ar, wf, sc, x_spill, rw_hbm)
    with ctx.scope() as sc:
        build_l1attn(ctx, ar, wm, cc1, sc, x_spill)
    with ctx.scope() as sc:
        build_l1ffn(ctx, ar, wff, sc)

    with ctx.scope() as sc:
        tmp_tm = [sc.sbuf(f"tout{i}", [128, D], F32) for i in range(2)]
        ps_t = [sc.psum(f"g_pso{i}", [128, TT]) for i in range(2)]
        ph = Ph(ctx, "out")
        emit_transpose_out(ph, ar=ar, hbm_tm=y_out, sem_tp=s_tp, sem_ev=s_ev,
                           sem_st=s_st, ps_t=ps_t, tmp_tm=tmp_tm)
        ph.wait("gpsimd", s_st[0])
        ph.wait("gpsimd", s_st[1])
        ph.run()

    ctx.close()
    return nc


def kernel(**inp):
    consts = make_consts()
    x = np.ascontiguousarray(np.asarray(inp["L0_attn_x"], np.float32).reshape(TSEQ, D))

    vb = VecBlob()
    wa, aidx = l0attn_weights_host(inp, vb)
    wf, fidx = l0ffn_weights_host(inp, vb)
    wm, midx = l1attn_weights_host(inp, vb)
    wff, ffidx = l1ffn_weights_host(inp, vb)
    blob = vb.blob()

    nc = build_fused_program(blob.shape[1], aidx, fidx, midx, ffidx)

    feeds = []
    for c in range(NCORES):
        oh = np.zeros((128, 8), np.float32)
        oh[:, c] = 1.0
        m = {"x_in": np.ascontiguousarray(x[c * T:(c + 1) * T, :]),
             "consts": consts, "indc": make_ind(), "coreoh": oh, "vecblob": blob}
        m.update(wa)
        m.update(wf)
        m.update({("out_wT" if k == "out_wT" else "m_" + k): v for k, v in wm.items()})
        m.update({"ff_" + k: v for k, v in wff.items()})
        feeds.append(m)
    res = run_bass_kernel_spmd(nc, feeds, core_ids=list(range(NCORES)))
    x = np.concatenate([np.asarray(res.results[c]["y_out"]) for c in range(NCORES)], axis=0)
    return x.reshape(1, TSEQ, D)


def _run_chain_launch(nc, per_core_feeds):
    res = run_bass_kernel_spmd(nc, per_core_feeds, core_ids=list(range(NCORES)))
    return np.concatenate([np.asarray(res.results[c]["y_out"]) for c in range(NCORES)], axis=0)


def kernel_multilaunch(**inp):
    consts = make_consts()

    x = np.ascontiguousarray(np.asarray(inp["L0_attn_x"], np.float32).reshape(TSEQ, D))

    vb = VecBlob()
    w, idx = l0attn_weights_host(inp, vb)
    blob = vb.blob()
    nc = build_l0attn_program(blob.shape[1], idx)
    feeds = []
    for c in range(NCORES):
        m = {"x_in": np.ascontiguousarray(x[c * T:(c + 1) * T, :]),
             "consts": consts, "indc": make_ind(), "vecblob": blob}
        m.update(w)
        feeds.append(m)
    x = _run_chain_launch(nc, feeds)

    vb = VecBlob()
    w, idx = l0ffn_weights_host(inp, vb)
    blob = vb.blob()
    nc = build_l0ffn_program(blob.shape[1], idx)
    feeds = []
    for c in range(NCORES):
        m = {"x_in": np.ascontiguousarray(x[c * T:(c + 1) * T, :]),
             "consts": consts, "vecblob": blob}
        m.update(w)
        feeds.append(m)
    x = _run_chain_launch(nc, feeds)

    vb = VecBlob()
    w, idx = l1attn_weights_host(inp, vb)
    blob = vb.blob()
    nc = build_l1attn_program(blob.shape[1], idx)
    feeds = []
    for c in range(NCORES):
        oh = np.zeros((128, 8), np.float32)
        oh[:, c] = 1.0
        m = {"x_in": np.ascontiguousarray(x[c * T:(c + 1) * T, :]),
             "consts": consts, "coreoh": oh, "vecblob": blob}
        m.update(w)
        feeds.append(m)
    x = _run_chain_launch(nc, feeds)

    vb = VecBlob()
    w, idx = l1ffn_weights_host(inp, vb)
    blob = vb.blob()
    nc = build_l1ffn_program(blob.shape[1], idx)
    feeds = []
    for c in range(NCORES):
        m = {"x_in": np.ascontiguousarray(x[c * T:(c + 1) * T, :]),
             "consts": consts, "vecblob": blob}
        m.update(w)
        feeds.append(m)
    x = _run_chain_launch(nc, feeds)

    return x.reshape(1, TSEQ, D)


# version 15
# speedup vs baseline: 1.7386x; 1.0126x over previous
import contextlib

import numpy as np
import ml_dtypes

import concourse.bass as bass
import concourse.mybir as mybir
from concourse.bass_utils import run_bass_kernel_spmd

F32 = mybir.dt.float32
BF16 = mybir.dt.bfloat16
AF = mybir.ActivationFunctionType
ALU = mybir.AluOpType
AX = mybir.AxisListType

NCORES = 8
TSEQ = 16384
T = TSEQ // NCORES
D = 1024
KC = D // 128
TT = 512
NTT = T // TT

ENGINES = ("sync", "vector", "scalar", "tensor", "gpsimd")


class Ctx:
    def __init__(self, nc):
        self.nc = nc
        self.es = contextlib.ExitStack()
        self.sems = {}
        self.cnt = {}

    def sem(self, name):
        if name in self.sems:
            return name
        h = self.es.enter_context(self.nc.semaphore(name))
        self.sems[name] = h
        self.cnt[name] = 0
        return name

    def sbuf(self, name, shape, dt):
        return self.es.enter_context(self.nc.sbuf_tensor(name, shape, dt))

    def psum(self, name, shape, dt=F32):
        return self.es.enter_context(self.nc.psum_tensor(name, shape, dt))

    @contextlib.contextmanager
    def scope(self):
        with contextlib.ExitStack() as es:
            class _S:
                def sbuf(_, name, shape, dt):
                    return es.enter_context(self.nc.sbuf_tensor(name, shape, dt))

                def psum(_, name, shape, dt=F32):
                    return es.enter_context(self.nc.psum_tensor(name, shape, dt))
            yield _S()

    def close(self):
        self.es.close()


class Ph:

    def __init__(self, ctx, name=""):
        self.ctx = ctx
        self.name = name
        self.q = {e: [] for e in ENGINES}

    def wait(self, eng, sem, val=None):
        v = self.ctx.cnt[sem] if val is None else val
        if v <= 0:
            return
        h = self.ctx.sems[sem]
        self.q[eng].append(lambda e: e.wait_ge(h, v))

    def op(self, eng, build, inc=None, n=1):
        if inc is not None:
            self.ctx.cnt[inc] += n
            h = self.ctx.sems[inc]
            self.q[eng].append(lambda e: build(e).then_inc(h, n))
            return self.ctx.cnt[inc]
        self.q[eng].append(lambda e: build(e))
        return None

    def dma(self, eng, sem, build):
        return self.op(eng, build, inc=sem, n=16)

    def run(self):
        with self.ctx.nc.Block(self.name or None) as blk:
            for e in ENGINES:
                if self.q[e]:
                    def body(eng, q=tuple(self.q[e])):
                        for f in q:
                            f(eng)
                    getattr(blk, e)(body)


def emit_transpose_in(ph, *, hbm_tm, ar, sem_ld, sem_tp, sem_ev, ps_t, tmp_tm):
    ctx = ph.ctx
    nb = len(ps_t)
    nt = 0
    for ti in range(T // 128):
        tm = tmp_tm[ti % 2]
        if ti >= 2:
            ph.wait("sync", sem_tp, ctx.cnt[sem_tp] - KC)
        ph.dma("sync", sem_ld[ti % 2],
               lambda e, tm=tm, ti=ti: e.dma_start(
                   out=tm[:], in_=hbm_tm[ti * 128:(ti + 1) * 128, :]))
        ph.wait("tensor", sem_ld[ti % 2])
        for kc in range(KC):
            ps = ps_t[nt % nb]
            if nt >= nb:
                ph.wait("tensor", sem_ev, ctx.cnt[sem_ev] - nb + 1)
            ph.op("tensor",
                  lambda e, tm=tm, kc=kc, ps=ps: e.transpose(
                      ps[:, 0:128], tm[:, kc * 128:(kc + 1) * 128], ctx.ident_f32[:]),
                  inc=sem_tp)
            ph.wait("vector", sem_tp)
            col = kc * T + ti * 128
            ph.op("vector",
                  lambda e, ps=ps, col=col: e.tensor_copy(
                      ar[:, col:col + 128], ps[:, 0:128]),
                  inc=sem_ev)
            nt += 1


def emit_transpose_out(ph, *, ar, hbm_tm, sem_tp, sem_ev, sem_st, ps_t, tmp_tm):
    ctx = ph.ctx
    nb = len(ps_t)
    nt = 0
    for ti in range(T // 128):
        tm = tmp_tm[ti % 2]
        if ti >= 2:
            ph.wait("vector", sem_st[ti % 2])
        for kc in range(KC):
            ps = ps_t[nt % nb]
            if nt >= nb:
                ph.wait("tensor", sem_ev, ctx.cnt[sem_ev] - nb + 1)
            col = kc * T + ti * 128
            ph.op("tensor",
                  lambda e, col=col, ps=ps: e.transpose(
                      ps[:, 0:128], ar[:, col:col + 128], ctx.ident_f32[:]),
                  inc=sem_tp)
            ph.wait("vector", sem_tp)
            ph.op("vector",
                  lambda e, ps=ps, tm=tm, kc=kc: e.tensor_copy(
                      tm[:, kc * 128:(kc + 1) * 128], ps[:, 0:128]),
                  inc=sem_ev)
            nt += 1
        ph.wait("sync", sem_ev)
        ph.dma("sync", sem_st[ti % 2],
               lambda e, tm=tm, ti=ti: e.dma_start(
                   out=hbm_tm[ti * 128:(ti + 1) * 128, :], in_=tm[:]))


def emit_postln(ph, *, ar, sq, ps_stat, ps_b, sem_mm, sem_ev, out_bf=None, eps_ap=None, stop_after=None):
    ctx = ph.ctx
    rows = ctx.rows
    for tt in range(NTT):
        c0 = tt * TT
        r_m = rows[0:1, 0 * TT:1 * TT]
        r_v = rows[0:1, 1 * TT:2 * TT]
        r_rs = rows[0:1, 2 * TT:3 * TT]
        r_mr = rows[0:1, 3 * TT:4 * TT]
        r_t = rows[0:1, 2 * TT:3 * TT]
        mm_sq0 = ctx.cnt[sem_mm]
        for kc in range(KC):
            sqt = sq[:, (kc % 2) * TT:(kc % 2 + 1) * TT]
            if kc >= 2:
                ph.wait("scalar", sem_mm, mm_sq0 + kc - 1)
            ph.op("scalar",
                  lambda e, kc=kc, c0=c0, sqt=sqt: e.activation(
                      sqt, ar[:, kc * T + c0: kc * T + c0 + TT],
                      AF.Square, bias=0.0, scale=1.0),
                  inc=sem_ev)
            ph.op("tensor",
                  lambda e, kc=kc, c0=c0: e.matmul(
                      ps_stat[0][0:1, :], ctx.ones_col_f32[:, 0:1],
                      ar[:, kc * T + c0: kc * T + c0 + TT],
                      start=(kc == 0), stop=(kc == KC - 1)))
            ph.wait("tensor", sem_ev)
            ph.op("tensor",
                  lambda e, kc=kc, sqt=sqt: e.matmul(
                      ps_stat[1][0:1, :], ctx.ones_col_f32[:, 0:1],
                      sqt, start=(kc == 0), stop=(kc == KC - 1)),
                  inc=sem_mm)
        ph.wait("scalar", sem_mm)
        if stop_after == "stats":
            continue
        ph.op("scalar", lambda e, r_m=r_m: e.mul(r_m, ps_stat[0][0:1, :], 1.0 / D))
        ph.op("scalar", lambda e, r_v=r_v: e.mul(r_v, ps_stat[1][0:1, :], 1.0 / D),
              inc=sem_ev)
        ph.wait("vector", sem_ev)
        ph.op("vector", lambda e, r_t=r_t, r_m=r_m: e.tensor_tensor(
            r_t, r_m, r_m, ALU.mult))
        ph.op("vector", lambda e, r_v=r_v, r_t=r_t: e.tensor_sub(r_v, r_v, r_t),
              inc=sem_ev)
        ph.wait("scalar", sem_ev)
        ph.op("scalar", lambda e, r_rs=r_rs, r_v=r_v: e.activation(
            r_rs, r_v, AF.Sqrt, bias=eps_ap, scale=1.0), inc=sem_ev)
        ph.wait("vector", sem_ev)
        ph.op("vector", lambda e, r_rs=r_rs: e.reciprocal(r_rs, r_rs))
        ph.op("vector", lambda e, r_mr=r_mr, r_m=r_m, r_rs=r_rs: e.tensor_tensor(
            r_mr, r_m, r_rs, ALU.mult), inc=sem_ev)
        ph.wait("tensor", sem_ev)
        if stop_after == "rows":
            continue
        ph.op("tensor", lambda e, r_rs=r_rs: e.matmul(
            ps_b[0][:, :], ctx.ones_row_f32[0:1, :], r_rs, start=True, stop=True))
        ph.op("tensor", lambda e, r_mr=r_mr: e.matmul(
            ps_b[1][:, :], ctx.ones_row_f32[0:1, :], r_mr, start=True, stop=True),
            inc=sem_mm)
        ph.wait("scalar", sem_mm)
        if stop_after == "bcast":
            continue
        ph.op("scalar", lambda e: e.copy(sq[:, 0:TT], ps_b[0][:, :]))
        ph.op("scalar", lambda e: e.copy(sq[:, TT:2 * TT], ps_b[1][:, :]), inc=sem_ev)
        ph.wait("vector", sem_ev)
        for kc in range(KC):
            c = kc * T + c0
            ph.op("vector",
                  lambda e, c=c: e.tensor_tensor(
                      ar[:, c:c + TT], ar[:, c:c + TT], sq[:, 0:TT], ALU.mult))
            ph.op("vector",
                  lambda e, c=c: e.tensor_tensor(
                      ar[:, c:c + TT], ar[:, c:c + TT], sq[:, TT:2 * TT], ALU.subtract),
                  inc=sem_ev)
            if out_bf is not None:
                ph.wait("scalar", sem_ev)
                ph.op("scalar",
                      lambda e, c=c: e.copy(out_bf[:, c:c + TT], ar[:, c:c + TT]),
                      inc=sem_ev)
        ph.wait("tensor", sem_ev)
        ph.wait("scalar", sem_ev)


def build_l1ffn(ctx, ar, w, sc):
    HC = 32
    s_ld = ctx.sem("ff_ld")
    s_ld2 = ctx.sem("ff_ld2")
    s_cv = ctx.sem("ff_cv")
    s_cv1 = ctx.sem("ff_cv1")
    s_mm = ctx.sem("ff_mm")
    s_ev = ctx.sem("ff_ev")

    h_a = sc.sbuf("ff_ha", [128, HC * TT], BF16)
    h_b = sc.sbuf("ff_hb", [128, HC * TT], BF16)
    stg2 = sc.sbuf("ff_stg", [128, 2 * 4096], F32)
    stg = stg2[:, 0:4096]
    wb = sc.sbuf("ff_wb", [128, 2 * 4096], BF16)
    xq = sc.sbuf("ff_xq", [128, KC * TT], BF16)
    silu_t = [sc.sbuf(f"ff_silu{i}", [128, TT], BF16) for i in range(2)]
    out_t = [sc.sbuf(f"ff_out{i}", [128, TT], F32) for i in range(2)]
    psA = [sc.psum(f"ff_psA{i}", [128, TT]) for i in range(2)]
    psB = [sc.psum(f"ff_psB{i}", [128, TT]) for i in range(2)]

    sv = lambda base, ch: ctx.vecs[:, base + ch: base + ch + 1]

    for q in range(NTT):
        t0 = q * TT

        ph = Ph(ctx, f"ffA{q}")
        for kc in range(KC):
            ph.op("vector",
                  lambda e, kc=kc, t0=t0: e.tensor_copy(
                      xq[:, kc * TT:(kc + 1) * TT], ar[:, kc * T + t0: kc * T + t0 + TT]),
                  inc=(s_cv if kc == KC - 1 else None))
        ph.wait("tensor", s_cv)
        cv0 = ctx.cnt[s_cv]
        mm0 = ctx.cnt[s_mm]
        ev0 = ctx.cnt[s_ev]
        for hch in range(HC):
            par = hch % 2
            sb = stg2[:, par * 4096: par * 4096 + 2048]
            if hch >= 2:
                ph.wait("sync", s_cv, cv0 + hch - 1)
            ph.dma("sync", s_ld,
                   lambda e, hch=hch, sb=sb: e.dma_start(
                       out=sb[:, 0:1024].rearrange("p (kc n) -> p kc n", kc=8),
                       in_=w["w1T"][:, hch * 128:(hch + 1) * 128].rearrange(
                           "(kc p) n -> p kc n", p=128)))
            ph.dma("sync", s_ld,
                   lambda e, hch=hch, sb=sb: e.dma_start(
                       out=sb[:, 1024:2048].rearrange("p (kc n) -> p kc n", kc=8),
                       in_=w["w2T"][:, hch * 128:(hch + 1) * 128].rearrange(
                           "(kc p) n -> p kc n", p=128)))
            ph.wait("scalar", s_ld)
            if hch >= 2:
                ph.wait("scalar", s_mm, mm0 + 2 * (hch - 1))
            ph.op("scalar",
                  lambda e, par=par, sb=sb: e.copy(
                      wb[:, par * 2048: par * 2048 + 2048], sb[:, 0:2048]),
                  inc=s_cv)
            ph.wait("tensor", s_cv)
            if hch >= 2:
                ph.wait("tensor", s_ev, ev0 + 2 * (hch - 1))
            for kc in range(KC):
                ph.op("tensor",
                      lambda e, kc=kc, par=par: e.matmul(
                          psA[par][:], wb[:, par * 2048 + kc * 128: par * 2048 + (kc + 1) * 128],
                          xq[:, kc * TT:(kc + 1) * TT],
                          start=(kc == 0), stop=(kc == KC - 1)),
                      inc=(s_mm if kc == KC - 1 else None))
            for kc in range(KC):
                ph.op("tensor",
                      lambda e, kc=kc, par=par: e.matmul(
                          psB[par][:], wb[:, par * 2048 + 1024 + kc * 128: par * 2048 + 1024 + (kc + 1) * 128],
                          xq[:, kc * TT:(kc + 1) * TT],
                          start=(kc == 0), stop=(kc == KC - 1)),
                      inc=(s_mm if kc == KC - 1 else None))
            ph.wait("scalar", s_mm, mm0 + 2 * hch + 1)
            if hch >= 2:
                ph.wait("scalar", s_ev, ev0 + 2 * (hch - 1))
            ph.op("scalar",
                  lambda e, par=par, hch=hch: e.activation(
                      silu_t[par][:, :], psA[par][:], AF.Silu,
                      bias=0.0, scale=sv(w["s1c"], hch)),
                  inc=s_ev)
            ph.wait("vector", s_ev, ev0 + 2 * hch + 1)
            ph.wait("vector", s_mm, mm0 + 2 * hch + 2)
            ph.op("vector",
                  lambda e, par=par, hch=hch: e.scalar_tensor_tensor(
                      h_a[:, hch * TT:(hch + 1) * TT], psB[par][:],
                      sv(w["s2c"], hch), silu_t[par][:, :], ALU.mult, ALU.mult),
                  inc=s_ev)
        ph.run()

        for nm, wt, sb, src, dst in (("ffB", "h1T", "sh1c", h_a, h_b),
                                     ("ffC", "h2T", "sh2c", h_b, h_a)):
            ph = Ph(ctx, f"{nm}{q}")
            mm0 = ctx.cnt[s_mm]
            ev0 = ctx.cnt[s_ev]
            for nch in range(HC):
                par = nch % 2
                cs = s_cv if par == 0 else s_cv1
                stgp = stg2[:, par * 4096:(par + 1) * 4096]
                dq = "sync" if par == 0 else "gpsimd"
                dsem = s_ld if par == 0 else s_ld2
                if nch >= 2:
                    ph.wait(dq, cs, ctx.cnt[cs])
                ph.dma(dq, dsem,
                       lambda e, nch=nch, wt=wt, stgp=stgp: e.dma_start(
                           out=stgp.rearrange("p (kc n) -> p kc n", kc=32),
                           in_=w[wt][:, nch * 128:(nch + 1) * 128].rearrange(
                               "(kc p) n -> p kc n", p=128)))
                ph.wait("scalar", dsem)
                if nch >= 2:
                    ph.wait("scalar", s_mm, mm0 + nch - 1)
                ph.op("scalar",
                      lambda e, par=par, stgp=stgp: e.copy(
                          wb[:, par * 4096: par * 4096 + 4096], stgp),
                      inc=cs)
                ph.wait("tensor", cs)
                if nch >= 2:
                    ph.wait("tensor", s_ev, ev0 + nch - 1)
                for kch in range(HC):
                    ph.op("tensor",
                          lambda e, kch=kch, par=par, src=src: e.matmul(
                              psA[par][:], wb[:, par * 4096 + kch * 128: par * 4096 + (kch + 1) * 128],
                              src[:, kch * TT:(kch + 1) * TT],
                              start=(kch == 0), stop=(kch == HC - 1)),
                          inc=(s_mm if kch == HC - 1 else None))
                ph.wait("scalar", s_mm, mm0 + nch + 1)
                ph.op("scalar",
                      lambda e, par=par, nch=nch, dst=dst, sb=sb: e.activation(
                          dst[:, nch * TT:(nch + 1) * TT], psA[par][:], AF.Gelu,
                          bias=0.0, scale=sv(w[sb], nch)),
                      inc=s_ev)
            ph.run()

        ph = Ph(ctx, f"ffD{q}")
        cv0 = ctx.cnt[s_cv]
        mm0 = ctx.cnt[s_mm]
        ev0 = ctx.cnt[s_ev]
        for dch in range(KC):
            par = dch % 2
            sb = stg2[:, par * 4096:(par + 1) * 4096]
            if dch >= 2:
                ph.wait("sync", s_cv, cv0 + dch - 1)
            ph.dma("sync", s_ld,
                   lambda e, dch=dch, sb=sb: e.dma_start(
                       out=sb.rearrange("p (kc n) -> p kc n", kc=32),
                       in_=w["w3T"][:, dch * 128:(dch + 1) * 128].rearrange(
                           "(kc p) n -> p kc n", p=128)))
            ph.wait("scalar", s_ld)
            if dch >= 2:
                ph.wait("scalar", s_mm, mm0 + dch - 1)
            ph.op("scalar",
                  lambda e, par=par, sb=sb: e.copy(
                      wb[:, par * 4096: par * 4096 + 4096], sb),
                  inc=s_cv)
            ph.wait("tensor", s_cv)
            if dch >= 2:
                ph.wait("tensor", s_ev, ev0 + 2 * (dch - 1))
            for kch in range(HC):
                ph.op("tensor",
                      lambda e, kch=kch, par=par: e.matmul(
                          psA[par][:], wb[:, par * 4096 + kch * 128: par * 4096 + (kch + 1) * 128],
                          h_a[:, kch * TT:(kch + 1) * TT],
                          start=(kch == 0), stop=(kch == HC - 1)),
                      inc=(s_mm if kch == HC - 1 else None))
            ph.wait("scalar", s_mm, mm0 + dch + 1)
            if dch >= 2:
                ph.wait("scalar", s_ev, ev0 + 2 * (dch - 1))
            ph.op("scalar",
                  lambda e, par=par, dch=dch: e.activation(
                      out_t[par][:, :], psA[par][:], AF.Copy,
                      bias=0.0, scale=sv(w["s3c"], dch)),
                  inc=s_ev)
            ph.wait("vector", s_ev, ev0 + 2 * dch + 1)
            c = dch * T + t0
            ph.op("vector",
                  lambda e, par=par, c=c: e.tensor_add(
                      ar[:, c:c + TT], ar[:, c:c + TT], out_t[par][:, :]),
                  inc=s_ev)
        ph.run()

    ph = Ph(ctx, "ffLN")
    emit_postln(ph, ar=ar, sq=stg, ps_stat=[psA[0], psA[1]], ps_b=[psB[0], psB[1]],
                sem_mm=s_mm, sem_ev=s_ev,
                eps_ap=ctx.vecs[0:1, w["keps"]:w["keps"] + 1])
    ph.run()


def emit_load_w1024(ph, sc_hbm, stg, wb, s_ld, s_cv, nin=1024, nout=1024):
    kc = nin // 128
    half = nout // 2
    for h in range(2):
        n0 = h * half
        ph.dma("sync", s_ld,
               lambda e, n0=n0: e.dma_start(
                   out=stg[:, 0:kc * half].rearrange("p (k n) -> p k n", k=kc),
                   in_=sc_hbm[:, n0:n0 + half].rearrange("(k p) n -> p k n", p=128)))
        ph.wait("gpsimd", s_ld)
        for k in range(kc):
            ph.op("gpsimd",
                  lambda e, k=k, n0=n0: e.tensor_copy(
                      wb[:, k * nout + n0: k * nout + n0 + half],
                      stg[:, k * half:(k + 1) * half]),
                  inc=(s_cv if k == kc - 1 else None))


def build_l0attn(ctx, ar, w, cc, sc, upto="H"):
    nc = ctx.nc
    EPS = 1e-6
    s_ld = ctx.sem("a_ld")
    s_cv = ctx.sem("a_cv")
    s_mm = ctx.sem("a_mm")
    s_ev = ctx.sem("a_ev")
    s_tp = ctx.sem("a_tp")
    s_cc = ctx.sem("a_cc")
    s_gld = ctx.sem("a_gld")

    Qb = sc.sbuf("a_Q", [128, KC * T], BF16)
    Kb = sc.sbuf("a_K", [128, KC * T], BF16)
    Vb = sc.sbuf("a_V", [128, KC * T], BF16)
    stats = sc.sbuf("a_stats", [16, 3 * T], BF16)
    ecol = sc.sbuf("a_ecol", [16, 8], F32)
    dnm = sc.sbuf("a_dnm", [16, 2], F32)
    blkA = sc.sbuf("a_blkA", [128, 8 * 16], BF16)
    blkB = sc.sbuf("a_blkB", [128, 8 * 16], BF16)
    r1 = sc.sbuf("a_r1", [128, 16], F32)
    r1e = sc.sbuf("a_r1e", [128, 16], F32)
    r2 = sc.sbuf("a_r2", [128, 16], F32)
    r2e = sc.sbuf("a_r2e", [128, 16], F32)
    kvb = sc.sbuf("a_kvb", [128, 16 * 64], BF16)
    blkkv = sc.sbuf("a_blkkv", [128, 128], BF16)
    ktt = sc.sbuf("a_ktt", [128, 2 * 128], BF16)
    vtt = sc.sbuf("a_vtt", [128, 2 * 128], BF16)
    stg = sc.sbuf("a_stg", [128, 2048], F32)
    s8k = stg
    kvsb = stg[:, 0:1024]
    sia_sb = stg[:, 1024:1024 + TT]
    srow32 = stg[0:16, 1536:1536 + TT]
    wb = sc.sbuf("a_wb", [128, 8 * 512], BF16)

    psA = sc.psum("a_psA", [128, TT])
    psB = sc.psum("a_psB", [128, TT])
    psD = sc.psum("a_psD", [16, TT])
    psC = sc.psum("a_psC", [16, 16])
    pstK = sc.psum("a_pstK", [128, 128], BF16)
    pstV = sc.psum("a_pstV", [128, 128], BF16)
    kve = sc.psum("a_kve", [64, 64])
    kvo = sc.psum("a_kvo", [64, 64])

    def xcvt(kc):
        return s8k[:, kc * 256:(kc + 1) * 256].bitcast(BF16)

    S_SI, S_SC, S_SA = 0, 1, 2

    def srow(blk, tt=None):
        if tt is None:
            return stats[:, blk * T:(blk + 1) * T]
        return stats[:, blk * T + tt * TT: blk * T + (tt + 1) * TT]

    kon = ctx.vecs[:, w["kone"]:w["kone"] + 1]

    for pi, (wt, bcol, func, arena) in enumerate((
            ("WqT", "bq", AF.Sigmoid, Qb),
            ("WkT", "bk", AF.Sigmoid, Kb),
            ("WvT", "bv", AF.Identity, Vb))):
        ph = Ph(ctx, f"a_proj{pi}")
        for h in range(2):
            n0 = h * 512
            if h == 1:
                ph.wait("sync", s_mm)
            for qtr in range(2):
                if qtr == 1:
                    ph.wait("sync", s_cv)
                ph.dma("sync", s_ld,
                       lambda e, n0=n0, qtr=qtr, wt=wt: e.dma_start(
                           out=stg[:, 0:2048].rearrange("p (k n) -> p k n", k=8),
                           in_=w[wt][:, n0 + qtr * 256: n0 + qtr * 256 + 256].rearrange(
                               "(k p) n -> p k n", p=128)))
                ph.wait("gpsimd", s_ld)
                for k in range(KC):
                    ph.op("gpsimd",
                          lambda e, k=k, qtr=qtr: e.tensor_copy(
                              wb[:, k * 512 + qtr * 256: k * 512 + qtr * 256 + 256],
                              stg[:, k * 256:(k + 1) * 256]),
                          inc=(s_cv if k == KC - 1 else None))
            cvh = ctx.cnt[s_cv]
            mm0 = ctx.cnt[s_mm]
            nidx = 0
            for tt in range(NTT):
                ph.wait("vector", s_cv, cvh)
                ph.wait("vector", s_mm, mm0 + 4 * tt)
                for kc in range(KC):
                    ph.op("vector",
                          lambda e, kc=kc, tt=tt: e.tensor_copy(
                              xcvt(kc), ar[:, kc * T + tt * TT: kc * T + (tt + 1) * TT]),
                          inc=(s_cv if kc == KC - 1 else None))
                ph.wait("tensor", s_cv)
                for nch in range(4):
                    ps = psA if nidx % 2 == 0 else psB
                    ph.wait("tensor", s_ev, ctx.cnt[s_ev] - 1)
                    for kc in range(KC):
                        ph.op("tensor",
                              lambda e, kc=kc, nch=nch, ps=ps: e.matmul(
                                  ps[:], wb[:, kc * 512 + nch * 128: kc * 512 + (nch + 1) * 128],
                                  xcvt(kc), start=(kc == 0), stop=(kc == KC - 1)),
                              inc=(s_mm if kc == KC - 1 else None))
                    ph.wait("scalar", s_mm)
                    nabs = h * 4 + nch
                    ph.op("scalar",
                          lambda e, ps=ps, nabs=nabs, tt=tt, arena=arena, func=func, bcol=bcol: e.activation(
                              arena[:, nabs * T + tt * TT: nabs * T + (tt + 1) * TT],
                              ps[:], func, bias=ctx.vecs[:, w[bcol] + nabs: w[bcol] + nabs + 1],
                              scale=1.0),
                          inc=s_ev)
                    nidx += 1
        ph.run()

    if upto == "A":
        return (Qb, Kb, Vb, stats, r1, r2, kvb, dnm, ecol)
    ph = Ph(ctx, "a_sums")
    for kc in range(KC):
        ph.op("vector",
              lambda e, kc=kc: e.reduce_sum(r1[:, kc:kc + 1], Kb[:, kc * T:(kc + 1) * T], axis=AX.X))
        ph.op("vector",
              lambda e, kc=kc: e.reduce_sum(r1[:, 8 + kc:9 + kc], Qb[:, kc * T:(kc + 1) * T], axis=AX.X),
              inc=(s_ev if kc == KC - 1 else None))
    ph.wait("gpsimd", s_ev)
    ph.dma("gpsimd", s_gld, lambda e: e.dma_start(out=cc["r16_in"], in_=r1[:, :]))
    ph.wait("gpsimd", s_gld)
    ph.op("gpsimd", lambda e: e.collective_compute(
        "AllReduce", ALU.add, replica_groups=[list(range(NCORES))],
        ins=[cc["r16_in"].opt()], outs=[cc["r16_out"].opt()]), inc=s_cc)
    ph.wait("gpsimd", s_cc)
    ph.dma("gpsimd", s_gld, lambda e: e.dma_start(out=r1[:, :], in_=cc["r16_out"]))
    ph.wait("vector", s_gld)
    ph.op("vector", lambda e: e.tensor_scalar_add(
        r1e[:, :], r1[:, :], ctx.vecs[:, w["keps6"]:w["keps6"] + 1]))
    ph.op("vector", lambda e: e.memset(blkA[:, :], 0.0))
    ph.op("vector", lambda e: e.memset(blkB[:, :], 0.0), inc=s_ev)
    for kc in range(KC):
        ph.op("vector", lambda e, kc=kc: e.tensor_copy(
            blkA[0:64, kc * 16 + 2 * kc: kc * 16 + 2 * kc + 1], r1e[0:64, kc:kc + 1]))
        ph.op("vector", lambda e, kc=kc: e.tensor_copy(
            blkA[64:128, kc * 16 + 2 * kc + 1: kc * 16 + 2 * kc + 2], r1e[64:128, kc:kc + 1]))
        ph.op("vector", lambda e, kc=kc: e.tensor_copy(
            blkB[0:64, kc * 16 + 2 * kc: kc * 16 + 2 * kc + 1], r1e[0:64, 8 + kc:9 + kc]))
        ph.op("vector", lambda e, kc=kc: e.tensor_copy(
            blkB[64:128, kc * 16 + 2 * kc + 1: kc * 16 + 2 * kc + 2], r1e[64:128, 8 + kc:9 + kc]),
            inc=(s_ev if kc == KC - 1 else None))
    ph.wait("tensor", s_ev)
    for j, blk in enumerate((blkA, blkB)):
        for kc in range(KC):
            ph.op("tensor",
                  lambda e, blk=blk, kc=kc: e.matmul(
                      psC[0:16, 0:1], blk[:, kc * 16:(kc + 1) * 16],
                      ctx.ones_col_bf[:, 0:1], start=(kc == 0), stop=(kc == KC - 1)),
                  inc=(s_mm if kc == KC - 1 else None))
        ph.wait("scalar", s_mm)
        ph.op("scalar", lambda e, j=j: e.mul(ecol[:, j:j + 1], psC[0:16, 0:1], EPS),
              inc=s_ev)
        ph.wait("tensor", s_ev)
    ph.wait("scalar", s_ev)
    for j, (blk, rhs_a, sblk, ec) in enumerate(
            ((blkA, Qb, S_SI, 0), (blkB, Kb, S_SC, 1))):
        for tt in range(NTT):
            for kc in range(KC):
                ph.op("tensor",
                      lambda e, blk=blk, kc=kc, rhs_a=rhs_a, tt=tt: e.matmul(
                          psD[0:16, :], blk[:, kc * 16:(kc + 1) * 16],
                          rhs_a[:, kc * T + tt * TT: kc * T + (tt + 1) * TT],
                          start=(kc == 0), stop=(kc == KC - 1)),
                      inc=(s_mm if kc == KC - 1 else None))
            ph.wait("scalar", s_mm)
            ph.wait("scalar", s_ev)
            ph.op("scalar",
                  lambda e, ec=ec: e.activation(
                      srow32[0:16, :], psD[0:16, :], AF.Identity,
                      bias=ecol[0:16, ec:ec + 1], scale=1.0),
                  inc=s_ev)
            ph.wait("vector", s_ev)
            ph.op("vector",
                  lambda e: e.reciprocal(srow32[0:16, :], srow32[0:16, :]))
            ph.op("vector",
                  lambda e, sblk=sblk, tt=tt: e.tensor_copy(srow(sblk, tt), srow32[0:16, :]),
                  inc=s_ev)
            ph.wait("tensor", s_ev)
    ph.run()

    if upto == "B":
        return (Qb, Kb, Vb, stats, r1, r2, kvb, dnm, ecol)
    ph = Ph(ctx, "a_r2")
    for j, (arena, sblk, col0) in enumerate(((Kb, S_SC, 0), (Qb, S_SI, 8))):
        for kc in range(KC):
            for tt in range(NTT):
                ph.op("tensor",
                      lambda e, kc=kc, tt=tt, sblk=sblk: e.matmul(
                          psB[:], ctx.ind_bf[0:16, kc * 128:(kc + 1) * 128],
                          srow(sblk, tt), start=True, stop=True),
                      inc=s_mm)
                ph.wait("vector", s_mm)
                ph.op("vector",
                      lambda e, kc=kc, tt=tt, arena=arena: e.scalar_tensor_tensor(
                          s8k[:, tt * TT:(tt + 1) * TT],
                          arena[:, kc * T + tt * TT: kc * T + (tt + 1) * TT],
                          kon, psB[:], ALU.bypass, ALU.mult),
                      inc=s_ev)
                ph.wait("tensor", s_ev)
            ph.wait("vector", s_ev)
            ph.op("vector",
                  lambda e, kc=kc, col0=col0: e.reduce_sum(
                      r2[:, col0 + kc: col0 + kc + 1], s8k[:, :], axis=AX.X),
                  inc=s_ev)
    ph.wait("gpsimd", s_ev)
    ph.dma("gpsimd", s_gld, lambda e: e.dma_start(out=cc["r16_in"], in_=r2[:, :]))
    ph.wait("gpsimd", s_gld)
    ph.op("gpsimd", lambda e: e.collective_compute(
        "AllReduce", ALU.add, replica_groups=[list(range(NCORES))],
        ins=[cc["r16_in"].opt()], outs=[cc["r16_out"].opt()]), inc=s_cc)
    ph.wait("gpsimd", s_cc)
    ph.dma("gpsimd", s_gld, lambda e: e.dma_start(out=r2[:, :], in_=cc["r16_out"]))
    ph.run()

    if upto == "C":
        return (Qb, Kb, Vb, stats, r1, r2, kvb, dnm, ecol)
    ph = Ph(ctx, "a_cs")
    ph.wait("vector", s_gld)
    ph.op("vector", lambda e: e.tensor_scalar_add(
        r2e[:, :], r2[:, :], ctx.vecs[:, w["keps6"]:w["keps6"] + 1]))
    ph.op("vector", lambda e: e.memset(blkA[:, :], 0.0))
    ph.op("vector", lambda e: e.memset(blkB[:, :], 0.0), inc=s_ev)
    for kc in range(KC):
        ph.op("vector", lambda e, kc=kc: e.tensor_copy(
            blkA[0:64, kc * 16 + 2 * kc: kc * 16 + 2 * kc + 1], r2e[0:64, kc:kc + 1]))
        ph.op("vector", lambda e, kc=kc: e.tensor_copy(
            blkA[64:128, kc * 16 + 2 * kc + 1: kc * 16 + 2 * kc + 2], r2e[64:128, kc:kc + 1]))
        ph.op("vector", lambda e, kc=kc: e.tensor_copy(
            blkB[0:64, kc * 16 + 2 * kc: kc * 16 + 2 * kc + 1], r2e[0:64, 8 + kc:9 + kc]))
        ph.op("vector", lambda e, kc=kc: e.tensor_copy(
            blkB[64:128, kc * 16 + 2 * kc + 1: kc * 16 + 2 * kc + 2], r2e[64:128, 8 + kc:9 + kc]),
            inc=(s_ev if kc == KC - 1 else None))
    ph.wait("tensor", s_ev)
    for j, blk in enumerate((blkA, blkB)):
        for kc in range(KC):
            ph.op("tensor",
                  lambda e, blk=blk, kc=kc: e.matmul(
                      psC[0:16, 0:1], blk[:, kc * 16:(kc + 1) * 16],
                      ctx.ones_col_bf[:, 0:1], start=(kc == 0), stop=(kc == KC - 1)),
                  inc=(s_mm if kc == KC - 1 else None))
        ph.wait("scalar", s_mm)
        ph.op("scalar", lambda e, j=j: e.mul(ecol[:, 2 + j:3 + j], psC[0:16, 0:1], EPS),
              inc=s_ev)
        ph.wait("tensor", s_ev)
    ph.wait("scalar", s_ev)
    for j, (blk, rhs_a, sblk, ec) in enumerate(
            ((blkA, Qb, S_SA, 2), (blkB, Kb, S_SC, 3))):
        for tt in range(NTT):
            for kc in range(KC):
                ph.op("tensor",
                      lambda e, blk=blk, kc=kc, rhs_a=rhs_a, tt=tt: e.matmul(
                          psD[0:16, :], blk[:, kc * 16:(kc + 1) * 16],
                          rhs_a[:, kc * T + tt * TT: kc * T + (tt + 1) * TT],
                          start=(kc == 0), stop=(kc == KC - 1)),
                      inc=(s_mm if kc == KC - 1 else None))
            ph.wait("scalar", s_mm)
            if j == 0:
                ph.op("scalar",
                      lambda e, tt=tt: e.activation(
                          srow(S_SA, tt), psD[0:16, :], AF.Sigmoid,
                          bias=ecol[0:16, 2:3], scale=1.0),
                      inc=s_ev)
            else:
                ph.op("scalar",
                      lambda e, tt=tt: e.activation(
                          srow(S_SC, tt), psD[0:16, :], AF.Identity,
                          bias=ecol[0:16, 3:4], scale=1.0),
                      inc=s_ev)
                ph.wait("vector", s_ev)
                ph.op("vector",
                      lambda e, tt=tt: e.tensor_scalar(
                          srow(S_SC, tt), srow(S_SC, tt), 1.0, -1.0,
                          ALU.min, ALU.max),
                      inc=s_ev)
                ph.wait("scalar", s_ev)
                ph.op("scalar",
                      lambda e, tt=tt: e.activation(
                          srow(S_SC, tt), srow(S_SC, tt), AF.Exp,
                          bias=0.0, scale=1.0),
                      inc=s_ev)
            ph.wait("tensor", s_ev)
    ph.wait("vector", s_ev)
    ph.op("vector", lambda e: e.reduce_sum(dnm[:, 0:1], srow(S_SC), axis=AX.X),
          inc=s_ev)
    ph.wait("gpsimd", s_ev)
    ph.dma("gpsimd", s_gld, lambda e: e.dma_start(out=cc["d_in"], in_=dnm[0:16, 0:1]))
    ph.wait("gpsimd", s_gld)
    ph.op("gpsimd", lambda e: e.collective_compute(
        "AllReduce", ALU.add, replica_groups=[list(range(NCORES))],
        ins=[cc["d_in"].opt()], outs=[cc["d_out"].opt()]), inc=s_cc)
    ph.wait("gpsimd", s_cc)
    ph.dma("gpsimd", s_gld, lambda e: e.dma_start(out=dnm[0:16, 1:2], in_=cc["d_out"]))
    ph.wait("vector", s_gld)
    ph.op("vector", lambda e: e.reciprocal(dnm[:, 1:2], dnm[:, 1:2]), inc=s_ev)
    ph.wait("vector", s_ev)
    ph.op("vector", lambda e: e.tensor_scalar(
        srow(S_SC), srow(S_SC), dnm[0:16, 1:2], float(TSEQ), ALU.mult, ALU.mult))
    ph.op("vector", lambda e: e.tensor_tensor(
        srow(S_SI), srow(S_SI), srow(S_SA), ALU.mult), inc=s_ev)
    ph.run()

    if upto == "D":
        return (Qb, Kb, Vb, stats, r1, r2, kvb, dnm, ecol)
    ph = Ph(ctx, "a_kv")
    mm0 = ctx.cnt[s_mm]
    for kc in range(KC):
        for tt in range(NTT):
            ph.op("tensor",
                  lambda e, kc=kc, tt=tt: e.matmul(
                      psB[:], ctx.ind_bf[0:16, kc * 128:(kc + 1) * 128],
                      srow(S_SC, tt), start=True, stop=True),
                  inc=s_mm)
            ph.wait("vector", s_mm)
            ph.op("vector",
                  lambda e, kc=kc, tt=tt: e.scalar_tensor_tensor(
                      Vb[:, kc * T + tt * TT: kc * T + (tt + 1) * TT],
                      Vb[:, kc * T + tt * TT: kc * T + (tt + 1) * TT],
                      kon, psB[:], ALU.bypass, ALU.mult),
                  inc=s_ev)
            ph.wait("tensor", s_ev)
    tp0 = ctx.cnt[s_tp]
    ev0 = ctx.cnt[s_ev]
    mm0 = ctx.cnt[s_mm]
    for kc in range(KC):
        for ts in range(T // 128):
            par = ts % 2
            i = kc * (T // 128) + ts
            if i >= 1:
                ph.wait("tensor", s_ev, ev0 + 2 * i)
            if i >= 2:
                ph.wait("vector", s_mm, mm0 + 2 * (i - 1))
            ph.op("tensor",
                  lambda e, kc=kc, ts=ts: e.transpose(
                      pstK[:, 0:128], Kb[:, kc * T + ts * 128: kc * T + (ts + 1) * 128],
                      ctx.ident_bf[:]),
                  inc=s_tp)
            ph.op("tensor",
                  lambda e, kc=kc, ts=ts: e.transpose(
                      pstV[:, 0:128], Vb[:, kc * T + ts * 128: kc * T + (ts + 1) * 128],
                      ctx.ident_bf[:]),
                  inc=s_tp)
            ph.wait("vector", s_tp)
            ph.op("vector",
                  lambda e, par=par: e.tensor_copy(ktt[:, par * 128:(par + 1) * 128], pstK[:, 0:128]),
                  inc=s_ev)
            ph.op("vector",
                  lambda e, par=par: e.tensor_copy(vtt[:, par * 128:(par + 1) * 128], pstV[:, 0:128]),
                  inc=s_ev)
            ph.wait("tensor", s_ev)
            ph.op("tensor",
                  lambda e, par=par, ts=ts: e.matmul(
                      kve[0:64, 0:64], ktt[:, par * 128: par * 128 + 64],
                      vtt[:, par * 128: par * 128 + 64],
                      start=(ts == 0), stop=(ts == T // 128 - 1)),
                  inc=s_mm)
            ph.op("tensor",
                  lambda e, par=par, ts=ts: e.matmul(
                      kvo[0:64, 0:64], ktt[:, par * 128 + 64: par * 128 + 128],
                      vtt[:, par * 128 + 64: par * 128 + 128],
                      start=(ts == 0), stop=(ts == T // 128 - 1)),
                  inc=s_mm)
        ph.wait("scalar", s_mm)
        ph.op("scalar",
              lambda e, kc=kc: e.copy(kvsb[0:64, (2 * kc) * 64:(2 * kc + 1) * 64], kve[0:64, 0:64]))
        ph.op("scalar",
              lambda e, kc=kc: e.copy(kvsb[0:64, (2 * kc + 1) * 64:(2 * kc + 2) * 64], kvo[0:64, 0:64]),
              inc=s_ev)
        ph.wait("tensor", s_ev)
    ph.wait("gpsimd", s_ev)
    ph.dma("gpsimd", s_gld, lambda e: e.dma_start(out=cc["kv_in"], in_=kvsb[:, :]))
    ph.wait("gpsimd", s_gld)
    ph.op("gpsimd", lambda e: e.collective_compute(
        "AllReduce", ALU.add, replica_groups=[list(range(NCORES))],
        ins=[cc["kv_in"].opt()], outs=[cc["kv_out"].opt()]), inc=s_cc)
    ph.wait("gpsimd", s_cc)
    ph.dma("gpsimd", s_gld, lambda e: e.dma_start(out=kvsb[:, :], in_=cc["kv_out"]))
    ph.wait("vector", s_gld)
    ph.op("vector", lambda e: e.tensor_copy(kvb[:, :], kvsb[:, :]), inc=s_cv)
    ph.run()

    if upto == "E":
        return (Qb, Kb, Vb, stats, r1, r2, kvb, dnm, ecol)
    ph = Ph(ctx, "a_out")
    out_a = Kb
    for kc in range(KC):
        ph.wait("vector", s_mm)
        ph.op("vector", lambda e: e.memset(blkkv[:, :], 0.0))
        ph.op("vector", lambda e, kc=kc: e.tensor_copy(
            blkkv[0:64, 0:64], kvb[0:64, (2 * kc) * 64:(2 * kc + 1) * 64]))
        ph.op("vector", lambda e, kc=kc: e.tensor_copy(
            blkkv[64:128, 64:128], kvb[0:64, (2 * kc + 1) * 64:(2 * kc + 2) * 64]),
            inc=s_cv)
        ph.wait("tensor", s_cv)
        for tt in range(NTT):
            ph.op("tensor",
                  lambda e, kc=kc, tt=tt: e.matmul(
                      psA[:], blkkv[:, :], Qb[:, kc * T + tt * TT: kc * T + (tt + 1) * TT],
                      start=True, stop=True),
                  inc=s_mm)
            ph.op("tensor",
                  lambda e, kc=kc, tt=tt: e.matmul(
                      psB[:], ctx.ind_bf[0:16, kc * 128:(kc + 1) * 128],
                      srow(S_SI, tt), start=True, stop=True),
                  inc=s_mm)
            ph.wait("scalar", s_mm)
            ph.op("scalar", lambda e: e.copy(sia_sb[:, :], psB[:]), inc=s_ev)
            ph.wait("vector", s_ev)
            ph.op("vector",
                  lambda e, kc=kc, tt=tt: e.scalar_tensor_tensor(
                      out_a[:, kc * T + tt * TT: kc * T + (tt + 1) * TT],
                      psA[:], kon, sia_sb[:, :], ALU.bypass, ALU.mult),
                  inc=s_ev)
            ph.wait("tensor", s_ev)
    ph.run()

    if upto == "F":
        return (Qb, Kb, Vb, stats, r1, r2, kvb, dnm, ecol)
    ph = Ph(ctx, "a_wo")
    for h in range(2):
        n0 = h * 512
        if h == 1:
            ph.wait("sync", s_mm)
        for qtr in range(2):
            if qtr == 1 or h == 1:
                ph.wait("sync", s_cv)
            ph.dma("sync", s_ld,
                   lambda e, n0=n0, qtr=qtr: e.dma_start(
                       out=stg[:, 0:2048].rearrange("p (k n) -> p k n", k=8),
                       in_=w["WoT"][:, n0 + qtr * 256: n0 + qtr * 256 + 256].rearrange(
                           "(k p) n -> p k n", p=128)))
            ph.wait("gpsimd", s_ld)
            for k in range(KC):
                ph.op("gpsimd",
                      lambda e, k=k, qtr=qtr: e.tensor_copy(
                          wb[:, k * 512 + qtr * 256: k * 512 + qtr * 256 + 256],
                          stg[:, k * 256:(k + 1) * 256]),
                      inc=(s_cv if k == KC - 1 else None))
        ph.wait("tensor", s_cv)
        nidx = 0
        for nch in range(4):
            nabs = h * 4 + nch
            for tt in range(NTT):
                ps = psA if nidx % 2 == 0 else psB
                ph.wait("tensor", s_ev, ctx.cnt[s_ev] - 1)
                for kc in range(KC):
                    ph.op("tensor",
                          lambda e, kc=kc, nch=nch, tt=tt, ps=ps: e.matmul(
                              ps[:], wb[:, kc * 512 + nch * 128: kc * 512 + (nch + 1) * 128],
                              out_a[:, kc * T + tt * TT: kc * T + (tt + 1) * TT],
                              start=(kc == 0), stop=(kc == KC - 1)),
                          inc=(s_mm if kc == KC - 1 else None))
                ph.wait("vector", s_mm)
                c = nabs * T + tt * TT
                ph.op("vector",
                      lambda e, ps=ps, nabs=nabs, c=c: e.scalar_tensor_tensor(
                          ar[:, c:c + TT], ps[:],
                          ctx.vecs[:, w["bo"] + nabs: w["bo"] + nabs + 1],
                          ar[:, c:c + TT], ALU.add, ALU.add),
                      inc=s_ev)
                nidx += 1
    ph.run()

    if upto == "G":
        return (Qb, Kb, Vb, stats, r1, r2, kvb, dnm, ecol)
    ph = Ph(ctx, "a_ln")
    emit_postln(ph, ar=ar, sq=stg, ps_stat=[psA, psB], ps_b=[psA, psB],
                sem_mm=s_mm, sem_ev=s_ev,
                eps_ap=ctx.vecs[0:1, w["keps"]:w["keps"] + 1],
                stop_after={"H1": "stats", "H2": "rows", "H3": "bcast"}.get(upto))
    ph.run()
    return (Qb, Kb, Vb, stats, r1, r2, kvb, dnm, ecol)


def emit_rms_rows(ph, ctx, *, src_ar, rrow, sq, ps_stat, ps_b, s_mm, s_ev, eps_ap):
    for tt in range(NTT):
        c0 = tt * TT
        mm_sq0 = ctx.cnt[s_mm]
        for kc in range(KC):
            sqt = sq[:, (kc % 2) * TT:(kc % 2 + 1) * TT]
            if kc >= 2:
                ph.wait("scalar", s_mm, mm_sq0 + kc - 1)
            ph.op("scalar",
                  lambda e, kc=kc, c0=c0, sqt=sqt: e.activation(
                      sqt, src_ar[:, kc * T + c0: kc * T + c0 + TT],
                      AF.Square, bias=0.0, scale=1.0),
                  inc=s_ev)
            ph.wait("tensor", s_ev)
            ph.op("tensor",
                  lambda e, kc=kc, sqt=sqt: e.matmul(
                      ps_stat[0:1, :], ctx.ones_col_f32[:, 0:1],
                      sqt, start=(kc == 0), stop=(kc == KC - 1)),
                  inc=s_mm)
        ph.wait("scalar", s_mm)
        ph.op("scalar",
              lambda e, c0=c0: e.activation(
                  rrow[0:1, c0:c0 + TT], ps_stat[0:1, :], AF.Sqrt,
                  bias=eps_ap, scale=1.0 / D),
              inc=s_ev)
        ph.wait("vector", s_ev)
        ph.op("vector",
              lambda e, c0=c0: e.reciprocal(rrow[0:1, c0:c0 + TT], rrow[0:1, c0:c0 + TT]),
              inc=s_ev)
        ph.wait("tensor", s_ev)


def build_l1attn(ctx, ar, w, cc, sc, x_spill, dbg_hook=None):
    nc = ctx.nc
    s_ld = ctx.sem("m_ld")
    s_gld = ctx.sem("m_gld")
    s_cv = ctx.sem("m_cv")
    s_mm = ctx.sem("m_mm")
    s_ev = ctx.sem("m_ev")
    s_cc = ctx.sem("m_cc")

    ybf = sc.sbuf("m_ybf", [128, KC * T], BF16)
    mbuf = sc.sbuf("m_mbuf", [128, 8 * T], BF16)
    stg = sc.sbuf("m_stg", [128, 4096], F32)
    wb = sc.sbuf("m_wb", [128, 2 * 4096], BF16)
    rrow = sc.sbuf("m_rrow", [1, T], F32)
    bxs = sc.sbuf("m_bxs", [128, T], F32)
    yss = sc.sbuf("m_yss", [128, T], F32)
    scb = sc.sbuf("m_scb", [128, TT], F32)
    av = sc.sbuf("m_av", [128, 8 * 4], F32)
    lc = sc.sbuf("m_lc", [128, 8], F32)
    lg = sc.sbuf("m_lg", [128, 64], F32)

    psA = sc.psum("m_psA", [128, TT])
    psB = sc.psum("m_psB", [128, TT])
    psS = sc.psum("m_psS", [1, TT])

    atv = av[:, 0:8]
    a2k = av[:, 8:16]
    cac = av[:, 16:24]
    cse = av[:, 24:32]

    vcol = lambda base, i: ctx.vecs[:, base + i: base + i + 1]

    ph = Ph(ctx, "m_rms1")
    ph.op("scalar", lambda e: e.activation(
        atv, ctx.vecs[:, w["a"]:w["a"] + 8], AF.Tanh, bias=0.0, scale=1.0),
        inc=s_ev)
    ph.wait("vector", s_ev)
    ph.op("vector", lambda e: e.tensor_mul(a2k, atv, atv), inc=s_ev)
    for i in range(10):
        src_b, dst_b = (a2k, cac) if i % 2 == 0 else (cac, a2k)
        ph.wait("vector", s_ev)
        ph.op("vector", lambda e, s=src_b, d=dst_b: e.tensor_mul(d, s, s), inc=s_ev)
    ph.wait("vector", s_ev)
    ph.op("vector", lambda e: e.memset(cse, 0.0), inc=s_ev)
    ph.wait("vector", s_ev)
    ph.op("vector", lambda e: e.memset(cac, 0.0), inc=s_ev)
    emit_rms_rows(ph, ctx, src_ar=ar, rrow=rrow, sq=stg, ps_stat=psS, ps_b=None,
                  s_mm=s_mm, s_ev=s_ev, eps_ap=ctx.vecs[0:1, w["keps5"]:w["keps5"] + 1])
    ph.run()

    def emit_bx(ph, kc):
        for tt in range(NTT):
            ph.op("tensor",
                  lambda e, tt=tt: e.matmul(
                      psB[:], ctx.ones_row_f32[0:1, :], rrow[0:1, tt * TT:(tt + 1) * TT],
                      start=True, stop=True),
                  inc=s_mm)
            ph.wait("scalar", s_mm)
            ph.op("scalar", lambda e: e.copy(scb[:, :], psB[:]), inc=s_ev)
            ph.wait("vector", s_ev)
            ph.op("vector",
                  lambda e, kc=kc, tt=tt: e.scalar_tensor_tensor(
                      bxs[:, tt * TT:(tt + 1) * TT],
                      ar[:, kc * T + tt * TT: kc * T + (tt + 1) * TT],
                      vcol(w["n1wb"], kc), scb[:, :], ALU.mult, ALU.mult),
                  inc=s_ev)
            ph.wait("tensor", s_ev)

    for kc in range(KC):
        ph = Ph(ctx, f"m_scan1_{kc}")
        if kc > 0:
            ph.op("scalar",
                  lambda e, kc=kc: e.copy(lc[:, kc - 1:kc], yss[:, T - 1:T]),
                  inc=s_ev)
            ph.wait("vector", s_ev)
        emit_bx(ph, kc)
        ph.op("vector",
              lambda e, kc=kc: e.tensor_tensor_scan(
                  yss[:, :], atv[:, kc:kc + 1].broadcast_to([128, T]), bxs[:, :],
                  0.0, ALU.mult, ALU.add))
        ph.run()
    ph = Ph(ctx, "m_scan1f")
    ph.op("scalar",
          lambda e: e.copy(lc[:, KC - 1:KC], yss[:, T - 1:T]),
          inc=s_ev)
    ph.wait("gpsimd", s_ev)
    ph.dma("gpsimd", s_gld, lambda e: e.dma_start(out=cc["ag_in"], in_=lc[:, :]))
    ph.wait("gpsimd", s_gld)
    ph.op("gpsimd", lambda e: e.collective_compute(
        "AllGather", ALU.bypass, replica_groups=[list(range(NCORES))],
        ins=[cc["ag_in"].opt()], outs=[cc["ag_out"].opt()]), inc=s_cc)
    ph.wait("gpsimd", s_cc)
    ph.dma("gpsimd", s_gld,
           lambda e: e.dma_start(out=lg[:, :].rearrange("p (r k) -> p r k", r=NCORES),
                                 in_=cc["ag_out"].rearrange("(r p) k -> p r k", p=128)))
    ph.run()

    ph = Ph(ctx, "m_carry")
    ph.wait("vector", s_gld)
    for j in range(NCORES):
        ph.op("vector",
              lambda e, j=j: e.scalar_tensor_tensor(
                  cse, cac, ctx.coreoh[:, j:j + 1], cse, ALU.mult, ALU.add),
              inc=s_ev)
        ph.wait("vector", s_ev)
        if j < NCORES - 1:
            ph.op("vector",
                  lambda e, j=j: e.scalar_tensor_tensor(
                      cac, cac, ctx.vecs[:, w["kone"]:w["kone"] + 1], a2k,
                      ALU.bypass, ALU.mult),
                  inc=s_ev)
            ph.wait("vector", s_ev)
            ph.op("vector",
                  lambda e, j=j: e.tensor_add(cac, cac, lg[:, j * 8:(j + 1) * 8]),
                  inc=s_ev)
            ph.wait("vector", s_ev)
    ph.run()

    if dbg_hook is not None:
        dbg_hook({"d_lc": lc, "d_lg": lg, "d_av": av, "d_rrow": rrow,
                  "d_bxs": bxs, "d_yss": yss})
    for kc in range(KC):
        ph = Ph(ctx, f"m_scan2_{kc}")
        if kc > 0:
            ph.op("scalar",
                  lambda e, kc=kc: e.copy(ybf[:, (kc - 1) * T: kc * T], yss[:, :]),
                  inc=s_ev)
            ph.wait("vector", s_ev)
        emit_bx(ph, kc)
        ph.op("vector",
              lambda e, kc=kc: e.scalar_tensor_tensor(
                  bxs[:, 0:1], cse[:, kc:kc + 1], atv[:, kc:kc + 1],
                  bxs[:, 0:1], ALU.mult, ALU.add),
              inc=s_ev)
        ph.wait("vector", s_ev)
        ph.op("vector",
              lambda e, kc=kc: e.tensor_tensor_scan(
                  yss[:, :], atv[:, kc:kc + 1].broadcast_to([128, T]), bxs[:, :],
                  0.0, ALU.mult, ALU.add))
        ph.run()
    ph = Ph(ctx, "m_scan2f")
    ph.op("scalar",
          lambda e: e.copy(ybf[:, (KC - 1) * T: KC * T], yss[:, :]),
          inc=s_ev)
    ph.run()

    s_sp = ctx.sem("m_sp")
    ph = Ph(ctx, "m_proj")
    ph.dma("sync", s_sp, lambda e: e.dma_start(out=x_spill, in_=ar[:, :]))
    for h in range(2):
        n0 = h * 512
        if h == 1:
            ph.wait("sync", s_mm)
        for qtr in range(2):
            if qtr == 1 or h == 1:
                ph.wait("sync", s_cv)
            ph.dma("sync", s_ld,
                   lambda e, n0=n0, qtr=qtr: e.dma_start(
                       out=stg[:, 0:2048].rearrange("p (k n) -> p k n", k=8),
                       in_=w["out_wT"][:, n0 + qtr * 256: n0 + qtr * 256 + 256].rearrange(
                           "(k p) n -> p k n", p=128)))
            ph.wait("scalar", s_ld)
            for k in range(KC):
                ph.op("scalar",
                      lambda e, k=k, qtr=qtr: e.copy(
                          wb[:, k * 512 + qtr * 256: k * 512 + qtr * 256 + 256],
                          stg[:, k * 256:(k + 1) * 256]),
                      inc=(s_cv if k == KC - 1 else None))
        ph.wait("tensor", s_cv)
        if h == 0:
            ph.wait("vector", s_sp)
        for nch in range(4):
            nabs = h * 4 + nch
            for tt in range(NTT):
                ps = psA if (nabs * NTT + tt) % 2 == 0 else psB
                ph.wait("tensor", s_ev, ctx.cnt[s_ev] - 1)
                for kc in range(KC):
                    ph.op("tensor",
                          lambda e, kc=kc, nch=nch, tt=tt, ps=ps: e.matmul(
                              ps[:], wb[:, kc * 512 + nch * 128: kc * 512 + (nch + 1) * 128],
                              ybf[:, kc * T + tt * TT: kc * T + (tt + 1) * TT],
                              start=(kc == 0), stop=(kc == KC - 1)),
                          inc=(s_mm if kc == KC - 1 else None))
                ph.wait("vector", s_mm)
                c = nabs * T + tt * TT
                ph.op("vector",
                      lambda e, ps=ps, nabs=nabs, c=c: e.scalar_tensor_tensor(
                          ar[:, c:c + TT], ps[:], vcol(w["out_b"], nabs),
                          ar[:, c:c + TT], ALU.add, ALU.add),
                      inc=s_ev)
    ph.run()

    ph = Ph(ctx, "m_rms2")
    emit_rms_rows(ph, ctx, src_ar=ar, rrow=rrow, sq=stg, ps_stat=psS, ps_b=None,
                  s_mm=s_mm, s_ev=s_ev, eps_ap=ctx.vecs[0:1, w["keps5"]:w["keps5"] + 1])
    for kc in range(KC):
        for tt in range(NTT):
            ph.op("tensor",
                  lambda e, tt=tt: e.matmul(
                      psB[:], ctx.ones_row_f32[0:1, :], rrow[0:1, tt * TT:(tt + 1) * TT],
                      start=True, stop=True),
                  inc=s_mm)
            ph.wait("scalar", s_mm)
            ph.op("scalar", lambda e: e.copy(scb[:, :], psB[:]), inc=s_ev)
            ph.wait("vector", s_ev)
            ph.op("vector",
                  lambda e, kc=kc, tt=tt: e.scalar_tensor_tensor(
                      ybf[:, kc * T + tt * TT: kc * T + (tt + 1) * TT],
                      ar[:, kc * T + tt * TT: kc * T + (tt + 1) * TT],
                      vcol(w["n2w"], kc), scb[:, :], ALU.mult, ALU.mult),
                  inc=s_ev)
            ph.wait("tensor", s_ev)
    ph.run()

    HC = 32
    for grp in range(4):
        ph = Ph(ctx, f"m_mlpA{grp}")
        cv0 = ctx.cnt[s_cv]
        for hh in range(8):
            hch = grp * 8 + hh
            par = hh % 2
            if hh >= 1:
                ph.wait("sync", s_cv, cv0 + hh)
            ph.dma("sync", s_ld,
                   lambda e, hch=hch: e.dma_start(
                       out=stg[:, 0:1024].rearrange("p (kc n) -> p kc n", kc=8),
                       in_=w["w2T"][:, hch * 128:(hch + 1) * 128].rearrange(
                           "(kc p) n -> p kc n", p=128)))
            ph.dma("sync", s_ld,
                   lambda e, hch=hch: e.dma_start(
                       out=stg[:, 1024:2048].rearrange("p (kc n) -> p kc n", kc=8),
                       in_=w["w1T"][:, hch * 128:(hch + 1) * 128].rearrange(
                           "(kc p) n -> p kc n", p=128)))
            ph.wait("scalar", s_ld)
            if hh >= 2:
                ph.wait("scalar", s_mm, ctx.cnt[s_mm] - NTT)
            ph.op("scalar",
                  lambda e, par=par: e.copy(
                      wb[:, par * 2048: par * 2048 + 2048], stg[:, 0:2048]),
                  inc=s_cv)
            ph.wait("tensor", s_cv)
            for tt in range(NTT):
                ph.wait("tensor", s_ev, ctx.cnt[s_ev] - 1)
                for kc in range(KC):
                    ph.op("tensor",
                          lambda e, kc=kc, tt=tt, par=par: e.matmul(
                              psA[:], wb[:, par * 2048 + kc * 128: par * 2048 + (kc + 1) * 128],
                              ybf[:, kc * T + tt * TT: kc * T + (tt + 1) * TT],
                              start=(kc == 0), stop=(kc == KC - 1)))
                for kc in range(KC):
                    ph.op("tensor",
                          lambda e, kc=kc, tt=tt, par=par: e.matmul(
                              psB[:], wb[:, par * 2048 + 1024 + kc * 128: par * 2048 + 1024 + (kc + 1) * 128],
                              ybf[:, kc * T + tt * TT: kc * T + (tt + 1) * TT],
                              start=(kc == 0), stop=(kc == KC - 1)),
                          inc=(s_mm if kc == KC - 1 else None))
                ph.wait("scalar", s_mm)
                ph.op("scalar",
                      lambda e, hch=hch: e.activation(
                          scb[:, :], psA[:], AF.Silu, bias=vcol(w["w2b"], hch), scale=1.0),
                      inc=s_ev)
                ph.wait("vector", s_ev)
                ph.op("vector",
                      lambda e, hh=hh, tt=tt, hch=hch: e.scalar_tensor_tensor(
                          mbuf[:, hh * T + tt * TT: hh * T + (tt + 1) * TT],
                          psB[:], vcol(w["w1b"], hch), scb[:, :], ALU.add, ALU.mult),
                      inc=s_ev)
        ph.run()

        ph = Ph(ctx, f"m_mlpB{grp}")
        cv0 = ctx.cnt[s_cv]
        for dch in range(KC):
            par = dch % 2
            if dch >= 1:
                ph.wait("sync", s_cv, cv0 + dch)
            ph.dma("sync", s_ld,
                   lambda e, grp=grp, dch=dch: e.dma_start(
                       out=stg[:, 0:1024].rearrange("p (kc n) -> p kc n", kc=8),
                       in_=w["w3T"][grp * 1024:(grp + 1) * 1024, dch * 128:(dch + 1) * 128].rearrange(
                           "(kc p) n -> p kc n", p=128)))
            ph.wait("scalar", s_ld)
            if dch >= 2:
                ph.wait("scalar", s_mm, ctx.cnt[s_mm] - NTT)
            ph.op("scalar",
                  lambda e, par=par: e.copy(
                      wb[:, par * 1024: par * 1024 + 1024], stg[:, 0:1024]),
                  inc=s_cv)
            ph.wait("tensor", s_cv)
            for tt in range(NTT):
                ps = psA if (dch * NTT + tt) % 2 == 0 else psB
                ph.wait("tensor", s_ev, ctx.cnt[s_ev] - 1)
                for kh in range(8):
                    ph.op("tensor",
                          lambda e, kh=kh, tt=tt, par=par, ps=ps: e.matmul(
                              ps[:], wb[:, par * 1024 + kh * 128: par * 1024 + (kh + 1) * 128],
                              mbuf[:, kh * T + tt * TT: kh * T + (tt + 1) * TT],
                              start=(kh == 0), stop=(kh == 7)),
                          inc=(s_mm if kh == 7 else None))
                ph.wait("vector", s_mm)
                c = dch * T + tt * TT
                bcol = w["w3b"] if grp == 0 else w["kzero"]
                ph.op("vector",
                      lambda e, ps=ps, dch=dch, c=c, bcol=bcol: e.scalar_tensor_tensor(
                          ar[:, c:c + TT], ps[:], vcol(bcol, dch if bcol == w["w3b"] else 0),
                          ar[:, c:c + TT], ALU.add, ALU.add),
                      inc=s_ev)
        ph.run()

    ph = Ph(ctx, "m_fin")
    for kc in range(KC):
        for hf in range(2):
            ph.wait("sync", s_ev)
            ph.dma("sync", s_ld,
                   lambda e, kc=kc, hf=hf: e.dma_start(
                       out=stg[:, 0:1024],
                       in_=x_spill[:, kc * T + hf * 1024: kc * T + (hf + 1) * 1024]))
            ph.wait("vector", s_ld)
            for tt in range(2):
                c = kc * T + hf * 1024 + tt * TT
                ph.op("vector",
                      lambda e, c=c, tt=tt: e.tensor_add(
                          ar[:, c:c + TT], ar[:, c:c + TT], stg[:, tt * TT:(tt + 1) * TT]),
                      inc=s_ev)
    ph.run()

    ph = Ph(ctx, "m_ln")
    emit_postln(ph, ar=ar, sq=stg, ps_stat=[psA, psB], ps_b=[psA, psB],
                sem_mm=s_mm, sem_ev=s_ev,
                eps_ap=ctx.vecs[0:1, w["keps"]:w["keps"] + 1])
    ph.run()


def l1attn_weights_host(inp, vb):
    w = {
        "out_wT": _t(inp["L1_attn_out_w"]),
        "w1T": _t(inp["L1_attn_w1_w"]),
        "w2T": _t(inp["L1_attn_w2_w"]),
        "w3T": _t(inp["L1_attn_w3_w"]),
    }
    n1wb = np.asarray(inp["L1_attn_n1_w"], np.float32) * np.asarray(inp["L1_attn_b"], np.float32)
    idx = {
        "a": vb.add("m_a", _chunkvec(inp["L1_attn_a"])),
        "n1wb": vb.add("m_n1wb", _chunkvec(n1wb)),
        "n2w": vb.add("m_n2w", _chunkvec(inp["L1_attn_n2_w"])),
        "out_b": vb.add("m_outb", _chunkvec(inp["L1_attn_out_b"])),
        "w1b": vb.add("m_w1b", _chunkvec(inp["L1_attn_w1_b"])),
        "w2b": vb.add("m_w2b", _chunkvec(inp["L1_attn_w2_b"])),
        "w3b": vb.add("m_w3b", _chunkvec(inp["L1_attn_w3_b"])),
        "kzero": vb.add("k_zero", np.zeros((128, 1), np.float32)),
        "kone": vb.add("k_one1", np.full((128, 1), 1.0, np.float32)),
        "keps5": vb.add("k_eps5", np.full((128, 1), 1e-5, np.float32)),
        "keps": vb.add("k_eps_ln1", np.full((128, 1), 1e-5, np.float32)),
    }
    return w, idx


def make_l1_cc(nc):
    return {
        "ag_in": nc.dram_tensor("cc_agi", [128, 8], F32).ap(),
        "ag_out": nc.dram_tensor("cc_ago", [NCORES * 128, 8], F32).ap(),
    }


def build_l1attn_program(nblob, vec_idx, debug=False):
    nc = bass.Bass("TRN2", target_bir_lowering=False, detect_race_conditions=False)
    ctx = Ctx(nc)
    x_in = nc.dram_tensor("x_in", [T, D], F32, kind="ExternalInput").ap()
    y_out = nc.dram_tensor("y_out", [T, D], F32, kind="ExternalOutput").ap()
    consts = nc.dram_tensor("consts", [130, 128], F32, kind="ExternalInput").ap()
    coreoh = nc.dram_tensor("coreoh", [128, 8], F32, kind="ExternalInput").ap()
    vecblob = nc.dram_tensor("vecblob", [128, nblob], F32, kind="ExternalInput").ap()
    w = {"out_wT": nc.dram_tensor("out_wT", [1024, 1024], F32, kind="ExternalInput").ap(),
         "w1T": nc.dram_tensor("w1T", [1024, 4096], F32, kind="ExternalInput").ap(),
         "w2T": nc.dram_tensor("w2T", [1024, 4096], F32, kind="ExternalInput").ap(),
         "w3T": nc.dram_tensor("w3T", [4096, 1024], F32, kind="ExternalInput").ap()}
    w.update(vec_idx)
    cc = make_l1_cc(nc)
    x_spill = nc.dram_tensor("x_spill", [128, KC * T], F32).ap()

    common_setup(ctx, nblob)
    ctx.coreoh = ctx.sbuf("c_coreoh", [128, 8], F32)
    ar = ctx.sbuf("ar", [128, KC * T], F32)

    s_ld = [ctx.sem("g_ld0"), ctx.sem("g_ld1")]
    s_cst = ctx.sem("g_cst")
    s_tp = ctx.sem("g_tp")
    s_ev = ctx.sem("g_ev")
    s_st = [ctx.sem("g_st0"), ctx.sem("g_st1")]

    with ctx.scope() as sc:
        tmp_tm = [sc.sbuf(f"tin{i}", [128, D], F32) for i in range(2)]
        ps_t = [sc.psum(f"g_ps{i}", [128, TT]) for i in range(2)]
        ph = Ph(ctx, "in")
        emit_load_consts(ph, ctx, consts, None, vecblob, s_cst)
        ph.dma("sync", s_cst, lambda e: e.dma_start(out=ctx.coreoh[:], in_=coreoh))
        ph.wait("vector", s_cst)
        emit_transpose_in(ph, hbm_tm=x_in, ar=ar, sem_ld=s_ld, sem_tp=s_tp,
                          sem_ev=s_ev, ps_t=ps_t, tmp_tm=tmp_tm)
        ph.run()

    hook = None
    if debug:
        shapes = {"d_lc": [128, 8], "d_lg": [128, 64], "d_av": [128, 32], "d_rrow": [1, T],
                  "d_bxs": [128, T], "d_yss": [128, T]}
        douts = {nm: nc.dram_tensor(nm, shp, F32, kind="ExternalOutput").ap()
                 for nm, shp in shapes.items()}
        s_dbg = ctx.sem("dbg_st")

        def hook(tens):
            ph = Ph(ctx, "dbg")
            for nm, t in tens.items():
                ph.dma("sync", s_dbg, lambda e, t=t, nm=nm: e.dma_start(out=douts[nm], in_=t[:, :] if nm != "d_rrow" else t[0:1, :]))
            ph.wait("gpsimd", s_dbg)
            ph.run()

    with ctx.scope() as sc:
        build_l1attn(ctx, ar, w, cc, sc, x_spill, dbg_hook=hook)

    with ctx.scope() as sc:
        tmp_tm = [sc.sbuf(f"tout{i}", [128, D], F32) for i in range(2)]
        ps_t = [sc.psum(f"g_pso{i}", [128, TT]) for i in range(2)]
        ph = Ph(ctx, "out")
        emit_transpose_out(ph, ar=ar, hbm_tm=y_out, sem_tp=s_tp, sem_ev=s_ev,
                           sem_st=s_st, ps_t=ps_t, tmp_tm=tmp_tm)
        ph.wait("gpsimd", s_st[0])
        ph.wait("gpsimd", s_st[1])
        ph.run()

    ctx.close()
    return nc


def build_l0ffn(ctx, ar, w, sc, x_spill, rw_hbm):
    nc = ctx.nc
    s_ld = ctx.sem("f_ld")
    s_cv = ctx.sem("f_cv")
    s_mm = ctx.sem("f_mm")
    s_ev = ctx.sem("f_ev")
    s_sp = ctx.sem("f_sp")
    s_rw = ctx.sem("f_rw")
    s_rw2 = ctx.sem("f_rw2")
    s_hp = ctx.sem("f_hp")

    x_bf = sc.sbuf("f_xbf", [128, KC * T], BF16)
    routed = ar

    psA = sc.psum("f_psA", [128, TT])
    psB = sc.psum("f_psB", [128, TT])
    psR = sc.psum("f_psR", [128, TT])
    psD = sc.psum("f_psD", [128, TT])
    psA2 = sc.psum("f_psA2", [128, TT])
    psB2 = sc.psum("f_psB2", [128, TT])
    psD2 = sc.psum("f_psD2", [128, TT])

    NCH = [(0, 128), (128, 128), (256, 128), (384, 128), (512, 128), (640, 64)]
    kone = ctx.vecs[:, w["kone"]:w["kone"] + 1]
    kzero = ctx.vecs[:, w["kzero"]:w["kzero"] + 1]

    with ctx.scope() as rsc:
        sco_fm = rsc.sbuf("f_sco", [64, T], F32)
        sfc_fm = rsc.sbuf("f_sfc", [64, T], F32)
        sco_tm = rsc.sbuf("f_scot", [128, 64], F32)
        sfc_tm = rsc.sbuf("f_sfct", [128, 64], F32)
        wrk = rsc.sbuf("f_wrk", [128, 64], F32)
        sel = rsc.sbuf("f_sel", [128, 64], F32)
        gsc = rsc.sbuf("f_gsc", [128, 8], F32)
        gmk = rsc.sbuf("f_gmk", [128, 8], F32)
        rwt = rsc.sbuf("f_rwt", [128, 64], F32)
        m1 = rsc.sbuf("f_m1", [128, 4], F32)
        gwtb = rsc.sbuf("f_gwtb", [128, 8 * 64], BF16)
        stg = rsc.sbuf("f_gstg", [128, 512], F32)
        psT = rsc.psum("f_psT", [128, 128])

        ph = Ph(ctx, "f_route")

        def vop(build):
            ph.op("vector", build, inc=s_ev)
            ph.wait("vector", s_ev)

        for kc in range(KC):
            ph.op("vector",
                  lambda e, kc=kc: e.tensor_copy(
                      x_bf[:, kc * T:(kc + 1) * T], ar[:, kc * T:(kc + 1) * T]),
                  inc=(s_cv if kc == KC - 1 else None))
        ph.wait("sync", s_cv)
        ph.dma("sync", s_sp, lambda e: e.dma_start(out=x_spill, in_=ar[:, :]))
        ph.wait("gpsimd", s_sp)
        ph.op("gpsimd", lambda e: e.memset(routed[:, :], 0.0), inc=s_cv)
        ph.dma("sync", s_ld,
               lambda e: e.dma_start(
                   out=stg[:, 0:512].rearrange("p (kc n) -> p kc n", kc=8),
                   in_=w["gate_wT"].rearrange("(kc p) n -> p kc n", p=128)))
        ph.wait("gpsimd", s_ld)
        ph.op("gpsimd", lambda e: e.tensor_copy(gwtb[:, :], stg[:, 0:512]), inc=s_cv)
        ph.wait("tensor", s_cv)
        for tt in range(NTT):
            if tt > 0:
                ph.wait("tensor", s_ev)
            for kc in range(KC):
                ph.op("tensor",
                      lambda e, kc=kc, tt=tt: e.matmul(
                          psA[0:64, 0:TT], gwtb[:, kc * 64:(kc + 1) * 64],
                          x_bf[:, kc * T + tt * TT: kc * T + (tt + 1) * TT],
                          start=(kc == 0), stop=(kc == KC - 1)),
                      inc=(s_mm if kc == KC - 1 else None))
            ph.wait("scalar", s_mm)
            ph.op("scalar",
                  lambda e, tt=tt: e.activation(
                      sco_fm[0:64, tt * TT:(tt + 1) * TT], psA[0:64, 0:TT],
                      AF.Sigmoid, bias=0.0, scale=1.0),
                  inc=s_ev)
        ph.wait("vector", s_ev)
        ph.op("vector", lambda e: e.tensor_scalar_add(
            sfc_fm[0:64, :], sco_fm[0:64, :], ctx.vecs[0:64, w["gbias"]:w["gbias"] + 1]),
            inc=s_ev)
        for ti in range(T // 128):
            ph.wait("tensor", s_ev)
            ph.op("tensor",
                  lambda e, ti=ti: e.transpose(
                      psT[:, 0:64], sco_fm[0:64, ti * 128:(ti + 1) * 128],
                      ctx.ident_f32[0:64, 0:64]),
                  inc=s_mm)
            ph.wait("vector", s_mm)
            vop(lambda e: e.tensor_copy(sco_tm[:, :], psT[:, 0:64]))
            ph.wait("tensor", s_ev)
            ph.op("tensor",
                  lambda e, ti=ti: e.transpose(
                      psT[:, 0:64], sfc_fm[0:64, ti * 128:(ti + 1) * 128],
                      ctx.ident_f32[0:64, 0:64]),
                  inc=s_mm)
            ph.wait("vector", s_mm)
            vop(lambda e: e.tensor_copy(sfc_tm[:, :], psT[:, 0:64]))
            for g in range(8):
                vop(lambda e, g=g: e.reduce_max(m1[:, 0:1], sfc_tm[:, g * 8:(g + 1) * 8], axis=AX.X))
                vop(lambda e, g=g: e.tensor_scalar(
                    wrk[:, 0:8], sfc_tm[:, g * 8:(g + 1) * 8], m1[:, 0:1], None, ALU.is_ge))
                vop(lambda e, g=g: e.scalar_tensor_tensor(
                    wrk[:, 8:16], wrk[:, 0:8], -1e30, sfc_tm[:, g * 8:(g + 1) * 8],
                    ALU.mult, ALU.add))
                vop(lambda e: e.reduce_max(m1[:, 1:2], wrk[:, 8:16], axis=AX.X))
                vop(lambda e, g=g: e.tensor_add(gsc[:, g:g + 1], m1[:, 0:1], m1[:, 1:2]))
            vop(lambda e: e.memset(gmk[:, :], 0.0))
            vop(lambda e: e.tensor_copy(wrk[:, 0:8], gsc[:, :]))
            for _ in range(4):
                vop(lambda e: e.reduce_max(m1[:, 0:1], wrk[:, 0:8], axis=AX.X))
                vop(lambda e: e.tensor_scalar(
                    wrk[:, 8:16], wrk[:, 0:8], m1[:, 0:1], None, ALU.is_ge))
                vop(lambda e: e.tensor_add(gmk[:, :], gmk[:, :], wrk[:, 8:16]))
                vop(lambda e: e.scalar_tensor_tensor(
                    wrk[:, 0:8], wrk[:, 8:16], -1e30, wrk[:, 0:8], ALU.mult, ALU.add))
            vop(lambda e: e.tensor_copy(
                wrk[:, :].rearrange("p (g x) -> p g x", g=8),
                gmk[:, :].rearrange("p (g o) -> p g o", o=1).broadcast_to([128, 8, 8])))
            vop(lambda e: e.tensor_mul(wrk[:, :], wrk[:, :], sfc_tm[:, :]))
            vop(lambda e: e.memset(sel[:, :], 0.0))
            for _ in range(8):
                vop(lambda e: e.reduce_max(m1[:, 0:1], wrk[:, :], axis=AX.X))
                vop(lambda e: e.tensor_scalar(
                    rwt[:, :], wrk[:, :], m1[:, 0:1], None, ALU.is_ge))
                vop(lambda e: e.tensor_add(sel[:, :], sel[:, :], rwt[:, :]))
                vop(lambda e: e.scalar_tensor_tensor(
                    wrk[:, :], rwt[:, :], -1e30, wrk[:, :], ALU.mult, ALU.add))
            vop(lambda e: e.tensor_mul(rwt[:, :], sel[:, :], sco_tm[:, :]))
            vop(lambda e: e.reduce_sum(m1[:, 2:3], rwt[:, :], axis=AX.X))
            vop(lambda e: e.tensor_scalar_add(m1[:, 2:3], m1[:, 2:3], 1e-20))
            vop(lambda e: e.reciprocal(m1[:, 2:3], m1[:, 2:3]))
            vop(lambda e: e.tensor_scalar(
                rwt[:, :], rwt[:, :], m1[:, 2:3], 2.5, ALU.mult, ALU.mult))
            ph.wait("tensor", s_ev)
            ph.op("tensor",
                  lambda e: e.transpose(psT[0:64, 0:128], rwt[:, 0:64], ctx.ident_f32[:]),
                  inc=s_mm)
            ph.wait("scalar", s_mm)
            ph.op("scalar",
                  lambda e, ti=ti: e.copy(
                      sco_fm[0:64, ti * 128:(ti + 1) * 128], psT[0:64, 0:128]),
                  inc=s_ev)
        ph.wait("vector", s_ev)
        ph.op("vector", lambda e: e.tensor_copy(
            sfc_fm[0:64, :].bitcast(BF16)[0:64, 0:T], sco_fm[0:64, :]), inc=s_ev)
        ph.wait("sync", s_ev)
        ph.dma("sync", s_rw, lambda e: e.dma_start(
            out=rw_hbm, in_=sfc_fm[0:64, :].bitcast(BF16)[0:64, 0:T]))
        ph.wait("gpsimd", s_rw)
        ph.run()

    gwb = [sc.sbuf(f"f_gw{i}", [128, 8 * 704], BF16) for i in range(2)]
    uwb = [sc.sbuf(f"f_uw{i}", [128, 8 * 704], BF16) for i in range(2)]
    dwb = [sc.sbuf(f"f_dw{i}", [128, 6 * 1024], BF16) for i in range(2)]
    stg = sc.sbuf("f_stg", [128, 2048], F32)
    sgt = sc.sbuf("f_sg", [128, 2 * TT], BF16)
    ht = sc.sbuf("f_h", [128, 12 * TT], BF16)
    rwrow = sc.sbuf("f_rwrow", [65, T], BF16)
    rwb = sc.sbuf("f_rwb", [128, TT], BF16)
    NEXP = 66

    def wsrc(e):
        if e < 64:
            return (w["WgT"][e], w["WuT"][e], w["WdT"][e])
        half = e - 64
        return (w["SgT"][:, half * 704:(half + 1) * 704],
                w["SuT"][:, half * 704:(half + 1) * 704],
                w["SdT"][half * 704:(half + 1) * 704, :])

    def wload_jobs(e, par):
        gsrc, usrc, dsrc = wsrc(e)
        jobs = []
        for mat, dst, nc_ in ((gsrc, gwb[par], 704), (usrc, uwb[par], 704)):
            for k0 in (0, 2, 4, 6):
                jobs.append((mat, dst, k0, 2, nc_, 128))
        jobs.append((dsrc, dwb[par], 0, 2, 1024, 128))
        jobs.append((dsrc, dwb[par], 2, 2, 1024, 128))
        jobs.append((dsrc, dwb[par], 4, 1, 1024, 128))
        jobs.append((dsrc, dwb[par], 5, 1, 1024, 64))
        return jobs

    def emit_wjob(ph, job):
        mat, dst, k0, kn, nc_, prt = job
        ph.wait("sync", s_cv)
        if prt == 128:
            ph.dma("sync", s_ld,
                   lambda e_, mat=mat, k0=k0, kn=kn, nc_=nc_: e_.dma_start(
                       out=stg[:, 0:kn * nc_].rearrange("p (k n) -> p k n", k=kn),
                       in_=mat[k0 * 128:(k0 + kn) * 128, :].rearrange(
                           "(k p) n -> p k n", p=128)))
        else:
            ph.dma("sync", s_ld,
                   lambda e_, mat=mat, k0=k0, nc_=nc_: e_.dma_start(
                       out=stg[0:64, 0:nc_], in_=mat[k0 * 128: k0 * 128 + 64, :]))
        ph.wait("scalar", s_ld)
        if prt == 128:
            ph.op("scalar",
                  lambda e_, dst=dst, k0=k0, kn=kn, nc_=nc_: e_.copy(
                      dst[:, k0 * nc_:(k0 + kn) * nc_], stg[:, 0:kn * nc_]),
                  inc=s_cv)
        else:
            ph.op("scalar",
                  lambda e_, dst=dst, k0=k0, nc_=nc_: e_.copy(
                      dst[0:64, k0 * nc_:(k0 + 1) * nc_], stg[0:64, 0:nc_]),
                  inc=s_cv)

    def emit_wload(ph, e, par):
        for j in wload_jobs(e, par):
            emit_wjob(ph, j)

    conv_done = {}
    dn_prev = {}
    bc_mark = {}
    ph = Ph(ctx, "f_experts")
    for e in range(NEXP):
        par = e % 2
        if e == 0:
            emit_wload(ph, 0, 0)
            conv_done[0] = ctx.cnt[s_cv]
        is_shared = e >= 64
        if not is_shared:
            if e >= 2 and (e - 2) in bc_mark:
                ph.wait("sync", s_mm, bc_mark[e - 2])
            ph.dma("sync", s_rw if par == 0 else s_rw2,
                   lambda e_, e=e, par=par: e_.dma_start(
                       out=rwrow[par * 64: par * 64 + 1, :], in_=rw_hbm[e:e + 1, :]))
        pjobs = []
        if e + 1 < NEXP:
            ph.wait("scalar", s_mm)
            pjobs = wload_jobs(e + 1, 1 - par)
            for j in pjobs[0:2]:
                emit_wjob(ph, j)
        if not is_shared:
            ph.wait("tensor", s_rw if par == 0 else s_rw2)
        ph.wait("tensor", s_cv, conv_done[e])
        nev = 2
        dn_info = {}

        def emit_gu(tt):
            hbase = (tt % 2) * 6 * TT
            for j in pjobs[2 + tt * 3: 5 + tt * 3]:
                emit_wjob(ph, j)
            if not is_shared:
                ph.wait("tensor", s_ev)
                ph.op("tensor",
                      lambda e_, tt=tt, par=par: e_.matmul(
                          psR[:], ctx.ones_row65[par * 64: par * 64 + 1, :],
                          rwrow[par * 64: par * 64 + 1, tt * TT:(tt + 1) * TT],
                          start=True, stop=True),
                      inc=s_mm)
                if tt == NTT - 1:
                    bc_mark[e] = ctx.cnt[s_mm]
                ph.wait("scalar", s_mm)
                ph.wait("scalar", s_hp)
                ph.op("scalar", lambda e_: e_.copy(rwb[:, :], psR[:]), inc=s_ev)
            ph.wait("vector", s_mm)
            if not is_shared:
                ph.wait("gpsimd", s_mm)
            for ci, (c0, cw) in enumerate(NCH):
                pg = psA if ci % 2 == 0 else psA2
                pu = psB if ci % 2 == 0 else psB2
                sg = sgt[0:cw, (ci % 2) * TT:(ci % 2) * TT + TT]
                ph.wait("tensor", s_ev, ctx.cnt[s_ev] - nev)
                for kc in range(KC):
                    ph.op("tensor",
                          lambda e_, kc=kc, tt=tt, c0=c0, cw=cw, par=par, pg=pg: e_.matmul(
                              pg[0:cw, 0:TT], gwb[par][:, kc * 704 + c0: kc * 704 + c0 + cw],
                              x_bf[:, kc * T + tt * TT: kc * T + (tt + 1) * TT],
                              start=(kc == 0), stop=(kc == KC - 1)),
                          inc=(s_mm if kc == KC - 1 else None))
                for kc in range(KC):
                    ph.op("tensor",
                          lambda e_, kc=kc, tt=tt, c0=c0, cw=cw, par=par, pu=pu: e_.matmul(
                              pu[0:cw, 0:TT], uwb[par][:, kc * 704 + c0: kc * 704 + c0 + cw],
                              x_bf[:, kc * T + tt * TT: kc * T + (tt + 1) * TT],
                              start=(kc == 0), stop=(kc == KC - 1)),
                          inc=(s_mm if kc == KC - 1 else None))
                ph.wait("scalar", s_mm, ctx.cnt[s_mm] - 1)
                ph.wait("scalar", s_ev, ctx.cnt[s_ev] - 2)
                if not is_shared:
                    ph.wait("scalar", s_hp, ctx.cnt[s_hp] - 1)
                ph.op("scalar",
                      lambda e_, cw=cw, pg=pg, sg=sg: e_.activation(
                          sg, pg[0:cw, 0:TT], AF.Silu, bias=0.0, scale=1.0),
                      inc=s_ev)
                ph.wait("vector", s_ev)
                ph.wait("vector", s_mm)
                if is_shared:
                    ph.op("vector",
                          lambda e_, ci=ci, cw=cw, pu=pu, sg=sg, hbase=hbase: e_.scalar_tensor_tensor(
                              ht[0:cw, hbase + ci * TT: hbase + (ci + 1) * TT], pu[0:cw, 0:TT],
                              kone[0:cw, :], sg, ALU.bypass, ALU.mult),
                          inc=s_ev)
                else:
                    ph.op("vector",
                          lambda e_, cw=cw, pu=pu, sg=sg: e_.scalar_tensor_tensor(
                              sg, pu[0:cw, 0:TT],
                              kone[0:cw, :], sg, ALU.bypass, ALU.mult),
                          inc=s_ev)
                    ph.wait("gpsimd", s_ev)
                    ph.op("gpsimd",
                          lambda e_, ci=ci, cw=cw, sg=sg, hbase=hbase: e_.tensor_mul(
                              ht[0:cw, hbase + ci * TT: hbase + (ci + 1) * TT], sg, rwb[0:cw, :]),
                          inc=s_hp)
            dn_info[tt] = (ctx.cnt[s_ev], ctx.cnt[s_hp])

        def emit_down(tt, dn_end):
            hbase = (tt % 2) * 6 * TT
            dn0, hp0 = dn_info[tt]
            for dch in range(KC):
                pd = psD if dch % 2 == 0 else psD2
                if dch < 2:
                    ph.wait("tensor", s_ev, dn0)
                    if dn_end is not None:
                        ph.wait("tensor", s_ev, dn_end)
                    if not is_shared:
                        ph.wait("tensor", s_hp, hp0)
                else:
                    ph.wait("tensor", s_ev, ctx.cnt[s_ev] - 2)
                for ci in range(6):
                    kw = 128 if ci < 5 else 64
                    ph.op("tensor",
                          lambda e_, ci=ci, kw=kw, dch=dch, par=par, pd=pd, hbase=hbase: e_.matmul(
                              pd[:], dwb[par][0:kw, ci * 1024 + dch * 128: ci * 1024 + (dch + 1) * 128],
                              ht[0:kw, hbase + ci * TT: hbase + (ci + 1) * TT],
                              start=(ci == 0), stop=(ci == 5)),
                          inc=(s_mm if ci == 5 else None))
                mmX = ctx.cnt[s_mm]
                ph.wait("vector", s_mm, mmX)
                c = dch * T + tt * TT
                ph.op("vector",
                      lambda e_, c=c, pd=pd: e_.scalar_tensor_tensor(
                          routed[:, c:c + TT], pd[:], kzero, routed[:, c:c + TT],
                          ALU.add, ALU.add),
                      inc=s_ev)
            return ctx.cnt[s_ev]

        dn_end = dn_prev.get("v")
        for tt in range(NTT):
            emit_gu(tt)
            if tt >= 1:
                dn_end = emit_down(tt - 1, dn_end)
        dn_end = emit_down(NTT - 1, dn_end)
        dn_prev["v"] = dn_end
        if e + 1 < NEXP:
            conv_done[e + 1] = ctx.cnt[s_cv]
    ph.run()

    ph = Ph(ctx, "f_fin")
    for kc in range(KC):
        for hf in range(2):
            ph.wait("sync", s_ev)
            ph.dma("sync", s_ld,
                   lambda e_, kc=kc, hf=hf: e_.dma_start(
                       out=stg[:, 0:1024],
                       in_=x_spill[:, kc * T + hf * 1024: kc * T + (hf + 1) * 1024]))
            ph.wait("vector", s_ld)
            for q in range(2):
                c = kc * T + hf * 1024 + q * TT
                ph.op("vector",
                      lambda e_, c=c, q=q: e_.tensor_add(
                          ar[:, c:c + TT], ar[:, c:c + TT], stg[:, q * TT:(q + 1) * TT]),
                      inc=s_ev)
    ph.run()

    ph = Ph(ctx, "f_ln")
    emit_postln(ph, ar=ar, sq=stg, ps_stat=[psA, psB], ps_b=[psA, psB],
                sem_mm=s_mm, sem_ev=s_ev,
                eps_ap=ctx.vecs[0:1, w["keps"]:w["keps"] + 1])
    ph.run()


def l0ffn_weights_host(inp, vb):
    w = {
        "gate_wT": _t(inp["L0_ffn_gate_w"]),
        "WgT": np.ascontiguousarray(np.transpose(np.asarray(inp["L0_ffn_Wg"], np.float32), (0, 2, 1))),
        "WuT": np.ascontiguousarray(np.transpose(np.asarray(inp["L0_ffn_Wu"], np.float32), (0, 2, 1))),
        "WdT": np.ascontiguousarray(np.transpose(np.asarray(inp["L0_ffn_Wd"], np.float32), (0, 2, 1))),
        "SgT": _t(inp["L0_ffn_Sg"]),
        "SuT": _t(inp["L0_ffn_Su"]),
        "SdT": _t(inp["L0_ffn_Sd"]),
    }
    gb = np.zeros((128, 1), np.float32)
    gb[0:64, 0] = np.asarray(inp["L0_ffn_gate_bias"], np.float32)
    idx = {
        "gbias": vb.add("f_gbias", gb),
        "kone": vb.add("k_one_f", np.full((128, 1), 1.0, np.float32)),
        "kzero": vb.add("k_zero_f", np.zeros((128, 1), np.float32)),
        "keps": vb.add("k_eps_f", np.full((128, 1), 1e-5, np.float32)),
    }
    return w, idx


def build_l0ffn_program(nblob, vec_idx):
    nc = bass.Bass("TRN2", target_bir_lowering=False, detect_race_conditions=False)
    ctx = Ctx(nc)
    x_in = nc.dram_tensor("x_in", [T, D], F32, kind="ExternalInput").ap()
    y_out = nc.dram_tensor("y_out", [T, D], F32, kind="ExternalOutput").ap()
    consts = nc.dram_tensor("consts", [130, 128], F32, kind="ExternalInput").ap()
    vecblob = nc.dram_tensor("vecblob", [128, nblob], F32, kind="ExternalInput").ap()
    w = {
        "gate_wT": nc.dram_tensor("gate_wT", [1024, 64], F32, kind="ExternalInput").ap(),
        "WgT": nc.dram_tensor("WgT", [64, 1024, 704], F32, kind="ExternalInput").ap(),
        "WuT": nc.dram_tensor("WuT", [64, 1024, 704], F32, kind="ExternalInput").ap(),
        "WdT": nc.dram_tensor("WdT", [64, 704, 1024], F32, kind="ExternalInput").ap(),
        "SgT": nc.dram_tensor("SgT", [1024, 1408], F32, kind="ExternalInput").ap(),
        "SuT": nc.dram_tensor("SuT", [1024, 1408], F32, kind="ExternalInput").ap(),
        "SdT": nc.dram_tensor("SdT", [1408, 1024], F32, kind="ExternalInput").ap(),
    }
    w.update(vec_idx)
    x_spill = nc.dram_tensor("x_spill", [128, KC * T], F32).ap()
    rw_hbm = nc.dram_tensor("rw_scr", [64, T], BF16).ap()

    common_setup(ctx, nblob)
    ar = ctx.sbuf("ar", [128, KC * T], F32)

    s_ld = [ctx.sem("g_ld0"), ctx.sem("g_ld1")]
    s_cst = ctx.sem("g_cst")
    s_tp = ctx.sem("g_tp")
    s_ev = ctx.sem("g_ev")
    s_st = [ctx.sem("g_st0"), ctx.sem("g_st1")]

    with ctx.scope() as sc:
        tmp_tm = [sc.sbuf(f"tin{i}", [128, D], F32) for i in range(2)]
        ps_t = [sc.psum(f"g_ps{i}", [128, TT]) for i in range(2)]
        ph = Ph(ctx, "in")
        emit_load_consts(ph, ctx, consts, None, vecblob, s_cst)
        emit_transpose_in(ph, hbm_tm=x_in, ar=ar, sem_ld=s_ld, sem_tp=s_tp,
                          sem_ev=s_ev, ps_t=ps_t, tmp_tm=tmp_tm)
        ph.run()

    with ctx.scope() as sc:
        build_l0ffn(ctx, ar, w, sc, x_spill, rw_hbm)

    with ctx.scope() as sc:
        tmp_tm = [sc.sbuf(f"tout{i}", [128, D], F32) for i in range(2)]
        ps_t = [sc.psum(f"g_pso{i}", [128, TT]) for i in range(2)]
        ph = Ph(ctx, "out")
        emit_transpose_out(ph, ar=ar, hbm_tm=y_out, sem_tp=s_tp, sem_ev=s_ev,
                           sem_st=s_st, ps_t=ps_t, tmp_tm=tmp_tm)
        ph.wait("gpsimd", s_st[0])
        ph.wait("gpsimd", s_st[1])
        ph.run()

    ctx.close()
    return nc


def common_setup(ctx, nblob):
    ctx.ones_col_f32 = ctx.sbuf("c_ones_col", [128, 1], F32)
    ctx.ones_col_bf = ctx.sbuf("c_ones_col_bf", [128, 1], BF16)
    ctx.ones_row_f32 = ctx.sbuf("c_ones_row", [1, 128], F32)
    ctx.ones_row65 = ctx.sbuf("c_ones_row65", [65, 128], BF16)
    ctx.ones_row_bf = ctx.sbuf("c_ones_row_bf", [1, 128], BF16)
    ctx.ident_f32 = ctx.sbuf("c_idf", [128, 128], F32)
    ctx.ident_bf = ctx.sbuf("c_idb", [128, 128], BF16)
    ctx.ind_bf = ctx.sbuf("c_indb", [16, 8 * 128], BF16)
    ctx.vecs = ctx.sbuf("c_vecs", [128, max(nblob, 1)], F32)
    ctx.rows = ctx.sbuf("c_rows", [1, 4 * TT], F32)


def emit_load_consts(ph, ctx, consts_ap, ind_ap, vec_ap, sem_ld, ind_stage=None):
    ph.dma("sync", sem_ld, lambda e: e.dma_start(
        out=ctx.ones_col_f32[:], in_=consts_ap[0:1, :].rearrange("r p -> p r")))
    ph.dma("sync", sem_ld, lambda e: e.dma_start(out=ctx.ones_row_f32[0:1, :], in_=consts_ap[0:1, :]))

    ph.dma("sync", sem_ld, lambda e: e.dma_start(out=ctx.ident_f32[:], in_=consts_ap[2:130, :]))
    if ind_ap is not None:
        ph.dma("sync", sem_ld, lambda e: e.dma_start(out=ind_stage[0:16, :], in_=ind_ap[:, :]))
    if vec_ap is not None:
        ph.dma("sync", sem_ld, lambda e: e.dma_start(out=ctx.vecs[:], in_=vec_ap[:, :]))
    ph.wait("vector", sem_ld)
    ph.wait("tensor", sem_ld)
    ph.wait("scalar", sem_ld)
    ph.wait("gpsimd", sem_ld)
    ph.op("vector", lambda e: e.tensor_copy(ctx.ident_bf[:], ctx.ident_f32[:]))
    ph.op("vector", lambda e: e.tensor_copy(ctx.ones_row_bf[0:1, :], ctx.ones_row_f32[0:1, :]))
    ph.op("vector", lambda e: e.tensor_copy(ctx.ones_row65[0:1, :], ctx.ones_row_f32[0:1, :]))
    ph.op("vector", lambda e: e.tensor_copy(ctx.ones_row65[64:65, :], ctx.ones_row_f32[0:1, :]))
    ph.op("vector", lambda e: e.tensor_copy(ctx.ones_col_bf[:], ctx.ones_col_f32[:]))
    if ind_ap is not None:
        ph.op("vector", lambda e: e.tensor_copy(ctx.ind_bf[0:16, :], ind_stage[0:16, :]))


def make_ind():
    ind = np.zeros((16, 8 * 128), dtype=np.float32)
    for kc in range(8):
        ind[2 * kc, kc * 128: kc * 128 + 64] = 1.0
        ind[2 * kc + 1, kc * 128 + 64: kc * 128 + 128] = 1.0
    return ind


def _t(a):
    return np.ascontiguousarray(np.asarray(a, dtype=np.float32).T)


def _chunkvec(v):
    v = np.asarray(v, dtype=np.float32)
    n = v.shape[0] // 128
    return np.ascontiguousarray(v.reshape(n, 128).T)


def make_consts():
    c = np.zeros((130, 128), dtype=np.float32)
    c[0, :] = 1.0
    c[2:130, :] = np.eye(128, dtype=np.float32)
    return c


class VecBlob:
    def __init__(self):
        self.cols = []
        self.idx = {}

    def add(self, name, arr):
        self.idx[name] = sum(a.shape[1] for a in self.cols)
        self.cols.append(np.asarray(arr, dtype=np.float32))
        return self.idx[name]

    def blob(self):
        b = np.concatenate(self.cols, axis=1) if self.cols else np.zeros((128, 1), np.float32)
        return np.ascontiguousarray(b)


def l1ffn_weights_host(inp, vb):
    w = {
        "w1T": _t(inp["L1_ffn_w1_w"]),
        "w2T": _t(inp["L1_ffn_w2_w"]),
        "h1T": _t(inp["L1_ffn_h1_w"]),
        "h2T": _t(inp["L1_ffn_h2_w"]),
        "w3T": _t(inp["L1_ffn_w3_w"]),
    }
    idx = {
        "keps": vb.add("k_eps_ln", np.full((128, 1), 1e-5, np.float32)),
        "s1c": vb.add("ff_s1", _chunkvec(inp["L1_ffn_w1_s"])),
        "s2c": vb.add("ff_s2", _chunkvec(inp["L1_ffn_w2_s"])),
        "sh1c": vb.add("ff_sh1", _chunkvec(inp["L1_ffn_h1_s"])),
        "sh2c": vb.add("ff_sh2", _chunkvec(inp["L1_ffn_h2_s"])),
        "s3c": vb.add("ff_s3", _chunkvec(inp["L1_ffn_w3_s"])),
    }
    return w, idx


def build_l1ffn_program(nblob, vec_idx):
    nc = bass.Bass("TRN2", target_bir_lowering=False, detect_race_conditions=False)
    ctx = Ctx(nc)
    x_in = nc.dram_tensor("x_in", [T, D], F32, kind="ExternalInput").ap()
    y_out = nc.dram_tensor("y_out", [T, D], F32, kind="ExternalOutput").ap()
    consts = nc.dram_tensor("consts", [130, 128], F32, kind="ExternalInput").ap()
    vecblob = nc.dram_tensor("vecblob", [128, nblob], F32, kind="ExternalInput").ap()
    w = {k: nc.dram_tensor(k, list(shp), F32, kind="ExternalInput").ap()
         for k, shp in (("w1T", (1024, 4096)), ("w2T", (1024, 4096)),
                        ("h1T", (4096, 4096)), ("h2T", (4096, 4096)),
                        ("w3T", (4096, 1024)))}
    w.update(vec_idx)

    common_setup(ctx, nblob)
    ar = ctx.sbuf("ar", [128, KC * T], F32)

    s_ld = [ctx.sem("g_ld0"), ctx.sem("g_ld1")]
    s_cst = ctx.sem("g_cst")
    s_tp = ctx.sem("g_tp")
    s_ev = ctx.sem("g_ev")
    s_st = [ctx.sem("g_st0"), ctx.sem("g_st1")]
    ps_t = [ctx.psum(f"g_ps{i}", [128, TT]) for i in range(2)]

    with ctx.scope() as sc:
        tmp_tm = [sc.sbuf(f"tin{i}", [128, D], F32) for i in range(2)]
        ph = Ph(ctx, "in")
        emit_load_consts(ph, ctx, consts, None, vecblob, s_cst)
        emit_transpose_in(ph, hbm_tm=x_in, ar=ar, sem_ld=s_ld, sem_tp=s_tp,
                          sem_ev=s_ev, ps_t=ps_t, tmp_tm=tmp_tm)
        ph.run()

    with ctx.scope() as sc:
        build_l1ffn(ctx, ar, w, sc)

    with ctx.scope() as sc:
        tmp_tm = [sc.sbuf(f"tout{i}", [128, D], F32) for i in range(2)]
        ph = Ph(ctx, "out")
        emit_transpose_out(ph, ar=ar, hbm_tm=y_out, sem_tp=s_tp, sem_ev=s_ev,
                           sem_st=s_st, ps_t=ps_t, tmp_tm=tmp_tm)
        ph.wait("gpsimd", s_st[0])
        ph.wait("gpsimd", s_st[1])
        ph.run()

    ctx.close()
    return nc


def l0attn_weights_host(inp, vb):
    w = {
        "WqT": _t(inp["L0_attn_Wq"]),
        "WkT": _t(inp["L0_attn_Wk"]),
        "WvT": _t(inp["L0_attn_Wv"]),
        "WoT": _t(inp["L0_attn_Wo"]),
    }
    idx = {
        "bq": vb.add("a_bq", _chunkvec(inp["L0_attn_bq"])),
        "bk": vb.add("a_bk", _chunkvec(inp["L0_attn_bk"])),
        "bv": vb.add("a_bv", _chunkvec(inp["L0_attn_bv"])),
        "bo": vb.add("a_bo", _chunkvec(inp["L0_attn_bo"])),
        "kone": vb.add("k_one", np.full((128, 1), 1.0, np.float32)),
        "keps6": vb.add("k_eps6", np.full((128, 1), 1e-6, np.float32)),
        "keps": vb.add("k_eps_ln0", np.full((128, 1), 1e-5, np.float32)),
    }
    return w, idx


def make_l0_cc(nc):
    return {
        "r16_in": nc.dram_tensor("cc_r16i", [128, 16], F32).ap(),
        "r16_out": nc.dram_tensor("cc_r16o", [128, 16], F32).ap(),
        "d_in": nc.dram_tensor("cc_di", [16, 1], F32).ap(),
        "d_out": nc.dram_tensor("cc_do", [16, 1], F32).ap(),
        "kv_in": nc.dram_tensor("cc_kvi", [128, 1024], F32).ap(),
        "kv_out": nc.dram_tensor("cc_kvo", [128, 1024], F32).ap(),
    }


def build_l0attn_program(nblob, vec_idx, upto="H", debug=False):
    nc = bass.Bass("TRN2", target_bir_lowering=False, detect_race_conditions=False)
    ctx = Ctx(nc)
    x_in = nc.dram_tensor("x_in", [T, D], F32, kind="ExternalInput").ap()
    y_out = nc.dram_tensor("y_out", [T, D], F32, kind="ExternalOutput").ap()
    consts = nc.dram_tensor("consts", [130, 128], F32, kind="ExternalInput").ap()
    indc = nc.dram_tensor("indc", [16, 1024], F32, kind="ExternalInput").ap()
    vecblob = nc.dram_tensor("vecblob", [128, nblob], F32, kind="ExternalInput").ap()
    w = {k: nc.dram_tensor(k, [1024, 1024], F32, kind="ExternalInput").ap()
         for k in ("WqT", "WkT", "WvT", "WoT")}
    w.update(vec_idx)
    cc = make_l0_cc(nc)

    common_setup(ctx, nblob)
    ar = ctx.sbuf("ar", [128, KC * T], F32)

    s_ld = [ctx.sem("g_ld0"), ctx.sem("g_ld1")]
    s_cst = ctx.sem("g_cst")
    s_tp = ctx.sem("g_tp")
    s_ev = ctx.sem("g_ev")
    s_st = [ctx.sem("g_st0"), ctx.sem("g_st1")]

    with ctx.scope() as sc:
        tmp_tm = [sc.sbuf(f"tin{i}", [128, D], F32) for i in range(2)]
        ps_t = [sc.psum(f"g_ps{i}", [128, TT]) for i in range(2)]
        ph = Ph(ctx, "in")
        emit_load_consts(ph, ctx, consts, indc, vecblob, s_cst)
        emit_transpose_in(ph, hbm_tm=x_in, ar=ar, sem_ld=s_ld, sem_tp=s_tp,
                          sem_ev=s_ev, ps_t=ps_t, tmp_tm=tmp_tm)
        ph.run()

    dbg = {}
    if debug:
        for nm, shp, dt in (("d_q", [128, KC * T], BF16), ("d_k", [128, KC * T], BF16),
                            ("d_v", [128, KC * T], BF16), ("d_stats", [16, 3 * T], BF16),
                            ("d_r1", [128, 16], F32), ("d_r2", [128, 16], F32),
                            ("d_kvb", [128, 1024], BF16), ("d_dnm", [16, 2], F32),
                            ("d_ecol", [16, 4], F32)):
            dbg[nm] = nc.dram_tensor(nm, shp, dt, kind="ExternalOutput").ap()

    with ctx.scope() as sc:
        arenas = build_l0attn(ctx, ar, w, cc, sc, upto=upto)
        if debug:
            Qb, Kb, Vb, stats, r1, r2, kvb, dnm, ecol = arenas
            s_dbg = ctx.sem("dbg_st")
            ph = Ph(ctx, "dbg")
            for nm, t in (("d_q", Qb), ("d_k", Kb), ("d_v", Vb), ("d_stats", stats[0:16, :]),
                          ("d_r1", r1), ("d_r2", r2), ("d_kvb", kvb), ("d_dnm", dnm[0:16, :]),
                          ("d_ecol", ecol[0:16, 0:4])):
                ap = t[:, :] if not isinstance(t, bass.AP) else t
                ph.dma("sync", s_dbg, lambda e, ap=ap, nm=nm: e.dma_start(out=dbg[nm], in_=ap))
            ph.wait("gpsimd", s_dbg)
            ph.run()

    with ctx.scope() as sc:
        tmp_tm = [sc.sbuf(f"tout{i}", [128, D], F32) for i in range(2)]
        ps_t = [sc.psum(f"g_pso{i}", [128, TT]) for i in range(2)]
        ph = Ph(ctx, "out")
        emit_transpose_out(ph, ar=ar, hbm_tm=y_out, sem_tp=s_tp, sem_ev=s_ev,
                           sem_st=s_st, ps_t=ps_t, tmp_tm=tmp_tm)
        ph.wait("gpsimd", s_st[0])
        ph.wait("gpsimd", s_st[1])
        ph.run()

    ctx.close()
    return nc


def build_fused_program(nblob, aidx, fidx, midx, ffidx):
    nc = bass.Bass("TRN2", target_bir_lowering=False, detect_race_conditions=False)
    ctx = Ctx(nc)
    x_in = nc.dram_tensor("x_in", [T, D], F32, kind="ExternalInput").ap()
    y_out = nc.dram_tensor("y_out", [T, D], F32, kind="ExternalOutput").ap()
    consts = nc.dram_tensor("consts", [130, 128], F32, kind="ExternalInput").ap()
    indc = nc.dram_tensor("indc", [16, 1024], F32, kind="ExternalInput").ap()
    coreoh = nc.dram_tensor("coreoh", [128, 8], F32, kind="ExternalInput").ap()
    vecblob = nc.dram_tensor("vecblob", [128, nblob], F32, kind="ExternalInput").ap()

    wa = {k: nc.dram_tensor(k, [1024, 1024], F32, kind="ExternalInput").ap()
          for k in ("WqT", "WkT", "WvT", "WoT")}
    wa.update(aidx)
    wf = {
        "gate_wT": nc.dram_tensor("gate_wT", [1024, 64], F32, kind="ExternalInput").ap(),
        "WgT": nc.dram_tensor("WgT", [64, 1024, 704], F32, kind="ExternalInput").ap(),
        "WuT": nc.dram_tensor("WuT", [64, 1024, 704], F32, kind="ExternalInput").ap(),
        "WdT": nc.dram_tensor("WdT", [64, 704, 1024], F32, kind="ExternalInput").ap(),
        "SgT": nc.dram_tensor("SgT", [1024, 1408], F32, kind="ExternalInput").ap(),
        "SuT": nc.dram_tensor("SuT", [1024, 1408], F32, kind="ExternalInput").ap(),
        "SdT": nc.dram_tensor("SdT", [1408, 1024], F32, kind="ExternalInput").ap(),
    }
    wf.update(fidx)
    wm = {"out_wT": nc.dram_tensor("out_wT", [1024, 1024], F32, kind="ExternalInput").ap(),
          "w1T": nc.dram_tensor("m_w1T", [1024, 4096], F32, kind="ExternalInput").ap(),
          "w2T": nc.dram_tensor("m_w2T", [1024, 4096], F32, kind="ExternalInput").ap(),
          "w3T": nc.dram_tensor("m_w3T", [4096, 1024], F32, kind="ExternalInput").ap()}
    wm.update(midx)
    wff = {k: nc.dram_tensor("ff_" + k, list(shp), F32, kind="ExternalInput").ap()
           for k, shp in (("w1T", (1024, 4096)), ("w2T", (1024, 4096)),
                          ("h1T", (4096, 4096)), ("h2T", (4096, 4096)),
                          ("w3T", (4096, 1024)))}
    wff.update(ffidx)

    cc0 = make_l0_cc(nc)
    cc1 = make_l1_cc(nc)
    x_spill = nc.dram_tensor("x_spill", [128, KC * T], F32).ap()
    rw_hbm = nc.dram_tensor("rw_scr", [64, T], BF16).ap()

    common_setup(ctx, nblob)
    ctx.coreoh = ctx.sbuf("c_coreoh", [128, 8], F32)
    ar = ctx.sbuf("ar", [128, KC * T], F32)

    s_ld = [ctx.sem("g_ld0"), ctx.sem("g_ld1")]
    s_cst = ctx.sem("g_cst")
    s_tp = ctx.sem("g_tp")
    s_ev = ctx.sem("g_ev")
    s_st = [ctx.sem("g_st0"), ctx.sem("g_st1")]

    with ctx.scope() as sc:
        tmp_tm = [sc.sbuf(f"tin{i}", [128, D], F32) for i in range(2)]
        ps_t = [sc.psum(f"g_ps{i}", [128, TT]) for i in range(2)]
        ph = Ph(ctx, "in")
        emit_load_consts(ph, ctx, consts, indc, vecblob, s_cst, ind_stage=tmp_tm[0])
        ph.dma("sync", s_cst, lambda e: e.dma_start(out=ctx.coreoh[:], in_=coreoh))
        ph.wait("vector", s_cst)
        ph.op("vector", lambda e: e.engine_nop(), inc=s_ev)
        ph.wait("sync", s_ev)
        emit_transpose_in(ph, hbm_tm=x_in, ar=ar, sem_ld=s_ld, sem_tp=s_tp,
                          sem_ev=s_ev, ps_t=ps_t, tmp_tm=tmp_tm)
        ph.run()

    with ctx.scope() as sc:
        build_l0attn(ctx, ar, wa, cc0, sc)
    with ctx.scope() as sc:
        build_l0ffn(ctx, ar, wf, sc, x_spill, rw_hbm)
    with ctx.scope() as sc:
        build_l1attn(ctx, ar, wm, cc1, sc, x_spill)
    with ctx.scope() as sc:
        build_l1ffn(ctx, ar, wff, sc)

    with ctx.scope() as sc:
        tmp_tm = [sc.sbuf(f"tout{i}", [128, D], F32) for i in range(2)]
        ps_t = [sc.psum(f"g_pso{i}", [128, TT]) for i in range(2)]
        ph = Ph(ctx, "out")
        emit_transpose_out(ph, ar=ar, hbm_tm=y_out, sem_tp=s_tp, sem_ev=s_ev,
                           sem_st=s_st, ps_t=ps_t, tmp_tm=tmp_tm)
        ph.wait("gpsimd", s_st[0])
        ph.wait("gpsimd", s_st[1])
        ph.run()

    ctx.close()
    return nc


def kernel(**inp):
    consts = make_consts()
    x = np.ascontiguousarray(np.asarray(inp["L0_attn_x"], np.float32).reshape(TSEQ, D))

    vb = VecBlob()
    wa, aidx = l0attn_weights_host(inp, vb)
    wf, fidx = l0ffn_weights_host(inp, vb)
    wm, midx = l1attn_weights_host(inp, vb)
    wff, ffidx = l1ffn_weights_host(inp, vb)
    blob = vb.blob()

    nc = build_fused_program(blob.shape[1], aidx, fidx, midx, ffidx)

    feeds = []
    for c in range(NCORES):
        oh = np.zeros((128, 8), np.float32)
        oh[:, c] = 1.0
        m = {"x_in": np.ascontiguousarray(x[c * T:(c + 1) * T, :]),
             "consts": consts, "indc": make_ind(), "coreoh": oh, "vecblob": blob}
        m.update(wa)
        m.update(wf)
        m.update({("out_wT" if k == "out_wT" else "m_" + k): v for k, v in wm.items()})
        m.update({"ff_" + k: v for k, v in wff.items()})
        feeds.append(m)
    res = run_bass_kernel_spmd(nc, feeds, core_ids=list(range(NCORES)))
    x = np.concatenate([np.asarray(res.results[c]["y_out"]) for c in range(NCORES)], axis=0)
    return x.reshape(1, TSEQ, D)


def _run_chain_launch(nc, per_core_feeds):
    res = run_bass_kernel_spmd(nc, per_core_feeds, core_ids=list(range(NCORES)))
    return np.concatenate([np.asarray(res.results[c]["y_out"]) for c in range(NCORES)], axis=0)


def kernel_multilaunch(**inp):
    consts = make_consts()

    x = np.ascontiguousarray(np.asarray(inp["L0_attn_x"], np.float32).reshape(TSEQ, D))

    vb = VecBlob()
    w, idx = l0attn_weights_host(inp, vb)
    blob = vb.blob()
    nc = build_l0attn_program(blob.shape[1], idx)
    feeds = []
    for c in range(NCORES):
        m = {"x_in": np.ascontiguousarray(x[c * T:(c + 1) * T, :]),
             "consts": consts, "indc": make_ind(), "vecblob": blob}
        m.update(w)
        feeds.append(m)
    x = _run_chain_launch(nc, feeds)

    vb = VecBlob()
    w, idx = l0ffn_weights_host(inp, vb)
    blob = vb.blob()
    nc = build_l0ffn_program(blob.shape[1], idx)
    feeds = []
    for c in range(NCORES):
        m = {"x_in": np.ascontiguousarray(x[c * T:(c + 1) * T, :]),
             "consts": consts, "vecblob": blob}
        m.update(w)
        feeds.append(m)
    x = _run_chain_launch(nc, feeds)

    vb = VecBlob()
    w, idx = l1attn_weights_host(inp, vb)
    blob = vb.blob()
    nc = build_l1attn_program(blob.shape[1], idx)
    feeds = []
    for c in range(NCORES):
        oh = np.zeros((128, 8), np.float32)
        oh[:, c] = 1.0
        m = {"x_in": np.ascontiguousarray(x[c * T:(c + 1) * T, :]),
             "consts": consts, "coreoh": oh, "vecblob": blob}
        m.update(w)
        feeds.append(m)
    x = _run_chain_launch(nc, feeds)

    vb = VecBlob()
    w, idx = l1ffn_weights_host(inp, vb)
    blob = vb.blob()
    nc = build_l1ffn_program(blob.shape[1], idx)
    feeds = []
    for c in range(NCORES):
        m = {"x_in": np.ascontiguousarray(x[c * T:(c + 1) * T, :]),
             "consts": consts, "vecblob": blob}
        m.update(w)
        feeds.append(m)
    x = _run_chain_launch(nc, feeds)

    return x.reshape(1, TSEQ, D)
